# version 07
# baseline (speedup 1.0000x reference)
import numpy as np
import ml_dtypes
import concourse.bass as bass
import concourse.mybir as mybir
from concourse.bass_utils import run_bass_kernel_spmd

F32 = mybir.dt.float32
BF16 = mybir.dt.bfloat16
AF = mybir.ActivationFunctionType
ALU = mybir.AluOpType
AX = mybir.AxisListType

NCORES = 8
HID = 1024
H = 16
DH = 64
C = 64
TILE = 512
NCH = TILE // C
NPAIR = NCH // 2
EPS_LN = 1e-5
CLAMP = 60.0
BIG_W = ["Wq", "Wk", "Wv", "Wg", "Wo", "Wr", "Wkf", "Wvf"]

CI_I128 = 0
CI_ISTK = 128
CI_MPM = 192
CI_MSU = 256
CI_MSL = 320
CI_IND = 384
CI_ONES = 512
CI_M3B = 640
CI_SCAL = 832
CI_IREP8 = 848
NCONST = 1360


def _host_consts():
    c = np.zeros((128, NCONST), dtype=np.float32)
    c[np.arange(128), CI_I128 + np.arange(128)] = 1.0
    c[np.arange(128), CI_ISTK + np.tile(np.arange(64), 2)] = 1.0
    s = np.tile(np.arange(64), 2)[:, None]
    t = np.arange(64)[None, :]
    c[:, CI_MPM:CI_MPM + 64] = (s <= t)
    c[:, CI_MSU:CI_MSU + 64] = (s < t)
    c[:, CI_MSL:CI_MSL + 64] = (t < s)
    for k in range(8):
        c[0:64, CI_IND + 16 * k + 2 * k] = 1.0
        c[64:128, CI_IND + 16 * k + 2 * k + 1] = 1.0
    c[:, CI_ONES:CI_ONES + 128] = 1.0
    c[:, CI_M3B:CI_M3B + 64] = (s <= t)
    c[:, CI_M3B + 64:CI_M3B + 128] = (s < t)
    c[:, CI_M3B + 128:CI_M3B + 192] = 1.0
    c[:, CI_SCAL] = EPS_LN
    for a in range(8):
        c[np.arange(128), CI_IREP8 + 64 * a + (np.arange(128) % 64)] = 1.0
    return c


def ap(t, pbase, pcount, off, dims):
    rowlen = t.shape[1]
    return bass.AP(t, pbase * rowlen + off, [[rowlen, pcount]] + list(dims))


class Prog:
    def __init__(self, nb, t_len, debug=False, stop_after=None):
        self.stop_after = stop_after
        self.nb, self.T, self.debug = nb, t_len, debug
        self.ntb = t_len // TILE
        self.ntile = nb * self.ntb
        self.nc = bass.Bass("TRN2", target_bir_lowering=False)
        self.ctx, self.cnt, self.sems, self.sb, self.ps = [], {}, {}, {}, {}
        self.prog = {"sync": [], "vector": [], "scalar": [], "tensor": [], "gpsimd": []}

    def sem(self, n):
        s = self.nc.semaphore(n)
        self.sems[n] = s.__enter__()
        self.ctx.append(s)
        self.cnt[n] = 0

    def sbuf(self, n, shape, dt):
        t = self.nc.sbuf_tensor(n, shape, dt)
        self.sb[n] = t.__enter__()
        self.ctx.append(t)

    def psum(self, n, shape, dt=F32):
        t = self.nc.psum_tensor(n, shape, dt)
        self.ps[n] = t.__enter__()
        self.ctx.append(t)

    def w(self, eng, sem, val=None):
        v = self.cnt[sem] if val is None else val
        if v <= 0:
            return
        s = self.sems[sem]
        self.prog[eng].append(lambda e, s=s, v=v: e.wait_ge(s, v))

    def o(self, eng, build, sem=None, n=1):
        if eng in ("vector", "scalar"):
            ssem = "sv" if eng == "vector" else "ss"
            v = self.cnt[ssem]
            if v > 0:
                sh = self.sems[ssem]
                self.prog[eng].append(lambda e, sh=sh, v=v: e.wait_ge(sh, v))
        if sem is None:
            self.prog[eng].append(build)
        else:
            s = self.sems[sem]
            self.prog[eng].append(lambda e, s=s, b=build, n=n: b(e).then_inc(s, n))
            self.cnt[sem] += n

    def tok(self, sem):
        return self.cnt[sem]

    def build(self):
        nc = self.nc
        NTOK = self.nb * self.T
        self.x_hbm = nc.dram_tensor("x", [NTOK, HID], F32, kind="ExternalInput").ap()
        self.out_hbm = nc.dram_tensor("out", [NTOK, HID], F32, kind="ExternalOutput").ap()
        self.w_hbm = {n: nc.dram_tensor("W_" + n, [HID, HID], F32, kind="ExternalInput").ap()
                      for n in BIG_W}
        self.wab_hbm = nc.dram_tensor("Wab", [32, HID], F32, kind="ExternalInput").ap()
        self.bab_hbm = nc.dram_tensor("bab", [16, 2], F32, kind="ExternalInput").ap()
        self.mu_hbm = nc.dram_tensor("mu", [128, 16], F32, kind="ExternalInput").ap()
        self.cst_hbm = nc.dram_tensor("consts", [128, NCONST], F32, kind="ExternalInput").ap()
        self.wt_dram = {n: nc.dram_tensor("WT_" + n, [HID, HID], BF16).ap() for n in BIG_W}

        for s in ["swa", "swb", "sxa", "sxb", "sgr", "sdo", "sv", "ss", "stt", "sg"]:
            self.sem(s)

        S = self.sbuf
        S("cstf", [128, NCONST], F32)
        self.nc.const_aps.register(F32, EPS_LN,
                                   self.sb["cstf"][:, CI_SCAL:CI_SCAL + 1])
        S("cstb", [128, NCONST], BF16)
        S("wtb0", [128, 8 * HID], BF16)
        S("wtb1", [128, 8 * HID], BF16)
        S("wabt", [128, 8 * 32], BF16)
        S("babs", [16, 2], F32)
        S("babn", [16, 2], F32)
        S("mus", [128, 16], F32)
        S("xld0", [128, HID], F32)
        S("xld1", [128, HID], F32)
        S("xsq", [128, HID], F32)
        S("xntm", [128, HID], BF16)
        S("xnf", [128, 8 * TILE], BF16)
        S("xbnd", [128, 8], BF16)
        S("qk", [128, 16 * TILE], BF16)
        S("vtm", [128, 4 * HID], BF16)
        S("gf", [128, 8 * TILE], BF16)
        S("sq", [128, TILE], BF16)
        S("stat", [128, 8], F32)
        S("zsig", [16, 2 * TILE], F32)
        for n in ["la", "lc", "gamv", "uv", "nqv", "nkv", "chiv", "zetv", "omv",
                  "t16a", "t16b"]:
            S(n, [16, TILE], F32)
        S("chiT", [128, 4 * 16], F32)
        S("zetT", [128, 4 * 16], F32)
        S("omT", [128, 4 * 16], F32)
        S("gamrow", [1, 128], F32)
        S("gamrep", [128, 128], F32)
        S("csk", [128, H * 192], BF16)
        S("susl", [128, H * 64], BF16)
        S("ktm", [128, H * 64], BF16)
        S("zknx", [128, H * 64], BF16)
        S("yA", [128, H * 128], BF16)
        S("yB", [128, H * 128], BF16)
        S("pwx", [128, 5 * H * 64], BF16)
        S("pwt", [128, 2 * H * 64], BF16)
        S("oin", [128, 2 * 512], BF16)
        S("qef", [128, 2 * 512], BF16)
        S("gtm", [128, 2 * 512], BF16)
        S("um", [128, 2 * 512], BF16)
        S("mst", [128, H * 64], BF16)
        S("otm", [128, 4 * HID], BF16)
        S("branch", [128, 8 * TILE], BF16)
        S("ffa", [128, 8 * TILE], BF16)
        S("ffb", [128, 8 * TILE], BF16)
        S("outs", [128, HID], F32)

        Pm = self.psum
        Pm("pp0", [128, 512])
        Pm("pp1", [128, 512])
        Pm("ppsk", [128, 512])
        Pm("psl0", [128, 512])
        Pm("psl1", [128, 512])
        Pm("psq", [128, 512])
        Pm("ptr", [128, 512], BF16)
        Pm("pscl", [128, 512])

        if self.debug:
            self.dbg = {}
            for n, src in [("d_xnf", "xnf"), ("d_qk", "qk"),
                           ("d_vtm", "vtm"), ("d_csk", "csk"), ("d_susl", "susl"),
                           ("d_y", "yA"), ("d_oin", "oin"), ("d_qef", "qef"),
                           ("d_gtm", "gtm"), ("d_um", "um"), ("d_mst", "mst"),
                           ("d_otm", "otm"), ("d_branch", "branch"),
                           ("d_gf", "gf"), ("d_ffa", "ffa")]:
                t = self.sb[src]
                self.dbg[n] = (src, self.nc.dram_tensor(
                    n, [t.shape[0], t.shape[1]], BF16, kind="ExternalOutput").ap())
            for n, src in [("d_chi", "chiv"), ("d_gam", "gamv"), ("d_zet", "zetv"),
                           ("d_om", "omv"), ("d_nq", "nqv"), ("d_nk", "nkv"),
                           ("d_la", "la"), ("d_gamrep", "gamrep")]:
                t = self.sb[src]
                self.dbg[n] = (src, self.nc.dram_tensor(
                    n, [t.shape[0], t.shape[1]], F32, kind="ExternalOutput").ap())

        self.emit_all()

        with nc.Block() as blk:
            pr = self.prog

            @blk.sync
            def _(e):
                for f in pr["sync"]:
                    f(e)

            @blk.vector
            def _(e):
                for f in pr["vector"]:
                    f(e)

            @blk.scalar
            def _(e):
                for f in pr["scalar"]:
                    f(e)

            @blk.tensor
            def _(e):
                for f in pr["tensor"]:
                    f(e)

            @blk.gpsimd
            def _(e):
                for f in pr["gpsimd"]:
                    f(e)
        return nc

    def emit_all(self):
        self.ph_setup()
        self.ph_weight_prep()
        self.slot_free = {0: self.tok("stt"), 1: self.tok("stt")}
        self.xld_free = {0: self.tok("sv"), 1: self.tok("sv")}
        self.wtok = {}
        self.wslot = {}
        self.wload_next = 0
        self.load_w("Wq")
        for it in range(self.ntile):
            self.ph_tile(it)
        self.ph_finish()

    def load_w(self, wn):
        sb = self.sb
        slot = self.wload_next
        self.wload_next ^= 1
        self.wslot[wn] = slot
        sem = "swa" if slot == 0 else "swb"
        dstw = sb["wtb0"] if slot == 0 else sb["wtb1"]
        self.w("sync", "stt", self.slot_free.get(slot, 0))
        self.w("sync", "sv", self.prep_guard.get(sem, 0))
        self.prep_guard[sem] = 0
        self.w("sync", "sg", self.tok("sg"))
        for k in range(8):
            self.o("sync", lambda e, wn=wn, k=k, d=dstw: e.dma_start(
                out=d[:, k * HID:(k + 1) * HID],
                in_=self.wt_dram[wn][k * 128:(k + 1) * 128, :]), sem, 16)
        self.wtok[wn] = (sem, self.tok(sem))

    def ph_setup(self):
        sb = self.sb
        self.o("sync", lambda e: e.dma_start(out=sb["cstf"][:, :], in_=self.cst_hbm), "swa", 16)
        self.o("sync", lambda e: e.dma_start(out=sb["babs"][:, :], in_=self.bab_hbm), "swa", 16)
        self.o("sync", lambda e: e.dma_start(out=sb["mus"][:, :], in_=self.mu_hbm), "swa", 16)
        self.w("vector", "swa")
        self.o("vector", lambda e: e.tensor_copy(out=sb["cstb"][:, :], in_=sb["cstf"][:, :]), "sv")
        self.o("vector", lambda e: e.tensor_scalar(
            out=sb["babn"][:, :], in0=sb["babs"][:, :], scalar1=-1.0, scalar2=None,
            op0=ALU.mult), "sv")
        self.o("vector", lambda e: e.memset(sb["mst"][:, :], 0.0), "sv")
        self.o("vector", lambda e: e.memset(sb["xbnd"][:, :], 0.0), "sv")

    def ph_weight_prep(self):
        sb, ps = self.sb, self.ps
        self.prep_guard = {"swa": self.tok("sv"), "swb": 0}
        I128b = sb["cstb"][:, CI_I128:CI_I128 + 128]
        for wn in BIG_W:
            for j in range(8):
                buf = sb["xld0"] if j % 2 == 0 else sb["xld1"]
                psem = "swa" if j % 2 == 0 else "swb"
                self.w("sync", "sv", self.prep_guard.get(psem, 0))
                self.o("sync", lambda e, s=self.w_hbm[wn], j=j, b=buf: e.dma_start(
                    out=b[:, :], in_=s[j * 128:(j + 1) * 128, :]), psem, 16)
                self.w("vector", psem)
                cols = (j % 2) * HID
                self.o("vector", lambda e, b=buf, c=cols: e.tensor_copy(
                    out=sb["wtb1"][:, c:c + HID], in_=b[:, :]), "sv")
                self.prep_guard[psem] = self.tok("sv")
                for r in range(2):
                    self.w("tensor", "sv")
                    self.w("tensor", "sg")
                    for i in range(4):
                        ii = r * 4 + i
                        self.o("tensor", lambda e, ii=ii, c=cols, i=i: e.transpose(
                            ps["ptr"][:, i * 128:(i + 1) * 128],
                            sb["wtb1"][:, c + ii * 128:c + ii * 128 + 128], I128b),
                            "stt" if i == 3 else None)
                    self.w("vector", "stt")
                    self.o("vector", lambda e, r=r: e.tensor_copy(
                        out=sb["wtb0"][:, r * 512:(r + 1) * 512],
                        in_=ps["ptr"][:, 0:512]), "sv")
                    self.w("gpsimd", "sv")
                    for i in range(4):
                        ii = r * 4 + i
                        self.o("gpsimd", lambda e, ii=ii, i=i, r=r, wn=wn, j=j: e.dma_start(
                            out=self.wt_dram[wn][ii * 128:(ii + 1) * 128,
                                                 j * 128:(j + 1) * 128],
                            in_=sb["wtb0"][:, r * 512 + i * 128: r * 512 + (i + 1) * 128]),
                            "sg", 16)
        self.w("sync", "sv", self.prep_guard.get("swa", 0))
        self.o("sync", lambda e: e.dma_start(out=sb["xld0"][0:32, :], in_=self.wab_hbm),
               "swa", 16)
        self.w("vector", "swa")
        self.w("vector", "sg")
        self.o("vector", lambda e: e.tensor_copy(
            out=sb["wtb1"][0:32, 0:HID], in_=sb["xld0"][0:32, :]), "sv")
        self.w("tensor", "sv")
        self.w("tensor", "sg")
        for i in range(8):
            self.o("tensor", lambda e, i=i: e.transpose(
                ps["ptr"][:, (i % 4) * 32:(i % 4) * 32 + 32],
                sb["wtb1"][0:32, i * 128:i * 128 + 128],
                sb["cstb"][0:32, CI_I128:CI_I128 + 32]),
                "stt" if i % 4 == 3 else None)
            if i % 4 == 3:
                self.w("vector", "stt")
                r = i // 4
                self.o("vector", lambda e, r=r: e.tensor_copy(
                    out=sb["wabt"][:, r * 128:(r + 1) * 128], in_=ps["ptr"][:, 0:128]),
                    "sv")
                if r == 0:
                    self.w("tensor", "sv")

    def mm_proj(self, wn, evac, src=None, tokmajor=False):
        sb, ps = self.sb, self.ps
        srct = sb["xnf"] if src is None else sb[src]
        wt = sb["wtb0"] if self.wslot[wn] == 0 else sb["wtb1"]
        self.w("tensor", self.wtok[wn][0], self.wtok[wn][1])
        self.w("tensor", "sv")
        self.w("tensor", "ss")
        ev_tok = {0: None, 1: None}
        for m in range(8):
            bank = ps["pp0"] if m % 2 == 0 else ps["pp1"]
            if ev_tok[m % 2] is not None:
                self.w("tensor", ev_tok[m % 2][0], ev_tok[m % 2][1])
            for k in range(8):
                if tokmajor:
                    tb, oh = m // 2, m % 2
                    self.o("tensor", lambda e, k=k, b=bank, wt=wt, s=srct, tb=tb,
                           oh=oh: e.matmul(
                        b[:, 0:TILE],
                        s[:, k * TILE + tb * 128: k * TILE + (tb + 1) * 128],
                        wt[:, k * HID + oh * 512: k * HID + (oh + 1) * 512],
                        start=(k == 0), stop=(k == 7)), "stt" if k == 7 else None)
                else:
                    self.o("tensor", lambda e, m=m, k=k, b=bank, wt=wt, s=srct: e.matmul(
                        b[:, 0:TILE], wt[:, k * HID + m * 128: k * HID + m * 128 + 128],
                        s[:, k * TILE:(k + 1) * TILE],
                        start=(k == 0), stop=(k == 7)), "stt" if k == 7 else None)
            ev_tok[m % 2] = evac(m, bank)
        self.slot_free[self.wslot[wn]] = self.tok("stt")

    def ev_copy(self, dst, off=0):
        sb = self.sb
        def f(m, bank):
            self.w("vector", "stt")
            self.o("vector", lambda e, m=m, b=bank: e.tensor_copy(
                out=sb[dst][:, off + m * TILE:off + (m + 1) * TILE],
                in_=b[:, 0:TILE]), "sv")
            return ("sv", self.tok("sv"))
        return f

    def ph_tile(self, it):
        sb, ps = self.sb, self.ps
        ntb = self.ntb
        tt = it % ntb
        t0 = it * TILE
        first_in_batch = (tt == 0)

        if first_in_batch and it > 0:
            self.w("vector", "stt")
            self.o("vector", lambda e: e.memset(sb["mst"][:, :], 0.0), "sv")
            self.o("vector", lambda e: e.memset(sb["xbnd"][:, :], 0.0), "sv")

        if self.stop_after == "prep":
            return
        for st in range(4):
            buf = sb["xld0"] if st % 2 == 0 else sb["xld1"]
            self.w("gpsimd", "sv", getattr(self, "xld_free", {}).get(st % 2, 0))
            xsem = "sxa" if st % 2 == 0 else "sxb"
            self.o("gpsimd", lambda e, b=buf, st=st, t0=t0: e.dma_start(
                out=b[:, :], in_=self.x_hbm[t0 + st * 128: t0 + (st + 1) * 128, :]),
                xsem, 16)
            self.w("vector", xsem)
            self.w("scalar", xsem)
            self.o("vector", lambda e, b=buf: e.reduce_sum(
                out=sb["stat"][:, 0:1], in_=b[:, :], axis=AX.X), "sv")
            self.w("scalar", "sv")
            self.o("scalar", lambda e, b=buf: e.activation(
                out=sb["xsq"][:, :], in_=b[:, :], func=AF.Square,
                accum_out=sb["stat"][:, 1:2]), "ss")
            self.w("vector", "ss")
            self.o("vector", lambda e: e.tensor_scalar(
                out=sb["stat"][:, 2:3], in0=sb["stat"][:, 0:1],
                scalar1=1.0 / HID, scalar2=None, op0=ALU.mult), "sv")
            self.o("vector", lambda e: e.tensor_scalar(
                out=sb["stat"][:, 4:5], in0=sb["stat"][:, 1:2],
                scalar1=1.0 / HID, scalar2=None, op0=ALU.mult), "sv")
            self.o("vector", lambda e: e.scalar_tensor_tensor(
                out=sb["stat"][:, 5:6], in0=sb["stat"][:, 2:3],
                scalar=sb["stat"][:, 2:3], in1=sb["stat"][:, 4:5],
                op0=ALU.mult, op1=ALU.subtract), "sv")
            self.w("scalar", "sv")
            self.o("scalar", lambda e: e.activation(
                out=sb["stat"][:, 6:7], in_=sb["stat"][:, 5:6], func=AF.Ln,
                scale=-1.0, bias=EPS_LN), "ss")
            self.o("scalar", lambda e: e.activation(
                out=sb["stat"][:, 7:8], in_=sb["stat"][:, 6:7], func=AF.Exp,
                scale=-0.5), "ss")
            self.w("vector", "ss")
            self.o("vector", lambda e, b=buf: e.tensor_scalar(
                out=sb["xntm"][:, :], in0=b[:, :], scalar1=sb["stat"][:, 2:3],
                scalar2=sb["stat"][:, 7:8], op0=ALU.subtract, op1=ALU.mult), "sv")
            if not hasattr(self, "xld_free"):
                self.xld_free = {}
            self.xld_free[st % 2] = self.tok("sv")
            self.w("tensor", "sv")
            for r in range(2):
                if r == 1:
                    self.w("tensor", "sv")
                for i in range(4):
                    k = r * 4 + i
                    self.o("tensor", lambda e, k=k, i=i: e.transpose(
                        ps["ptr"][:, i * 128:(i + 1) * 128],
                        sb["xntm"][:, k * 128:(k + 1) * 128],
                        sb["cstb"][:, CI_I128:CI_I128 + 128]),
                        "stt" if i == 3 else None)
                self.w("vector", "stt")
                self.o("vector", lambda e, r=r, st=st: e.tensor_copy(
                    out=ap(sb["xnf"], 0, 128, r * 4 * TILE + st * 128,
                           [[TILE, 4], [1, 128]]),
                    in_=ps["ptr"][:, 0:512]), "sv")

        if self.stop_after == "ln":
            return
        self.load_w("Wk")
        self.mm_proj("Wq", self.ev_copy("qk", 0))
        self.load_w("Wv")
        self.mm_proj("Wk", self.ev_copy("qk", 8 * TILE))
        self.load_w("Wr")

        def ev_vtm(m, bank):
            tb, oh = m // 2, m % 2
            self.w("vector", "stt")
            self.o("vector", lambda e, b=bank, tb=tb, oh=oh: e.tensor_copy(
                out=sb["vtm"][:, tb * HID + oh * 512: tb * HID + (oh + 1) * 512],
                in_=b[:, 0:TILE]), "sv")
            return ("sv", self.tok("sv"))
        self.mm_proj("Wv", ev_vtm, tokmajor=True)

        if self.stop_after == "proj":
            return
        self.ph_scales(it)

        if self.stop_after == "scales":
            return
        for a in range(NPAIR):
            self.ph_pair(it, a)
        if self.stop_after in ("pskv", "pskv1", "pskvnoev", "pskv1noev",
                               "builds", "solve", "post", "pairs"):
            return

        self.ph_ffn(it, first_in_batch)
        if self.stop_after == "ffn":
            return

        self.ph_gate_wo(it)

        self.ph_epilogue(it)

        if self.debug and it == self.ntile - 1:
            self.ph_debug_dump()

    def ph_scales(self, it):
        sb, ps = self.sb, self.ps
        self.w("tensor", "sv")
        for k in range(8):
            self.o("tensor", lambda e, k=k: e.matmul(
                ps["pscl"][0:16, 0:TILE], sb["wabt"][:, k * 32:k * 32 + 16],
                sb["xnf"][:, k * TILE:(k + 1) * TILE],
                start=(k == 0), stop=(k == 7)))
            self.o("tensor", lambda e, k=k: e.matmul(
                ps["psq"][0:16, 0:TILE], sb["wabt"][:, k * 32 + 16:k * 32 + 32],
                sb["xnf"][:, k * TILE:(k + 1) * TILE],
                start=(k == 0), stop=(k == 7)), "stt" if k == 7 else None)
        self.w("scalar", "stt")
        self.o("scalar", lambda e: e.activation(
            out=sb["zsig"][:, 0:TILE], in_=ps["pscl"][0:16, 0:TILE], func=AF.Sigmoid,
            bias=sb["babs"][:, 0:1]), "ss")
        self.o("scalar", lambda e: e.activation(
            out=sb["zsig"][:, TILE:2 * TILE], in_=ps["psq"][0:16, 0:TILE], func=AF.Sigmoid,
            bias=sb["babs"][:, 1:2]), "ss")
        self.w("vector", "ss")
        self.o("vector", lambda e: e.tensor_scalar(
            out=sb["t16a"][:, :], in0=sb["zsig"][0:16, 0:TILE], scalar1=1e-35,
            scalar2=None, op0=ALU.max), "sv")
        self.w("scalar", "sv")
        self.o("scalar", lambda e: e.activation(
            out=sb["la"][:, :], in_=sb["t16a"][:, :], func=AF.Ln), "ss")
        self.w("vector", "ss")
        for c in range(NCH):
            self.o("vector", lambda e, c=c: e.tensor_tensor_scan(
                out=sb["lc"][:, c * C:(c + 1) * C],
                data0=sb["la"][:, c * C:(c + 1) * C],
                data1=sb["la"][:, c * C:(c + 1) * C], initial=0.0,
                op0=ALU.add, op1=ALU.bypass), "sv")
        self.w("scalar", "sv")
        self.o("scalar", lambda e: e.activation(
            out=sb["gamv"][:, :], in_=sb["lc"][:, :], func=AF.Exp), "ss")
        self.w("vector", "ss")
        self.o("vector", lambda e: e.tensor_scalar(
            out=sb["t16a"][:, :], in0=sb["lc"][:, :], scalar1=-1.0, scalar2=CLAMP,
            op0=ALU.mult, op1=ALU.min), "sv")
        self.w("scalar", "sv")
        self.o("scalar", lambda e: e.activation(
            out=sb["uv"][:, :], in_=sb["t16a"][:, :], func=AF.Exp), "ss")

        for (nm, soff) in [("nqv", 0), ("nkv", 8 * TILE)]:
            for k in range(8):
                self.w("vector", "stt")
                self.o("vector", lambda e, k=k, soff=soff: e.tensor_tensor(
                    out=sb["sq"][:, :],
                    in0=sb["qk"][:, soff + k * TILE:soff + (k + 1) * TILE],
                    in1=sb["qk"][:, soff + k * TILE:soff + (k + 1) * TILE],
                    op=ALU.mult), "sv")
                self.w("tensor", "sv")
                self.o("tensor", lambda e, k=k: e.matmul(
                    ps["psq"][0:16, 0:TILE],
                    sb["cstb"][:, CI_IND + 16 * k: CI_IND + 16 * (k + 1)],
                    sb["sq"][:, :], start=(k == 0), stop=(k == 7)), "stt")
            self.w("vector", "stt")
            self.w("vector", "ss")
            self.o("vector", lambda e: e.tensor_scalar(
                out=sb["t16a"][:, :], in0=ps["psq"][0:16, 0:TILE],
                scalar1=1e-12, scalar2=None, op0=ALU.max), "sv")
            self.w("scalar", "sv")
            self.o("scalar", lambda e: e.activation(
                out=sb["t16b"][:, :], in_=sb["t16a"][:, :], func=AF.Ln), "ss")
            self.o("scalar", lambda e, nm=nm: e.activation(
                out=sb[nm][:, :], in_=sb["t16b"][:, :], func=AF.Exp, scale=-0.5), "ss")

        self.w("vector", "ss")
        self.o("vector", lambda e: e.tensor_tensor(
            out=sb["t16a"][:, :], in0=sb["nkv"][:, :], in1=sb["nkv"][:, :],
            op=ALU.mult), "sv")
        self.o("vector", lambda e: e.tensor_tensor(
            out=sb["t16a"][:, :], in0=sb["t16a"][:, :], in1=sb["uv"][:, :],
            op=ALU.mult), "sv")
        self.o("vector", lambda e: e.tensor_tensor(
            out=sb["t16a"][:, :], in0=sb["t16a"][:, :], in1=sb["gamv"][:, :],
            op=ALU.mult), "sv")
        self.o("vector", lambda e: e.tensor_tensor(
            out=sb["chiv"][:, :], in0=sb["t16a"][:, :], in1=sb["zsig"][0:16, TILE:2 * TILE],
            op=ALU.mult), "sv")
        self.o("vector", lambda e: e.tensor_tensor(
            out=sb["t16b"][:, :], in0=sb["nkv"][:, :], in1=sb["gamv"][:, :],
            op=ALU.mult), "sv")
        self.o("vector", lambda e: e.reciprocal(
            out=sb["zetv"][:, :], in_=sb["t16b"][:, :]), "sv")
        self.o("vector", lambda e: e.tensor_tensor(
            out=sb["omv"][:, :], in0=sb["nqv"][:, :], in1=sb["gamv"][:, :],
            op=ALU.mult), "sv")

        self.w("tensor", "sv")
        for (vn, tn) in [("chiv", "chiT"), ("zetv", "zetT"), ("omv", "omT")]:
            for blk in range(4):
                self.o("tensor", lambda e, vn=vn, blk=blk: e.transpose(
                    ps["pscl"][:, 256 + blk * 16: 256 + (blk + 1) * 16],
                    sb[vn][:, blk * 128:(blk + 1) * 128],
                    sb["cstf"][0:16, CI_I128:CI_I128 + 16]),
                    "stt" if blk == 3 else None)
            self.w("vector", "stt")
            self.o("vector", lambda e, tn=tn: e.tensor_copy(
                out=sb[tn][:, :], in_=ps["pscl"][:, 256:256 + 64]), "sv")
            self.w("tensor", "sv")

        self.w("gpsimd", "sv")
        self.o("gpsimd", lambda e: e.dma_start(
            out=sb["gamrow"][0:1, 0:128],
            in_=ap(sb["gamv"], 0, 16, C - 1, [[C, NCH]]),
            allow_slow_non_contiguous=True), "sgr", 16)
        self.w("tensor", "sgr")
        self.o("tensor", lambda e: e.matmul(
            ps["pscl"][:, 384:384 + 128], sb["cstf"][0:1, CI_ONES:CI_ONES + 128],
            sb["gamrow"][0:1, 0:128], start=True, stop=True), "stt")
        self.w("vector", "stt")
        self.o("vector", lambda e: e.tensor_copy(
            out=sb["gamrep"][:, :], in_=ps["pscl"][:, 384:384 + 128]), "sv")

    def ph_pair(self, it, a):
        sb, ps = self.sb, self.ps

        def oin_gather(par, pc):
            return ap(self.sb["oin"], pc, 64, par * 64, [[512, 2], [128, 4], [1, 64]])

        def um_gather(par, phh, cc):
            return ap(self.sb["um"], phh, 64, cc * 512, [[64, 8], [1, 64]])
        cstb = self.sb["cstb"]
        c0 = 2 * a
        if not hasattr(self, "btok"):
            self.btok = {}
        btok = self.btok

        def bank_wait(bn):
            t = btok.get(bn)
            if t is not None:
                self.w("tensor", t[0], t[1])

        def chT(h):
            return sb["chiT"][:, a * 16 + h: a * 16 + h + 1]

        def zeT(h):
            return sb["zetT"][:, a * 16 + h: a * 16 + h + 1]

        POOL = [("ppsk", "psl1"), ("psq", "psl0")]

        nrounds = 1 if self.stop_after in ("pskv1", "pskv1noev") else 8
        noev = self.stop_after in ("pskvnoev", "pskv1noev")
        for rnd in range(nrounds):
            be, bo = POOL[rnd % 2]
            bank_wait(be)
            bank_wait(bo)
            for cc in (0, 1):
                pc = 64 * cc
                for si in range(2):
                    h = 2 * rnd + si
                    ph = 64 * (h % 2)
                    bank = ps[be] if h % 2 == 0 else ps[bo]
                    cl = (h // 2) * TILE + (c0 + cc) * C
                    self.o("tensor", lambda e, ph=ph, cl=cl, pc=pc, b=bank: e.matmul(
                        b[pc:pc + 64, 0:128],
                        sb["qk"][ph:ph + 64, 8 * TILE + cl:8 * TILE + cl + 64],
                        ap(sb["qk"], ph, 64, cl, [[8 * TILE, 2], [1, 64]]),
                        start=True, stop=True, tile_position=(ph, pc)))
                    self.o("tensor", lambda e, ph=ph, cl=cl, pc=pc, b=bank: e.matmul(
                        b[pc:pc + 64, 128:192],
                        sb["qk"][ph:ph + 64, 8 * TILE + cl:8 * TILE + cl + 64],
                        cstb[ph:ph + 64, CI_ISTK:CI_ISTK + 64],
                        start=True, stop=True, tile_position=(ph, pc)),
                        "stt" if (si == 1 and cc == 1) else None)
            if noev:
                continue
            mmtok = self.tok("stt")

            def dve_evacs(h, bn):
                self.o("vector", lambda e, h=h, b=ps[bn]: e.scalar_tensor_tensor(
                    out=ap(sb["csk"], 0, 128, h * 192, [[1, 192]]),
                    in0=ap(b, 0, 128, 0, [[1, 192]]),
                    scalar=chT(h),
                    in1=ap(sb["cstf"], 0, 128, CI_M3B, [[1, 192]]),
                    op0=ALU.mult, op1=ALU.mult), "sv")
                self.o("vector", lambda e, h=h, b=ps[bn]: e.tensor_tensor(
                    out=ap(sb["susl"], 0, 128, h * 64, [[1, 64]]),
                    in0=ap(b, 0, 128, 64, [[1, 64]]),
                    in1=ap(sb["cstf"], 0, 128, CI_MSL, [[1, 64]]),
                    op=ALU.mult), "sv")

            def sc_evacs(h, bn):
                self.o("scalar", lambda e, h=h, a=a: e.activation(
                    out=ap(sb["yA"], 0, 128, h * 128, [[1, 64]]),
                    in_=ap(sb["vtm"], 0, 128, a * HID + h * 64, [[1, 64]]),
                    func=AF.Copy, scale=zeT(h)), "ss")
                self.o("scalar", lambda e, h=h, b=ps[bn]: e.activation(
                    out=ap(sb["yA"], 0, 128, h * 128 + 64, [[1, 64]]),
                    in_=ap(b, 0, 128, 128, [[1, 64]]),
                    func=AF.Copy, scale=-1.0), "ss")
                self.o("scalar", lambda e, h=h, b=ps[bn]: e.activation(
                    out=ap(sb["ktm"], 0, 128, h * 64, [[1, 64]]),
                    in_=ap(b, 0, 128, 128, [[1, 64]]),
                    func=AF.Copy), "ss")

            he, ho = 2 * rnd, 2 * rnd + 1
            self.w("vector", "stt", mmtok)
            self.w("scalar", "stt", mmtok)
            dve_evacs(he, be)
            sc_evacs(ho, bo)
            self.w("vector", "ss")
            self.w("scalar", "sv")
            dve_evacs(ho, bo)
            sc_evacs(he, be)
            btok[be] = ("ss", self.tok("ss"))
            btok[bo] = ("sv", self.tok("sv"))
        if self.stop_after in ("pskv", "pskv1", "pskvnoev", "pskv1noev"):
            return

        def sl_slice(h, pc):
            return ap(sb["susl"], pc, 64, h * 64, [[1, 64]])

        def sux_slice(h, pc):
            return ap(sb["csk"], pc, 64, h * 192 + 64, [[1, 64]])

        def pwx_slice(l, h, pc):
            return ap(sb["pwx"], pc, 64, l * H * 64 + h * 64, [[1, 64]])

        def pwt_slice(l, h, pc):
            return ap(sb["pwt"], pc, 64, (l % 2) * H * 64 + h * 64, [[1, 64]])

        self.w("tensor", "sv")
        self.w("tensor", "ss")
        wv = 0
        for L in range(5):
            for part in range(2):
                if L == 4 and part == 1:
                    break
                be, bo = POOL[wv % 2]
                wv += 1
                bank_wait(be)
                bank_wait(bo)
                if L > 0:
                    self.w("tensor", "sv")
                    self.w("tensor", "ss")
                for cc in (0, 1):
                    pc = 64 * cc
                    for h in range(H):
                        bank = ps[be] if h < 8 else ps[bo]
                        colb = (h % 8) * 64
                        if L == 0:
                            stat = sl_slice(h, pc) if part == 0 else sux_slice(h, pc)
                            mv = sux_slice(h, pc) if part == 0 else sl_slice(h, pc)
                        else:
                            stat = pwt_slice(L - 1, h, pc) if part == 0 else pwx_slice(L - 1, h, pc)
                            mv = pwx_slice(L - 1, h, pc) if part == 0 else pwt_slice(L - 1, h, pc)
                        self.o("tensor", lambda e, st=stat, mv=mv, pc=pc, colb=colb,
                               bank=bank: e.matmul(
                            bank[pc:pc + 64, colb:colb + 64], st, mv,
                            start=True, stop=True, tile_position=(pc, pc)),
                            "stt" if (h == H - 1 and cc == 1) else None)
                self.w("vector", "stt")
                self.w("scalar", "stt")
                for bi, bn in enumerate((be, bo)):
                    if part == 0:
                        self.o("vector", lambda e, L=L, bi=bi, bn=bn: e.tensor_tensor(
                            out=ap(sb["pwx"], 0, 128, L * H * 64 + bi * 512, [[1, 512]]),
                            in0=ap(ps[bn], 0, 128, 0, [[1, 512]]),
                            in1=ap(sb["chiT"], 0, 128, a * 16 + bi * 8, [[1, 8], [0, 64]]),
                            op=ALU.mult), "sv")
                    else:
                        self.o("scalar", lambda e, L=L, bi=bi, bn=bn: e.activation(
                            out=ap(sb["pwt"], 0, 128, (L % 2) * H * 64 + bi * 512,
                                   [[1, 512]]),
                            in_=ap(ps[bn], 0, 128, 0, [[1, 512]]),
                            func=AF.Copy), "ss")
                    btok[bn] = ("sv", self.tok("sv")) if part == 0 else \
                        ("ss", self.tok("ss"))
        if self.stop_after == "builds":
            return

        stages = [(4, ALU.add), (3, ALU.add), (2, ALU.add), (1, ALU.add),
                  (0, ALU.add), (-1, ALU.subtract)]
        cur, nxt = "yA", "yB"
        self.w("tensor", "ss")
        bi = 0
        for sidx, (L, op) in enumerate(stages):
            stage_in_tok = self.tok("sv")
            for quad in range(4):
                bn = [b for pr in POOL for b in pr][bi % 4]
                bi += 1
                bank_wait(bn)
                self.w("tensor", "sv", stage_in_tok)
                bank = ps[bn]
                for hh in range(4):
                    h = quad * 4 + hh
                    for cc in (0, 1):
                        pc = 64 * cc
                        stat = sux_slice(h, pc) if L < 0 else pwx_slice(L, h, pc)
                        self.o("tensor", lambda e, st=stat, h=h, hh=hh, pc=pc,
                               bank=bank, cur=cur: e.matmul(
                            bank[pc:pc + 64, hh * 128:(hh + 1) * 128], st,
                            ap(sb[cur], pc, 64, h * 128, [[1, 128]]),
                            start=True, stop=True, tile_position=(pc, pc)),
                            "stt" if (hh == 3 and cc == 1) else None)
                self.w("vector", "stt")
                if op == ALU.add:
                    self.o("vector", lambda e, quad=quad, bank=bank, cur=cur,
                           nxt=nxt: e.tensor_tensor(
                        out=ap(sb[nxt], 0, 128, quad * 512, [[1, 512]]),
                        in0=ap(bank, 0, 128, 0, [[1, 512]]),
                        in1=ap(sb[cur], 0, 128, quad * 512, [[1, 512]]),
                        op=ALU.add), "sv")
                else:
                    self.o("vector", lambda e, quad=quad, bank=bank, cur=cur,
                           nxt=nxt: e.tensor_tensor(
                        out=ap(sb[nxt], 0, 128, quad * 512, [[1, 512]]),
                        in0=ap(sb[cur], 0, 128, quad * 512, [[1, 512]]),
                        in1=ap(bank, 0, 128, 0, [[1, 512]]),
                        op=ALU.subtract), "sv")
                btok[bn] = ("sv", self.tok("sv"))
            cur, nxt = nxt, cur
        zf = cur

        self.o("vector", lambda e, zf=zf: e.tensor_tensor(
            out=ap(sb["zknx"], 0, 128, 0, [[64, 16], [1, 64]]),
            in0=ap(sb[zf], 0, 128, 64, [[128, 16], [1, 64]]),
            in1=ap(sb["chiT"], 0, 128, a * 16, [[1, 16], [0, 64]]),
            op=ALU.mult), "sv")
        zknx_tok = self.tok("sv")
        if self.stop_after == "solve":
            return

        def bankcol(h):
            return (h // 2) * 64

        post_in_tok = self.tok("sv")

        def post_wave(mm_fn, evac_fn, need_tok):
            be, bo = POOL[self.pwave % 2]
            self.pwave += 1
            bank_wait(be)
            bank_wait(bo)
            self.w("tensor", "sv", need_tok)
            mm_fn(ps[be], ps[bo])
            self.w("vector", "stt")
            evac_fn(be, bo)
            btok[be] = ("sv", self.tok("sv"))
            btok[bo] = ("sv", self.tok("sv"))

        self.pwave = getattr(self, "pwave", 0)

        def mm_oin(bA, bB):
            for h in range(H):
                bk = bA if h < 8 else bB
                cb = (h % 8) * 64
                for cc in (0, 1):
                    pc = 64 * cc
                    self.o("tensor", lambda e, h=h, pc=pc, b=bk, cb=cb, zf=zf: e.matmul(
                        b[pc:pc + 64, cb:cb + 64],
                        ap(sb["csk"], pc, 64, h * 192, [[1, 64]]),
                        ap(sb[zf], pc, 64, h * 128, [[1, 64]]),
                        start=True, stop=True, tile_position=(pc, pc)),
                        "stt" if (h == H - 1 and cc == 1) else None)

        def ev_oin(be, bo):
            for g, bn in enumerate((be, bo)):
                self.o("vector", lambda e, g=g, bn=bn, a=a: e.tensor_tensor(
                    out=ap(sb["oin"], 0, 128, g * 512, [[1, 512]]),
                    in0=ap(ps[bn], 0, 128, 0, [[1, 512]]),
                    in1=ap(sb["omT"], 0, 128, a * 16 + g * 8, [[1, 8], [0, 64]]),
                    op=ALU.mult), "sv")
        post_wave(mm_oin, ev_oin, post_in_tok)

        def mm_qef(bA, bB):
            for h in range(H):
                ph = 64 * (h % 2)
                for cc in (0, 1):
                    pc = 64 * cc
                    b = bA if cc == 0 else bB
                    self.o("tensor", lambda e, h=h, ph=ph, pc=pc, b=b, zf=zf: e.matmul(
                        b[ph:ph + 64, bankcol(h):bankcol(h) + 64],
                        ap(sb[zf], pc, 64, h * 128 + 64, [[1, 64]]),
                        ap(sb["csk"], pc, 64, h * 192, [[1, 64]]),
                        start=True, stop=True, tile_position=(pc, ph)),
                        "stt" if (h == H - 1 and cc == 1) else None)

        def ev_qef(be, bo):
            for cc, bn in enumerate((be, bo)):
                self.o("vector", lambda e, cc=cc, bn=bn, c0=c0: e.tensor_tensor(
                    out=ap(sb["qef"], 0, 128, cc * 512, [[1, 512]]),
                    in0=ap(ps[bn], 0, 128, 0, [[1, 512]]),
                    in1=ap(sb["qk"], 0, 128, (c0 + cc) * C, [[TILE, 8], [1, 64]]),
                    op=ALU.add), "sv")
        post_wave(mm_qef, ev_qef, post_in_tok)

        def mm_gtm(bA, bB):
            for h in range(H):
                ph = 64 * (h % 2)
                for cc in (0, 1):
                    pc = 64 * cc
                    b = bA if cc == 0 else bB
                    self.o("tensor", lambda e, h=h, ph=ph, pc=pc, b=b: e.matmul(
                        b[ph:ph + 64, bankcol(h):bankcol(h) + 64],
                        ap(sb["zknx"], pc, 64, h * 64, [[1, 64]]),
                        ap(sb["ktm"], pc, 64, h * 64, [[1, 64]]),
                        start=True, stop=True, tile_position=(pc, ph)),
                        "stt" if (h == H - 1 and cc == 1) else None)

        def ev_gtm(be, bo):
            for cc, bn in enumerate((be, bo)):
                self.o("vector", lambda e, cc=cc, bn=bn: e.tensor_tensor(
                    out=ap(sb["gtm"], 0, 128, cc * 512, [[1, 512]]),
                    in0=ap(ps[bn], 0, 128, 0, [[1, 512]]),
                    in1=ap(sb["cstb"], 0, 128, CI_IREP8, [[1, 512]]),
                    op=ALU.add), "sv")
        post_wave(mm_gtm, ev_gtm, zknx_tok)

        def mm_um(bA, bB):
            for h in range(H):
                ph = 64 * (h % 2)
                for cc in (0, 1):
                    pc = 64 * cc
                    b = bA if cc == 0 else bB
                    self.o("tensor", lambda e, h=h, ph=ph, pc=pc, b=b, zf=zf: e.matmul(
                        b[ph:ph + 64, bankcol(h):bankcol(h) + 64],
                        ap(sb["csk"], pc, 64, h * 192 + 128, [[1, 64]]),
                        ap(sb[zf], pc, 64, h * 128, [[1, 64]]),
                        start=True, stop=True, tile_position=(pc, ph)),
                        "stt" if (h == H - 1 and cc == 1) else None)

        def ev_um(be, bo):
            for cc, bn in enumerate((be, bo)):
                for half in (0, 1):
                    self.o("vector", lambda e, cc=cc, half=half, bn=bn, c0=c0: e.tensor_tensor(
                        out=ap(sb["um"], 64 * half, 64, cc * 512, [[1, 512]]),
                        in0=ap(ps[bn], 64 * half, 64, 0, [[1, 512]]),
                        in1=ap(sb["gamrep"], 64 * half, 64, half * 8 + c0 + cc,
                               [[16, 8], [0, 64]]),
                        op=ALU.mult), "sv")
        post_wave(mm_um, ev_um, post_in_tok)
        if self.stop_after == "post":
            return

        for cc in (0, 1):
            pc = 64 * cc
            self.w("tensor", "sv")
            for bn in ("psq", "ppsk", "psl0", "psl1"):
                bank_wait(bn)
            for h in range(H):
                ph = 64 * (h % 2)
                bank = ps["psq"] if h % 2 == 0 else ps["ppsk"]
                self.o("tensor", lambda e, h=h, ph=ph, pc=pc, cc=cc, bank=bank: e.matmul(
                    bank[pc:pc + 64, bankcol(h):bankcol(h) + 64],
                    ap(sb["qef"], ph, 64, cc * 512 + bankcol(h), [[1, 64]]),
                    ap(sb["mst"], ph, 64, h * 64, [[1, 64]]),
                    start=True, stop=True, tile_position=(ph, pc)),
                    "stt" if h == H - 1 else None)
            for h in range(H):
                ph = 64 * (h % 2)
                bank = ps["psl0"] if h % 2 == 0 else ps["psl1"]
                self.o("tensor", lambda e, h=h, ph=ph, cc=cc, bank=bank: e.matmul(
                    bank[ph:ph + 64, bankcol(h):bankcol(h) + 64],
                    ap(sb["gtm"], ph, 64, cc * 512 + bankcol(h), [[1, 64]]),
                    ap(sb["mst"], ph, 64, h * 64, [[1, 64]]),
                    start=True, stop=True, tile_position=(ph, ph)),
                    "stt" if h == H - 1 else None)
            self.w("vector", "stt")
            for par, bn in ((0, "psq"), (1, "ppsk")):
                self.o("vector", lambda e, par=par, bn=bn, pc=pc, a=a: e.tensor_tensor(
                    out=ap(sb["xsq"], pc, 64, 0, [[1, 512]]),
                    in0=ap(ps[bn], pc, 64, 0, [[1, 512]]),
                    in1=ap(sb["omT"], pc, 64, a * 16 + par, [[2, 8], [0, 64]]),
                    op=ALU.mult), "sv")
                self.o("vector", lambda e, par=par, pc=pc, a=a: e.tensor_tensor(
                    out=ap(sb["otm"], pc, 64, a * HID + par * 64, [[128, 8], [1, 64]]),
                    in0=ap(sb["xsq"], pc, 64, 0, [[1, 512]]),
                    in1=oin_gather(par, pc),
                    op=ALU.add), "sv")
            for par, bn in ((0, "psl0"), (1, "psl1")):
                phh = 64 * par
                self.o("vector", lambda e, par=par, bn=bn, phh=phh, cc=cc, c0=c0:
                       e.tensor_tensor(
                    out=ap(sb["xsq"], phh, 64, 512, [[1, 512]]),
                    in0=ap(ps[bn], phh, 64, 0, [[1, 512]]),
                    in1=ap(sb["gamrep"], phh, 64, par * 8 + c0 + cc, [[16, 8], [0, 64]]),
                    op=ALU.mult), "sv")
                self.o("vector", lambda e, par=par, phh=phh, cc=cc: e.tensor_tensor(
                    out=ap(sb["mst"], phh, 64, par * 64, [[128, 8], [1, 64]]),
                    in0=ap(sb["xsq"], phh, 64, 512, [[1, 512]]),
                    in1=um_gather(par, phh, cc),
                    op=ALU.add), "sv")
            for bn in ("psq", "ppsk", "psl0", "psl1"):
                btok[bn] = ("sv", self.tok("sv"))

    def ph_ffn(self, it, first_in_batch):
        sb, ps = self.sb, self.ps
        self.w("vector", "stt")
        for m in range(8):
            self.o("vector", lambda e, m=m: e.tensor_tensor(
                out=sb["sq"][:, 1:TILE], in0=sb["xnf"][:, m * TILE + 1:(m + 1) * TILE],
                in1=sb["xnf"][:, m * TILE:(m + 1) * TILE - 1], op=ALU.subtract), "sv")
            self.o("vector", lambda e, m=m: e.tensor_tensor(
                out=sb["sq"][:, 0:1], in0=sb["xnf"][:, m * TILE:m * TILE + 1],
                in1=sb["xbnd"][:, m:m + 1], op=ALU.subtract), "sv")
            self.o("vector", lambda e, m=m: e.tensor_copy(
                out=sb["ffb"][:, m * TILE + 1:(m + 1) * TILE],
                in_=sb["xnf"][:, m * TILE:(m + 1) * TILE - 1]), "sv")
            self.o("vector", lambda e, m=m: e.tensor_copy(
                out=sb["ffb"][:, m * TILE:m * TILE + 1], in_=sb["xbnd"][:, m:m + 1]), "sv")
            self.o("vector", lambda e, m=m: e.scalar_tensor_tensor(
                out=sb["ffa"][:, m * TILE:(m + 1) * TILE], in0=sb["sq"][:, :],
                scalar=sb["mus"][:, 2 * m:2 * m + 1],
                in1=sb["ffb"][:, m * TILE:(m + 1) * TILE],
                op0=ALU.mult, op1=ALU.add), "sv")
            self.o("vector", lambda e, m=m: e.scalar_tensor_tensor(
                out=sb["gf"][:, m * TILE:(m + 1) * TILE], in0=sb["sq"][:, :],
                scalar=sb["mus"][:, 2 * m + 1:2 * m + 2],
                in1=sb["ffb"][:, m * TILE:(m + 1) * TILE],
                op0=ALU.mult, op1=ALU.add), "sv")
        self.o("vector", lambda e: e.tensor_copy(
            out=sb["xbnd"][:, 0:8],
            in_=ap(sb["xnf"], 0, 128, TILE - 1, [[TILE, 8], [1, 1]])), "sv")

        def ev_r(m, bank):
            self.w("scalar", "stt")
            self.o("scalar", lambda e, m=m, b=bank: e.activation(
                out=sb["ffb"][:, m * TILE:(m + 1) * TILE], in_=b[:, 0:TILE],
                func=AF.Sigmoid), "ss")
            return ("ss", self.tok("ss"))
        self.load_w("Wkf")
        self.mm_proj("Wr", ev_r, src="ffa")

        def ev_k(m, bank):
            self.w("scalar", "stt")
            self.o("scalar", lambda e, m=m, b=bank: e.activation(
                out=sb["ffa"][:, m * TILE:(m + 1) * TILE], in_=b[:, 0:TILE],
                func=AF.Relu), "ss")
            self.w("vector", "ss")
            self.o("vector", lambda e, m=m: e.tensor_tensor(
                out=sb["ffa"][:, m * TILE:(m + 1) * TILE],
                in0=sb["ffa"][:, m * TILE:(m + 1) * TILE],
                in1=sb["ffa"][:, m * TILE:(m + 1) * TILE], op=ALU.mult), "sv")
            return ("ss", self.tok("ss"))
        self.load_w("Wvf")
        self.mm_proj("Wkf", ev_k, src="gf")

        def ev_v(m, bank):
            self.w("vector", "stt")
            self.o("vector", lambda e, m=m, b=bank: e.tensor_tensor(
                out=sb["branch"][:, m * TILE:(m + 1) * TILE], in0=b[:, 0:TILE],
                in1=sb["ffb"][:, m * TILE:(m + 1) * TILE], op=ALU.mult), "sv")
            return ("sv", self.tok("sv"))
        self.load_w("Wg")
        self.mm_proj("Wvf", ev_v, src="ffa")

    def ph_gate_wo(self, it):
        sb, ps = self.sb, self.ps

        def ev_g(m, bank):
            self.w("scalar", "stt")
            self.o("scalar", lambda e, m=m, b=bank: e.activation(
                out=sb["gf"][:, m * TILE:(m + 1) * TILE], in_=b[:, 0:TILE],
                func=AF.Silu), "ss")
            return ("ss", self.tok("ss"))
        self.load_w("Wo")
        self.mm_proj("Wg", ev_g)

        self.w("tensor", "sv")
        self.w("vector", "ss")
        for k in range(8):
            for blk in range(4):
                self.o("tensor", lambda e, k=k, blk=blk: e.transpose(
                    ps["ptr"][:, blk * 128:(blk + 1) * 128],
                    sb["otm"][:, blk * HID + k * 128: blk * HID + (k + 1) * 128],
                    sb["cstb"][:, CI_I128:CI_I128 + 128]),
                    "stt" if blk == 3 else None)
            self.w("vector", "stt")
            self.o("vector", lambda e, k=k: e.tensor_tensor(
                out=sb["ffa"][:, k * TILE:(k + 1) * TILE], in0=ps["ptr"][:, 0:512],
                in1=sb["gf"][:, k * TILE:(k + 1) * TILE], op=ALU.mult), "sv")
            self.w("tensor", "sv")

        def ev_o(m, bank):
            self.w("vector", "stt")
            self.o("vector", lambda e, m=m, b=bank: e.tensor_tensor(
                out=sb["branch"][:, m * TILE:(m + 1) * TILE], in0=b[:, 0:TILE],
                in1=sb["branch"][:, m * TILE:(m + 1) * TILE], op=ALU.add), "sv")
            return ("sv", self.tok("sv"))
        if it < self.ntile - 1:
            self.load_w("Wq")
        self.mm_proj("Wo", ev_o, src="ffa")

    def ph_epilogue(self, it):
        sb, ps = self.sb, self.ps
        t0 = it * TILE
        for st in range(4):
            buf = sb["xld0"] if st % 2 == 0 else sb["xld1"]
            xsem = "sxa" if st % 2 == 0 else "sxb"
            self.w("gpsimd", "sv", self.xld_free.get(st % 2, 0))
            self.o("gpsimd", lambda e, b=buf, st=st, t0=t0: e.dma_start(
                out=b[:, :], in_=self.x_hbm[t0 + st * 128: t0 + (st + 1) * 128, :]),
                xsem, 16)
            self.w("tensor", "sv")
            for r in range(2):
                if r == 1:
                    self.w("tensor", "sv")
                for i in range(4):
                    k = r * 4 + i
                    self.o("tensor", lambda e, k=k, i=i, st=st: e.transpose(
                        ps["ptr"][:, i * 128:(i + 1) * 128],
                        ap(sb["branch"], 0, 128, k * TILE + st * 128, [[1, 128]]),
                        sb["cstb"][:, CI_I128:CI_I128 + 128]),
                        "stt" if i == 3 else None)
                self.w("vector", "stt")
                self.w("vector", xsem)
                if r == 0:
                    self.w("vector", "sdo", self.tok("sdo"))
                self.o("vector", lambda e, r=r, b=buf: e.tensor_tensor(
                    out=sb["outs"][:, r * 512:(r + 1) * 512], in0=ps["ptr"][:, 0:512],
                    in1=b[:, r * 512:(r + 1) * 512], op=ALU.add), "sv")
                self.w("tensor", "sv")
            self.w("gpsimd", "sv")
            self.o("gpsimd", lambda e, st=st, t0=t0: e.dma_start(
                out=self.out_hbm[t0 + st * 128: t0 + (st + 1) * 128, :],
                in_=sb["outs"][:, :]), "sdo", 16)
            self.xld_free[st % 2] = self.tok("sv")

    def ph_debug_dump(self):
        sb = self.sb
        self.w("gpsimd", "sv")
        self.w("gpsimd", "ss")
        for n, (src, dst) in self.dbg.items():
            self.o("gpsimd", lambda e, s=src, d=dst: e.dma_start(
                out=d, in_=sb[s][:, :]), "sdo", 16)

    def ph_finish(self):
        self.w("gpsimd", "sdo")


def _prep_inputs(core, x, ins):
    xs = np.ascontiguousarray(x[4 * core: 4 * core + 4].reshape(-1, HID))
    d = {"x": xs}
    d["W_Wq"] = ins["L0_attn_Wq"]
    d["W_Wk"] = ins["L0_attn_Wk"]
    d["W_Wv"] = ins["L0_attn_Wv"]
    d["W_Wg"] = ins["L0_attn_Wg"]
    d["W_Wo"] = ins["L0_attn_Wo"]
    d["W_Wr"] = ins["L0_ffn_W_r"]
    d["W_Wkf"] = ins["L0_ffn_W_k"]
    d["W_Wvf"] = ins["L0_ffn_W_v"]
    d["Wab"] = np.concatenate([ins["L0_attn_Wa"], ins["L0_attn_Wb"]], axis=0)
    bab = np.zeros((16, 2), dtype=np.float32)
    bab[:, 0] = ins["L0_attn_ba"]
    bab[:, 1] = ins["L0_attn_bb"]
    d["bab"] = bab
    mu = np.zeros((128, 16), dtype=np.float32)
    for k in range(8):
        mu[:, 2 * k] = ins["L0_ffn_mu_r"][k * 128:(k + 1) * 128]
        mu[:, 2 * k + 1] = ins["L0_ffn_mu_k"][k * 128:(k + 1) * 128]
    d["mu"] = mu
    d["consts"] = _host_consts()
    return {k: np.ascontiguousarray(v) for k, v in d.items()}


def kernel(**inputs):
    x = inputs["L0_attn_x"]
    B = x.shape[0]
    p = Prog(nb=B // NCORES, t_len=x.shape[1])
    nc = p.build()
    in_maps = [_prep_inputs(c, x, inputs) for c in range(NCORES)]
    res = run_bass_kernel_spmd(nc, in_maps, core_ids=list(range(NCORES)))
    out = np.concatenate(
        [r["out"].reshape(B // NCORES, x.shape[1], HID) for r in res.results], axis=0)
    return out.astype(np.float32)


# version 08
# speedup vs baseline: 1.2716x; 1.2716x over previous
import numpy as np
import ml_dtypes
import concourse.bass as bass
import concourse.mybir as mybir
from concourse.bass_utils import run_bass_kernel_spmd

F32 = mybir.dt.float32
BF16 = mybir.dt.bfloat16
AF = mybir.ActivationFunctionType
ALU = mybir.AluOpType
AX = mybir.AxisListType

NCORES = 8
HID = 1024
H = 16
DH = 64
C = 64
TILE = 512
NCH = TILE // C
NPAIR = NCH // 2
EPS_LN = 1e-5
CLAMP = 60.0
BIG_W = ["Wq", "Wk", "Wv", "Wg", "Wo", "Wr", "Wkf", "Wvf"]

CI_I128 = 0
CI_ISTK = 128
CI_MPM = 192
CI_MSU = 256
CI_MSL = 320
CI_IND = 384
CI_ONES = 512
CI_M3B = 640
CI_SCAL = 832
CI_IREP8 = 848
NCONST = 1360


def _host_consts():
    c = np.zeros((128, NCONST), dtype=np.float32)
    c[np.arange(128), CI_I128 + np.arange(128)] = 1.0
    c[np.arange(128), CI_ISTK + np.tile(np.arange(64), 2)] = 1.0
    s = np.tile(np.arange(64), 2)[:, None]
    t = np.arange(64)[None, :]
    c[:, CI_MPM:CI_MPM + 64] = (s <= t)
    c[:, CI_MSU:CI_MSU + 64] = (s < t)
    c[:, CI_MSL:CI_MSL + 64] = (t < s)
    for k in range(8):
        c[0:64, CI_IND + 16 * k + 2 * k] = 1.0
        c[64:128, CI_IND + 16 * k + 2 * k + 1] = 1.0
    c[:, CI_ONES:CI_ONES + 128] = 1.0
    c[:, CI_M3B:CI_M3B + 64] = (s <= t)
    c[:, CI_M3B + 64:CI_M3B + 128] = (s < t)
    c[:, CI_M3B + 128:CI_M3B + 192] = 1.0
    c[:, CI_SCAL] = EPS_LN
    for a in range(8):
        c[np.arange(128), CI_IREP8 + 64 * a + (np.arange(128) % 64)] = 1.0
    return c


def ap(t, pbase, pcount, off, dims):
    rowlen = t.shape[1]
    return bass.AP(t, pbase * rowlen + off, [[rowlen, pcount]] + list(dims))


class Prog:
    def __init__(self, nb, t_len, debug=False, stop_after=None):
        self.stop_after = stop_after
        self.nb, self.T, self.debug = nb, t_len, debug
        self.ntb = t_len // TILE
        self.ntile = nb * self.ntb
        self.nc = bass.Bass("TRN2", target_bir_lowering=False)
        self.ctx, self.cnt, self.sems, self.sb, self.ps = [], {}, {}, {}, {}
        self.prog = {"sync": [], "vector": [], "scalar": [], "tensor": [], "gpsimd": []}

    def sem(self, n):
        s = self.nc.semaphore(n)
        self.sems[n] = s.__enter__()
        self.ctx.append(s)
        self.cnt[n] = 0

    def sbuf(self, n, shape, dt):
        t = self.nc.sbuf_tensor(n, shape, dt)
        self.sb[n] = t.__enter__()
        self.ctx.append(t)

    def psum(self, n, shape, dt=F32):
        t = self.nc.psum_tensor(n, shape, dt)
        self.ps[n] = t.__enter__()
        self.ctx.append(t)

    def w(self, eng, sem, val=None):
        v = self.cnt[sem] if val is None else val
        if v <= 0:
            return
        s = self.sems[sem]
        self.prog[eng].append(lambda e, s=s, v=v: e.wait_ge(s, v))

    def o(self, eng, build, sem=None, n=1):
        if eng in ("vector", "scalar"):
            ssem = "sv" if eng == "vector" else "ss"
            v = self.cnt[ssem]
            if v > 0:
                sh = self.sems[ssem]
                self.prog[eng].append(lambda e, sh=sh, v=v: e.wait_ge(sh, v))
        if sem is None:
            self.prog[eng].append(build)
        else:
            s = self.sems[sem]
            self.prog[eng].append(lambda e, s=s, b=build, n=n: b(e).then_inc(s, n))
            self.cnt[sem] += n

    def tok(self, sem):
        return self.cnt[sem]

    def build(self):
        nc = self.nc
        NTOK = self.nb * self.T
        self.x_hbm = nc.dram_tensor("x", [NTOK, HID], F32, kind="ExternalInput").ap()
        self.out_hbm = nc.dram_tensor("out", [NTOK, HID], F32, kind="ExternalOutput").ap()
        self.w_hbm = {n: nc.dram_tensor("W_" + n, [HID, HID], F32, kind="ExternalInput").ap()
                      for n in BIG_W}
        self.wab_hbm = nc.dram_tensor("Wab", [32, HID], F32, kind="ExternalInput").ap()
        self.bab_hbm = nc.dram_tensor("bab", [16, 2], F32, kind="ExternalInput").ap()
        self.mu_hbm = nc.dram_tensor("mu", [128, 16], F32, kind="ExternalInput").ap()
        self.cst_hbm = nc.dram_tensor("consts", [128, NCONST], F32, kind="ExternalInput").ap()
        self.wt_dram = {n: nc.dram_tensor("WT_" + n, [HID, HID], BF16).ap() for n in BIG_W}

        for s in ["swa", "swb", "sxa", "sxb", "sgr", "sdo", "sv", "ss", "stt", "sg"]:
            self.sem(s)

        S = self.sbuf
        S("cstf", [128, NCONST], F32)
        self.nc.const_aps.register(F32, EPS_LN,
                                   self.sb["cstf"][:, CI_SCAL:CI_SCAL + 1])
        S("cstb", [128, NCONST], BF16)
        S("wtb0", [128, 8 * HID], BF16)
        S("wtb1", [128, 8 * HID], BF16)
        S("wabt", [128, 8 * 32], BF16)
        S("babs", [16, 2], F32)
        S("babn", [16, 2], F32)
        S("mus", [128, 16], F32)
        S("xld0", [128, HID], F32)
        S("xld1", [128, HID], F32)
        S("xsq", [128, HID], F32)
        S("xntm", [128, HID], BF16)
        S("xnf", [128, 8 * TILE], BF16)
        S("xbnd", [128, 8], BF16)
        S("qk", [128, 16 * TILE], BF16)
        S("vtm", [128, 4 * HID], BF16)
        S("gf", [128, 8 * TILE], BF16)
        S("sq", [128, TILE], BF16)
        S("stat", [128, 8], F32)
        S("zsig", [16, 2 * TILE], F32)
        for n in ["la", "lc", "gamv", "uv", "nqv", "nkv", "chiv", "zetv", "omv",
                  "t16a", "t16b"]:
            S(n, [16, TILE], F32)
        S("chiT", [128, 4 * 16], F32)
        S("zetT", [128, 4 * 16], F32)
        S("omT", [128, 4 * 16], F32)
        S("gamrow", [1, 128], F32)
        S("gamrep", [128, 128], F32)
        S("csk", [128, H * 192], BF16)
        S("susl", [128, H * 64], BF16)
        S("ktm", [128, H * 64], BF16)
        S("zknx", [128, H * 64], BF16)
        S("yA", [128, H * 128], BF16)
        S("yB", [128, H * 128], BF16)
        S("pwx", [128, 5 * H * 64], BF16)
        S("pwt", [128, 2 * H * 64], BF16)
        S("oin", [128, 2 * 512], BF16)
        S("qef", [128, 2 * 512], BF16)
        S("gtm", [128, 2 * 512], BF16)
        S("um", [128, 2 * 512], BF16)
        S("mst", [128, H * 64], BF16)
        S("otm", [128, 4 * HID], BF16)
        S("branch", [128, 8 * TILE], BF16)
        S("ffa", [128, 8 * TILE], BF16)
        S("ffb", [128, 8 * TILE], BF16)
        S("outs", [128, HID], F32)

        Pm = self.psum
        Pm("pp0", [128, 512])
        Pm("pp1", [128, 512])
        Pm("ppsk", [128, 512])
        Pm("psl0", [128, 512])
        Pm("psl1", [128, 512])
        Pm("psq", [128, 512])
        Pm("ptr", [128, 512], BF16)
        Pm("pscl", [128, 512])

        if self.debug:
            self.dbg = {}
            for n, src in [("d_xnf", "xnf"), ("d_qk", "qk"),
                           ("d_vtm", "vtm"), ("d_csk", "csk"), ("d_susl", "susl"),
                           ("d_y", "yA"), ("d_oin", "oin"), ("d_qef", "qef"),
                           ("d_gtm", "gtm"), ("d_um", "um"), ("d_mst", "mst"),
                           ("d_otm", "otm"), ("d_branch", "branch"),
                           ("d_gf", "gf"), ("d_ffa", "ffa")]:
                t = self.sb[src]
                self.dbg[n] = (src, self.nc.dram_tensor(
                    n, [t.shape[0], t.shape[1]], BF16, kind="ExternalOutput").ap())
            for n, src in [("d_chi", "chiv"), ("d_gam", "gamv"), ("d_zet", "zetv"),
                           ("d_om", "omv"), ("d_nq", "nqv"), ("d_nk", "nkv"),
                           ("d_la", "la"), ("d_gamrep", "gamrep")]:
                t = self.sb[src]
                self.dbg[n] = (src, self.nc.dram_tensor(
                    n, [t.shape[0], t.shape[1]], F32, kind="ExternalOutput").ap())

        self.emit_all()

        with nc.Block() as blk:
            pr = self.prog

            @blk.sync
            def _(e):
                for f in pr["sync"]:
                    f(e)

            @blk.vector
            def _(e):
                for f in pr["vector"]:
                    f(e)

            @blk.scalar
            def _(e):
                for f in pr["scalar"]:
                    f(e)

            @blk.tensor
            def _(e):
                for f in pr["tensor"]:
                    f(e)

            @blk.gpsimd
            def _(e):
                for f in pr["gpsimd"]:
                    f(e)
        return nc

    def emit_all(self):
        self.ph_setup()
        self.ph_weight_prep()
        self.slot_free = {0: self.tok("stt"), 1: self.tok("stt")}
        self.xld_free = {0: self.tok("sv"), 1: self.tok("sv")}
        self.wtok = {}
        self.wslot = {}
        self.wload_next = 0
        self.load_w("Wq")
        for it in range(self.ntile):
            self.ph_tile(it)
        self.ph_finish()

    def load_w(self, wn):
        sb = self.sb
        slot = self.wload_next
        self.wload_next ^= 1
        self.wslot[wn] = slot
        sem = "swa" if slot == 0 else "swb"
        dstw = sb["wtb0"] if slot == 0 else sb["wtb1"]
        self.w("sync", "stt", self.slot_free.get(slot, 0))
        self.w("sync", "sv", self.prep_guard.get(sem, 0))
        self.prep_guard[sem] = 0
        self.w("sync", "sg", self.tok("sg"))
        for k in range(8):
            self.o("sync", lambda e, wn=wn, k=k, d=dstw: e.dma_start(
                out=d[:, k * HID:(k + 1) * HID],
                in_=self.wt_dram[wn][k * 128:(k + 1) * 128, :]), sem, 16)
        self.wtok[wn] = (sem, self.tok(sem))

    def ph_setup(self):
        sb = self.sb
        self.o("sync", lambda e: e.dma_start(out=sb["cstf"][:, :], in_=self.cst_hbm), "swa", 16)
        self.o("sync", lambda e: e.dma_start(out=sb["babs"][:, :], in_=self.bab_hbm), "swa", 16)
        self.o("sync", lambda e: e.dma_start(out=sb["mus"][:, :], in_=self.mu_hbm), "swa", 16)
        self.w("vector", "swa")
        self.o("vector", lambda e: e.tensor_copy(out=sb["cstb"][:, :], in_=sb["cstf"][:, :]), "sv")
        self.o("vector", lambda e: e.tensor_scalar(
            out=sb["babn"][:, :], in0=sb["babs"][:, :], scalar1=-1.0, scalar2=None,
            op0=ALU.mult), "sv")
        self.o("vector", lambda e: e.memset(sb["mst"][:, :], 0.0), "sv")
        self.o("vector", lambda e: e.memset(sb["xbnd"][:, :], 0.0), "sv")

    def ph_weight_prep(self):
        sb, ps = self.sb, self.ps
        self.prep_guard = {"swa": self.tok("sv"), "swb": 0}
        I128b = sb["cstb"][:, CI_I128:CI_I128 + 128]
        for wn in BIG_W:
            for j in range(8):
                buf = sb["xld0"] if j % 2 == 0 else sb["xld1"]
                psem = "swa" if j % 2 == 0 else "swb"
                self.w("sync", "sv", self.prep_guard.get(psem, 0))
                self.o("sync", lambda e, s=self.w_hbm[wn], j=j, b=buf: e.dma_start(
                    out=b[:, :], in_=s[j * 128:(j + 1) * 128, :]), psem, 16)
                self.w("vector", psem)
                cols = (j % 2) * HID
                self.o("vector", lambda e, b=buf, c=cols: e.tensor_copy(
                    out=sb["wtb1"][:, c:c + HID], in_=b[:, :]), "sv")
                self.prep_guard[psem] = self.tok("sv")
                for r in range(2):
                    self.w("tensor", "sv")
                    self.w("tensor", "sg")
                    for i in range(4):
                        ii = r * 4 + i
                        self.o("tensor", lambda e, ii=ii, c=cols, i=i: e.transpose(
                            ps["ptr"][:, i * 128:(i + 1) * 128],
                            sb["wtb1"][:, c + ii * 128:c + ii * 128 + 128], I128b),
                            "stt" if i == 3 else None)
                    self.w("vector", "stt")
                    self.o("vector", lambda e, r=r: e.tensor_copy(
                        out=sb["wtb0"][:, r * 512:(r + 1) * 512],
                        in_=ps["ptr"][:, 0:512]), "sv")
                    self.w("gpsimd", "sv")
                    for i in range(4):
                        ii = r * 4 + i
                        self.o("gpsimd", lambda e, ii=ii, i=i, r=r, wn=wn, j=j: e.dma_start(
                            out=self.wt_dram[wn][ii * 128:(ii + 1) * 128,
                                                 j * 128:(j + 1) * 128],
                            in_=sb["wtb0"][:, r * 512 + i * 128: r * 512 + (i + 1) * 128]),
                            "sg", 16)
        self.w("sync", "sv", self.prep_guard.get("swa", 0))
        self.o("sync", lambda e: e.dma_start(out=sb["xld0"][0:32, :], in_=self.wab_hbm),
               "swa", 16)
        self.w("vector", "swa")
        self.w("vector", "sg")
        self.o("vector", lambda e: e.tensor_copy(
            out=sb["wtb1"][0:32, 0:HID], in_=sb["xld0"][0:32, :]), "sv")
        self.w("tensor", "sv")
        self.w("tensor", "sg")
        for i in range(8):
            self.o("tensor", lambda e, i=i: e.transpose(
                ps["ptr"][:, (i % 4) * 32:(i % 4) * 32 + 32],
                sb["wtb1"][0:32, i * 128:i * 128 + 128],
                sb["cstb"][0:32, CI_I128:CI_I128 + 32]),
                "stt" if i % 4 == 3 else None)
            if i % 4 == 3:
                self.w("vector", "stt")
                r = i // 4
                self.o("vector", lambda e, r=r: e.tensor_copy(
                    out=sb["wabt"][:, r * 128:(r + 1) * 128], in_=ps["ptr"][:, 0:128]),
                    "sv")
                if r == 0:
                    self.w("tensor", "sv")

    def mm_proj(self, wn, evac, src=None, tokmajor=False):
        sb, ps = self.sb, self.ps
        srct = sb["xnf"] if src is None else sb[src]
        wt = sb["wtb0"] if self.wslot[wn] == 0 else sb["wtb1"]
        self.w("tensor", self.wtok[wn][0], self.wtok[wn][1])
        self.w("tensor", "sv")
        self.w("tensor", "ss")
        ev_tok = {0: None, 1: None}
        for m in range(8):
            bank = ps["pp0"] if m % 2 == 0 else ps["pp1"]
            if ev_tok[m % 2] is not None:
                self.w("tensor", ev_tok[m % 2][0], ev_tok[m % 2][1])
            for k in range(8):
                if tokmajor:
                    tb, oh = m // 2, m % 2
                    self.o("tensor", lambda e, k=k, b=bank, wt=wt, s=srct, tb=tb,
                           oh=oh: e.matmul(
                        b[:, 0:TILE],
                        s[:, k * TILE + tb * 128: k * TILE + (tb + 1) * 128],
                        wt[:, k * HID + oh * 512: k * HID + (oh + 1) * 512],
                        start=(k == 0), stop=(k == 7)), "stt" if k == 7 else None)
                else:
                    self.o("tensor", lambda e, m=m, k=k, b=bank, wt=wt, s=srct: e.matmul(
                        b[:, 0:TILE], wt[:, k * HID + m * 128: k * HID + m * 128 + 128],
                        s[:, k * TILE:(k + 1) * TILE],
                        start=(k == 0), stop=(k == 7)), "stt" if k == 7 else None)
            ev_tok[m % 2] = evac(m, bank)
        self.slot_free[self.wslot[wn]] = self.tok("stt")

    def ev_copy(self, dst, off=0):
        sb = self.sb
        def f(m, bank):
            self.w("vector", "stt")
            self.o("vector", lambda e, m=m, b=bank: e.tensor_copy(
                out=sb[dst][:, off + m * TILE:off + (m + 1) * TILE],
                in_=b[:, 0:TILE]), "sv")
            return ("sv", self.tok("sv"))
        return f

    def ph_tile(self, it):
        sb, ps = self.sb, self.ps
        ntb = self.ntb
        tt = it % ntb
        t0 = it * TILE
        first_in_batch = (tt == 0)

        if first_in_batch and it > 0:
            self.w("vector", "stt")
            self.o("vector", lambda e: e.memset(sb["mst"][:, :], 0.0), "sv")
            self.o("vector", lambda e: e.memset(sb["xbnd"][:, :], 0.0), "sv")

        if self.stop_after == "prep":
            return
        for st in range(4):
            buf = sb["xld0"] if st % 2 == 0 else sb["xld1"]
            self.w("gpsimd", "sv", getattr(self, "xld_free", {}).get(st % 2, 0))
            xsem = "sxa" if st % 2 == 0 else "sxb"
            self.o("gpsimd", lambda e, b=buf, st=st, t0=t0: e.dma_start(
                out=b[:, :], in_=self.x_hbm[t0 + st * 128: t0 + (st + 1) * 128, :]),
                xsem, 16)
            self.w("vector", xsem)
            self.w("scalar", xsem)
            self.o("vector", lambda e, b=buf: e.reduce_sum(
                out=sb["stat"][:, 0:1], in_=b[:, :], axis=AX.X), "sv")
            self.w("scalar", "sv")
            self.o("scalar", lambda e, b=buf: e.activation(
                out=sb["xsq"][:, :], in_=b[:, :], func=AF.Square,
                accum_out=sb["stat"][:, 1:2]), "ss")
            self.w("vector", "ss")
            self.o("vector", lambda e: e.tensor_scalar(
                out=sb["stat"][:, 2:3], in0=sb["stat"][:, 0:1],
                scalar1=1.0 / HID, scalar2=None, op0=ALU.mult), "sv")
            self.o("vector", lambda e: e.tensor_scalar(
                out=sb["stat"][:, 4:5], in0=sb["stat"][:, 1:2],
                scalar1=1.0 / HID, scalar2=None, op0=ALU.mult), "sv")
            self.o("vector", lambda e: e.scalar_tensor_tensor(
                out=sb["stat"][:, 5:6], in0=sb["stat"][:, 2:3],
                scalar=sb["stat"][:, 2:3], in1=sb["stat"][:, 4:5],
                op0=ALU.mult, op1=ALU.subtract), "sv")
            self.w("scalar", "sv")
            self.o("scalar", lambda e: e.activation(
                out=sb["stat"][:, 6:7], in_=sb["stat"][:, 5:6], func=AF.Ln,
                scale=-1.0, bias=EPS_LN), "ss")
            self.o("scalar", lambda e: e.activation(
                out=sb["stat"][:, 7:8], in_=sb["stat"][:, 6:7], func=AF.Exp,
                scale=-0.5), "ss")
            self.w("vector", "ss")
            self.o("vector", lambda e, b=buf: e.tensor_scalar(
                out=sb["xntm"][:, :], in0=b[:, :], scalar1=sb["stat"][:, 2:3],
                scalar2=sb["stat"][:, 7:8], op0=ALU.subtract, op1=ALU.mult), "sv")
            if not hasattr(self, "xld_free"):
                self.xld_free = {}
            self.xld_free[st % 2] = self.tok("sv")
            self.w("tensor", "sv")
            for r in range(2):
                if r == 1:
                    self.w("tensor", "sv")
                for i in range(4):
                    k = r * 4 + i
                    self.o("tensor", lambda e, k=k, i=i: e.transpose(
                        ps["ptr"][:, i * 128:(i + 1) * 128],
                        sb["xntm"][:, k * 128:(k + 1) * 128],
                        sb["cstb"][:, CI_I128:CI_I128 + 128]),
                        "stt" if i == 3 else None)
                self.w("vector", "stt")
                self.o("vector", lambda e, r=r, st=st: e.tensor_copy(
                    out=ap(sb["xnf"], 0, 128, r * 4 * TILE + st * 128,
                           [[TILE, 4], [1, 128]]),
                    in_=ps["ptr"][:, 0:512]), "sv")

        if self.stop_after == "ln":
            return
        self.load_w("Wk")
        self.mm_proj("Wq", self.ev_copy("qk", 0))
        self.load_w("Wv")
        self.mm_proj("Wk", self.ev_copy("qk", 8 * TILE))
        self.load_w("Wr")

        def ev_vtm(m, bank):
            tb, oh = m // 2, m % 2
            self.w("vector", "stt")
            self.o("vector", lambda e, b=bank, tb=tb, oh=oh: e.tensor_copy(
                out=sb["vtm"][:, tb * HID + oh * 512: tb * HID + (oh + 1) * 512],
                in_=b[:, 0:TILE]), "sv")
            return ("sv", self.tok("sv"))
        self.mm_proj("Wv", ev_vtm, tokmajor=True)

        if self.stop_after == "proj":
            return
        self.ph_scales(it)

        if self.stop_after == "scales":
            return
        for a in range(NPAIR):
            self.ph_pair(it, a)
        if self.stop_after in ("pskv", "pskv1", "pskvnoev", "pskv1noev",
                               "builds", "solve", "post", "pairs"):
            return

        self.ph_ffn(it, first_in_batch)
        if self.stop_after == "ffn":
            return

        self.ph_gate_wo(it)

        self.ph_epilogue(it)

        if self.debug and it == self.ntile - 1:
            self.ph_debug_dump()

    def ph_scales(self, it):
        sb, ps = self.sb, self.ps
        self.w("tensor", "sv")
        for k in range(8):
            self.o("tensor", lambda e, k=k: e.matmul(
                ps["pscl"][0:16, 0:TILE], sb["wabt"][:, k * 32:k * 32 + 16],
                sb["xnf"][:, k * TILE:(k + 1) * TILE],
                start=(k == 0), stop=(k == 7)))
            self.o("tensor", lambda e, k=k: e.matmul(
                ps["psq"][0:16, 0:TILE], sb["wabt"][:, k * 32 + 16:k * 32 + 32],
                sb["xnf"][:, k * TILE:(k + 1) * TILE],
                start=(k == 0), stop=(k == 7)), "stt" if k == 7 else None)
        self.w("scalar", "stt")
        self.o("scalar", lambda e: e.activation(
            out=sb["zsig"][:, 0:TILE], in_=ps["pscl"][0:16, 0:TILE], func=AF.Sigmoid,
            bias=sb["babs"][:, 0:1]), "ss")
        self.o("scalar", lambda e: e.activation(
            out=sb["zsig"][:, TILE:2 * TILE], in_=ps["psq"][0:16, 0:TILE], func=AF.Sigmoid,
            bias=sb["babs"][:, 1:2]), "ss")
        self.w("vector", "ss")
        self.o("vector", lambda e: e.tensor_scalar(
            out=sb["t16a"][:, :], in0=sb["zsig"][0:16, 0:TILE], scalar1=1e-35,
            scalar2=None, op0=ALU.max), "sv")
        self.w("scalar", "sv")
        self.o("scalar", lambda e: e.activation(
            out=sb["la"][:, :], in_=sb["t16a"][:, :], func=AF.Ln), "ss")
        self.w("vector", "ss")
        for c in range(NCH):
            self.o("vector", lambda e, c=c: e.tensor_tensor_scan(
                out=sb["lc"][:, c * C:(c + 1) * C],
                data0=sb["la"][:, c * C:(c + 1) * C],
                data1=sb["la"][:, c * C:(c + 1) * C], initial=0.0,
                op0=ALU.add, op1=ALU.bypass), "sv")
        self.w("scalar", "sv")
        self.o("scalar", lambda e: e.activation(
            out=sb["gamv"][:, :], in_=sb["lc"][:, :], func=AF.Exp), "ss")
        self.w("vector", "ss")
        self.o("vector", lambda e: e.tensor_scalar(
            out=sb["t16a"][:, :], in0=sb["lc"][:, :], scalar1=-1.0, scalar2=CLAMP,
            op0=ALU.mult, op1=ALU.min), "sv")
        self.w("scalar", "sv")
        self.o("scalar", lambda e: e.activation(
            out=sb["uv"][:, :], in_=sb["t16a"][:, :], func=AF.Exp), "ss")

        for (nm, soff) in [("nqv", 0), ("nkv", 8 * TILE)]:
            for k in range(8):
                self.w("vector", "stt")
                self.o("vector", lambda e, k=k, soff=soff: e.tensor_tensor(
                    out=sb["sq"][:, :],
                    in0=sb["qk"][:, soff + k * TILE:soff + (k + 1) * TILE],
                    in1=sb["qk"][:, soff + k * TILE:soff + (k + 1) * TILE],
                    op=ALU.mult), "sv")
                self.w("tensor", "sv")
                self.o("tensor", lambda e, k=k: e.matmul(
                    ps["psq"][0:16, 0:TILE],
                    sb["cstb"][:, CI_IND + 16 * k: CI_IND + 16 * (k + 1)],
                    sb["sq"][:, :], start=(k == 0), stop=(k == 7)), "stt")
            self.w("vector", "stt")
            self.w("vector", "ss")
            self.o("vector", lambda e: e.tensor_scalar(
                out=sb["t16a"][:, :], in0=ps["psq"][0:16, 0:TILE],
                scalar1=1e-12, scalar2=None, op0=ALU.max), "sv")
            self.w("scalar", "sv")
            self.o("scalar", lambda e: e.activation(
                out=sb["t16b"][:, :], in_=sb["t16a"][:, :], func=AF.Ln), "ss")
            self.o("scalar", lambda e, nm=nm: e.activation(
                out=sb[nm][:, :], in_=sb["t16b"][:, :], func=AF.Exp, scale=-0.5), "ss")

        self.w("vector", "ss")
        self.o("vector", lambda e: e.tensor_tensor(
            out=sb["t16a"][:, :], in0=sb["nkv"][:, :], in1=sb["nkv"][:, :],
            op=ALU.mult), "sv")
        self.o("vector", lambda e: e.tensor_tensor(
            out=sb["t16a"][:, :], in0=sb["t16a"][:, :], in1=sb["uv"][:, :],
            op=ALU.mult), "sv")
        self.o("vector", lambda e: e.tensor_tensor(
            out=sb["t16a"][:, :], in0=sb["t16a"][:, :], in1=sb["gamv"][:, :],
            op=ALU.mult), "sv")
        self.o("vector", lambda e: e.tensor_tensor(
            out=sb["chiv"][:, :], in0=sb["t16a"][:, :], in1=sb["zsig"][0:16, TILE:2 * TILE],
            op=ALU.mult), "sv")
        self.o("vector", lambda e: e.tensor_tensor(
            out=sb["t16b"][:, :], in0=sb["nkv"][:, :], in1=sb["gamv"][:, :],
            op=ALU.mult), "sv")
        self.o("vector", lambda e: e.reciprocal(
            out=sb["zetv"][:, :], in_=sb["t16b"][:, :]), "sv")
        self.o("vector", lambda e: e.tensor_tensor(
            out=sb["omv"][:, :], in0=sb["nqv"][:, :], in1=sb["gamv"][:, :],
            op=ALU.mult), "sv")

        self.w("tensor", "sv")
        for (vn, tn) in [("chiv", "chiT"), ("zetv", "zetT"), ("omv", "omT")]:
            for blk in range(4):
                self.o("tensor", lambda e, vn=vn, blk=blk: e.transpose(
                    ps["pscl"][:, 256 + blk * 16: 256 + (blk + 1) * 16],
                    sb[vn][:, blk * 128:(blk + 1) * 128],
                    sb["cstf"][0:16, CI_I128:CI_I128 + 16]),
                    "stt" if blk == 3 else None)
            self.w("vector", "stt")
            self.o("vector", lambda e, tn=tn: e.tensor_copy(
                out=sb[tn][:, :], in_=ps["pscl"][:, 256:256 + 64]), "sv")
            self.w("tensor", "sv")

        self.w("gpsimd", "sv")
        self.o("gpsimd", lambda e: e.dma_start(
            out=sb["gamrow"][0:1, 0:128],
            in_=ap(sb["gamv"], 0, 16, C - 1, [[C, NCH]]),
            allow_slow_non_contiguous=True), "sgr", 16)
        self.w("tensor", "sgr")
        self.o("tensor", lambda e: e.matmul(
            ps["pscl"][:, 384:384 + 128], sb["cstf"][0:1, CI_ONES:CI_ONES + 128],
            sb["gamrow"][0:1, 0:128], start=True, stop=True), "stt")
        self.w("vector", "stt")
        self.o("vector", lambda e: e.tensor_copy(
            out=sb["gamrep"][:, :], in_=ps["pscl"][:, 384:384 + 128]), "sv")

    def ph_pair(self, it, a):
        sb, ps = self.sb, self.ps

        def oin_gather(par, pc):
            return ap(self.sb["oin"], pc, 64, par * 64, [[512, 2], [128, 4], [1, 64]])

        def um_gather(par, phh, cc):
            return ap(self.sb["um"], phh, 64, cc * 512, [[64, 8], [1, 64]])
        cstb = self.sb["cstb"]
        c0 = 2 * a
        if not hasattr(self, "btok"):
            self.btok = {}
        btok = self.btok

        def bank_wait(bn):
            t = btok.get(bn)
            if t is not None:
                self.w("tensor", t[0], t[1])

        def chT(h):
            return sb["chiT"][:, a * 16 + h: a * 16 + h + 1]

        def zeT(h):
            return sb["zetT"][:, a * 16 + h: a * 16 + h + 1]

        POOL = [("ppsk", "psl1"), ("psq", "psl0")]

        nrounds = 1 if self.stop_after in ("pskv1", "pskv1noev") else 8
        noev = self.stop_after in ("pskvnoev", "pskv1noev")
        for rnd in range(nrounds):
            be, bo = POOL[rnd % 2]
            bank_wait(be)
            bank_wait(bo)
            for cc in (0, 1):
                pc = 64 * cc
                for si in range(2):
                    h = 2 * rnd + si
                    ph = 64 * (h % 2)
                    bank = ps[be] if h % 2 == 0 else ps[bo]
                    cl = (h // 2) * TILE + (c0 + cc) * C
                    self.o("tensor", lambda e, ph=ph, cl=cl, pc=pc, b=bank: e.matmul(
                        b[pc:pc + 64, 0:128],
                        sb["qk"][ph:ph + 64, 8 * TILE + cl:8 * TILE + cl + 64],
                        ap(sb["qk"], ph, 64, cl, [[8 * TILE, 2], [1, 64]]),
                        start=True, stop=True, tile_position=(ph, pc)))
                    self.o("tensor", lambda e, ph=ph, cl=cl, pc=pc, b=bank: e.matmul(
                        b[pc:pc + 64, 128:192],
                        sb["qk"][ph:ph + 64, 8 * TILE + cl:8 * TILE + cl + 64],
                        cstb[ph:ph + 64, CI_ISTK:CI_ISTK + 64],
                        start=True, stop=True, tile_position=(ph, pc)),
                        "stt" if (si == 1 and cc == 1) else None)
            if noev:
                continue
            mmtok = self.tok("stt")

            def dve_evacs(h, bn):
                self.o("vector", lambda e, h=h, b=ps[bn]: e.scalar_tensor_tensor(
                    out=ap(sb["csk"], 0, 128, h * 192, [[1, 192]]),
                    in0=ap(b, 0, 128, 0, [[1, 192]]),
                    scalar=chT(h),
                    in1=ap(sb["cstf"], 0, 128, CI_M3B, [[1, 192]]),
                    op0=ALU.mult, op1=ALU.mult), "sv")
                self.o("vector", lambda e, h=h, b=ps[bn]: e.tensor_tensor(
                    out=ap(sb["susl"], 0, 128, h * 64, [[1, 64]]),
                    in0=ap(b, 0, 128, 64, [[1, 64]]),
                    in1=ap(sb["cstf"], 0, 128, CI_MSL, [[1, 64]]),
                    op=ALU.mult), "sv")

            def sc_evacs(h, bn):
                self.o("scalar", lambda e, h=h, a=a: e.activation(
                    out=ap(sb["yA"], 0, 128, h * 128, [[1, 64]]),
                    in_=ap(sb["vtm"], 0, 128, a * HID + h * 64, [[1, 64]]),
                    func=AF.Copy, scale=zeT(h)), "ss")
                self.o("scalar", lambda e, h=h, b=ps[bn]: e.activation(
                    out=ap(sb["yA"], 0, 128, h * 128 + 64, [[1, 64]]),
                    in_=ap(b, 0, 128, 128, [[1, 64]]),
                    func=AF.Copy, scale=-1.0), "ss")
                self.o("scalar", lambda e, h=h, b=ps[bn]: e.activation(
                    out=ap(sb["ktm"], 0, 128, h * 64, [[1, 64]]),
                    in_=ap(b, 0, 128, 128, [[1, 64]]),
                    func=AF.Copy), "ss")

            he, ho = 2 * rnd, 2 * rnd + 1
            self.w("vector", "stt", mmtok)
            self.w("scalar", "stt", mmtok)
            dve_evacs(he, be)
            sc_evacs(ho, bo)
            self.w("vector", "ss")
            self.w("scalar", "sv")
            dve_evacs(ho, bo)
            sc_evacs(he, be)
            btok[be] = ("ss", self.tok("ss"))
            btok[bo] = ("sv", self.tok("sv"))
        if self.stop_after in ("pskv", "pskv1", "pskvnoev", "pskv1noev"):
            return

        def sl_slice(h, pc):
            return ap(sb["susl"], pc, 64, h * 64, [[1, 64]])

        def sux_slice(h, pc):
            return ap(sb["csk"], pc, 64, h * 192 + 64, [[1, 64]])

        def pwx_slice(l, h, pc):
            return ap(sb["pwx"], pc, 64, l * H * 64 + h * 64, [[1, 64]])

        def pwt_slice(l, h, pc):
            return ap(sb["pwt"], pc, 64, (l % 2) * H * 64 + h * 64, [[1, 64]])

        self.w("tensor", "sv")
        self.w("tensor", "ss")
        wv = 0
        for L in range(5):
            for part in range(2):
                if L == 4 and part == 1:
                    break
                be, bo = POOL[wv % 2]
                wv += 1
                bank_wait(be)
                bank_wait(bo)
                if L > 0 and part == 0:
                    self.lvl_tok = (self.tok("sv"), self.tok("ss"))
                if L > 0:
                    self.w("tensor", "sv", self.lvl_tok[0])
                    self.w("tensor", "ss", self.lvl_tok[1])
                for cc in (0, 1):
                    pc = 64 * cc
                    for h in range(H):
                        bank = ps[be] if h < 8 else ps[bo]
                        colb = (h % 8) * 64
                        if L == 0:
                            stat = sl_slice(h, pc) if part == 0 else sux_slice(h, pc)
                            mv = sux_slice(h, pc) if part == 0 else sl_slice(h, pc)
                        else:
                            stat = pwt_slice(L - 1, h, pc) if part == 0 else pwx_slice(L - 1, h, pc)
                            mv = pwx_slice(L - 1, h, pc) if part == 0 else pwt_slice(L - 1, h, pc)
                        self.o("tensor", lambda e, st=stat, mv=mv, pc=pc, colb=colb,
                               bank=bank: e.matmul(
                            bank[pc:pc + 64, colb:colb + 64], st, mv,
                            start=True, stop=True, tile_position=(pc, pc)),
                            "stt" if (h == H - 1 and cc == 1) else None)
                self.w("vector", "stt")
                self.w("scalar", "stt")
                for bi, bn in enumerate((be, bo)):
                    if part == 0:
                        self.o("vector", lambda e, L=L, bi=bi, bn=bn: e.tensor_tensor(
                            out=ap(sb["pwx"], 0, 128, L * H * 64 + bi * 512, [[1, 512]]),
                            in0=ap(ps[bn], 0, 128, 0, [[1, 512]]),
                            in1=ap(sb["chiT"], 0, 128, a * 16 + bi * 8, [[1, 8], [0, 64]]),
                            op=ALU.mult), "sv")
                    else:
                        self.o("scalar", lambda e, L=L, bi=bi, bn=bn: e.activation(
                            out=ap(sb["pwt"], 0, 128, (L % 2) * H * 64 + bi * 512,
                                   [[1, 512]]),
                            in_=ap(ps[bn], 0, 128, 0, [[1, 512]]),
                            func=AF.Copy), "ss")
                    btok[bn] = ("sv", self.tok("sv")) if part == 0 else \
                        ("ss", self.tok("ss"))
        if self.stop_after == "builds":
            return

        stages = [(4, ALU.add), (3, ALU.add), (2, ALU.add), (1, ALU.add),
                  (0, ALU.add), (-1, ALU.subtract)]
        cur, nxt = "yA", "yB"
        self.w("tensor", "ss")
        bi = 0
        qtok = {q: self.tok("sv") for q in range(4)}
        for sidx, (L, op) in enumerate(stages):
            for quad in range(4):
                bn = [b for pr in POOL for b in pr][bi % 4]
                bi += 1
                bank_wait(bn)
                self.w("tensor", "sv", qtok[quad])
                bank = ps[bn]
                for hh in range(4):
                    h = quad * 4 + hh
                    for cc in (0, 1):
                        pc = 64 * cc
                        stat = sux_slice(h, pc) if L < 0 else pwx_slice(L, h, pc)
                        self.o("tensor", lambda e, st=stat, h=h, hh=hh, pc=pc,
                               bank=bank, cur=cur: e.matmul(
                            bank[pc:pc + 64, hh * 128:(hh + 1) * 128], st,
                            ap(sb[cur], pc, 64, h * 128, [[1, 128]]),
                            start=True, stop=True, tile_position=(pc, pc)),
                            "stt" if (hh == 3 and cc == 1) else None)
                self.w("vector", "stt")
                if op == ALU.add:
                    self.o("vector", lambda e, quad=quad, bank=bank, cur=cur,
                           nxt=nxt: e.tensor_tensor(
                        out=ap(sb[nxt], 0, 128, quad * 512, [[1, 512]]),
                        in0=ap(bank, 0, 128, 0, [[1, 512]]),
                        in1=ap(sb[cur], 0, 128, quad * 512, [[1, 512]]),
                        op=ALU.add), "sv")
                else:
                    self.o("vector", lambda e, quad=quad, bank=bank, cur=cur,
                           nxt=nxt: e.tensor_tensor(
                        out=ap(sb[nxt], 0, 128, quad * 512, [[1, 512]]),
                        in0=ap(sb[cur], 0, 128, quad * 512, [[1, 512]]),
                        in1=ap(bank, 0, 128, 0, [[1, 512]]),
                        op=ALU.subtract), "sv")
                btok[bn] = ("sv", self.tok("sv"))
                qtok[quad] = self.tok("sv")
            cur, nxt = nxt, cur
        zf = cur

        self.o("vector", lambda e, zf=zf: e.tensor_tensor(
            out=ap(sb["zknx"], 0, 128, 0, [[64, 16], [1, 64]]),
            in0=ap(sb[zf], 0, 128, 64, [[128, 16], [1, 64]]),
            in1=ap(sb["chiT"], 0, 128, a * 16, [[1, 16], [0, 64]]),
            op=ALU.mult), "sv")
        zknx_tok = self.tok("sv")
        if self.stop_after == "solve":
            return

        def bankcol(h):
            return (h // 2) * 64

        post_in_tok = self.tok("sv")

        def post_wave(mm_fn, evac_fn, need_tok):
            be, bo = POOL[self.pwave % 2]
            self.pwave += 1
            bank_wait(be)
            bank_wait(bo)
            self.w("tensor", "sv", need_tok)
            mm_fn(ps[be], ps[bo])
            self.w("vector", "stt")
            evac_fn(be, bo)
            btok[be] = ("sv", self.tok("sv"))
            btok[bo] = ("sv", self.tok("sv"))

        self.pwave = getattr(self, "pwave", 0)

        def mm_oin(bA, bB):
            for h in range(H):
                bk = bA if h < 8 else bB
                cb = (h % 8) * 64
                for cc in (0, 1):
                    pc = 64 * cc
                    self.o("tensor", lambda e, h=h, pc=pc, b=bk, cb=cb, zf=zf: e.matmul(
                        b[pc:pc + 64, cb:cb + 64],
                        ap(sb["csk"], pc, 64, h * 192, [[1, 64]]),
                        ap(sb[zf], pc, 64, h * 128, [[1, 64]]),
                        start=True, stop=True, tile_position=(pc, pc)),
                        "stt" if (h == H - 1 and cc == 1) else None)

        def ev_oin(be, bo):
            for g, bn in enumerate((be, bo)):
                self.o("vector", lambda e, g=g, bn=bn, a=a: e.tensor_tensor(
                    out=ap(sb["oin"], 0, 128, g * 512, [[1, 512]]),
                    in0=ap(ps[bn], 0, 128, 0, [[1, 512]]),
                    in1=ap(sb["omT"], 0, 128, a * 16 + g * 8, [[1, 8], [0, 64]]),
                    op=ALU.mult), "sv")
        post_wave(mm_oin, ev_oin, post_in_tok)

        def mm_qef(bA, bB):
            for h in range(H):
                ph = 64 * (h % 2)
                for cc in (0, 1):
                    pc = 64 * cc
                    b = bA if cc == 0 else bB
                    self.o("tensor", lambda e, h=h, ph=ph, pc=pc, b=b, zf=zf: e.matmul(
                        b[ph:ph + 64, bankcol(h):bankcol(h) + 64],
                        ap(sb[zf], pc, 64, h * 128 + 64, [[1, 64]]),
                        ap(sb["csk"], pc, 64, h * 192, [[1, 64]]),
                        start=True, stop=True, tile_position=(pc, ph)),
                        "stt" if (h == H - 1 and cc == 1) else None)

        def ev_qef(be, bo):
            for cc, bn in enumerate((be, bo)):
                self.o("vector", lambda e, cc=cc, bn=bn, c0=c0: e.tensor_tensor(
                    out=ap(sb["qef"], 0, 128, cc * 512, [[1, 512]]),
                    in0=ap(ps[bn], 0, 128, 0, [[1, 512]]),
                    in1=ap(sb["qk"], 0, 128, (c0 + cc) * C, [[TILE, 8], [1, 64]]),
                    op=ALU.add), "sv")
        post_wave(mm_qef, ev_qef, post_in_tok)

        def mm_gtm(bA, bB):
            for h in range(H):
                ph = 64 * (h % 2)
                for cc in (0, 1):
                    pc = 64 * cc
                    b = bA if cc == 0 else bB
                    self.o("tensor", lambda e, h=h, ph=ph, pc=pc, b=b: e.matmul(
                        b[ph:ph + 64, bankcol(h):bankcol(h) + 64],
                        ap(sb["zknx"], pc, 64, h * 64, [[1, 64]]),
                        ap(sb["ktm"], pc, 64, h * 64, [[1, 64]]),
                        start=True, stop=True, tile_position=(pc, ph)),
                        "stt" if (h == H - 1 and cc == 1) else None)

        def ev_gtm(be, bo):
            for cc, bn in enumerate((be, bo)):
                self.o("vector", lambda e, cc=cc, bn=bn: e.tensor_tensor(
                    out=ap(sb["gtm"], 0, 128, cc * 512, [[1, 512]]),
                    in0=ap(ps[bn], 0, 128, 0, [[1, 512]]),
                    in1=ap(sb["cstb"], 0, 128, CI_IREP8, [[1, 512]]),
                    op=ALU.add), "sv")
        post_wave(mm_gtm, ev_gtm, zknx_tok)

        def mm_um(bA, bB):
            for h in range(H):
                ph = 64 * (h % 2)
                for cc in (0, 1):
                    pc = 64 * cc
                    b = bA if cc == 0 else bB
                    self.o("tensor", lambda e, h=h, ph=ph, pc=pc, b=b, zf=zf: e.matmul(
                        b[ph:ph + 64, bankcol(h):bankcol(h) + 64],
                        ap(sb["csk"], pc, 64, h * 192 + 128, [[1, 64]]),
                        ap(sb[zf], pc, 64, h * 128, [[1, 64]]),
                        start=True, stop=True, tile_position=(pc, ph)),
                        "stt" if (h == H - 1 and cc == 1) else None)

        def ev_um(be, bo):
            for cc, bn in enumerate((be, bo)):
                for half in (0, 1):
                    self.o("vector", lambda e, cc=cc, half=half, bn=bn, c0=c0: e.tensor_tensor(
                        out=ap(sb["um"], 64 * half, 64, cc * 512, [[1, 512]]),
                        in0=ap(ps[bn], 64 * half, 64, 0, [[1, 512]]),
                        in1=ap(sb["gamrep"], 64 * half, 64, half * 8 + c0 + cc,
                               [[16, 8], [0, 64]]),
                        op=ALU.mult), "sv")
        post_wave(mm_um, ev_um, post_in_tok)
        if self.stop_after == "post":
            return

        for cc in (0, 1):
            pc = 64 * cc
            self.w("tensor", "sv")
            for bn in ("psq", "ppsk", "psl0", "psl1"):
                bank_wait(bn)
            for h in range(H):
                ph = 64 * (h % 2)
                bank = ps["psq"] if h % 2 == 0 else ps["ppsk"]
                self.o("tensor", lambda e, h=h, ph=ph, pc=pc, cc=cc, bank=bank: e.matmul(
                    bank[pc:pc + 64, bankcol(h):bankcol(h) + 64],
                    ap(sb["qef"], ph, 64, cc * 512 + bankcol(h), [[1, 64]]),
                    ap(sb["mst"], ph, 64, h * 64, [[1, 64]]),
                    start=True, stop=True, tile_position=(ph, pc)),
                    "stt" if h == H - 1 else None)
            owave_tok = self.tok("stt")
            for h in range(H):
                ph = 64 * (h % 2)
                bank = ps["psl0"] if h % 2 == 0 else ps["psl1"]
                self.o("tensor", lambda e, h=h, ph=ph, cc=cc, bank=bank: e.matmul(
                    bank[ph:ph + 64, bankcol(h):bankcol(h) + 64],
                    ap(sb["gtm"], ph, 64, cc * 512 + bankcol(h), [[1, 64]]),
                    ap(sb["mst"], ph, 64, h * 64, [[1, 64]]),
                    start=True, stop=True, tile_position=(ph, ph)),
                    "stt" if h == H - 1 else None)
            self.w("vector", "stt", owave_tok)
            for par, bn in ((0, "psq"), (1, "ppsk")):
                self.o("vector", lambda e, par=par, bn=bn, pc=pc, a=a: e.tensor_tensor(
                    out=ap(sb["xsq"], pc, 64, 0, [[1, 512]]),
                    in0=ap(ps[bn], pc, 64, 0, [[1, 512]]),
                    in1=ap(sb["omT"], pc, 64, a * 16 + par, [[2, 8], [0, 64]]),
                    op=ALU.mult), "sv")
                self.o("vector", lambda e, par=par, pc=pc, a=a: e.tensor_tensor(
                    out=ap(sb["otm"], pc, 64, a * HID + par * 64, [[128, 8], [1, 64]]),
                    in0=ap(sb["xsq"], pc, 64, 0, [[1, 512]]),
                    in1=oin_gather(par, pc),
                    op=ALU.add), "sv")
            self.w("vector", "stt")
            for par, bn in ((0, "psl0"), (1, "psl1")):
                phh = 64 * par
                self.o("vector", lambda e, par=par, bn=bn, phh=phh, cc=cc, c0=c0:
                       e.tensor_tensor(
                    out=ap(sb["xsq"], phh, 64, 512, [[1, 512]]),
                    in0=ap(ps[bn], phh, 64, 0, [[1, 512]]),
                    in1=ap(sb["gamrep"], phh, 64, par * 8 + c0 + cc, [[16, 8], [0, 64]]),
                    op=ALU.mult), "sv")
                self.o("vector", lambda e, par=par, phh=phh, cc=cc: e.tensor_tensor(
                    out=ap(sb["mst"], phh, 64, par * 64, [[128, 8], [1, 64]]),
                    in0=ap(sb["xsq"], phh, 64, 512, [[1, 512]]),
                    in1=um_gather(par, phh, cc),
                    op=ALU.add), "sv")
            for bn in ("psq", "ppsk", "psl0", "psl1"):
                btok[bn] = ("sv", self.tok("sv"))

    def ph_ffn(self, it, first_in_batch):
        sb, ps = self.sb, self.ps
        self.w("vector", "stt")
        for m in range(8):
            self.o("vector", lambda e, m=m: e.tensor_tensor(
                out=sb["sq"][:, 1:TILE], in0=sb["xnf"][:, m * TILE + 1:(m + 1) * TILE],
                in1=sb["xnf"][:, m * TILE:(m + 1) * TILE - 1], op=ALU.subtract), "sv")
            self.o("vector", lambda e, m=m: e.tensor_tensor(
                out=sb["sq"][:, 0:1], in0=sb["xnf"][:, m * TILE:m * TILE + 1],
                in1=sb["xbnd"][:, m:m + 1], op=ALU.subtract), "sv")
            self.o("vector", lambda e, m=m: e.tensor_copy(
                out=sb["ffb"][:, m * TILE + 1:(m + 1) * TILE],
                in_=sb["xnf"][:, m * TILE:(m + 1) * TILE - 1]), "sv")
            self.o("vector", lambda e, m=m: e.tensor_copy(
                out=sb["ffb"][:, m * TILE:m * TILE + 1], in_=sb["xbnd"][:, m:m + 1]), "sv")
            self.o("vector", lambda e, m=m: e.scalar_tensor_tensor(
                out=sb["ffa"][:, m * TILE:(m + 1) * TILE], in0=sb["sq"][:, :],
                scalar=sb["mus"][:, 2 * m:2 * m + 1],
                in1=sb["ffb"][:, m * TILE:(m + 1) * TILE],
                op0=ALU.mult, op1=ALU.add), "sv")
            self.o("vector", lambda e, m=m: e.scalar_tensor_tensor(
                out=sb["gf"][:, m * TILE:(m + 1) * TILE], in0=sb["sq"][:, :],
                scalar=sb["mus"][:, 2 * m + 1:2 * m + 2],
                in1=sb["ffb"][:, m * TILE:(m + 1) * TILE],
                op0=ALU.mult, op1=ALU.add), "sv")
        self.o("vector", lambda e: e.tensor_copy(
            out=sb["xbnd"][:, 0:8],
            in_=ap(sb["xnf"], 0, 128, TILE - 1, [[TILE, 8], [1, 1]])), "sv")

        def ev_r(m, bank):
            self.w("scalar", "stt")
            self.o("scalar", lambda e, m=m, b=bank: e.activation(
                out=sb["ffb"][:, m * TILE:(m + 1) * TILE], in_=b[:, 0:TILE],
                func=AF.Sigmoid), "ss")
            return ("ss", self.tok("ss"))
        self.load_w("Wkf")
        self.mm_proj("Wr", ev_r, src="ffa")

        def ev_k(m, bank):
            self.w("scalar", "stt")
            self.o("scalar", lambda e, m=m, b=bank: e.activation(
                out=sb["ffa"][:, m * TILE:(m + 1) * TILE], in_=b[:, 0:TILE],
                func=AF.Relu), "ss")
            self.w("vector", "ss")
            self.o("vector", lambda e, m=m: e.tensor_tensor(
                out=sb["ffa"][:, m * TILE:(m + 1) * TILE],
                in0=sb["ffa"][:, m * TILE:(m + 1) * TILE],
                in1=sb["ffa"][:, m * TILE:(m + 1) * TILE], op=ALU.mult), "sv")
            return ("ss", self.tok("ss"))
        self.load_w("Wvf")
        self.mm_proj("Wkf", ev_k, src="gf")

        def ev_v(m, bank):
            self.w("vector", "stt")
            self.o("vector", lambda e, m=m, b=bank: e.tensor_tensor(
                out=sb["branch"][:, m * TILE:(m + 1) * TILE], in0=b[:, 0:TILE],
                in1=sb["ffb"][:, m * TILE:(m + 1) * TILE], op=ALU.mult), "sv")
            return ("sv", self.tok("sv"))
        self.load_w("Wg")
        self.mm_proj("Wvf", ev_v, src="ffa")

    def ph_gate_wo(self, it):
        sb, ps = self.sb, self.ps

        def ev_g(m, bank):
            self.w("scalar", "stt")
            self.o("scalar", lambda e, m=m, b=bank: e.activation(
                out=sb["gf"][:, m * TILE:(m + 1) * TILE], in_=b[:, 0:TILE],
                func=AF.Silu), "ss")
            return ("ss", self.tok("ss"))
        self.load_w("Wo")
        self.mm_proj("Wg", ev_g)

        self.w("tensor", "sv")
        self.w("vector", "ss")
        for k in range(8):
            for blk in range(4):
                self.o("tensor", lambda e, k=k, blk=blk: e.transpose(
                    ps["ptr"][:, blk * 128:(blk + 1) * 128],
                    sb["otm"][:, blk * HID + k * 128: blk * HID + (k + 1) * 128],
                    sb["cstb"][:, CI_I128:CI_I128 + 128]),
                    "stt" if blk == 3 else None)
            self.w("vector", "stt")
            self.o("vector", lambda e, k=k: e.tensor_tensor(
                out=sb["ffa"][:, k * TILE:(k + 1) * TILE], in0=ps["ptr"][:, 0:512],
                in1=sb["gf"][:, k * TILE:(k + 1) * TILE], op=ALU.mult), "sv")
            self.w("tensor", "sv")

        def ev_o(m, bank):
            self.w("vector", "stt")
            self.o("vector", lambda e, m=m, b=bank: e.tensor_tensor(
                out=sb["branch"][:, m * TILE:(m + 1) * TILE], in0=b[:, 0:TILE],
                in1=sb["branch"][:, m * TILE:(m + 1) * TILE], op=ALU.add), "sv")
            return ("sv", self.tok("sv"))
        if it < self.ntile - 1:
            self.load_w("Wq")
        self.mm_proj("Wo", ev_o, src="ffa")

    def ph_epilogue(self, it):
        sb, ps = self.sb, self.ps
        t0 = it * TILE
        for st in range(4):
            buf = sb["xld0"] if st % 2 == 0 else sb["xld1"]
            xsem = "sxa" if st % 2 == 0 else "sxb"
            self.w("gpsimd", "sv", self.xld_free.get(st % 2, 0))
            self.o("gpsimd", lambda e, b=buf, st=st, t0=t0: e.dma_start(
                out=b[:, :], in_=self.x_hbm[t0 + st * 128: t0 + (st + 1) * 128, :]),
                xsem, 16)
            self.w("tensor", "sv")
            for r in range(2):
                if r == 1:
                    self.w("tensor", "sv")
                for i in range(4):
                    k = r * 4 + i
                    self.o("tensor", lambda e, k=k, i=i, st=st: e.transpose(
                        ps["ptr"][:, i * 128:(i + 1) * 128],
                        ap(sb["branch"], 0, 128, k * TILE + st * 128, [[1, 128]]),
                        sb["cstb"][:, CI_I128:CI_I128 + 128]),
                        "stt" if i == 3 else None)
                self.w("vector", "stt")
                self.w("vector", xsem)
                if r == 0:
                    self.w("vector", "sdo", self.tok("sdo"))
                self.o("vector", lambda e, r=r, b=buf: e.tensor_tensor(
                    out=sb["outs"][:, r * 512:(r + 1) * 512], in0=ps["ptr"][:, 0:512],
                    in1=b[:, r * 512:(r + 1) * 512], op=ALU.add), "sv")
                self.w("tensor", "sv")
            self.w("gpsimd", "sv")
            self.o("gpsimd", lambda e, st=st, t0=t0: e.dma_start(
                out=self.out_hbm[t0 + st * 128: t0 + (st + 1) * 128, :],
                in_=sb["outs"][:, :]), "sdo", 16)
            self.xld_free[st % 2] = self.tok("sv")

    def ph_debug_dump(self):
        sb = self.sb
        self.w("gpsimd", "sv")
        self.w("gpsimd", "ss")
        for n, (src, dst) in self.dbg.items():
            self.o("gpsimd", lambda e, s=src, d=dst: e.dma_start(
                out=d, in_=sb[s][:, :]), "sdo", 16)

    def ph_finish(self):
        self.w("gpsimd", "sdo")


def _prep_inputs(core, x, ins):
    xs = np.ascontiguousarray(x[4 * core: 4 * core + 4].reshape(-1, HID))
    d = {"x": xs}
    d["W_Wq"] = ins["L0_attn_Wq"]
    d["W_Wk"] = ins["L0_attn_Wk"]
    d["W_Wv"] = ins["L0_attn_Wv"]
    d["W_Wg"] = ins["L0_attn_Wg"]
    d["W_Wo"] = ins["L0_attn_Wo"]
    d["W_Wr"] = ins["L0_ffn_W_r"]
    d["W_Wkf"] = ins["L0_ffn_W_k"]
    d["W_Wvf"] = ins["L0_ffn_W_v"]
    d["Wab"] = np.concatenate([ins["L0_attn_Wa"], ins["L0_attn_Wb"]], axis=0)
    bab = np.zeros((16, 2), dtype=np.float32)
    bab[:, 0] = ins["L0_attn_ba"]
    bab[:, 1] = ins["L0_attn_bb"]
    d["bab"] = bab
    mu = np.zeros((128, 16), dtype=np.float32)
    for k in range(8):
        mu[:, 2 * k] = ins["L0_ffn_mu_r"][k * 128:(k + 1) * 128]
        mu[:, 2 * k + 1] = ins["L0_ffn_mu_k"][k * 128:(k + 1) * 128]
    d["mu"] = mu
    d["consts"] = _host_consts()
    return {k: np.ascontiguousarray(v) for k, v in d.items()}


def kernel(**inputs):
    x = inputs["L0_attn_x"]
    B = x.shape[0]
    p = Prog(nb=B // NCORES, t_len=x.shape[1])
    nc = p.build()
    in_maps = [_prep_inputs(c, x, inputs) for c in range(NCORES)]
    res = run_bass_kernel_spmd(nc, in_maps, core_ids=list(range(NCORES)))
    out = np.concatenate(
        [r["out"].reshape(B // NCORES, x.shape[1], HID) for r in res.results], axis=0)
    return out.astype(np.float32)


# version 09
# speedup vs baseline: 1.2962x; 1.0194x over previous
import numpy as np
import ml_dtypes
import concourse.bass as bass
import concourse.mybir as mybir
from concourse.bass_utils import run_bass_kernel_spmd

F32 = mybir.dt.float32
BF16 = mybir.dt.bfloat16
AF = mybir.ActivationFunctionType
ALU = mybir.AluOpType
AX = mybir.AxisListType

NCORES = 8
HID = 1024
H = 16
DH = 64
C = 64
TILE = 512
NCH = TILE // C
NPAIR = NCH // 2
EPS_LN = 1e-5
CLAMP = 60.0
BIG_W = ["Wq", "Wk", "Wv", "Wg", "Wo", "Wr", "Wkf", "Wvf"]

CI_I128 = 0
CI_ISTK = 128
CI_MPM = 192
CI_MSU = 256
CI_MSL = 320
CI_IND = 384
CI_ONES = 512
CI_M3B = 640
CI_SCAL = 832
CI_IREP8 = 848
NCONST = 1360


def _host_consts():
    c = np.zeros((128, NCONST), dtype=np.float32)
    c[np.arange(128), CI_I128 + np.arange(128)] = 1.0
    c[np.arange(128), CI_ISTK + np.tile(np.arange(64), 2)] = 1.0
    s = np.tile(np.arange(64), 2)[:, None]
    t = np.arange(64)[None, :]
    c[:, CI_MPM:CI_MPM + 64] = (s <= t)
    c[:, CI_MSU:CI_MSU + 64] = (s < t)
    c[:, CI_MSL:CI_MSL + 64] = (t < s)
    for k in range(8):
        c[0:64, CI_IND + 16 * k + 2 * k] = 1.0
        c[64:128, CI_IND + 16 * k + 2 * k + 1] = 1.0
    c[:, CI_ONES:CI_ONES + 128] = 1.0
    c[:, CI_M3B:CI_M3B + 64] = (s <= t)
    c[:, CI_M3B + 64:CI_M3B + 128] = (s < t)
    c[:, CI_M3B + 128:CI_M3B + 192] = 1.0
    c[:, CI_SCAL] = EPS_LN
    for a in range(8):
        c[np.arange(128), CI_IREP8 + 64 * a + (np.arange(128) % 64)] = 1.0
    return c


def ap(t, pbase, pcount, off, dims):
    rowlen = t.shape[1]
    return bass.AP(t, pbase * rowlen + off, [[rowlen, pcount]] + list(dims))


class Prog:
    def __init__(self, nb, t_len, debug=False, stop_after=None):
        self.stop_after = stop_after
        self.nb, self.T, self.debug = nb, t_len, debug
        self.ntb = t_len // TILE
        self.ntile = nb * self.ntb
        self.nc = bass.Bass("TRN2", target_bir_lowering=False)
        self.ctx, self.cnt, self.sems, self.sb, self.ps = [], {}, {}, {}, {}
        self.prog = {"sync": [], "vector": [], "scalar": [], "tensor": [], "gpsimd": []}

    def sem(self, n):
        s = self.nc.semaphore(n)
        self.sems[n] = s.__enter__()
        self.ctx.append(s)
        self.cnt[n] = 0

    def sbuf(self, n, shape, dt):
        t = self.nc.sbuf_tensor(n, shape, dt)
        self.sb[n] = t.__enter__()
        self.ctx.append(t)

    def psum(self, n, shape, dt=F32):
        t = self.nc.psum_tensor(n, shape, dt)
        self.ps[n] = t.__enter__()
        self.ctx.append(t)

    def w(self, eng, sem, val=None):
        v = self.cnt[sem] if val is None else val
        if v <= 0:
            return
        s = self.sems[sem]
        self.prog[eng].append(lambda e, s=s, v=v: e.wait_ge(s, v))

    def o(self, eng, build, sem=None, n=1):
        if eng in ("vector", "scalar"):
            ssem = "sv" if eng == "vector" else "ss"
            v = self.cnt[ssem]
            if v > 0:
                sh = self.sems[ssem]
                self.prog[eng].append(lambda e, sh=sh, v=v: e.wait_ge(sh, v))
        if sem is None:
            self.prog[eng].append(build)
        else:
            s = self.sems[sem]
            self.prog[eng].append(lambda e, s=s, b=build, n=n: b(e).then_inc(s, n))
            self.cnt[sem] += n

    def tok(self, sem):
        return self.cnt[sem]

    def build(self):
        nc = self.nc
        NTOK = self.nb * self.T
        self.x_hbm = nc.dram_tensor("x", [NTOK, HID], F32, kind="ExternalInput").ap()
        self.out_hbm = nc.dram_tensor("out", [NTOK, HID], F32, kind="ExternalOutput").ap()
        self.w_hbm = {n: nc.dram_tensor("W_" + n, [HID, HID], F32, kind="ExternalInput").ap()
                      for n in BIG_W}
        self.wab_hbm = nc.dram_tensor("Wab", [32, HID], F32, kind="ExternalInput").ap()
        self.bab_hbm = nc.dram_tensor("bab", [16, 2], F32, kind="ExternalInput").ap()
        self.mu_hbm = nc.dram_tensor("mu", [128, 16], F32, kind="ExternalInput").ap()
        self.cst_hbm = nc.dram_tensor("consts", [128, NCONST], F32, kind="ExternalInput").ap()
        self.wt_dram = {n: nc.dram_tensor("WT_" + n, [HID, HID], BF16).ap() for n in BIG_W}

        for s in ["swa", "swb", "sxa", "sxb", "sgr", "sdo", "sv", "ss", "stt", "sg"]:
            self.sem(s)

        S = self.sbuf
        S("cstf", [128, NCONST], F32)
        self.nc.const_aps.register(F32, EPS_LN,
                                   self.sb["cstf"][:, CI_SCAL:CI_SCAL + 1])
        S("cstb", [128, NCONST], BF16)
        S("wtb0", [128, 8 * HID], BF16)
        S("wtb1", [128, 8 * HID], BF16)
        S("wabt", [128, 8 * 32], BF16)
        S("babs", [16, 2], F32)
        S("babn", [16, 2], F32)
        S("mus", [128, 16], F32)
        S("xld0", [128, HID], F32)
        S("xld1", [128, HID], F32)
        S("xsq", [128, HID], F32)
        S("xntm", [128, HID], BF16)
        S("xnf", [128, 8 * TILE], BF16)
        S("xbnd", [128, 8], BF16)
        S("qk", [128, 16 * TILE], BF16)
        S("vtm", [128, 4 * HID], BF16)
        S("gf", [128, 8 * TILE], BF16)
        S("sq", [128, TILE], BF16)
        S("stat", [128, 8], F32)
        S("zsig", [16, 2 * TILE], F32)
        for n in ["la", "lc", "gamv", "uv", "nqv", "nkv", "chiv", "zetv", "omv",
                  "t16a", "t16b"]:
            S(n, [16, TILE], F32)
        S("chiT", [128, 4 * 16], F32)
        S("zetT", [128, 4 * 16], F32)
        S("omT", [128, 4 * 16], F32)
        S("gamrow", [1, 128], F32)
        S("gamrep", [128, 128], F32)
        S("csk", [128, H * 192], BF16)
        S("susl", [128, H * 64], BF16)
        S("ktm", [128, H * 64], BF16)
        S("zknx", [128, H * 64], BF16)
        S("yA", [128, H * 128], BF16)
        S("yB", [128, H * 128], BF16)
        S("pwx", [128, 5 * H * 64], BF16)
        S("pwt", [128, 2 * H * 64], BF16)
        S("oin", [128, 2 * 512], BF16)
        S("qef", [128, 2 * 512], BF16)
        S("gtm", [128, 2 * 512], BF16)
        S("um", [128, 2 * 512], BF16)
        S("mst", [128, H * 64], BF16)
        S("otm", [128, 4 * HID], BF16)
        S("branch", [128, 8 * TILE], BF16)
        S("ffa", [128, 8 * TILE], BF16)
        S("ffb", [128, 8 * TILE], BF16)
        S("outs", [128, HID], F32)

        Pm = self.psum
        Pm("pp0", [128, 512])
        Pm("pp1", [128, 512])
        Pm("ppsk", [128, 512])
        Pm("psl0", [128, 512])
        Pm("psl1", [128, 512])
        Pm("psq", [128, 512])
        Pm("ptr", [128, 512], BF16)
        Pm("pscl", [128, 512])

        if self.debug:
            self.dbg = {}
            for n, src in [("d_xnf", "xnf"), ("d_qk", "qk"),
                           ("d_vtm", "vtm"), ("d_csk", "csk"), ("d_susl", "susl"),
                           ("d_y", "yA"), ("d_oin", "oin"), ("d_qef", "qef"),
                           ("d_gtm", "gtm"), ("d_um", "um"), ("d_mst", "mst"),
                           ("d_otm", "otm"), ("d_branch", "branch"),
                           ("d_gf", "gf"), ("d_ffa", "ffa")]:
                t = self.sb[src]
                self.dbg[n] = (src, self.nc.dram_tensor(
                    n, [t.shape[0], t.shape[1]], BF16, kind="ExternalOutput").ap())
            for n, src in [("d_chi", "chiv"), ("d_gam", "gamv"), ("d_zet", "zetv"),
                           ("d_om", "omv"), ("d_nq", "nqv"), ("d_nk", "nkv"),
                           ("d_la", "la"), ("d_gamrep", "gamrep")]:
                t = self.sb[src]
                self.dbg[n] = (src, self.nc.dram_tensor(
                    n, [t.shape[0], t.shape[1]], F32, kind="ExternalOutput").ap())

        self.emit_all()

        with nc.Block() as blk:
            pr = self.prog

            @blk.sync
            def _(e):
                for f in pr["sync"]:
                    f(e)

            @blk.vector
            def _(e):
                for f in pr["vector"]:
                    f(e)

            @blk.scalar
            def _(e):
                for f in pr["scalar"]:
                    f(e)

            @blk.tensor
            def _(e):
                for f in pr["tensor"]:
                    f(e)

            @blk.gpsimd
            def _(e):
                for f in pr["gpsimd"]:
                    f(e)
        return nc

    def emit_all(self):
        self.ph_setup()
        self.ph_weight_prep()
        self.slot_free = {0: self.tok("stt"), 1: self.tok("stt")}
        self.xld_free = {0: self.tok("sv"), 1: self.tok("sv")}
        self.wtok = {}
        self.wslot = {}
        self.wload_next = 0
        self.load_w("Wq")
        for it in range(self.ntile):
            self.ph_tile(it)
        self.ph_finish()

    def load_w(self, wn):
        sb = self.sb
        slot = self.wload_next
        self.wload_next ^= 1
        self.wslot[wn] = slot
        sem = "swa" if slot == 0 else "swb"
        dstw = sb["wtb0"] if slot == 0 else sb["wtb1"]
        self.w("sync", "stt", self.slot_free.get(slot, 0))
        self.w("sync", "sv", self.prep_guard.get(sem, 0))
        self.prep_guard[sem] = 0
        self.w("sync", "sg", self.tok("sg"))
        for k in range(8):
            self.o("sync", lambda e, wn=wn, k=k, d=dstw: e.dma_start(
                out=d[:, k * HID:(k + 1) * HID],
                in_=self.wt_dram[wn][k * 128:(k + 1) * 128, :]), sem, 16)
        self.wtok[wn] = (sem, self.tok(sem))

    def ph_setup(self):
        sb = self.sb
        self.o("sync", lambda e: e.dma_start(out=sb["cstf"][:, :], in_=self.cst_hbm), "swa", 16)
        self.o("sync", lambda e: e.dma_start(out=sb["babs"][:, :], in_=self.bab_hbm), "swa", 16)
        self.o("sync", lambda e: e.dma_start(out=sb["mus"][:, :], in_=self.mu_hbm), "swa", 16)
        self.w("vector", "swa")
        self.o("vector", lambda e: e.tensor_copy(out=sb["cstb"][:, :], in_=sb["cstf"][:, :]), "sv")
        self.o("vector", lambda e: e.tensor_scalar(
            out=sb["babn"][:, :], in0=sb["babs"][:, :], scalar1=-1.0, scalar2=None,
            op0=ALU.mult), "sv")
        self.o("vector", lambda e: e.memset(sb["mst"][:, :], 0.0), "sv")
        self.o("vector", lambda e: e.memset(sb["xbnd"][:, :], 0.0), "sv")

    def ph_weight_prep(self):
        sb, ps = self.sb, self.ps
        self.prep_guard = {"swa": self.tok("sv"), "swb": 0}
        I128b = sb["cstb"][:, CI_I128:CI_I128 + 128]
        for wn in BIG_W:
            for j in range(8):
                buf = sb["xld0"] if j % 2 == 0 else sb["xld1"]
                psem = "swa" if j % 2 == 0 else "swb"
                self.w("sync", "sv", self.prep_guard.get(psem, 0))
                self.o("sync", lambda e, s=self.w_hbm[wn], j=j, b=buf: e.dma_start(
                    out=b[:, :], in_=s[j * 128:(j + 1) * 128, :]), psem, 16)
                self.w("vector", psem)
                cols = (j % 2) * HID
                self.o("vector", lambda e, b=buf, c=cols: e.tensor_copy(
                    out=sb["wtb1"][:, c:c + HID], in_=b[:, :]), "sv")
                self.prep_guard[psem] = self.tok("sv")
                for r in range(2):
                    self.w("tensor", "sv")
                    self.w("tensor", "sg")
                    for i in range(4):
                        ii = r * 4 + i
                        self.o("tensor", lambda e, ii=ii, c=cols, i=i: e.transpose(
                            ps["ptr"][:, i * 128:(i + 1) * 128],
                            sb["wtb1"][:, c + ii * 128:c + ii * 128 + 128], I128b),
                            "stt" if i == 3 else None)
                    self.w("vector", "stt")
                    self.o("vector", lambda e, r=r: e.tensor_copy(
                        out=sb["wtb0"][:, r * 512:(r + 1) * 512],
                        in_=ps["ptr"][:, 0:512]), "sv")
                    self.w("gpsimd", "sv")
                    for i in range(4):
                        ii = r * 4 + i
                        self.o("gpsimd", lambda e, ii=ii, i=i, r=r, wn=wn, j=j: e.dma_start(
                            out=self.wt_dram[wn][ii * 128:(ii + 1) * 128,
                                                 j * 128:(j + 1) * 128],
                            in_=sb["wtb0"][:, r * 512 + i * 128: r * 512 + (i + 1) * 128]),
                            "sg", 16)
        self.w("sync", "sv", self.prep_guard.get("swa", 0))
        self.o("sync", lambda e: e.dma_start(out=sb["xld0"][0:32, :], in_=self.wab_hbm),
               "swa", 16)
        self.w("vector", "swa")
        self.w("vector", "sg")
        self.o("vector", lambda e: e.tensor_copy(
            out=sb["wtb1"][0:32, 0:HID], in_=sb["xld0"][0:32, :]), "sv")
        self.w("tensor", "sv")
        self.w("tensor", "sg")
        for i in range(8):
            self.o("tensor", lambda e, i=i: e.transpose(
                ps["ptr"][:, (i % 4) * 32:(i % 4) * 32 + 32],
                sb["wtb1"][0:32, i * 128:i * 128 + 128],
                sb["cstb"][0:32, CI_I128:CI_I128 + 32]),
                "stt" if i % 4 == 3 else None)
            if i % 4 == 3:
                self.w("vector", "stt")
                r = i // 4
                self.o("vector", lambda e, r=r: e.tensor_copy(
                    out=sb["wabt"][:, r * 128:(r + 1) * 128], in_=ps["ptr"][:, 0:128]),
                    "sv")
                if r == 0:
                    self.w("tensor", "sv")

    def mm_proj_start(self, wn, evac, src=None, tokmajor=False, src_tok=None):
        st = {"wn": wn, "evac": evac, "tokmajor": tokmajor,
              "srct": self.sb["xnf"] if src is None else self.sb[src],
              "wt": self.sb["wtb0"] if self.wslot[wn] == 0 else self.sb["wtb1"],
              "ev_tok": {0: None, 1: None}, "first": True,
              "src_tok": src_tok}
        return st

    def mm_proj_step(self, st, m):
        sb, ps = self.sb, self.ps
        wn, evac, srct, wt = st["wn"], st["evac"], st["srct"], st["wt"]
        tokmajor, ev_tok = st["tokmajor"], st["ev_tok"]
        if st["first"]:
            st["first"] = False
            self.w("tensor", self.wtok[wn][0], self.wtok[wn][1])
            if st["src_tok"] is None:
                self.w("tensor", "sv")
                self.w("tensor", "ss")
            else:
                for sem, tok in st["src_tok"]:
                    self.w("tensor", sem, tok)
        bank = ps["pp0"] if m % 2 == 0 else ps["pp1"]
        if ev_tok[m % 2] is not None:
            self.w("tensor", ev_tok[m % 2][0], ev_tok[m % 2][1])
        for k in range(8):
            if tokmajor:
                tb, oh = m // 2, m % 2
                self.o("tensor", lambda e, k=k, b=bank, wt=wt, s=srct, tb=tb,
                       oh=oh: e.matmul(
                    b[:, 0:TILE],
                    s[:, k * TILE + tb * 128: k * TILE + (tb + 1) * 128],
                    wt[:, k * HID + oh * 512: k * HID + (oh + 1) * 512],
                    start=(k == 0), stop=(k == 7)), "stt" if k == 7 else None)
            else:
                self.o("tensor", lambda e, m=m, k=k, b=bank, wt=wt, s=srct: e.matmul(
                    b[:, 0:TILE], wt[:, k * HID + m * 128: k * HID + m * 128 + 128],
                    s[:, k * TILE:(k + 1) * TILE],
                    start=(k == 0), stop=(k == 7)), "stt" if k == 7 else None)
        ev_tok[m % 2] = evac(m, bank)
        if m == 7:
            self.slot_free[self.wslot[wn]] = self.tok("stt")

    def mm_proj(self, wn, evac, src=None, tokmajor=False):
        st = self.mm_proj_start(wn, evac, src, tokmajor)
        for m in range(8):
            self.mm_proj_step(st, m)
        return


    def ev_copy(self, dst, off=0):
        sb = self.sb
        def f(m, bank):
            self.w("vector", "stt")
            self.o("vector", lambda e, m=m, b=bank: e.tensor_copy(
                out=sb[dst][:, off + m * TILE:off + (m + 1) * TILE],
                in_=b[:, 0:TILE]), "sv")
            return ("sv", self.tok("sv"))
        return f

    def ph_tile(self, it):
        sb, ps = self.sb, self.ps
        ntb = self.ntb
        tt = it % ntb
        t0 = it * TILE
        first_in_batch = (tt == 0)

        if first_in_batch and it > 0:
            self.w("vector", "stt")
            self.o("vector", lambda e: e.memset(sb["mst"][:, :], 0.0), "sv")
            self.o("vector", lambda e: e.memset(sb["xbnd"][:, :], 0.0), "sv")

        if self.stop_after == "prep":
            return
        for st in range(4):
            buf = sb["xld0"] if st % 2 == 0 else sb["xld1"]
            self.w("gpsimd", "sv", getattr(self, "xld_free", {}).get(st % 2, 0))
            xsem = "sxa" if st % 2 == 0 else "sxb"
            self.o("gpsimd", lambda e, b=buf, st=st, t0=t0: e.dma_start(
                out=b[:, :], in_=self.x_hbm[t0 + st * 128: t0 + (st + 1) * 128, :]),
                xsem, 16)
            self.w("vector", xsem)
            self.w("scalar", xsem)
            self.o("vector", lambda e, b=buf: e.reduce_sum(
                out=sb["stat"][:, 0:1], in_=b[:, :], axis=AX.X), "sv")
            self.w("scalar", "sv")
            self.o("scalar", lambda e, b=buf: e.activation(
                out=sb["xsq"][:, :], in_=b[:, :], func=AF.Square,
                accum_out=sb["stat"][:, 1:2]), "ss")
            self.w("vector", "ss")
            self.o("vector", lambda e: e.tensor_scalar(
                out=sb["stat"][:, 2:3], in0=sb["stat"][:, 0:1],
                scalar1=1.0 / HID, scalar2=None, op0=ALU.mult), "sv")
            self.o("vector", lambda e: e.tensor_scalar(
                out=sb["stat"][:, 4:5], in0=sb["stat"][:, 1:2],
                scalar1=1.0 / HID, scalar2=None, op0=ALU.mult), "sv")
            self.o("vector", lambda e: e.scalar_tensor_tensor(
                out=sb["stat"][:, 5:6], in0=sb["stat"][:, 2:3],
                scalar=sb["stat"][:, 2:3], in1=sb["stat"][:, 4:5],
                op0=ALU.mult, op1=ALU.subtract), "sv")
            self.w("scalar", "sv")
            self.o("scalar", lambda e: e.activation(
                out=sb["stat"][:, 6:7], in_=sb["stat"][:, 5:6], func=AF.Ln,
                scale=-1.0, bias=EPS_LN), "ss")
            self.o("scalar", lambda e: e.activation(
                out=sb["stat"][:, 7:8], in_=sb["stat"][:, 6:7], func=AF.Exp,
                scale=-0.5), "ss")
            self.w("vector", "ss")
            self.o("vector", lambda e, b=buf: e.tensor_scalar(
                out=sb["xntm"][:, :], in0=b[:, :], scalar1=sb["stat"][:, 2:3],
                scalar2=sb["stat"][:, 7:8], op0=ALU.subtract, op1=ALU.mult), "sv")
            if not hasattr(self, "xld_free"):
                self.xld_free = {}
            self.xld_free[st % 2] = self.tok("sv")
            self.w("tensor", "sv")
            for r in range(2):
                if r == 1:
                    self.w("tensor", "sv")
                for i in range(4):
                    k = r * 4 + i
                    self.o("tensor", lambda e, k=k, i=i: e.transpose(
                        ps["ptr"][:, i * 128:(i + 1) * 128],
                        sb["xntm"][:, k * 128:(k + 1) * 128],
                        sb["cstb"][:, CI_I128:CI_I128 + 128]),
                        "stt" if i == 3 else None)
                self.w("vector", "stt")
                self.o("vector", lambda e, r=r, st=st: e.tensor_copy(
                    out=ap(sb["xnf"], 0, 128, r * 4 * TILE + st * 128,
                           [[TILE, 4], [1, 128]]),
                    in_=ps["ptr"][:, 0:512]), "sv")

        if self.stop_after == "ln":
            return
        self.load_w("Wk")
        self.mm_proj("Wq", self.ev_copy("qk", 0))
        self.load_w("Wv")
        self.mm_proj("Wk", self.ev_copy("qk", 8 * TILE))
        self.load_w("Wr")

        def ev_vtm(m, bank):
            tb, oh = m // 2, m % 2
            self.w("vector", "stt")
            self.o("vector", lambda e, b=bank, tb=tb, oh=oh: e.tensor_copy(
                out=sb["vtm"][:, tb * HID + oh * 512: tb * HID + (oh + 1) * 512],
                in_=b[:, 0:TILE]), "sv")
            return ("sv", self.tok("sv"))
        self.mm_proj("Wv", ev_vtm, tokmajor=True)

        if self.stop_after == "proj":
            return
        self.ph_scales(it)

        if self.stop_after == "scales":
            return

        self.ph_ffn_mix()

        self.projq = []
        if self.stop_after is None:
            self.build_projq(it)

        for a in range(NPAIR):
            self.ph_pair(it, a)
        while self.projq:
            self.projq.pop(0)()
        if self.stop_after in ("pskv", "pskv1", "pskvnoev", "pskv1noev",
                               "builds", "solve", "post", "pairs", "ffn"):
            return

        self.ph_gate_wo(it)

        self.ph_epilogue(it)

        if self.debug and it == self.ntile - 1:
            self.ph_debug_dump()

    def ph_scales(self, it):
        sb, ps = self.sb, self.ps
        self.w("tensor", "sv")
        for k in range(8):
            self.o("tensor", lambda e, k=k: e.matmul(
                ps["pscl"][0:16, 0:TILE], sb["wabt"][:, k * 32:k * 32 + 16],
                sb["xnf"][:, k * TILE:(k + 1) * TILE],
                start=(k == 0), stop=(k == 7)))
            self.o("tensor", lambda e, k=k: e.matmul(
                ps["psq"][0:16, 0:TILE], sb["wabt"][:, k * 32 + 16:k * 32 + 32],
                sb["xnf"][:, k * TILE:(k + 1) * TILE],
                start=(k == 0), stop=(k == 7)), "stt" if k == 7 else None)
        self.w("scalar", "stt")
        self.o("scalar", lambda e: e.activation(
            out=sb["zsig"][:, 0:TILE], in_=ps["pscl"][0:16, 0:TILE], func=AF.Sigmoid,
            bias=sb["babs"][:, 0:1]), "ss")
        self.o("scalar", lambda e: e.activation(
            out=sb["zsig"][:, TILE:2 * TILE], in_=ps["psq"][0:16, 0:TILE], func=AF.Sigmoid,
            bias=sb["babs"][:, 1:2]), "ss")
        self.w("vector", "ss")
        self.o("vector", lambda e: e.tensor_scalar(
            out=sb["t16a"][:, :], in0=sb["zsig"][0:16, 0:TILE], scalar1=1e-35,
            scalar2=None, op0=ALU.max), "sv")
        self.w("scalar", "sv")
        self.o("scalar", lambda e: e.activation(
            out=sb["la"][:, :], in_=sb["t16a"][:, :], func=AF.Ln), "ss")
        self.w("vector", "ss")
        for c in range(NCH):
            self.o("vector", lambda e, c=c: e.tensor_tensor_scan(
                out=sb["lc"][:, c * C:(c + 1) * C],
                data0=sb["la"][:, c * C:(c + 1) * C],
                data1=sb["la"][:, c * C:(c + 1) * C], initial=0.0,
                op0=ALU.add, op1=ALU.bypass), "sv")
        self.w("scalar", "sv")
        self.o("scalar", lambda e: e.activation(
            out=sb["gamv"][:, :], in_=sb["lc"][:, :], func=AF.Exp), "ss")
        self.w("vector", "ss")
        self.o("vector", lambda e: e.tensor_scalar(
            out=sb["t16a"][:, :], in0=sb["lc"][:, :], scalar1=-1.0, scalar2=CLAMP,
            op0=ALU.mult, op1=ALU.min), "sv")
        self.w("scalar", "sv")
        self.o("scalar", lambda e: e.activation(
            out=sb["uv"][:, :], in_=sb["t16a"][:, :], func=AF.Exp), "ss")

        for (nm, soff) in [("nqv", 0), ("nkv", 8 * TILE)]:
            for k in range(8):
                self.w("vector", "stt")
                self.o("vector", lambda e, k=k, soff=soff: e.tensor_tensor(
                    out=sb["sq"][:, :],
                    in0=sb["qk"][:, soff + k * TILE:soff + (k + 1) * TILE],
                    in1=sb["qk"][:, soff + k * TILE:soff + (k + 1) * TILE],
                    op=ALU.mult), "sv")
                self.w("tensor", "sv")
                self.o("tensor", lambda e, k=k: e.matmul(
                    ps["psq"][0:16, 0:TILE],
                    sb["cstb"][:, CI_IND + 16 * k: CI_IND + 16 * (k + 1)],
                    sb["sq"][:, :], start=(k == 0), stop=(k == 7)), "stt")
            self.w("vector", "stt")
            self.w("vector", "ss")
            self.o("vector", lambda e: e.tensor_scalar(
                out=sb["t16a"][:, :], in0=ps["psq"][0:16, 0:TILE],
                scalar1=1e-12, scalar2=None, op0=ALU.max), "sv")
            self.w("scalar", "sv")
            self.o("scalar", lambda e: e.activation(
                out=sb["t16b"][:, :], in_=sb["t16a"][:, :], func=AF.Ln), "ss")
            self.o("scalar", lambda e, nm=nm: e.activation(
                out=sb[nm][:, :], in_=sb["t16b"][:, :], func=AF.Exp, scale=-0.5), "ss")

        self.w("vector", "ss")
        self.o("vector", lambda e: e.tensor_tensor(
            out=sb["t16a"][:, :], in0=sb["nkv"][:, :], in1=sb["nkv"][:, :],
            op=ALU.mult), "sv")
        self.o("vector", lambda e: e.tensor_tensor(
            out=sb["t16a"][:, :], in0=sb["t16a"][:, :], in1=sb["uv"][:, :],
            op=ALU.mult), "sv")
        self.o("vector", lambda e: e.tensor_tensor(
            out=sb["t16a"][:, :], in0=sb["t16a"][:, :], in1=sb["gamv"][:, :],
            op=ALU.mult), "sv")
        self.o("vector", lambda e: e.tensor_tensor(
            out=sb["chiv"][:, :], in0=sb["t16a"][:, :], in1=sb["zsig"][0:16, TILE:2 * TILE],
            op=ALU.mult), "sv")
        self.o("vector", lambda e: e.tensor_tensor(
            out=sb["t16b"][:, :], in0=sb["nkv"][:, :], in1=sb["gamv"][:, :],
            op=ALU.mult), "sv")
        self.o("vector", lambda e: e.reciprocal(
            out=sb["zetv"][:, :], in_=sb["t16b"][:, :]), "sv")
        self.o("vector", lambda e: e.tensor_tensor(
            out=sb["omv"][:, :], in0=sb["nqv"][:, :], in1=sb["gamv"][:, :],
            op=ALU.mult), "sv")

        self.w("tensor", "sv")
        for (vn, tn) in [("chiv", "chiT"), ("zetv", "zetT"), ("omv", "omT")]:
            for blk in range(4):
                self.o("tensor", lambda e, vn=vn, blk=blk: e.transpose(
                    ps["pscl"][:, 256 + blk * 16: 256 + (blk + 1) * 16],
                    sb[vn][:, blk * 128:(blk + 1) * 128],
                    sb["cstf"][0:16, CI_I128:CI_I128 + 16]),
                    "stt" if blk == 3 else None)
            self.w("vector", "stt")
            self.o("vector", lambda e, tn=tn: e.tensor_copy(
                out=sb[tn][:, :], in_=ps["pscl"][:, 256:256 + 64]), "sv")
            self.w("tensor", "sv")

        self.w("gpsimd", "sv")
        self.o("gpsimd", lambda e: e.dma_start(
            out=sb["gamrow"][0:1, 0:128],
            in_=ap(sb["gamv"], 0, 16, C - 1, [[C, NCH]]),
            allow_slow_non_contiguous=True), "sgr", 16)
        self.w("tensor", "sgr")
        self.o("tensor", lambda e: e.matmul(
            ps["pscl"][:, 384:384 + 128], sb["cstf"][0:1, CI_ONES:CI_ONES + 128],
            sb["gamrow"][0:1, 0:128], start=True, stop=True), "stt")
        self.w("vector", "stt")
        self.o("vector", lambda e: e.tensor_copy(
            out=sb["gamrep"][:, :], in_=ps["pscl"][:, 384:384 + 128]), "sv")

    def pq_run(self, n):
        for _ in range(n):
            if self.projq:
                self.projq.pop(0)()

    def ph_pair(self, it, a):
        sb, ps = self.sb, self.ps
        if not hasattr(self, "projq"):
            self.projq = []

        def oin_gather(par, pc):
            return ap(self.sb["oin"], pc, 64, par * 64, [[512, 2], [128, 4], [1, 64]])

        def um_gather(par, phh, cc):
            return ap(self.sb["um"], phh, 64, cc * 512, [[64, 8], [1, 64]])
        cstb = self.sb["cstb"]
        c0 = 2 * a
        if not hasattr(self, "btok"):
            self.btok = {}
        btok = self.btok

        def bank_wait(bn):
            t = btok.get(bn)
            if t is not None:
                self.w("tensor", t[0], t[1])

        def chT(h):
            return sb["chiT"][:, a * 16 + h: a * 16 + h + 1]

        def zeT(h):
            return sb["zetT"][:, a * 16 + h: a * 16 + h + 1]

        POOL = [("ppsk", "psl1"), ("psq", "psl0")]

        nrounds = 1 if self.stop_after in ("pskv1", "pskv1noev") else 8
        noev = self.stop_after in ("pskvnoev", "pskv1noev")
        for rnd in range(nrounds):
            be, bo = POOL[rnd % 2]
            bank_wait(be)
            bank_wait(bo)
            for cc in (0, 1):
                pc = 64 * cc
                for si in range(2):
                    h = 2 * rnd + si
                    ph = 64 * (h % 2)
                    bank = ps[be] if h % 2 == 0 else ps[bo]
                    cl = (h // 2) * TILE + (c0 + cc) * C
                    self.o("tensor", lambda e, ph=ph, cl=cl, pc=pc, b=bank: e.matmul(
                        b[pc:pc + 64, 0:128],
                        sb["qk"][ph:ph + 64, 8 * TILE + cl:8 * TILE + cl + 64],
                        ap(sb["qk"], ph, 64, cl, [[8 * TILE, 2], [1, 64]]),
                        start=True, stop=True, tile_position=(ph, pc)))
                    self.o("tensor", lambda e, ph=ph, cl=cl, pc=pc, b=bank: e.matmul(
                        b[pc:pc + 64, 128:192],
                        sb["qk"][ph:ph + 64, 8 * TILE + cl:8 * TILE + cl + 64],
                        cstb[ph:ph + 64, CI_ISTK:CI_ISTK + 64],
                        start=True, stop=True, tile_position=(ph, pc)),
                        "stt" if (si == 1 and cc == 1) else None)
            if noev:
                continue
            mmtok = self.tok("stt")

            def dve_evacs(h, bn):
                self.o("vector", lambda e, h=h, b=ps[bn]: e.scalar_tensor_tensor(
                    out=ap(sb["csk"], 0, 128, h * 192, [[1, 192]]),
                    in0=ap(b, 0, 128, 0, [[1, 192]]),
                    scalar=chT(h),
                    in1=ap(sb["cstf"], 0, 128, CI_M3B, [[1, 192]]),
                    op0=ALU.mult, op1=ALU.mult), "sv")
                self.o("vector", lambda e, h=h, b=ps[bn]: e.tensor_tensor(
                    out=ap(sb["susl"], 0, 128, h * 64, [[1, 64]]),
                    in0=ap(b, 0, 128, 64, [[1, 64]]),
                    in1=ap(sb["cstf"], 0, 128, CI_MSL, [[1, 64]]),
                    op=ALU.mult), "sv")

            def sc_evacs(h, bn):
                self.o("scalar", lambda e, h=h, a=a: e.activation(
                    out=ap(sb["yA"], 0, 128, h * 128, [[1, 64]]),
                    in_=ap(sb["vtm"], 0, 128, a * HID + h * 64, [[1, 64]]),
                    func=AF.Copy, scale=zeT(h)), "ss")
                self.o("scalar", lambda e, h=h, b=ps[bn]: e.activation(
                    out=ap(sb["yA"], 0, 128, h * 128 + 64, [[1, 64]]),
                    in_=ap(b, 0, 128, 128, [[1, 64]]),
                    func=AF.Copy, scale=-1.0), "ss")
                self.o("scalar", lambda e, h=h, b=ps[bn]: e.activation(
                    out=ap(sb["ktm"], 0, 128, h * 64, [[1, 64]]),
                    in_=ap(b, 0, 128, 128, [[1, 64]]),
                    func=AF.Copy), "ss")

            he, ho = 2 * rnd, 2 * rnd + 1
            self.w("vector", "stt", mmtok)
            self.w("scalar", "stt", mmtok)
            dve_evacs(he, be)
            sc_evacs(ho, bo)
            self.w("vector", "ss")
            self.w("scalar", "sv")
            dve_evacs(ho, bo)
            sc_evacs(he, be)
            btok[be] = ("ss", self.tok("ss"))
            btok[bo] = ("sv", self.tok("sv"))
        if self.stop_after in ("pskv", "pskv1", "pskvnoev", "pskv1noev"):
            return
        self.pq_run(2)

        def sl_slice(h, pc):
            return ap(sb["susl"], pc, 64, h * 64, [[1, 64]])

        def sux_slice(h, pc):
            return ap(sb["csk"], pc, 64, h * 192 + 64, [[1, 64]])

        def pwx_slice(l, h, pc):
            return ap(sb["pwx"], pc, 64, l * H * 64 + h * 64, [[1, 64]])

        def pwt_slice(l, h, pc):
            return ap(sb["pwt"], pc, 64, (l % 2) * H * 64 + h * 64, [[1, 64]])

        self.w("tensor", "sv")
        self.w("tensor", "ss")
        wv = 0
        for L in range(5):
            for part in range(2):
                if L == 4 and part == 1:
                    break
                be, bo = POOL[wv % 2]
                wv += 1
                bank_wait(be)
                bank_wait(bo)
                if L > 0 and part == 0:
                    self.lvl_tok = (self.tok("sv"), self.tok("ss"))
                if L > 0:
                    self.w("tensor", "sv", self.lvl_tok[0])
                    self.w("tensor", "ss", self.lvl_tok[1])
                for cc in (0, 1):
                    pc = 64 * cc
                    for h in range(H):
                        bank = ps[be] if h < 8 else ps[bo]
                        colb = (h % 8) * 64
                        if L == 0:
                            stat = sl_slice(h, pc) if part == 0 else sux_slice(h, pc)
                            mv = sux_slice(h, pc) if part == 0 else sl_slice(h, pc)
                        else:
                            stat = pwt_slice(L - 1, h, pc) if part == 0 else pwx_slice(L - 1, h, pc)
                            mv = pwx_slice(L - 1, h, pc) if part == 0 else pwt_slice(L - 1, h, pc)
                        self.o("tensor", lambda e, st=stat, mv=mv, pc=pc, colb=colb,
                               bank=bank: e.matmul(
                            bank[pc:pc + 64, colb:colb + 64], st, mv,
                            start=True, stop=True, tile_position=(pc, pc)),
                            "stt" if (h == H - 1 and cc == 1) else None)
                self.w("vector", "stt")
                self.w("scalar", "stt")
                for bi, bn in enumerate((be, bo)):
                    if part == 0:
                        self.o("vector", lambda e, L=L, bi=bi, bn=bn: e.tensor_tensor(
                            out=ap(sb["pwx"], 0, 128, L * H * 64 + bi * 512, [[1, 512]]),
                            in0=ap(ps[bn], 0, 128, 0, [[1, 512]]),
                            in1=ap(sb["chiT"], 0, 128, a * 16 + bi * 8, [[1, 8], [0, 64]]),
                            op=ALU.mult), "sv")
                    else:
                        self.o("scalar", lambda e, L=L, bi=bi, bn=bn: e.activation(
                            out=ap(sb["pwt"], 0, 128, (L % 2) * H * 64 + bi * 512,
                                   [[1, 512]]),
                            in_=ap(ps[bn], 0, 128, 0, [[1, 512]]),
                            func=AF.Copy), "ss")
                    btok[bn] = ("sv", self.tok("sv")) if part == 0 else \
                        ("ss", self.tok("ss"))
        if self.stop_after == "builds":
            return
        self.pq_run(2)

        stages = [(4, ALU.add), (3, ALU.add), (2, ALU.add), (1, ALU.add),
                  (0, ALU.add), (-1, ALU.subtract)]
        cur, nxt = "yA", "yB"
        self.w("tensor", "ss")
        bi = 0
        qtok = {q: self.tok("sv") for q in range(4)}
        for sidx, (L, op) in enumerate(stages):
            for quad in range(4):
                bn = [b for pr in POOL for b in pr][bi % 4]
                bi += 1
                bank_wait(bn)
                self.w("tensor", "sv", qtok[quad])
                bank = ps[bn]
                for hh in range(4):
                    h = quad * 4 + hh
                    for cc in (0, 1):
                        pc = 64 * cc
                        stat = sux_slice(h, pc) if L < 0 else pwx_slice(L, h, pc)
                        self.o("tensor", lambda e, st=stat, h=h, hh=hh, pc=pc,
                               bank=bank, cur=cur: e.matmul(
                            bank[pc:pc + 64, hh * 128:(hh + 1) * 128], st,
                            ap(sb[cur], pc, 64, h * 128, [[1, 128]]),
                            start=True, stop=True, tile_position=(pc, pc)),
                            "stt" if (hh == 3 and cc == 1) else None)
                self.w("vector", "stt")
                if op == ALU.add:
                    self.o("vector", lambda e, quad=quad, bank=bank, cur=cur,
                           nxt=nxt: e.tensor_tensor(
                        out=ap(sb[nxt], 0, 128, quad * 512, [[1, 512]]),
                        in0=ap(bank, 0, 128, 0, [[1, 512]]),
                        in1=ap(sb[cur], 0, 128, quad * 512, [[1, 512]]),
                        op=ALU.add), "sv")
                else:
                    self.o("vector", lambda e, quad=quad, bank=bank, cur=cur,
                           nxt=nxt: e.tensor_tensor(
                        out=ap(sb[nxt], 0, 128, quad * 512, [[1, 512]]),
                        in0=ap(sb[cur], 0, 128, quad * 512, [[1, 512]]),
                        in1=ap(bank, 0, 128, 0, [[1, 512]]),
                        op=ALU.subtract), "sv")
                btok[bn] = ("sv", self.tok("sv"))
                qtok[quad] = self.tok("sv")
            cur, nxt = nxt, cur
        zf = cur

        self.o("vector", lambda e, zf=zf: e.tensor_tensor(
            out=ap(sb["zknx"], 0, 128, 0, [[64, 16], [1, 64]]),
            in0=ap(sb[zf], 0, 128, 64, [[128, 16], [1, 64]]),
            in1=ap(sb["chiT"], 0, 128, a * 16, [[1, 16], [0, 64]]),
            op=ALU.mult), "sv")
        zknx_tok = self.tok("sv")
        if self.stop_after == "solve":
            return
        self.pq_run(2)

        def bankcol(h):
            return (h // 2) * 64

        post_in_tok = self.tok("sv")

        def post_wave(mm_fn, evac_fn, need_tok):
            be, bo = POOL[self.pwave % 2]
            self.pwave += 1
            bank_wait(be)
            bank_wait(bo)
            self.w("tensor", "sv", need_tok)
            mm_fn(ps[be], ps[bo])
            self.w("vector", "stt")
            evac_fn(be, bo)
            btok[be] = ("sv", self.tok("sv"))
            btok[bo] = ("sv", self.tok("sv"))

        self.pwave = getattr(self, "pwave", 0)

        def mm_oin(bA, bB):
            for h in range(H):
                bk = bA if h < 8 else bB
                cb = (h % 8) * 64
                for cc in (0, 1):
                    pc = 64 * cc
                    self.o("tensor", lambda e, h=h, pc=pc, b=bk, cb=cb, zf=zf: e.matmul(
                        b[pc:pc + 64, cb:cb + 64],
                        ap(sb["csk"], pc, 64, h * 192, [[1, 64]]),
                        ap(sb[zf], pc, 64, h * 128, [[1, 64]]),
                        start=True, stop=True, tile_position=(pc, pc)),
                        "stt" if (h == H - 1 and cc == 1) else None)

        def ev_oin(be, bo):
            for g, bn in enumerate((be, bo)):
                self.o("vector", lambda e, g=g, bn=bn, a=a: e.tensor_tensor(
                    out=ap(sb["oin"], 0, 128, g * 512, [[1, 512]]),
                    in0=ap(ps[bn], 0, 128, 0, [[1, 512]]),
                    in1=ap(sb["omT"], 0, 128, a * 16 + g * 8, [[1, 8], [0, 64]]),
                    op=ALU.mult), "sv")
        post_wave(mm_oin, ev_oin, post_in_tok)

        def mm_qef(bA, bB):
            for h in range(H):
                ph = 64 * (h % 2)
                for cc in (0, 1):
                    pc = 64 * cc
                    b = bA if cc == 0 else bB
                    self.o("tensor", lambda e, h=h, ph=ph, pc=pc, b=b, zf=zf: e.matmul(
                        b[ph:ph + 64, bankcol(h):bankcol(h) + 64],
                        ap(sb[zf], pc, 64, h * 128 + 64, [[1, 64]]),
                        ap(sb["csk"], pc, 64, h * 192, [[1, 64]]),
                        start=True, stop=True, tile_position=(pc, ph)),
                        "stt" if (h == H - 1 and cc == 1) else None)

        def ev_qef(be, bo):
            for cc, bn in enumerate((be, bo)):
                self.o("vector", lambda e, cc=cc, bn=bn, c0=c0: e.tensor_tensor(
                    out=ap(sb["qef"], 0, 128, cc * 512, [[1, 512]]),
                    in0=ap(ps[bn], 0, 128, 0, [[1, 512]]),
                    in1=ap(sb["qk"], 0, 128, (c0 + cc) * C, [[TILE, 8], [1, 64]]),
                    op=ALU.add), "sv")
        post_wave(mm_qef, ev_qef, post_in_tok)

        def mm_gtm(bA, bB):
            for h in range(H):
                ph = 64 * (h % 2)
                for cc in (0, 1):
                    pc = 64 * cc
                    b = bA if cc == 0 else bB
                    self.o("tensor", lambda e, h=h, ph=ph, pc=pc, b=b: e.matmul(
                        b[ph:ph + 64, bankcol(h):bankcol(h) + 64],
                        ap(sb["zknx"], pc, 64, h * 64, [[1, 64]]),
                        ap(sb["ktm"], pc, 64, h * 64, [[1, 64]]),
                        start=True, stop=True, tile_position=(pc, ph)),
                        "stt" if (h == H - 1 and cc == 1) else None)

        def ev_gtm(be, bo):
            for cc, bn in enumerate((be, bo)):
                self.o("vector", lambda e, cc=cc, bn=bn: e.tensor_tensor(
                    out=ap(sb["gtm"], 0, 128, cc * 512, [[1, 512]]),
                    in0=ap(ps[bn], 0, 128, 0, [[1, 512]]),
                    in1=ap(sb["cstb"], 0, 128, CI_IREP8, [[1, 512]]),
                    op=ALU.add), "sv")
        post_wave(mm_gtm, ev_gtm, zknx_tok)

        def mm_um(bA, bB):
            for h in range(H):
                ph = 64 * (h % 2)
                for cc in (0, 1):
                    pc = 64 * cc
                    b = bA if cc == 0 else bB
                    self.o("tensor", lambda e, h=h, ph=ph, pc=pc, b=b, zf=zf: e.matmul(
                        b[ph:ph + 64, bankcol(h):bankcol(h) + 64],
                        ap(sb["csk"], pc, 64, h * 192 + 128, [[1, 64]]),
                        ap(sb[zf], pc, 64, h * 128, [[1, 64]]),
                        start=True, stop=True, tile_position=(pc, ph)),
                        "stt" if (h == H - 1 and cc == 1) else None)

        def ev_um(be, bo):
            for cc, bn in enumerate((be, bo)):
                for half in (0, 1):
                    self.o("vector", lambda e, cc=cc, half=half, bn=bn, c0=c0: e.tensor_tensor(
                        out=ap(sb["um"], 64 * half, 64, cc * 512, [[1, 512]]),
                        in0=ap(ps[bn], 64 * half, 64, 0, [[1, 512]]),
                        in1=ap(sb["gamrep"], 64 * half, 64, half * 8 + c0 + cc,
                               [[16, 8], [0, 64]]),
                        op=ALU.mult), "sv")
        post_wave(mm_um, ev_um, post_in_tok)
        if self.stop_after == "post":
            return
        self.pq_run(2)

        for cc in (0, 1):
            pc = 64 * cc
            self.w("tensor", "sv")
            for bn in ("psq", "ppsk", "psl0", "psl1"):
                bank_wait(bn)
            for h in range(H):
                ph = 64 * (h % 2)
                bank = ps["psq"] if h % 2 == 0 else ps["ppsk"]
                self.o("tensor", lambda e, h=h, ph=ph, pc=pc, cc=cc, bank=bank: e.matmul(
                    bank[pc:pc + 64, bankcol(h):bankcol(h) + 64],
                    ap(sb["qef"], ph, 64, cc * 512 + bankcol(h), [[1, 64]]),
                    ap(sb["mst"], ph, 64, h * 64, [[1, 64]]),
                    start=True, stop=True, tile_position=(ph, pc)),
                    "stt" if h == H - 1 else None)
            owave_tok = self.tok("stt")
            for h in range(H):
                ph = 64 * (h % 2)
                bank = ps["psl0"] if h % 2 == 0 else ps["psl1"]
                self.o("tensor", lambda e, h=h, ph=ph, cc=cc, bank=bank: e.matmul(
                    bank[ph:ph + 64, bankcol(h):bankcol(h) + 64],
                    ap(sb["gtm"], ph, 64, cc * 512 + bankcol(h), [[1, 64]]),
                    ap(sb["mst"], ph, 64, h * 64, [[1, 64]]),
                    start=True, stop=True, tile_position=(ph, ph)),
                    "stt" if h == H - 1 else None)
            self.w("vector", "stt", owave_tok)
            for par, bn in ((0, "psq"), (1, "ppsk")):
                self.o("vector", lambda e, par=par, bn=bn, pc=pc, a=a: e.tensor_tensor(
                    out=ap(sb["xsq"], pc, 64, 0, [[1, 512]]),
                    in0=ap(ps[bn], pc, 64, 0, [[1, 512]]),
                    in1=ap(sb["omT"], pc, 64, a * 16 + par, [[2, 8], [0, 64]]),
                    op=ALU.mult), "sv")
                self.o("vector", lambda e, par=par, pc=pc, a=a: e.tensor_tensor(
                    out=ap(sb["otm"], pc, 64, a * HID + par * 64, [[128, 8], [1, 64]]),
                    in0=ap(sb["xsq"], pc, 64, 0, [[1, 512]]),
                    in1=oin_gather(par, pc),
                    op=ALU.add), "sv")
            self.w("vector", "stt")
            for par, bn in ((0, "psl0"), (1, "psl1")):
                phh = 64 * par
                self.o("vector", lambda e, par=par, bn=bn, phh=phh, cc=cc, c0=c0:
                       e.tensor_tensor(
                    out=ap(sb["xsq"], phh, 64, 512, [[1, 512]]),
                    in0=ap(ps[bn], phh, 64, 0, [[1, 512]]),
                    in1=ap(sb["gamrep"], phh, 64, par * 8 + c0 + cc, [[16, 8], [0, 64]]),
                    op=ALU.mult), "sv")
                self.o("vector", lambda e, par=par, phh=phh, cc=cc: e.tensor_tensor(
                    out=ap(sb["mst"], phh, 64, par * 64, [[128, 8], [1, 64]]),
                    in0=ap(sb["xsq"], phh, 64, 512, [[1, 512]]),
                    in1=um_gather(par, phh, cc),
                    op=ALU.add), "sv")
            for bn in ("psq", "ppsk", "psl0", "psl1"):
                btok[bn] = ("sv", self.tok("sv"))

    def ph_ffn_mix(self):
        sb = self.sb
        self.w("vector", "stt")
        for m in range(8):
            self.o("vector", lambda e, m=m: e.tensor_tensor(
                out=sb["sq"][:, 1:TILE], in0=sb["xnf"][:, m * TILE + 1:(m + 1) * TILE],
                in1=sb["xnf"][:, m * TILE:(m + 1) * TILE - 1], op=ALU.subtract), "sv")
            self.o("vector", lambda e, m=m: e.tensor_tensor(
                out=sb["sq"][:, 0:1], in0=sb["xnf"][:, m * TILE:m * TILE + 1],
                in1=sb["xbnd"][:, m:m + 1], op=ALU.subtract), "sv")
            self.o("vector", lambda e, m=m: e.tensor_copy(
                out=sb["ffb"][:, m * TILE + 1:(m + 1) * TILE],
                in_=sb["xnf"][:, m * TILE:(m + 1) * TILE - 1]), "sv")
            self.o("vector", lambda e, m=m: e.tensor_copy(
                out=sb["ffb"][:, m * TILE:m * TILE + 1], in_=sb["xbnd"][:, m:m + 1]), "sv")
            self.o("vector", lambda e, m=m: e.scalar_tensor_tensor(
                out=sb["ffa"][:, m * TILE:(m + 1) * TILE], in0=sb["sq"][:, :],
                scalar=sb["mus"][:, 2 * m:2 * m + 1],
                in1=sb["ffb"][:, m * TILE:(m + 1) * TILE],
                op0=ALU.mult, op1=ALU.add), "sv")
            self.o("vector", lambda e, m=m: e.scalar_tensor_tensor(
                out=sb["gf"][:, m * TILE:(m + 1) * TILE], in0=sb["sq"][:, :],
                scalar=sb["mus"][:, 2 * m + 1:2 * m + 2],
                in1=sb["ffb"][:, m * TILE:(m + 1) * TILE],
                op0=ALU.mult, op1=ALU.add), "sv")
        self.o("vector", lambda e: e.tensor_copy(
            out=sb["xbnd"][:, 0:8],
            in_=ap(sb["xnf"], 0, 128, TILE - 1, [[TILE, 8], [1, 1]])), "sv")
        self.mix_tok = self.tok("sv")

    def build_projq(self, it):
        sb = self.sb
        q = self.projq

        def ev_r(m, bank):
            self.w("scalar", "stt")
            self.o("scalar", lambda e, m=m, b=bank: e.activation(
                out=sb["ffb"][:, m * TILE:(m + 1) * TILE], in_=b[:, 0:TILE],
                func=AF.Sigmoid), "ss")
            return ("ss", self.tok("ss"))

        def ev_k(m, bank):
            self.w("scalar", "stt")
            self.o("scalar", lambda e, m=m, b=bank: e.activation(
                out=sb["ffa"][:, m * TILE:(m + 1) * TILE], in_=b[:, 0:TILE],
                func=AF.Relu), "ss")
            self.w("vector", "ss")
            self.o("vector", lambda e, m=m: e.tensor_tensor(
                out=sb["ffa"][:, m * TILE:(m + 1) * TILE],
                in0=sb["ffa"][:, m * TILE:(m + 1) * TILE],
                in1=sb["ffa"][:, m * TILE:(m + 1) * TILE], op=ALU.mult), "sv")
            return ("ss", self.tok("ss"))

        def ev_v(m, bank):
            self.w("vector", "stt")
            self.w("vector", "ss")
            self.o("vector", lambda e, m=m, b=bank: e.tensor_tensor(
                out=sb["branch"][:, m * TILE:(m + 1) * TILE], in0=b[:, 0:TILE],
                in1=sb["ffb"][:, m * TILE:(m + 1) * TILE], op=ALU.mult), "sv")
            return ("sv", self.tok("sv"))

        def ev_g(m, bank):
            self.w("scalar", "stt")
            self.o("scalar", lambda e, m=m, b=bank: e.activation(
                out=sb["gf"][:, m * TILE:(m + 1) * TILE], in_=b[:, 0:TILE],
                func=AF.Silu), "ss")
            return ("ss", self.tok("ss"))

        mixt = [("sv", self.mix_tok)]
        plan = [("load", "Wkf"), ("proj", "Wr", ev_r, "ffa", mixt),
                ("load", "Wvf"), ("proj", "Wkf", ev_k, "gf", mixt),
                ("load", "Wg"), ("proj", "Wvf", ev_v, "ffa", None),
                ("load", "Wo"), ("proj", "Wg", ev_g, None, None)]
        if it < self.ntile - 1:
            plan.append(("load", "Wq"))
        for item in plan:
            if item[0] == "load":
                q.append(lambda wn=item[1]: self.load_w(wn))
            else:
                _, wn, ev, src, stok = item
                holder = {}

                def start(wn=wn, ev=ev, src=src, stok=stok, holder=holder):
                    holder["st"] = self.mm_proj_start(wn, ev, src=src, src_tok=stok)

                q.append(start)
                for m in range(8):
                    q.append(lambda m=m, holder=holder: self.mm_proj_step(holder["st"], m))

    def ph_gate_wo(self, it):
        sb, ps = self.sb, self.ps
        self.w("tensor", "sv")
        self.w("vector", "ss")
        for k in range(8):
            for blk in range(4):
                self.o("tensor", lambda e, k=k, blk=blk: e.transpose(
                    ps["ptr"][:, blk * 128:(blk + 1) * 128],
                    sb["otm"][:, blk * HID + k * 128: blk * HID + (k + 1) * 128],
                    sb["cstb"][:, CI_I128:CI_I128 + 128]),
                    "stt" if blk == 3 else None)
            self.w("vector", "stt")
            self.o("vector", lambda e, k=k: e.tensor_tensor(
                out=sb["ffb"][:, k * TILE:(k + 1) * TILE], in0=ps["ptr"][:, 0:512],
                in1=sb["gf"][:, k * TILE:(k + 1) * TILE], op=ALU.mult), "sv")
            self.w("tensor", "sv")

        def ev_o(m, bank):
            self.w("vector", "stt")
            self.o("vector", lambda e, m=m, b=bank: e.tensor_tensor(
                out=sb["branch"][:, m * TILE:(m + 1) * TILE], in0=b[:, 0:TILE],
                in1=sb["branch"][:, m * TILE:(m + 1) * TILE], op=ALU.add), "sv")
            return ("sv", self.tok("sv"))
        self.mm_proj("Wo", ev_o, src="ffb")

    def ph_epilogue(self, it):
        sb, ps = self.sb, self.ps
        t0 = it * TILE
        for st in range(4):
            buf = sb["xld0"] if st % 2 == 0 else sb["xld1"]
            xsem = "sxa" if st % 2 == 0 else "sxb"
            self.w("gpsimd", "sv", self.xld_free.get(st % 2, 0))
            self.o("gpsimd", lambda e, b=buf, st=st, t0=t0: e.dma_start(
                out=b[:, :], in_=self.x_hbm[t0 + st * 128: t0 + (st + 1) * 128, :]),
                xsem, 16)
            self.w("tensor", "sv")
            for r in range(2):
                if r == 1:
                    self.w("tensor", "sv")
                for i in range(4):
                    k = r * 4 + i
                    self.o("tensor", lambda e, k=k, i=i, st=st: e.transpose(
                        ps["ptr"][:, i * 128:(i + 1) * 128],
                        ap(sb["branch"], 0, 128, k * TILE + st * 128, [[1, 128]]),
                        sb["cstb"][:, CI_I128:CI_I128 + 128]),
                        "stt" if i == 3 else None)
                self.w("vector", "stt")
                self.w("vector", xsem)
                if r == 0:
                    self.w("vector", "sdo", self.tok("sdo"))
                self.o("vector", lambda e, r=r, b=buf: e.tensor_tensor(
                    out=sb["outs"][:, r * 512:(r + 1) * 512], in0=ps["ptr"][:, 0:512],
                    in1=b[:, r * 512:(r + 1) * 512], op=ALU.add), "sv")
                self.w("tensor", "sv")
            self.w("gpsimd", "sv")
            self.o("gpsimd", lambda e, st=st, t0=t0: e.dma_start(
                out=self.out_hbm[t0 + st * 128: t0 + (st + 1) * 128, :],
                in_=sb["outs"][:, :]), "sdo", 16)
            self.xld_free[st % 2] = self.tok("sv")

    def ph_debug_dump(self):
        sb = self.sb
        self.w("gpsimd", "sv")
        self.w("gpsimd", "ss")
        for n, (src, dst) in self.dbg.items():
            self.o("gpsimd", lambda e, s=src, d=dst: e.dma_start(
                out=d, in_=sb[s][:, :]), "sdo", 16)

    def ph_finish(self):
        self.w("gpsimd", "sdo")


def _prep_inputs(core, x, ins):
    xs = np.ascontiguousarray(x[4 * core: 4 * core + 4].reshape(-1, HID))
    d = {"x": xs}
    d["W_Wq"] = ins["L0_attn_Wq"]
    d["W_Wk"] = ins["L0_attn_Wk"]
    d["W_Wv"] = ins["L0_attn_Wv"]
    d["W_Wg"] = ins["L0_attn_Wg"]
    d["W_Wo"] = ins["L0_attn_Wo"]
    d["W_Wr"] = ins["L0_ffn_W_r"]
    d["W_Wkf"] = ins["L0_ffn_W_k"]
    d["W_Wvf"] = ins["L0_ffn_W_v"]
    d["Wab"] = np.concatenate([ins["L0_attn_Wa"], ins["L0_attn_Wb"]], axis=0)
    bab = np.zeros((16, 2), dtype=np.float32)
    bab[:, 0] = ins["L0_attn_ba"]
    bab[:, 1] = ins["L0_attn_bb"]
    d["bab"] = bab
    mu = np.zeros((128, 16), dtype=np.float32)
    for k in range(8):
        mu[:, 2 * k] = ins["L0_ffn_mu_r"][k * 128:(k + 1) * 128]
        mu[:, 2 * k + 1] = ins["L0_ffn_mu_k"][k * 128:(k + 1) * 128]
    d["mu"] = mu
    d["consts"] = _host_consts()
    return {k: np.ascontiguousarray(v) for k, v in d.items()}


def kernel(**inputs):
    x = inputs["L0_attn_x"]
    B = x.shape[0]
    p = Prog(nb=B // NCORES, t_len=x.shape[1])
    nc = p.build()
    in_maps = [_prep_inputs(c, x, inputs) for c in range(NCORES)]
    res = run_bass_kernel_spmd(nc, in_maps, core_ids=list(range(NCORES)))
    out = np.concatenate(
        [r["out"].reshape(B // NCORES, x.shape[1], HID) for r in res.results], axis=0)
    return out.astype(np.float32)


# version 12
# speedup vs baseline: 1.3007x; 1.0034x over previous
import numpy as np
import ml_dtypes
import concourse.bass as bass
import concourse.mybir as mybir
from concourse.bass_utils import run_bass_kernel_spmd

F32 = mybir.dt.float32
BF16 = mybir.dt.bfloat16
AF = mybir.ActivationFunctionType
ALU = mybir.AluOpType
AX = mybir.AxisListType

NCORES = 8
HID = 1024
H = 16
DH = 64
C = 64
TILE = 512
NCH = TILE // C
NPAIR = NCH // 2
EPS_LN = 1e-5
CLAMP = 60.0
BIG_W = ["Wq", "Wk", "Wv", "Wg", "Wo", "Wr", "Wkf", "Wvf"]

CI_I128 = 0
CI_ISTK = 128
CI_MPM = 192
CI_MSU = 256
CI_MSL = 320
CI_IND = 384
CI_ONES = 512
CI_M3B = 640
CI_SCAL = 832
CI_IREP8 = 848
NCONST = 1360


def _host_consts():
    c = np.zeros((128, NCONST), dtype=np.float32)
    c[np.arange(128), CI_I128 + np.arange(128)] = 1.0
    c[np.arange(128), CI_ISTK + np.tile(np.arange(64), 2)] = 1.0
    s = np.tile(np.arange(64), 2)[:, None]
    t = np.arange(64)[None, :]
    c[:, CI_MPM:CI_MPM + 64] = (s <= t)
    c[:, CI_MSU:CI_MSU + 64] = (s < t)
    c[:, CI_MSL:CI_MSL + 64] = (t < s)
    for k in range(8):
        c[0:64, CI_IND + 16 * k + 2 * k] = 1.0
        c[64:128, CI_IND + 16 * k + 2 * k + 1] = 1.0
    c[:, CI_ONES:CI_ONES + 128] = 1.0
    c[:, CI_M3B:CI_M3B + 64] = (s <= t)
    c[:, CI_M3B + 64:CI_M3B + 128] = (s < t)
    c[:, CI_M3B + 128:CI_M3B + 192] = 1.0
    c[:, CI_SCAL] = EPS_LN
    for a in range(8):
        c[np.arange(128), CI_IREP8 + 64 * a + (np.arange(128) % 64)] = 1.0
    return c


def ap(t, pbase, pcount, off, dims):
    rowlen = t.shape[1]
    return bass.AP(t, pbase * rowlen + off, [[rowlen, pcount]] + list(dims))


class Prog:
    def __init__(self, nb, t_len, debug=False, stop_after=None):
        self.stop_after = stop_after
        self.nb, self.T, self.debug = nb, t_len, debug
        self.ntb = t_len // TILE
        self.ntile = nb * self.ntb
        self.nc = bass.Bass("TRN2", target_bir_lowering=False)
        self.ctx, self.cnt, self.sems, self.sb, self.ps = [], {}, {}, {}, {}
        self.prog = {"sync": [], "vector": [], "scalar": [], "tensor": [], "gpsimd": []}

    def sem(self, n):
        s = self.nc.semaphore(n)
        self.sems[n] = s.__enter__()
        self.ctx.append(s)
        self.cnt[n] = 0

    def sbuf(self, n, shape, dt):
        t = self.nc.sbuf_tensor(n, shape, dt)
        self.sb[n] = t.__enter__()
        self.ctx.append(t)

    def psum(self, n, shape, dt=F32):
        t = self.nc.psum_tensor(n, shape, dt)
        self.ps[n] = t.__enter__()
        self.ctx.append(t)

    def w(self, eng, sem, val=None):
        v = self.cnt[sem] if val is None else val
        if v <= 0:
            return
        s = self.sems[sem]
        self.prog[eng].append(lambda e, s=s, v=v: e.wait_ge(s, v))

    def o(self, eng, build, sem=None, n=1):
        if eng in ("vector", "scalar") or (eng == "gpsimd" and sem == "sp"):
            ssem = {"vector": "sv", "scalar": "ss", "gpsimd": "sp"}[eng]
            v = self.cnt[ssem]
            if v > 0:
                sh = self.sems[ssem]
                self.prog[eng].append(lambda e, sh=sh, v=v: e.wait_ge(sh, v))
        if sem is None:
            self.prog[eng].append(build)
        else:
            s = self.sems[sem]
            self.prog[eng].append(lambda e, s=s, b=build, n=n: b(e).then_inc(s, n))
            self.cnt[sem] += n

    def tok(self, sem):
        return self.cnt[sem]

    def build(self):
        nc = self.nc
        NTOK = self.nb * self.T
        self.x_hbm = nc.dram_tensor("x", [NTOK, HID], F32, kind="ExternalInput").ap()
        self.out_hbm = nc.dram_tensor("out", [NTOK, HID], F32, kind="ExternalOutput").ap()
        self.w_hbm = {n: nc.dram_tensor("W_" + n, [HID, HID], F32, kind="ExternalInput").ap()
                      for n in BIG_W}
        self.wab_hbm = nc.dram_tensor("Wab", [32, HID], F32, kind="ExternalInput").ap()
        self.bab_hbm = nc.dram_tensor("bab", [16, 2], F32, kind="ExternalInput").ap()
        self.mu_hbm = nc.dram_tensor("mu", [128, 16], F32, kind="ExternalInput").ap()
        self.cst_hbm = nc.dram_tensor("consts", [128, NCONST], F32, kind="ExternalInput").ap()
        self.wt_dram = {n: nc.dram_tensor("WT_" + n, [HID, HID], BF16).ap() for n in BIG_W}

        for s in ["swa", "swb", "sxa", "sxb", "sgr", "sdo", "sv", "ss", "stt", "sg",
                  "sp"]:
            self.sem(s)

        S = self.sbuf
        S("cstf", [128, NCONST], F32)
        self.nc.const_aps.register(F32, EPS_LN,
                                   self.sb["cstf"][:, CI_SCAL:CI_SCAL + 1])
        S("cstb", [128, NCONST], BF16)
        S("wtb0", [128, 8 * HID], BF16)
        S("wtb1", [128, 8 * HID], BF16)
        S("wabt", [128, 8 * 32], BF16)
        S("babs", [16, 2], F32)
        S("babn", [16, 2], F32)
        S("mus", [128, 16], F32)
        S("xld0", [128, HID], F32)
        S("xld1", [128, HID], F32)
        S("xsq", [128, HID], F32)
        S("xntm", [128, HID], BF16)
        S("xnf", [128, 8 * TILE], BF16)
        S("xbnd", [128, 8], BF16)
        S("qk", [128, 16 * TILE], BF16)
        S("vtm", [128, 4 * HID], BF16)
        S("gf", [128, 8 * TILE], BF16)
        S("sq", [128, TILE], BF16)
        S("stat", [128, 8], F32)
        S("zsig", [16, 2 * TILE], F32)
        for n in ["la", "lc", "gamv", "uv", "nqv", "nkv", "chiv", "zetv", "omv",
                  "t16a", "t16b"]:
            S(n, [16, TILE], F32)
        S("chiT", [128, 4 * 16], F32)
        S("zetT", [128, 4 * 16], F32)
        S("omT", [128, 4 * 16], F32)
        S("gamrow", [1, 128], F32)
        S("gamrep", [128, 128], F32)
        S("csk", [128, H * 192], BF16)
        S("susl", [128, H * 64], BF16)
        S("ktm", [128, H * 64], BF16)
        S("zknx", [128, H * 64], BF16)
        S("yA", [128, H * 128], BF16)
        S("yB", [128, H * 128], BF16)
        S("pwx", [128, 5 * H * 64], BF16)
        S("pwt", [128, 2 * H * 64], BF16)
        S("oin", [128, 2 * 512], BF16)
        S("qef", [128, 2 * 512], BF16)
        S("gtm", [128, 2 * 512], BF16)
        S("um", [128, 2 * 512], BF16)
        S("mst", [128, H * 64], BF16)
        S("otm", [128, 4 * HID], BF16)
        S("branch", [128, 8 * TILE], BF16)
        S("ffa", [128, 8 * TILE], BF16)
        S("ffb", [128, 8 * TILE], BF16)
        S("outs", [128, HID], F32)

        Pm = self.psum
        Pm("pp0", [128, 512])
        Pm("pp1", [128, 512])
        Pm("ppsk", [128, 512])
        Pm("psl0", [128, 512])
        Pm("psl1", [128, 512])
        Pm("psq", [128, 512])
        Pm("ptr", [128, 512], BF16)
        Pm("pscl", [128, 512])

        if self.debug:
            self.dbg = {}
            for n, src in [("d_xnf", "xnf"), ("d_qk", "qk"),
                           ("d_vtm", "vtm"), ("d_csk", "csk"), ("d_susl", "susl"),
                           ("d_y", "yA"), ("d_oin", "oin"), ("d_qef", "qef"),
                           ("d_gtm", "gtm"), ("d_um", "um"), ("d_mst", "mst"),
                           ("d_otm", "otm"), ("d_branch", "branch"),
                           ("d_gf", "gf"), ("d_ffa", "ffa")]:
                t = self.sb[src]
                self.dbg[n] = (src, self.nc.dram_tensor(
                    n, [t.shape[0], t.shape[1]], BF16, kind="ExternalOutput").ap())
            for n, src in [("d_chi", "chiv"), ("d_gam", "gamv"), ("d_zet", "zetv"),
                           ("d_om", "omv"), ("d_nq", "nqv"), ("d_nk", "nkv"),
                           ("d_la", "la"), ("d_gamrep", "gamrep")]:
                t = self.sb[src]
                self.dbg[n] = (src, self.nc.dram_tensor(
                    n, [t.shape[0], t.shape[1]], F32, kind="ExternalOutput").ap())

        self.emit_all()

        with nc.Block() as blk:
            pr = self.prog

            @blk.sync
            def _(e):
                for f in pr["sync"]:
                    f(e)

            @blk.vector
            def _(e):
                for f in pr["vector"]:
                    f(e)

            @blk.scalar
            def _(e):
                for f in pr["scalar"]:
                    f(e)

            @blk.tensor
            def _(e):
                for f in pr["tensor"]:
                    f(e)

            @blk.gpsimd
            def _(e):
                for f in pr["gpsimd"]:
                    f(e)
        return nc

    def emit_all(self):
        self.ph_setup()
        self.ph_weight_prep()
        self.slot_free = {0: self.tok("stt"), 1: self.tok("stt")}
        self.xld_free = {0: self.tok("sv"), 1: self.tok("sv")}
        self.wtok = {}
        self.wslot = {}
        self.wload_next = 0
        self.load_w("Wq")
        for it in range(self.ntile):
            self.ph_tile(it)
        self.ph_finish()

    def load_w(self, wn):
        sb = self.sb
        slot = self.wload_next
        self.wload_next ^= 1
        self.wslot[wn] = slot
        sem = "swa" if slot == 0 else "swb"
        dstw = sb["wtb0"] if slot == 0 else sb["wtb1"]
        self.w("sync", "stt", self.slot_free.get(slot, 0))
        self.w("sync", "sv", self.prep_guard.get(sem, 0))
        self.prep_guard[sem] = 0
        self.w("sync", "sg", self.tok("sg"))
        for k in range(8):
            self.o("sync", lambda e, wn=wn, k=k, d=dstw: e.dma_start(
                out=d[:, k * HID:(k + 1) * HID],
                in_=self.wt_dram[wn][k * 128:(k + 1) * 128, :]), sem, 16)
        self.wtok[wn] = (sem, self.tok(sem))

    def ph_setup(self):
        sb = self.sb
        self.o("sync", lambda e: e.dma_start(out=sb["cstf"][:, :], in_=self.cst_hbm), "swa", 16)
        self.o("sync", lambda e: e.dma_start(out=sb["babs"][:, :], in_=self.bab_hbm), "swa", 16)
        self.o("sync", lambda e: e.dma_start(out=sb["mus"][:, :], in_=self.mu_hbm), "swa", 16)
        self.w("vector", "swa")
        self.o("vector", lambda e: e.tensor_copy(out=sb["cstb"][:, :], in_=sb["cstf"][:, :]), "sv")
        self.o("vector", lambda e: e.tensor_scalar(
            out=sb["babn"][:, :], in0=sb["babs"][:, :], scalar1=-1.0, scalar2=None,
            op0=ALU.mult), "sv")
        self.o("vector", lambda e: e.memset(sb["mst"][:, :], 0.0), "sv")
        self.o("vector", lambda e: e.memset(sb["xbnd"][:, :], 0.0), "sv")

    def ph_weight_prep(self):
        sb, ps = self.sb, self.ps
        self.prep_guard = {"swa": self.tok("sv"), "swb": 0}
        I128b = sb["cstb"][:, CI_I128:CI_I128 + 128]
        for wn in BIG_W:
            for j in range(8):
                buf = sb["xld0"] if j % 2 == 0 else sb["xld1"]
                psem = "swa" if j % 2 == 0 else "swb"
                self.w("sync", "sv", self.prep_guard.get(psem, 0))
                self.o("sync", lambda e, s=self.w_hbm[wn], j=j, b=buf: e.dma_start(
                    out=b[:, :], in_=s[j * 128:(j + 1) * 128, :]), psem, 16)
                self.w("vector", psem)
                cols = (j % 2) * HID
                self.o("vector", lambda e, b=buf, c=cols: e.tensor_copy(
                    out=sb["wtb1"][:, c:c + HID], in_=b[:, :]), "sv")
                self.prep_guard[psem] = self.tok("sv")
                for r in range(2):
                    self.w("tensor", "sv")
                    self.w("tensor", "sg")
                    for i in range(4):
                        ii = r * 4 + i
                        self.o("tensor", lambda e, ii=ii, c=cols, i=i: e.transpose(
                            ps["ptr"][:, i * 128:(i + 1) * 128],
                            sb["wtb1"][:, c + ii * 128:c + ii * 128 + 128], I128b),
                            "stt" if i == 3 else None)
                    self.w("vector", "stt")
                    self.o("vector", lambda e, r=r: e.tensor_copy(
                        out=sb["wtb0"][:, r * 512:(r + 1) * 512],
                        in_=ps["ptr"][:, 0:512]), "sv")
                    self.w("gpsimd", "sv")
                    for i in range(4):
                        ii = r * 4 + i
                        self.o("gpsimd", lambda e, ii=ii, i=i, r=r, wn=wn, j=j: e.dma_start(
                            out=self.wt_dram[wn][ii * 128:(ii + 1) * 128,
                                                 j * 128:(j + 1) * 128],
                            in_=sb["wtb0"][:, r * 512 + i * 128: r * 512 + (i + 1) * 128]),
                            "sg", 16)
        self.w("sync", "sv", self.prep_guard.get("swa", 0))
        self.o("sync", lambda e: e.dma_start(out=sb["xld0"][0:32, :], in_=self.wab_hbm),
               "swa", 16)
        self.w("vector", "swa")
        self.w("vector", "sg")
        self.o("vector", lambda e: e.tensor_copy(
            out=sb["wtb1"][0:32, 0:HID], in_=sb["xld0"][0:32, :]), "sv")
        self.w("tensor", "sv")
        self.w("tensor", "sg")
        for i in range(8):
            self.o("tensor", lambda e, i=i: e.transpose(
                ps["ptr"][:, (i % 4) * 32:(i % 4) * 32 + 32],
                sb["wtb1"][0:32, i * 128:i * 128 + 128],
                sb["cstb"][0:32, CI_I128:CI_I128 + 32]),
                "stt" if i % 4 == 3 else None)
            if i % 4 == 3:
                self.w("vector", "stt")
                r = i // 4
                self.o("vector", lambda e, r=r: e.tensor_copy(
                    out=sb["wabt"][:, r * 128:(r + 1) * 128], in_=ps["ptr"][:, 0:128]),
                    "sv")
                if r == 0:
                    self.w("tensor", "sv")

    def mm_proj_start(self, wn, evac, src=None, tokmajor=False, src_tok=None):
        st = {"wn": wn, "evac": evac, "tokmajor": tokmajor,
              "srct": self.sb["xnf"] if src is None else self.sb[src],
              "wt": self.sb["wtb0"] if self.wslot[wn] == 0 else self.sb["wtb1"],
              "ev_tok": {0: None, 1: None}, "first": True,
              "src_tok": src_tok}
        return st

    def mm_proj_step(self, st, m):
        sb, ps = self.sb, self.ps
        wn, evac, srct, wt = st["wn"], st["evac"], st["srct"], st["wt"]
        tokmajor, ev_tok = st["tokmajor"], st["ev_tok"]
        if st["first"]:
            st["first"] = False
            self.w("tensor", self.wtok[wn][0], self.wtok[wn][1])
            if st["src_tok"] is None:
                self.w("tensor", "sv")
                self.w("tensor", "ss")
                self.w("tensor", "sp")
            else:
                for sem, tok in st["src_tok"]:
                    self.w("tensor", sem, tok)
        bank = ps["pp0"] if m % 2 == 0 else ps["pp1"]
        if ev_tok[m % 2] is not None:
            self.w("tensor", ev_tok[m % 2][0], ev_tok[m % 2][1])
        for k in range(8):
            if tokmajor:
                tb, oh = m // 2, m % 2
                self.o("tensor", lambda e, k=k, b=bank, wt=wt, s=srct, tb=tb,
                       oh=oh: e.matmul(
                    b[:, 0:TILE],
                    s[:, k * TILE + tb * 128: k * TILE + (tb + 1) * 128],
                    wt[:, k * HID + oh * 512: k * HID + (oh + 1) * 512],
                    start=(k == 0), stop=(k == 7)), "stt" if k == 7 else None)
            else:
                self.o("tensor", lambda e, m=m, k=k, b=bank, wt=wt, s=srct: e.matmul(
                    b[:, 0:TILE], wt[:, k * HID + m * 128: k * HID + m * 128 + 128],
                    s[:, k * TILE:(k + 1) * TILE],
                    start=(k == 0), stop=(k == 7)), "stt" if k == 7 else None)
        ev_tok[m % 2] = evac(m, bank)
        if m == 7:
            self.slot_free[self.wslot[wn]] = self.tok("stt")

    def mm_proj(self, wn, evac, src=None, tokmajor=False):
        st = self.mm_proj_start(wn, evac, src, tokmajor)
        for m in range(8):
            self.mm_proj_step(st, m)
        return


    def ev_copy(self, dst, off=0):
        sb = self.sb
        def f(m, bank):
            self.w("scalar", "stt")
            self.o("scalar", lambda e, m=m, b=bank: e.activation(
                out=sb[dst][:, off + m * TILE:off + (m + 1) * TILE],
                in_=b[:, 0:TILE], func=AF.Copy), "ss")
            return ("ss", self.tok("ss"))
        return f

    def ph_tile(self, it):
        sb, ps = self.sb, self.ps
        ntb = self.ntb
        tt = it % ntb
        t0 = it * TILE
        first_in_batch = (tt == 0)

        if first_in_batch and it > 0:
            self.w("vector", "stt")
            self.o("vector", lambda e: e.memset(sb["mst"][:, :], 0.0), "sv")
            self.o("vector", lambda e: e.memset(sb["xbnd"][:, :], 0.0), "sv")

        if self.stop_after == "prep":
            return
        for st in range(4):
            buf = sb["xld0"] if st % 2 == 0 else sb["xld1"]
            self.w("gpsimd", "sv", getattr(self, "xld_free", {}).get(st % 2, 0))
            xsem = "sxa" if st % 2 == 0 else "sxb"
            self.o("gpsimd", lambda e, b=buf, st=st, t0=t0: e.dma_start(
                out=b[:, :], in_=self.x_hbm[t0 + st * 128: t0 + (st + 1) * 128, :]),
                xsem, 16)
            self.w("vector", xsem)
            self.w("scalar", xsem)
            self.o("vector", lambda e, b=buf: e.reduce_sum(
                out=sb["stat"][:, 0:1], in_=b[:, :], axis=AX.X), "sv")
            self.w("scalar", "sv")
            self.o("scalar", lambda e, b=buf: e.activation(
                out=sb["xsq"][:, :], in_=b[:, :], func=AF.Square,
                accum_out=sb["stat"][:, 1:2]), "ss")
            self.w("vector", "ss")
            self.o("vector", lambda e: e.tensor_scalar(
                out=sb["stat"][:, 2:3], in0=sb["stat"][:, 0:1],
                scalar1=1.0 / HID, scalar2=None, op0=ALU.mult), "sv")
            self.o("vector", lambda e: e.tensor_scalar(
                out=sb["stat"][:, 4:5], in0=sb["stat"][:, 1:2],
                scalar1=1.0 / HID, scalar2=None, op0=ALU.mult), "sv")
            self.o("vector", lambda e: e.scalar_tensor_tensor(
                out=sb["stat"][:, 5:6], in0=sb["stat"][:, 2:3],
                scalar=sb["stat"][:, 2:3], in1=sb["stat"][:, 4:5],
                op0=ALU.mult, op1=ALU.subtract), "sv")
            self.w("scalar", "sv")
            self.o("scalar", lambda e: e.activation(
                out=sb["stat"][:, 6:7], in_=sb["stat"][:, 5:6], func=AF.Ln,
                scale=-1.0, bias=EPS_LN), "ss")
            self.o("scalar", lambda e: e.activation(
                out=sb["stat"][:, 7:8], in_=sb["stat"][:, 6:7], func=AF.Exp,
                scale=-0.5), "ss")
            self.w("vector", "ss")
            self.o("vector", lambda e, b=buf: e.tensor_scalar(
                out=sb["xntm"][:, :], in0=b[:, :], scalar1=sb["stat"][:, 2:3],
                scalar2=sb["stat"][:, 7:8], op0=ALU.subtract, op1=ALU.mult), "sv")
            if not hasattr(self, "xld_free"):
                self.xld_free = {}
            self.xld_free[st % 2] = self.tok("sv")
            self.w("tensor", "sv")
            for r in range(2):
                if r == 1:
                    self.w("tensor", "sv")
                for i in range(4):
                    k = r * 4 + i
                    self.o("tensor", lambda e, k=k, i=i: e.transpose(
                        ps["ptr"][:, i * 128:(i + 1) * 128],
                        sb["xntm"][:, k * 128:(k + 1) * 128],
                        sb["cstb"][:, CI_I128:CI_I128 + 128]),
                        "stt" if i == 3 else None)
                self.w("vector", "stt")
                self.o("vector", lambda e, r=r, st=st: e.tensor_copy(
                    out=ap(sb["xnf"], 0, 128, r * 4 * TILE + st * 128,
                           [[TILE, 4], [1, 128]]),
                    in_=ps["ptr"][:, 0:512]), "sv")

        if self.stop_after == "ln":
            return
        self.load_w("Wk")
        self.mm_proj("Wq", self.ev_copy("qk", 0))
        self.load_w("Wv")
        self.mm_proj("Wk", self.ev_copy("qk", 8 * TILE))
        self.load_w("Wr")

        def ev_vtm(m, bank):
            tb, oh = m // 2, m % 2
            self.w("scalar", "stt")
            self.o("scalar", lambda e, b=bank, tb=tb, oh=oh: e.activation(
                out=sb["vtm"][:, tb * HID + oh * 512: tb * HID + (oh + 1) * 512],
                in_=b[:, 0:TILE], func=AF.Copy), "ss")
            return ("ss", self.tok("ss"))
        self.mm_proj("Wv", ev_vtm, tokmajor=True)
        self.qk_tok = self.tok("ss")

        if self.stop_after == "proj":
            return
        self.ph_scales(it)

        if self.stop_after == "scales":
            return

        self.ph_ffn_mix()

        self.projq = []
        if self.stop_after is None:
            self.build_projq(it)

        for a in range(NPAIR):
            self.ph_pair(it, a)
        while self.projq:
            self.projq.pop(0)()
        if self.stop_after in ("pskv", "pskv1", "pskvnoev", "pskv1noev",
                               "builds", "solve", "post", "pairs", "ffn"):
            return

        self.ph_gate_wo(it)

        self.ph_epilogue(it)

        if self.debug and it == self.ntile - 1:
            self.ph_debug_dump()

    def ph_scales(self, it):
        sb, ps = self.sb, self.ps
        self.w("tensor", "sv")
        for k in range(8):
            self.o("tensor", lambda e, k=k: e.matmul(
                ps["pscl"][0:16, 0:TILE], sb["wabt"][:, k * 32:k * 32 + 16],
                sb["xnf"][:, k * TILE:(k + 1) * TILE],
                start=(k == 0), stop=(k == 7)))
            self.o("tensor", lambda e, k=k: e.matmul(
                ps["psq"][0:16, 0:TILE], sb["wabt"][:, k * 32 + 16:k * 32 + 32],
                sb["xnf"][:, k * TILE:(k + 1) * TILE],
                start=(k == 0), stop=(k == 7)), "stt" if k == 7 else None)
        self.w("scalar", "stt")
        self.o("scalar", lambda e: e.activation(
            out=sb["zsig"][:, 0:TILE], in_=ps["pscl"][0:16, 0:TILE], func=AF.Sigmoid,
            bias=sb["babs"][:, 0:1]), "ss")
        self.o("scalar", lambda e: e.activation(
            out=sb["zsig"][:, TILE:2 * TILE], in_=ps["psq"][0:16, 0:TILE], func=AF.Sigmoid,
            bias=sb["babs"][:, 1:2]), "ss")
        self.w("vector", "ss")
        self.o("vector", lambda e: e.tensor_scalar(
            out=sb["t16a"][:, :], in0=sb["zsig"][0:16, 0:TILE], scalar1=1e-35,
            scalar2=None, op0=ALU.max), "sv")
        self.w("scalar", "sv")
        self.o("scalar", lambda e: e.activation(
            out=sb["la"][:, :], in_=sb["t16a"][:, :], func=AF.Ln), "ss")
        self.w("vector", "ss")
        for c in range(NCH):
            self.o("vector", lambda e, c=c: e.tensor_tensor_scan(
                out=sb["lc"][:, c * C:(c + 1) * C],
                data0=sb["la"][:, c * C:(c + 1) * C],
                data1=sb["la"][:, c * C:(c + 1) * C], initial=0.0,
                op0=ALU.add, op1=ALU.bypass), "sv")
        self.w("scalar", "sv")
        self.o("scalar", lambda e: e.activation(
            out=sb["gamv"][:, :], in_=sb["lc"][:, :], func=AF.Exp), "ss")
        self.w("vector", "ss")
        self.o("vector", lambda e: e.tensor_scalar(
            out=sb["t16a"][:, :], in0=sb["lc"][:, :], scalar1=-1.0, scalar2=CLAMP,
            op0=ALU.mult, op1=ALU.min), "sv")
        self.w("scalar", "sv")
        self.o("scalar", lambda e: e.activation(
            out=sb["uv"][:, :], in_=sb["t16a"][:, :], func=AF.Exp), "ss")

        self.w("vector", "ss", self.qk_tok)
        for (nm, soff) in [("nqv", 0), ("nkv", 8 * TILE)]:
            for k in range(8):
                self.w("vector", "stt")
                self.o("vector", lambda e, k=k, soff=soff: e.tensor_tensor(
                    out=sb["sq"][:, :],
                    in0=sb["qk"][:, soff + k * TILE:soff + (k + 1) * TILE],
                    in1=sb["qk"][:, soff + k * TILE:soff + (k + 1) * TILE],
                    op=ALU.mult), "sv")
                self.w("tensor", "sv")
                self.o("tensor", lambda e, k=k: e.matmul(
                    ps["psq"][0:16, 0:TILE],
                    sb["cstb"][:, CI_IND + 16 * k: CI_IND + 16 * (k + 1)],
                    sb["sq"][:, :], start=(k == 0), stop=(k == 7)), "stt")
            self.w("vector", "stt")
            self.w("vector", "ss")
            self.o("vector", lambda e: e.tensor_scalar(
                out=sb["t16a"][:, :], in0=ps["psq"][0:16, 0:TILE],
                scalar1=1e-12, scalar2=None, op0=ALU.max), "sv")
            self.w("scalar", "sv")
            self.o("scalar", lambda e: e.activation(
                out=sb["t16b"][:, :], in_=sb["t16a"][:, :], func=AF.Ln), "ss")
            self.o("scalar", lambda e, nm=nm: e.activation(
                out=sb[nm][:, :], in_=sb["t16b"][:, :], func=AF.Exp, scale=-0.5), "ss")

        self.w("vector", "ss")
        self.o("vector", lambda e: e.tensor_tensor(
            out=sb["t16a"][:, :], in0=sb["nkv"][:, :], in1=sb["nkv"][:, :],
            op=ALU.mult), "sv")
        self.o("vector", lambda e: e.tensor_tensor(
            out=sb["t16a"][:, :], in0=sb["t16a"][:, :], in1=sb["uv"][:, :],
            op=ALU.mult), "sv")
        self.o("vector", lambda e: e.tensor_tensor(
            out=sb["t16a"][:, :], in0=sb["t16a"][:, :], in1=sb["gamv"][:, :],
            op=ALU.mult), "sv")
        self.o("vector", lambda e: e.tensor_tensor(
            out=sb["chiv"][:, :], in0=sb["t16a"][:, :], in1=sb["zsig"][0:16, TILE:2 * TILE],
            op=ALU.mult), "sv")
        self.o("vector", lambda e: e.tensor_tensor(
            out=sb["t16b"][:, :], in0=sb["nkv"][:, :], in1=sb["gamv"][:, :],
            op=ALU.mult), "sv")
        self.o("vector", lambda e: e.reciprocal(
            out=sb["zetv"][:, :], in_=sb["t16b"][:, :]), "sv")
        self.o("vector", lambda e: e.tensor_tensor(
            out=sb["omv"][:, :], in0=sb["nqv"][:, :], in1=sb["gamv"][:, :],
            op=ALU.mult), "sv")

        self.w("tensor", "sv")
        for (vn, tn) in [("chiv", "chiT"), ("zetv", "zetT"), ("omv", "omT")]:
            for blk in range(4):
                self.o("tensor", lambda e, vn=vn, blk=blk: e.transpose(
                    ps["pscl"][:, 256 + blk * 16: 256 + (blk + 1) * 16],
                    sb[vn][:, blk * 128:(blk + 1) * 128],
                    sb["cstf"][0:16, CI_I128:CI_I128 + 16]),
                    "stt" if blk == 3 else None)
            self.w("vector", "stt")
            self.o("vector", lambda e, tn=tn: e.tensor_copy(
                out=sb[tn][:, :], in_=ps["pscl"][:, 256:256 + 64]), "sv")
            self.w("tensor", "sv")

        self.w("gpsimd", "sv")
        self.o("gpsimd", lambda e: e.dma_start(
            out=sb["gamrow"][0:1, 0:128],
            in_=ap(sb["gamv"], 0, 16, C - 1, [[C, NCH]]),
            allow_slow_non_contiguous=True), "sgr", 16)
        self.w("tensor", "sgr")
        self.o("tensor", lambda e: e.matmul(
            ps["pscl"][:, 384:384 + 128], sb["cstf"][0:1, CI_ONES:CI_ONES + 128],
            sb["gamrow"][0:1, 0:128], start=True, stop=True), "stt")
        self.w("vector", "stt")
        self.o("vector", lambda e: e.tensor_copy(
            out=sb["gamrep"][:, :], in_=ps["pscl"][:, 384:384 + 128]), "sv")

    def pq_run(self, n):
        for _ in range(n):
            if self.projq:
                self.projq.pop(0)()

    def ph_pair(self, it, a):
        sb, ps = self.sb, self.ps
        if not hasattr(self, "projq"):
            self.projq = []

        def oin_gather(par, pc):
            return ap(self.sb["oin"], pc, 64, par * 64, [[512, 2], [128, 4], [1, 64]])

        def um_gather(par, phh, cc):
            return ap(self.sb["um"], phh, 64, cc * 512, [[64, 8], [1, 64]])
        cstb = self.sb["cstb"]
        c0 = 2 * a
        if not hasattr(self, "btok"):
            self.btok = {}
        btok = self.btok

        def bank_wait(bn):
            t = btok.get(bn)
            if t is not None:
                self.w("tensor", t[0], t[1])

        def chT(h):
            return sb["chiT"][:, a * 16 + h: a * 16 + h + 1]

        def zeT(h):
            return sb["zetT"][:, a * 16 + h: a * 16 + h + 1]

        POOL = [("ppsk", "psl1"), ("psq", "psl0")]

        nrounds = 1 if self.stop_after in ("pskv1", "pskv1noev") else 8
        noev = self.stop_after in ("pskvnoev", "pskv1noev")
        for rnd in range(nrounds):
            be, bo = POOL[rnd % 2]
            bank_wait(be)
            bank_wait(bo)
            for cc in (0, 1):
                pc = 64 * cc
                for si in range(2):
                    h = 2 * rnd + si
                    ph = 64 * (h % 2)
                    bank = ps[be] if h % 2 == 0 else ps[bo]
                    cl = (h // 2) * TILE + (c0 + cc) * C
                    self.o("tensor", lambda e, ph=ph, cl=cl, pc=pc, b=bank: e.matmul(
                        b[pc:pc + 64, 0:128],
                        sb["qk"][ph:ph + 64, 8 * TILE + cl:8 * TILE + cl + 64],
                        ap(sb["qk"], ph, 64, cl, [[8 * TILE, 2], [1, 64]]),
                        start=True, stop=True, tile_position=(ph, pc)))
                    self.o("tensor", lambda e, ph=ph, cl=cl, pc=pc, b=bank: e.matmul(
                        b[pc:pc + 64, 128:192],
                        sb["qk"][ph:ph + 64, 8 * TILE + cl:8 * TILE + cl + 64],
                        cstb[ph:ph + 64, CI_ISTK:CI_ISTK + 64],
                        start=True, stop=True, tile_position=(ph, pc)),
                        "stt" if (si == 1 and cc == 1) else None)
            if noev:
                continue
            mmtok = self.tok("stt")

            def dve_evacs(h, bn):
                self.o("vector", lambda e, h=h, b=ps[bn]: e.scalar_tensor_tensor(
                    out=ap(sb["csk"], 0, 128, h * 192, [[1, 192]]),
                    in0=ap(b, 0, 128, 0, [[1, 192]]),
                    scalar=chT(h),
                    in1=ap(sb["cstf"], 0, 128, CI_M3B, [[1, 192]]),
                    op0=ALU.mult, op1=ALU.mult), "sv")
                self.o("vector", lambda e, h=h, b=ps[bn]: e.tensor_tensor(
                    out=ap(sb["susl"], 0, 128, h * 64, [[1, 64]]),
                    in0=ap(b, 0, 128, 64, [[1, 64]]),
                    in1=ap(sb["cstf"], 0, 128, CI_MSL, [[1, 64]]),
                    op=ALU.mult), "sv")

            def sc_evacs(h, bn):
                self.o("scalar", lambda e, h=h, a=a: e.activation(
                    out=ap(sb["yA"], 0, 128, h * 128, [[1, 64]]),
                    in_=ap(sb["vtm"], 0, 128, a * HID + h * 64, [[1, 64]]),
                    func=AF.Copy, scale=zeT(h)), "ss")
                self.o("scalar", lambda e, h=h, b=ps[bn]: e.activation(
                    out=ap(sb["yA"], 0, 128, h * 128 + 64, [[1, 64]]),
                    in_=ap(b, 0, 128, 128, [[1, 64]]),
                    func=AF.Copy, scale=-1.0), "ss")
                self.o("scalar", lambda e, h=h, b=ps[bn]: e.activation(
                    out=ap(sb["ktm"], 0, 128, h * 64, [[1, 64]]),
                    in_=ap(b, 0, 128, 128, [[1, 64]]),
                    func=AF.Copy), "ss")

            he, ho = 2 * rnd, 2 * rnd + 1
            self.w("vector", "stt", mmtok)
            self.w("scalar", "stt", mmtok)
            dve_evacs(he, be)
            sc_evacs(ho, bo)
            self.w("vector", "ss")
            self.w("scalar", "sv")
            dve_evacs(ho, bo)
            sc_evacs(he, be)
            btok[be] = ("ss", self.tok("ss"))
            btok[bo] = ("sv", self.tok("sv"))
        if self.stop_after in ("pskv", "pskv1", "pskvnoev", "pskv1noev"):
            return
        self.pq_run(2)

        def sl_slice(h, pc):
            return ap(sb["susl"], pc, 64, h * 64, [[1, 64]])

        def sux_slice(h, pc):
            return ap(sb["csk"], pc, 64, h * 192 + 64, [[1, 64]])

        def pwx_slice(l, h, pc):
            return ap(sb["pwx"], pc, 64, l * H * 64 + h * 64, [[1, 64]])

        def pwt_slice(l, h, pc):
            return ap(sb["pwt"], pc, 64, (l % 2) * H * 64 + h * 64, [[1, 64]])

        self.w("tensor", "sv")
        self.w("tensor", "ss")
        wv = 0
        for L in range(5):
            for part in range(2):
                if L == 4 and part == 1:
                    break
                be, bo = POOL[wv % 2]
                wv += 1
                bank_wait(be)
                bank_wait(bo)
                if L > 0 and part == 0:
                    self.lvl_tok = (self.tok("sv"), self.tok("ss"))
                if L > 0:
                    self.w("tensor", "sv", self.lvl_tok[0])
                    self.w("tensor", "ss", self.lvl_tok[1])
                for cc in (0, 1):
                    pc = 64 * cc
                    for h in range(H):
                        bank = ps[be] if h < 8 else ps[bo]
                        colb = (h % 8) * 64
                        if L == 0:
                            stat = sl_slice(h, pc) if part == 0 else sux_slice(h, pc)
                            mv = sux_slice(h, pc) if part == 0 else sl_slice(h, pc)
                        else:
                            stat = pwt_slice(L - 1, h, pc) if part == 0 else pwx_slice(L - 1, h, pc)
                            mv = pwx_slice(L - 1, h, pc) if part == 0 else pwt_slice(L - 1, h, pc)
                        self.o("tensor", lambda e, st=stat, mv=mv, pc=pc, colb=colb,
                               bank=bank: e.matmul(
                            bank[pc:pc + 64, colb:colb + 64], st, mv,
                            start=True, stop=True, tile_position=(pc, pc)),
                            "stt" if (h == H - 1 and cc == 1) else None)
                self.w("vector", "stt")
                self.w("scalar", "stt")
                for bi, bn in enumerate((be, bo)):
                    if part == 0:
                        self.o("vector", lambda e, L=L, bi=bi, bn=bn: e.tensor_tensor(
                            out=ap(sb["pwx"], 0, 128, L * H * 64 + bi * 512, [[1, 512]]),
                            in0=ap(ps[bn], 0, 128, 0, [[1, 512]]),
                            in1=ap(sb["chiT"], 0, 128, a * 16 + bi * 8, [[1, 8], [0, 64]]),
                            op=ALU.mult), "sv")
                    else:
                        self.o("scalar", lambda e, L=L, bi=bi, bn=bn: e.activation(
                            out=ap(sb["pwt"], 0, 128, (L % 2) * H * 64 + bi * 512,
                                   [[1, 512]]),
                            in_=ap(ps[bn], 0, 128, 0, [[1, 512]]),
                            func=AF.Copy), "ss")
                    btok[bn] = ("sv", self.tok("sv")) if part == 0 else \
                        ("ss", self.tok("ss"))
        if self.stop_after == "builds":
            return
        self.pq_run(2)

        stages = [(4, ALU.add), (3, ALU.add), (2, ALU.add), (1, ALU.add),
                  (0, ALU.add), (-1, ALU.subtract)]
        cur, nxt = "yA", "yB"
        self.w("tensor", "ss")
        bi = 0
        qtok = {q: self.tok("sv") for q in range(4)}
        for sidx, (L, op) in enumerate(stages):
            for quad in range(4):
                bn = [b for pr in POOL for b in pr][bi % 4]
                bi += 1
                bank_wait(bn)
                self.w("tensor", "sv", qtok[quad])
                bank = ps[bn]
                for hh in range(4):
                    h = quad * 4 + hh
                    for cc in (0, 1):
                        pc = 64 * cc
                        stat = sux_slice(h, pc) if L < 0 else pwx_slice(L, h, pc)
                        self.o("tensor", lambda e, st=stat, h=h, hh=hh, pc=pc,
                               bank=bank, cur=cur: e.matmul(
                            bank[pc:pc + 64, hh * 128:(hh + 1) * 128], st,
                            ap(sb[cur], pc, 64, h * 128, [[1, 128]]),
                            start=True, stop=True, tile_position=(pc, pc)),
                            "stt" if (hh == 3 and cc == 1) else None)
                self.w("vector", "stt")
                if op == ALU.add:
                    self.o("vector", lambda e, quad=quad, bank=bank, cur=cur,
                           nxt=nxt: e.tensor_tensor(
                        out=ap(sb[nxt], 0, 128, quad * 512, [[1, 512]]),
                        in0=ap(bank, 0, 128, 0, [[1, 512]]),
                        in1=ap(sb[cur], 0, 128, quad * 512, [[1, 512]]),
                        op=ALU.add), "sv")
                else:
                    self.o("vector", lambda e, quad=quad, bank=bank, cur=cur,
                           nxt=nxt: e.tensor_tensor(
                        out=ap(sb[nxt], 0, 128, quad * 512, [[1, 512]]),
                        in0=ap(sb[cur], 0, 128, quad * 512, [[1, 512]]),
                        in1=ap(bank, 0, 128, 0, [[1, 512]]),
                        op=ALU.subtract), "sv")
                btok[bn] = ("sv", self.tok("sv"))
                qtok[quad] = self.tok("sv")
            cur, nxt = nxt, cur
        zf = cur

        self.o("vector", lambda e, zf=zf: e.tensor_tensor(
            out=ap(sb["zknx"], 0, 128, 0, [[64, 16], [1, 64]]),
            in0=ap(sb[zf], 0, 128, 64, [[128, 16], [1, 64]]),
            in1=ap(sb["chiT"], 0, 128, a * 16, [[1, 16], [0, 64]]),
            op=ALU.mult), "sv")
        zknx_tok = self.tok("sv")
        if self.stop_after == "solve":
            return
        self.pq_run(2)

        def bankcol(h):
            return (h // 2) * 64

        post_in_tok = self.tok("sv")

        def post_wave(mm_fn, evac_fn, need_tok):
            be, bo = POOL[self.pwave % 2]
            self.pwave += 1
            bank_wait(be)
            bank_wait(bo)
            self.w("tensor", "sv", need_tok)
            mm_fn(ps[be], ps[bo])
            self.w("vector", "stt")
            evac_fn(be, bo)
            btok[be] = ("sv", self.tok("sv"))
            btok[bo] = ("sv", self.tok("sv"))

        self.pwave = getattr(self, "pwave", 0)

        def mm_oin(bA, bB):
            for h in range(H):
                bk = bA if h < 8 else bB
                cb = (h % 8) * 64
                for cc in (0, 1):
                    pc = 64 * cc
                    self.o("tensor", lambda e, h=h, pc=pc, b=bk, cb=cb, zf=zf: e.matmul(
                        b[pc:pc + 64, cb:cb + 64],
                        ap(sb["csk"], pc, 64, h * 192, [[1, 64]]),
                        ap(sb[zf], pc, 64, h * 128, [[1, 64]]),
                        start=True, stop=True, tile_position=(pc, pc)),
                        "stt" if (h == H - 1 and cc == 1) else None)

        def ev_oin(be, bo):
            for g, bn in enumerate((be, bo)):
                self.o("vector", lambda e, g=g, bn=bn, a=a: e.tensor_tensor(
                    out=ap(sb["oin"], 0, 128, g * 512, [[1, 512]]),
                    in0=ap(ps[bn], 0, 128, 0, [[1, 512]]),
                    in1=ap(sb["omT"], 0, 128, a * 16 + g * 8, [[1, 8], [0, 64]]),
                    op=ALU.mult), "sv")
        post_wave(mm_oin, ev_oin, post_in_tok)

        def mm_qef(bA, bB):
            for h in range(H):
                ph = 64 * (h % 2)
                for cc in (0, 1):
                    pc = 64 * cc
                    b = bA if cc == 0 else bB
                    self.o("tensor", lambda e, h=h, ph=ph, pc=pc, b=b, zf=zf: e.matmul(
                        b[ph:ph + 64, bankcol(h):bankcol(h) + 64],
                        ap(sb[zf], pc, 64, h * 128 + 64, [[1, 64]]),
                        ap(sb["csk"], pc, 64, h * 192, [[1, 64]]),
                        start=True, stop=True, tile_position=(pc, ph)),
                        "stt" if (h == H - 1 and cc == 1) else None)

        def ev_qef(be, bo):
            for cc, bn in enumerate((be, bo)):
                self.o("vector", lambda e, cc=cc, bn=bn, c0=c0: e.tensor_tensor(
                    out=ap(sb["qef"], 0, 128, cc * 512, [[1, 512]]),
                    in0=ap(ps[bn], 0, 128, 0, [[1, 512]]),
                    in1=ap(sb["qk"], 0, 128, (c0 + cc) * C, [[TILE, 8], [1, 64]]),
                    op=ALU.add), "sv")
        post_wave(mm_qef, ev_qef, post_in_tok)

        def mm_gtm(bA, bB):
            for h in range(H):
                ph = 64 * (h % 2)
                for cc in (0, 1):
                    pc = 64 * cc
                    b = bA if cc == 0 else bB
                    self.o("tensor", lambda e, h=h, ph=ph, pc=pc, b=b: e.matmul(
                        b[ph:ph + 64, bankcol(h):bankcol(h) + 64],
                        ap(sb["zknx"], pc, 64, h * 64, [[1, 64]]),
                        ap(sb["ktm"], pc, 64, h * 64, [[1, 64]]),
                        start=True, stop=True, tile_position=(pc, ph)),
                        "stt" if (h == H - 1 and cc == 1) else None)

        def ev_gtm(be, bo):
            for cc, bn in enumerate((be, bo)):
                self.o("vector", lambda e, cc=cc, bn=bn: e.tensor_tensor(
                    out=ap(sb["gtm"], 0, 128, cc * 512, [[1, 512]]),
                    in0=ap(ps[bn], 0, 128, 0, [[1, 512]]),
                    in1=ap(sb["cstb"], 0, 128, CI_IREP8, [[1, 512]]),
                    op=ALU.add), "sv")
        post_wave(mm_gtm, ev_gtm, zknx_tok)

        def mm_um(bA, bB):
            for h in range(H):
                ph = 64 * (h % 2)
                for cc in (0, 1):
                    pc = 64 * cc
                    b = bA if cc == 0 else bB
                    self.o("tensor", lambda e, h=h, ph=ph, pc=pc, b=b, zf=zf: e.matmul(
                        b[ph:ph + 64, bankcol(h):bankcol(h) + 64],
                        ap(sb["csk"], pc, 64, h * 192 + 128, [[1, 64]]),
                        ap(sb[zf], pc, 64, h * 128, [[1, 64]]),
                        start=True, stop=True, tile_position=(pc, ph)),
                        "stt" if (h == H - 1 and cc == 1) else None)

        def ev_um(be, bo):
            for cc, bn in enumerate((be, bo)):
                for half in (0, 1):
                    self.o("vector", lambda e, cc=cc, half=half, bn=bn, c0=c0: e.tensor_tensor(
                        out=ap(sb["um"], 64 * half, 64, cc * 512, [[1, 512]]),
                        in0=ap(ps[bn], 64 * half, 64, 0, [[1, 512]]),
                        in1=ap(sb["gamrep"], 64 * half, 64, half * 8 + c0 + cc,
                               [[16, 8], [0, 64]]),
                        op=ALU.mult), "sv")
        post_wave(mm_um, ev_um, post_in_tok)
        if self.stop_after == "post":
            return
        self.pq_run(2)

        for cc in (0, 1):
            pc = 64 * cc
            self.w("tensor", "sv")
            for bn in ("psq", "ppsk", "psl0", "psl1"):
                bank_wait(bn)
            for h in range(H):
                ph = 64 * (h % 2)
                bank = ps["psq"] if h % 2 == 0 else ps["ppsk"]
                self.o("tensor", lambda e, h=h, ph=ph, pc=pc, cc=cc, bank=bank: e.matmul(
                    bank[pc:pc + 64, bankcol(h):bankcol(h) + 64],
                    ap(sb["qef"], ph, 64, cc * 512 + bankcol(h), [[1, 64]]),
                    ap(sb["mst"], ph, 64, h * 64, [[1, 64]]),
                    start=True, stop=True, tile_position=(ph, pc)),
                    "stt" if h == H - 1 else None)
            owave_tok = self.tok("stt")
            for h in range(H):
                ph = 64 * (h % 2)
                bank = ps["psl0"] if h % 2 == 0 else ps["psl1"]
                self.o("tensor", lambda e, h=h, ph=ph, cc=cc, bank=bank: e.matmul(
                    bank[ph:ph + 64, bankcol(h):bankcol(h) + 64],
                    ap(sb["gtm"], ph, 64, cc * 512 + bankcol(h), [[1, 64]]),
                    ap(sb["mst"], ph, 64, h * 64, [[1, 64]]),
                    start=True, stop=True, tile_position=(ph, ph)),
                    "stt" if h == H - 1 else None)
            self.w("vector", "stt", owave_tok)
            for par, bn in ((0, "psq"), (1, "ppsk")):
                self.o("vector", lambda e, par=par, bn=bn, pc=pc, a=a: e.tensor_tensor(
                    out=ap(sb["xsq"], pc, 64, 0, [[1, 512]]),
                    in0=ap(ps[bn], pc, 64, 0, [[1, 512]]),
                    in1=ap(sb["omT"], pc, 64, a * 16 + par, [[2, 8], [0, 64]]),
                    op=ALU.mult), "sv")
                self.o("vector", lambda e, par=par, pc=pc, a=a: e.tensor_tensor(
                    out=ap(sb["otm"], pc, 64, a * HID + par * 64, [[128, 8], [1, 64]]),
                    in0=ap(sb["xsq"], pc, 64, 0, [[1, 512]]),
                    in1=oin_gather(par, pc),
                    op=ALU.add), "sv")
            self.w("vector", "stt")
            for par, bn in ((0, "psl0"), (1, "psl1")):
                phh = 64 * par
                self.o("vector", lambda e, par=par, bn=bn, phh=phh, cc=cc, c0=c0:
                       e.tensor_tensor(
                    out=ap(sb["xsq"], phh, 64, 512, [[1, 512]]),
                    in0=ap(ps[bn], phh, 64, 0, [[1, 512]]),
                    in1=ap(sb["gamrep"], phh, 64, par * 8 + c0 + cc, [[16, 8], [0, 64]]),
                    op=ALU.mult), "sv")
                self.o("vector", lambda e, par=par, phh=phh, cc=cc: e.tensor_tensor(
                    out=ap(sb["mst"], phh, 64, par * 64, [[128, 8], [1, 64]]),
                    in0=ap(sb["xsq"], phh, 64, 512, [[1, 512]]),
                    in1=um_gather(par, phh, cc),
                    op=ALU.add), "sv")
            for bn in ("psq", "ppsk", "psl0", "psl1"):
                btok[bn] = ("sv", self.tok("sv"))

    def ph_ffn_mix(self):
        sb = self.sb
        self.w("gpsimd", "stt")
        self.w("gpsimd", "sv")
        for m in range(8):
            self.o("gpsimd", lambda e, m=m: e.tensor_tensor(
                out=sb["sq"][:, 1:TILE], in0=sb["xnf"][:, m * TILE + 1:(m + 1) * TILE],
                in1=sb["xnf"][:, m * TILE:(m + 1) * TILE - 1], op=ALU.subtract), "sp")
            self.o("gpsimd", lambda e, m=m: e.tensor_tensor(
                out=sb["sq"][:, 0:1], in0=sb["xnf"][:, m * TILE:m * TILE + 1],
                in1=sb["xbnd"][:, m:m + 1], op=ALU.subtract), "sp")
            self.o("gpsimd", lambda e, m=m: e.tensor_copy(
                out=sb["ffb"][:, m * TILE + 1:(m + 1) * TILE],
                in_=sb["xnf"][:, m * TILE:(m + 1) * TILE - 1]), "sp")
            self.o("gpsimd", lambda e, m=m: e.tensor_copy(
                out=sb["ffb"][:, m * TILE:m * TILE + 1], in_=sb["xbnd"][:, m:m + 1]), "sp")
            for dst, col in (("ffa", 2 * m), ("gf", 2 * m + 1)):
                self.o("gpsimd", lambda e, m=m, dst=dst, col=col: e.tensor_tensor(
                    out=sb[dst][:, m * TILE:(m + 1) * TILE], in0=sb["sq"][:, :],
                    in1=ap(sb["mus"], 0, 128, col, [[0, TILE]]),
                    op=ALU.mult), "sp")
                self.o("gpsimd", lambda e, m=m, dst=dst: e.tensor_tensor(
                    out=sb[dst][:, m * TILE:(m + 1) * TILE],
                    in0=sb[dst][:, m * TILE:(m + 1) * TILE],
                    in1=sb["ffb"][:, m * TILE:(m + 1) * TILE],
                    op=ALU.add), "sp")
        self.o("gpsimd", lambda e: e.tensor_copy(
            out=sb["xbnd"][:, 0:8],
            in_=ap(sb["xnf"], 0, 128, TILE - 1, [[TILE, 8], [1, 1]])), "sp")
        self.mix_tok = self.tok("sp")

    def build_projq(self, it):
        sb = self.sb
        q = self.projq

        def ev_r(m, bank):
            self.w("scalar", "stt")
            self.o("scalar", lambda e, m=m, b=bank: e.activation(
                out=sb["ffb"][:, m * TILE:(m + 1) * TILE], in_=b[:, 0:TILE],
                func=AF.Sigmoid), "ss")
            return ("ss", self.tok("ss"))

        def ev_k(m, bank):
            self.w("scalar", "stt")
            self.o("scalar", lambda e, m=m, b=bank: e.activation(
                out=sb["ffa"][:, m * TILE:(m + 1) * TILE], in_=b[:, 0:TILE],
                func=AF.Relu), "ss")
            t = ("ss", self.tok("ss"))
            self.w("gpsimd", "ss")
            self.o("gpsimd", lambda e, m=m: e.tensor_tensor(
                out=sb["ffa"][:, m * TILE:(m + 1) * TILE],
                in0=sb["ffa"][:, m * TILE:(m + 1) * TILE],
                in1=sb["ffa"][:, m * TILE:(m + 1) * TILE], op=ALU.mult), "sp")
            return t

        def ev_v(m, bank):
            self.w("vector", "stt")
            self.w("vector", "ss")
            self.o("vector", lambda e, m=m, b=bank: e.tensor_tensor(
                out=sb["branch"][:, m * TILE:(m + 1) * TILE], in0=b[:, 0:TILE],
                in1=sb["ffb"][:, m * TILE:(m + 1) * TILE], op=ALU.mult), "sv")
            return ("sv", self.tok("sv"))

        def ev_g(m, bank):
            self.w("scalar", "stt")
            self.o("scalar", lambda e, m=m, b=bank: e.activation(
                out=sb["gf"][:, m * TILE:(m + 1) * TILE], in_=b[:, 0:TILE],
                func=AF.Silu), "ss")
            return ("ss", self.tok("ss"))

        mixt = [("sp", self.mix_tok), ("ss", self.tok("ss")), ("sv", self.tok("sv"))]
        plan = [("load", "Wkf"), ("proj", "Wr", ev_r, "ffa", mixt),
                ("load", "Wvf"), ("proj", "Wkf", ev_k, "gf", mixt),
                ("load", "Wg"), ("proj", "Wvf", ev_v, "ffa", None),
                ("load", "Wo"), ("proj", "Wg", ev_g, None, None)]
        if it < self.ntile - 1:
            plan.append(("load", "Wq"))
        for item in plan:
            if item[0] == "load":
                q.append(lambda wn=item[1]: self.load_w(wn))
            else:
                _, wn, ev, src, stok = item
                holder = {}

                def start(wn=wn, ev=ev, src=src, stok=stok, holder=holder):
                    holder["st"] = self.mm_proj_start(wn, ev, src=src, src_tok=stok)

                q.append(start)
                for m in range(8):
                    q.append(lambda m=m, holder=holder: self.mm_proj_step(holder["st"], m))

    def ph_gate_wo(self, it):
        sb, ps = self.sb, self.ps
        self.w("tensor", "sv")
        self.w("vector", "ss")
        for k in range(8):
            for blk in range(4):
                self.o("tensor", lambda e, k=k, blk=blk: e.transpose(
                    ps["ptr"][:, blk * 128:(blk + 1) * 128],
                    sb["otm"][:, blk * HID + k * 128: blk * HID + (k + 1) * 128],
                    sb["cstb"][:, CI_I128:CI_I128 + 128]),
                    "stt" if blk == 3 else None)
            self.w("vector", "stt")
            self.o("vector", lambda e, k=k: e.tensor_tensor(
                out=sb["ffb"][:, k * TILE:(k + 1) * TILE], in0=ps["ptr"][:, 0:512],
                in1=sb["gf"][:, k * TILE:(k + 1) * TILE], op=ALU.mult), "sv")
            self.w("tensor", "sv")

        def ev_o(m, bank):
            self.w("vector", "stt")
            self.o("vector", lambda e, m=m, b=bank: e.tensor_tensor(
                out=sb["branch"][:, m * TILE:(m + 1) * TILE], in0=b[:, 0:TILE],
                in1=sb["branch"][:, m * TILE:(m + 1) * TILE], op=ALU.add), "sv")
            return ("sv", self.tok("sv"))
        self.mm_proj("Wo", ev_o, src="ffb")

    def ph_epilogue(self, it):
        sb, ps = self.sb, self.ps
        t0 = it * TILE
        self.w("tensor", "sp")
        for st in range(4):
            buf = sb["xld0"] if st % 2 == 0 else sb["xld1"]
            xsem = "sxa" if st % 2 == 0 else "sxb"
            self.w("gpsimd", "sv", self.xld_free.get(st % 2, 0))
            self.o("gpsimd", lambda e, b=buf, st=st, t0=t0: e.dma_start(
                out=b[:, :], in_=self.x_hbm[t0 + st * 128: t0 + (st + 1) * 128, :]),
                xsem, 16)
            self.w("tensor", "sv")
            for r in range(2):
                if r == 1:
                    self.w("tensor", "sv")
                for i in range(4):
                    k = r * 4 + i
                    self.o("tensor", lambda e, k=k, i=i, st=st: e.transpose(
                        ps["ptr"][:, i * 128:(i + 1) * 128],
                        ap(sb["branch"], 0, 128, k * TILE + st * 128, [[1, 128]]),
                        sb["cstb"][:, CI_I128:CI_I128 + 128]),
                        "stt" if i == 3 else None)
                self.w("vector", "stt")
                self.w("vector", xsem)
                if r == 0:
                    self.w("vector", "sdo", self.tok("sdo"))
                self.o("vector", lambda e, r=r, b=buf: e.tensor_tensor(
                    out=sb["outs"][:, r * 512:(r + 1) * 512], in0=ps["ptr"][:, 0:512],
                    in1=b[:, r * 512:(r + 1) * 512], op=ALU.add), "sv")
                self.w("tensor", "sv")
            self.w("gpsimd", "sv")
            self.o("gpsimd", lambda e, st=st, t0=t0: e.dma_start(
                out=self.out_hbm[t0 + st * 128: t0 + (st + 1) * 128, :],
                in_=sb["outs"][:, :]), "sdo", 16)
            self.xld_free[st % 2] = self.tok("sv")

    def ph_debug_dump(self):
        sb = self.sb
        self.w("gpsimd", "sv")
        self.w("gpsimd", "ss")
        for n, (src, dst) in self.dbg.items():
            self.o("gpsimd", lambda e, s=src, d=dst: e.dma_start(
                out=d, in_=sb[s][:, :]), "sdo", 16)

    def ph_finish(self):
        self.w("gpsimd", "sdo")


def _prep_inputs(core, x, ins):
    xs = np.ascontiguousarray(x[4 * core: 4 * core + 4].reshape(-1, HID))
    d = {"x": xs}
    d["W_Wq"] = ins["L0_attn_Wq"]
    d["W_Wk"] = ins["L0_attn_Wk"]
    d["W_Wv"] = ins["L0_attn_Wv"]
    d["W_Wg"] = ins["L0_attn_Wg"]
    d["W_Wo"] = ins["L0_attn_Wo"]
    d["W_Wr"] = ins["L0_ffn_W_r"]
    d["W_Wkf"] = ins["L0_ffn_W_k"]
    d["W_Wvf"] = ins["L0_ffn_W_v"]
    d["Wab"] = np.concatenate([ins["L0_attn_Wa"], ins["L0_attn_Wb"]], axis=0)
    bab = np.zeros((16, 2), dtype=np.float32)
    bab[:, 0] = ins["L0_attn_ba"]
    bab[:, 1] = ins["L0_attn_bb"]
    d["bab"] = bab
    mu = np.zeros((128, 16), dtype=np.float32)
    for k in range(8):
        mu[:, 2 * k] = ins["L0_ffn_mu_r"][k * 128:(k + 1) * 128]
        mu[:, 2 * k + 1] = ins["L0_ffn_mu_k"][k * 128:(k + 1) * 128]
    d["mu"] = mu
    d["consts"] = _host_consts()
    return {k: np.ascontiguousarray(v) for k, v in d.items()}


def kernel(**inputs):
    x = inputs["L0_attn_x"]
    B = x.shape[0]
    p = Prog(nb=B // NCORES, t_len=x.shape[1])
    nc = p.build()
    in_maps = [_prep_inputs(c, x, inputs) for c in range(NCORES)]
    res = run_bass_kernel_spmd(nc, in_maps, core_ids=list(range(NCORES)))
    out = np.concatenate(
        [r["out"].reshape(B // NCORES, x.shape[1], HID) for r in res.results], axis=0)
    return out.astype(np.float32)


# version 13
# speedup vs baseline: 1.3018x; 1.0009x over previous
import numpy as np
import ml_dtypes
import concourse.bass as bass
import concourse.mybir as mybir
from concourse.bass_utils import run_bass_kernel_spmd

F32 = mybir.dt.float32
BF16 = mybir.dt.bfloat16
AF = mybir.ActivationFunctionType
ALU = mybir.AluOpType
AX = mybir.AxisListType

NCORES = 8
HID = 1024
H = 16
DH = 64
C = 64
TILE = 512
NCH = TILE // C
NPAIR = NCH // 2
EPS_LN = 1e-5
CLAMP = 60.0
BIG_W = ["Wq", "Wk", "Wv", "Wg", "Wo", "Wr", "Wkf", "Wvf"]

CI_I128 = 0
CI_ISTK = 128
CI_MPM = 192
CI_MSU = 256
CI_MSL = 320
CI_IND = 384
CI_ONES = 512
CI_M3B = 640
CI_SCAL = 832
CI_IREP8 = 848
NCONST = 1360


def _host_consts():
    c = np.zeros((128, NCONST), dtype=np.float32)
    c[np.arange(128), CI_I128 + np.arange(128)] = 1.0
    c[np.arange(128), CI_ISTK + np.tile(np.arange(64), 2)] = 1.0
    s = np.tile(np.arange(64), 2)[:, None]
    t = np.arange(64)[None, :]
    c[:, CI_MPM:CI_MPM + 64] = (s <= t)
    c[:, CI_MSU:CI_MSU + 64] = (s < t)
    c[:, CI_MSL:CI_MSL + 64] = (t < s)
    for k in range(8):
        c[0:64, CI_IND + 16 * k + 2 * k] = 1.0
        c[64:128, CI_IND + 16 * k + 2 * k + 1] = 1.0
    c[:, CI_ONES:CI_ONES + 128] = 1.0
    c[:, CI_M3B:CI_M3B + 64] = (s <= t)
    c[:, CI_M3B + 64:CI_M3B + 128] = (s < t)
    c[:, CI_M3B + 128:CI_M3B + 192] = 1.0
    c[:, CI_SCAL] = EPS_LN
    for a in range(8):
        c[np.arange(128), CI_IREP8 + 64 * a + (np.arange(128) % 64)] = 1.0
    return c


def ap(t, pbase, pcount, off, dims):
    rowlen = t.shape[1]
    return bass.AP(t, pbase * rowlen + off, [[rowlen, pcount]] + list(dims))


class Prog:
    def __init__(self, nb, t_len, debug=False, stop_after=None):
        self.stop_after = stop_after
        self.nb, self.T, self.debug = nb, t_len, debug
        self.ntb = t_len // TILE
        self.ntile = nb * self.ntb
        self.nc = bass.Bass("TRN2", target_bir_lowering=False)
        self.ctx, self.cnt, self.sems, self.sb, self.ps = [], {}, {}, {}, {}
        self.prog = {"sync": [], "vector": [], "scalar": [], "tensor": [], "gpsimd": []}

    def sem(self, n):
        s = self.nc.semaphore(n)
        self.sems[n] = s.__enter__()
        self.ctx.append(s)
        self.cnt[n] = 0

    def sbuf(self, n, shape, dt):
        t = self.nc.sbuf_tensor(n, shape, dt)
        self.sb[n] = t.__enter__()
        self.ctx.append(t)

    def psum(self, n, shape, dt=F32):
        t = self.nc.psum_tensor(n, shape, dt)
        self.ps[n] = t.__enter__()
        self.ctx.append(t)

    def w(self, eng, sem, val=None):
        v = self.cnt[sem] if val is None else val
        if v <= 0:
            return
        s = self.sems[sem]
        self.prog[eng].append(lambda e, s=s, v=v: e.wait_ge(s, v))

    def o(self, eng, build, sem=None, n=1):
        if eng in ("vector", "scalar") or (eng == "gpsimd" and sem == "sp"):
            ssem = {"vector": "sv", "scalar": "ss", "gpsimd": "sp"}[eng]
            v = self.cnt[ssem]
            if v > 0:
                sh = self.sems[ssem]
                self.prog[eng].append(lambda e, sh=sh, v=v: e.wait_ge(sh, v))
        if sem is None:
            self.prog[eng].append(build)
        else:
            s = self.sems[sem]
            self.prog[eng].append(lambda e, s=s, b=build, n=n: b(e).then_inc(s, n))
            self.cnt[sem] += n

    def tok(self, sem):
        return self.cnt[sem]

    def build(self):
        nc = self.nc
        NTOK = self.nb * self.T
        self.x_hbm = nc.dram_tensor("x", [NTOK, HID], F32, kind="ExternalInput").ap()
        self.out_hbm = nc.dram_tensor("out", [NTOK, HID], F32, kind="ExternalOutput").ap()
        self.w_hbm = {n: nc.dram_tensor("W_" + n, [HID, HID], F32, kind="ExternalInput").ap()
                      for n in BIG_W}
        self.wab_hbm = nc.dram_tensor("Wab", [32, HID], F32, kind="ExternalInput").ap()
        self.bab_hbm = nc.dram_tensor("bab", [16, 2], F32, kind="ExternalInput").ap()
        self.mu_hbm = nc.dram_tensor("mu", [128, 16], F32, kind="ExternalInput").ap()
        self.cst_hbm = nc.dram_tensor("consts", [128, NCONST], F32, kind="ExternalInput").ap()
        self.wt_dram = {n: nc.dram_tensor("WT_" + n, [HID, HID], BF16).ap() for n in BIG_W}

        for s in ["swa", "swb", "sxa", "sxb", "sgr", "sdo", "sv", "ss", "stt", "sg",
                  "sp"]:
            self.sem(s)

        S = self.sbuf
        S("cstf", [128, NCONST], F32)
        self.nc.const_aps.register(F32, EPS_LN,
                                   self.sb["cstf"][:, CI_SCAL:CI_SCAL + 1])
        S("cstb", [128, NCONST], BF16)
        S("wtb0", [128, 8 * HID], BF16)
        S("wtb1", [128, 8 * HID], BF16)
        S("wabt", [128, 8 * 32], BF16)
        S("babs", [16, 2], F32)
        S("babn", [16, 2], F32)
        S("mus", [128, 16], F32)
        S("xld0", [128, HID], F32)
        S("xld1", [128, HID], F32)
        S("xsq", [128, HID], F32)
        S("xntm", [128, HID], BF16)
        S("xnf", [128, 8 * TILE], BF16)
        S("xbnd", [128, 8], BF16)
        S("qk", [128, 16 * TILE], BF16)
        S("vtm", [128, 4 * HID], BF16)
        S("gf", [128, 8 * TILE], BF16)
        S("sq", [128, TILE], BF16)
        S("stat", [128, 8], F32)
        S("zsig", [16, 2 * TILE], F32)
        for n in ["la", "lc", "gamv", "uv", "nqv", "nkv", "chiv", "zetv", "omv",
                  "t16a", "t16b"]:
            S(n, [16, TILE], F32)
        S("chiT", [128, 4 * 16], F32)
        S("zetT", [128, 4 * 16], F32)
        S("omT", [128, 4 * 16], F32)
        S("gamrow", [1, 128], F32)
        S("gamrep", [128, 128], F32)
        S("csk", [128, H * 192], BF16)
        S("susl", [128, H * 64], BF16)
        S("ktm", [128, H * 64], BF16)
        S("zknx", [128, H * 64], BF16)
        S("yA", [128, H * 128], BF16)
        S("yB", [128, H * 128], BF16)
        S("pwx", [128, 5 * H * 64], BF16)
        S("pwt", [128, 2 * H * 64], BF16)
        S("oin", [128, 2 * 512], BF16)
        S("qef", [128, 2 * 512], BF16)
        S("gtm", [128, 2 * 512], BF16)
        S("um", [128, 2 * 512], BF16)
        S("mst", [128, H * 64], BF16)
        S("otm", [128, 4 * HID], BF16)
        S("branch", [128, 8 * TILE], BF16)
        S("ffa", [128, 8 * TILE], BF16)
        S("ffb", [128, 8 * TILE], BF16)
        S("outs", [128, HID], F32)

        Pm = self.psum
        Pm("pp0", [128, 512])
        Pm("pp1", [128, 512])
        Pm("ppsk", [128, 512])
        Pm("psl0", [128, 512])
        Pm("psl1", [128, 512])
        Pm("psq", [128, 512])
        Pm("ptr", [128, 512], BF16)
        Pm("pscl", [128, 512])

        if self.debug:
            self.dbg = {}
            for n, src in [("d_xnf", "xnf"), ("d_qk", "qk"),
                           ("d_vtm", "vtm"), ("d_csk", "csk"), ("d_susl", "susl"),
                           ("d_y", "yA"), ("d_oin", "oin"), ("d_qef", "qef"),
                           ("d_gtm", "gtm"), ("d_um", "um"), ("d_mst", "mst"),
                           ("d_otm", "otm"), ("d_branch", "branch"),
                           ("d_gf", "gf"), ("d_ffa", "ffa")]:
                t = self.sb[src]
                self.dbg[n] = (src, self.nc.dram_tensor(
                    n, [t.shape[0], t.shape[1]], BF16, kind="ExternalOutput").ap())
            for n, src in [("d_chi", "chiv"), ("d_gam", "gamv"), ("d_zet", "zetv"),
                           ("d_om", "omv"), ("d_nq", "nqv"), ("d_nk", "nkv"),
                           ("d_la", "la"), ("d_gamrep", "gamrep")]:
                t = self.sb[src]
                self.dbg[n] = (src, self.nc.dram_tensor(
                    n, [t.shape[0], t.shape[1]], F32, kind="ExternalOutput").ap())

        self.emit_all()

        with nc.Block() as blk:
            pr = self.prog

            @blk.sync
            def _(e):
                for f in pr["sync"]:
                    f(e)

            @blk.vector
            def _(e):
                for f in pr["vector"]:
                    f(e)

            @blk.scalar
            def _(e):
                for f in pr["scalar"]:
                    f(e)

            @blk.tensor
            def _(e):
                for f in pr["tensor"]:
                    f(e)

            @blk.gpsimd
            def _(e):
                for f in pr["gpsimd"]:
                    f(e)
        return nc

    def emit_all(self):
        self.ph_setup()
        self.ph_weight_prep()
        self.slot_free = {0: self.tok("stt"), 1: self.tok("stt")}
        self.xld_free = {0: self.tok("sv"), 1: self.tok("sv")}
        self.wtok = {}
        self.wslot = {}
        self.wload_next = 0
        self.load_w("Wq")
        for it in range(self.ntile):
            self.ph_tile(it)
        self.ph_finish()

    def load_w(self, wn):
        sb = self.sb
        slot = self.wload_next
        self.wload_next ^= 1
        self.wslot[wn] = slot
        sem = "swa" if slot == 0 else "swb"
        dstw = sb["wtb0"] if slot == 0 else sb["wtb1"]
        self.w("sync", "stt", self.slot_free.get(slot, 0))
        self.w("sync", "sv", self.prep_guard.get(sem, 0))
        self.prep_guard[sem] = 0
        self.w("sync", "sg", self.tok("sg"))
        for k in range(8):
            self.o("sync", lambda e, wn=wn, k=k, d=dstw: e.dma_start(
                out=d[:, k * HID:(k + 1) * HID],
                in_=self.wt_dram[wn][k * 128:(k + 1) * 128, :]), sem, 16)
        self.wtok[wn] = (sem, self.tok(sem))

    def ph_setup(self):
        sb = self.sb
        self.o("sync", lambda e: e.dma_start(out=sb["cstf"][:, :], in_=self.cst_hbm), "swa", 16)
        self.o("sync", lambda e: e.dma_start(out=sb["babs"][:, :], in_=self.bab_hbm), "swa", 16)
        self.o("sync", lambda e: e.dma_start(out=sb["mus"][:, :], in_=self.mu_hbm), "swa", 16)
        self.w("vector", "swa")
        self.o("vector", lambda e: e.tensor_copy(out=sb["cstb"][:, :], in_=sb["cstf"][:, :]), "sv")
        self.o("vector", lambda e: e.tensor_scalar(
            out=sb["babn"][:, :], in0=sb["babs"][:, :], scalar1=-1.0, scalar2=None,
            op0=ALU.mult), "sv")
        self.o("vector", lambda e: e.memset(sb["mst"][:, :], 0.0), "sv")
        self.o("vector", lambda e: e.memset(sb["xbnd"][:, :], 0.0), "sv")

    def ph_weight_prep(self):
        sb, ps = self.sb, self.ps
        self.prep_guard = {"swa": self.tok("sv"), "swb": 0}
        I128b = sb["cstb"][:, CI_I128:CI_I128 + 128]
        for wn in BIG_W:
            for j in range(8):
                buf = sb["xld0"] if j % 2 == 0 else sb["xld1"]
                psem = "swa" if j % 2 == 0 else "swb"
                self.w("sync", "sv", self.prep_guard.get(psem, 0))
                self.o("sync", lambda e, s=self.w_hbm[wn], j=j, b=buf: e.dma_start(
                    out=b[:, :], in_=s[j * 128:(j + 1) * 128, :]), psem, 16)
                self.w("vector", psem)
                cols = (j % 2) * HID
                self.o("vector", lambda e, b=buf, c=cols: e.tensor_copy(
                    out=sb["wtb1"][:, c:c + HID], in_=b[:, :]), "sv")
                self.prep_guard[psem] = self.tok("sv")
                for r in range(2):
                    self.w("tensor", "sv")
                    self.w("tensor", "sg")
                    for i in range(4):
                        ii = r * 4 + i
                        self.o("tensor", lambda e, ii=ii, c=cols, i=i: e.transpose(
                            ps["ptr"][:, i * 128:(i + 1) * 128],
                            sb["wtb1"][:, c + ii * 128:c + ii * 128 + 128], I128b),
                            "stt" if i == 3 else None)
                    self.w("vector", "stt")
                    self.o("vector", lambda e, r=r: e.tensor_copy(
                        out=sb["wtb0"][:, r * 512:(r + 1) * 512],
                        in_=ps["ptr"][:, 0:512]), "sv")
                    self.w("gpsimd", "sv")
                    for i in range(4):
                        ii = r * 4 + i
                        self.o("gpsimd", lambda e, ii=ii, i=i, r=r, wn=wn, j=j: e.dma_start(
                            out=self.wt_dram[wn][ii * 128:(ii + 1) * 128,
                                                 j * 128:(j + 1) * 128],
                            in_=sb["wtb0"][:, r * 512 + i * 128: r * 512 + (i + 1) * 128]),
                            "sg", 16)
        self.w("sync", "sv", self.prep_guard.get("swa", 0))
        self.o("sync", lambda e: e.dma_start(out=sb["xld0"][0:32, :], in_=self.wab_hbm),
               "swa", 16)
        self.w("vector", "swa")
        self.w("vector", "sg")
        self.o("vector", lambda e: e.tensor_copy(
            out=sb["wtb1"][0:32, 0:HID], in_=sb["xld0"][0:32, :]), "sv")
        self.w("tensor", "sv")
        self.w("tensor", "sg")
        for i in range(8):
            self.o("tensor", lambda e, i=i: e.transpose(
                ps["ptr"][:, (i % 4) * 32:(i % 4) * 32 + 32],
                sb["wtb1"][0:32, i * 128:i * 128 + 128],
                sb["cstb"][0:32, CI_I128:CI_I128 + 32]),
                "stt" if i % 4 == 3 else None)
            if i % 4 == 3:
                self.w("vector", "stt")
                r = i // 4
                self.o("vector", lambda e, r=r: e.tensor_copy(
                    out=sb["wabt"][:, r * 128:(r + 1) * 128], in_=ps["ptr"][:, 0:128]),
                    "sv")
                if r == 0:
                    self.w("tensor", "sv")

    def mm_proj_start(self, wn, evac, src=None, tokmajor=False, src_tok=None):
        st = {"wn": wn, "evac": evac, "tokmajor": tokmajor,
              "srct": self.sb["xnf"] if src is None else self.sb[src],
              "wt": self.sb["wtb0"] if self.wslot[wn] == 0 else self.sb["wtb1"],
              "ev_tok": {0: None, 1: None}, "first": True,
              "src_tok": src_tok}
        return st

    def mm_proj_step(self, st, m):
        sb, ps = self.sb, self.ps
        wn, evac, srct, wt = st["wn"], st["evac"], st["srct"], st["wt"]
        tokmajor, ev_tok = st["tokmajor"], st["ev_tok"]
        if st["first"]:
            st["first"] = False
            self.w("tensor", self.wtok[wn][0], self.wtok[wn][1])
            if st["src_tok"] is None:
                self.w("tensor", "sv")
                self.w("tensor", "ss")
                self.w("tensor", "sp")
            else:
                for sem, tok in st["src_tok"]:
                    self.w("tensor", sem, tok)
        bank = ps["pp0"] if m % 2 == 0 else ps["pp1"]
        if ev_tok[m % 2] is not None:
            self.w("tensor", ev_tok[m % 2][0], ev_tok[m % 2][1])
        for k in range(8):
            if tokmajor:
                tb, oh = m // 2, m % 2
                self.o("tensor", lambda e, k=k, b=bank, wt=wt, s=srct, tb=tb,
                       oh=oh: e.matmul(
                    b[:, 0:TILE],
                    s[:, k * TILE + tb * 128: k * TILE + (tb + 1) * 128],
                    wt[:, k * HID + oh * 512: k * HID + (oh + 1) * 512],
                    start=(k == 0), stop=(k == 7)), "stt" if k == 7 else None)
            else:
                self.o("tensor", lambda e, m=m, k=k, b=bank, wt=wt, s=srct: e.matmul(
                    b[:, 0:TILE], wt[:, k * HID + m * 128: k * HID + m * 128 + 128],
                    s[:, k * TILE:(k + 1) * TILE],
                    start=(k == 0), stop=(k == 7)), "stt" if k == 7 else None)
        ev_tok[m % 2] = evac(m, bank)
        if m == 7:
            self.slot_free[self.wslot[wn]] = self.tok("stt")

    def mm_proj(self, wn, evac, src=None, tokmajor=False):
        st = self.mm_proj_start(wn, evac, src, tokmajor)
        for m in range(8):
            self.mm_proj_step(st, m)
        return


    def ev_copy(self, dst, off=0):
        sb = self.sb
        def f(m, bank):
            self.w("scalar", "stt")
            self.o("scalar", lambda e, m=m, b=bank: e.activation(
                out=sb[dst][:, off + m * TILE:off + (m + 1) * TILE],
                in_=b[:, 0:TILE], func=AF.Copy), "ss")
            return ("ss", self.tok("ss"))
        return f

    def ph_tile(self, it):
        sb, ps = self.sb, self.ps
        ntb = self.ntb
        tt = it % ntb
        t0 = it * TILE
        first_in_batch = (tt == 0)

        if first_in_batch and it > 0:
            self.w("vector", "stt")
            self.o("vector", lambda e: e.memset(sb["mst"][:, :], 0.0), "sv")
            self.o("gpsimd", lambda e: e.memset(sb["xbnd"][:, :], 0.0), "sp")

        if self.stop_after == "prep":
            return
        for st in range(4):
            buf = sb["xld0"] if st % 2 == 0 else sb["xld1"]
            self.w("gpsimd", "sv", getattr(self, "xld_free", {}).get(st % 2, 0))
            xsem = "sxa" if st % 2 == 0 else "sxb"
            self.o("gpsimd", lambda e, b=buf, st=st, t0=t0: e.dma_start(
                out=b[:, :], in_=self.x_hbm[t0 + st * 128: t0 + (st + 1) * 128, :]),
                xsem, 16)
            self.w("vector", xsem)
            self.w("scalar", xsem)
            self.o("vector", lambda e, b=buf: e.reduce_sum(
                out=sb["stat"][:, 0:1], in_=b[:, :], axis=AX.X), "sv")
            self.w("scalar", "sv")
            self.o("scalar", lambda e, b=buf: e.activation(
                out=sb["xsq"][:, :], in_=b[:, :], func=AF.Square,
                accum_out=sb["stat"][:, 1:2]), "ss")
            self.w("vector", "ss")
            self.o("vector", lambda e: e.tensor_scalar(
                out=sb["stat"][:, 2:3], in0=sb["stat"][:, 0:1],
                scalar1=1.0 / HID, scalar2=None, op0=ALU.mult), "sv")
            self.o("vector", lambda e: e.tensor_scalar(
                out=sb["stat"][:, 4:5], in0=sb["stat"][:, 1:2],
                scalar1=1.0 / HID, scalar2=None, op0=ALU.mult), "sv")
            self.o("vector", lambda e: e.scalar_tensor_tensor(
                out=sb["stat"][:, 5:6], in0=sb["stat"][:, 2:3],
                scalar=sb["stat"][:, 2:3], in1=sb["stat"][:, 4:5],
                op0=ALU.mult, op1=ALU.subtract), "sv")
            self.w("scalar", "sv")
            self.o("scalar", lambda e: e.activation(
                out=sb["stat"][:, 6:7], in_=sb["stat"][:, 5:6], func=AF.Ln,
                scale=-1.0, bias=EPS_LN), "ss")
            self.o("scalar", lambda e: e.activation(
                out=sb["stat"][:, 7:8], in_=sb["stat"][:, 6:7], func=AF.Exp,
                scale=-0.5), "ss")
            self.w("vector", "ss")
            self.o("vector", lambda e, b=buf: e.tensor_scalar(
                out=sb["xntm"][:, :], in0=b[:, :], scalar1=sb["stat"][:, 2:3],
                scalar2=sb["stat"][:, 7:8], op0=ALU.subtract, op1=ALU.mult), "sv")
            if not hasattr(self, "xld_free"):
                self.xld_free = {}
            self.xld_free[st % 2] = self.tok("sv")
            self.w("tensor", "sv")
            for r in range(2):
                if r == 1:
                    self.w("tensor", "sv")
                for i in range(4):
                    k = r * 4 + i
                    self.o("tensor", lambda e, k=k, i=i: e.transpose(
                        ps["ptr"][:, i * 128:(i + 1) * 128],
                        sb["xntm"][:, k * 128:(k + 1) * 128],
                        sb["cstb"][:, CI_I128:CI_I128 + 128]),
                        "stt" if i == 3 else None)
                self.w("vector", "stt")
                self.o("vector", lambda e, r=r, st=st: e.tensor_copy(
                    out=ap(sb["xnf"], 0, 128, r * 4 * TILE + st * 128,
                           [[TILE, 4], [1, 128]]),
                    in_=ps["ptr"][:, 0:512]), "sv")

        if self.stop_after == "ln":
            return
        self.load_w("Wk")
        self.mm_proj("Wq", self.ev_copy("qk", 0))
        self.load_w("Wv")
        self.mm_proj("Wk", self.ev_copy("qk", 8 * TILE))
        self.load_w("Wr")

        def ev_vtm(m, bank):
            tb, oh = m // 2, m % 2
            self.w("scalar", "stt")
            self.o("scalar", lambda e, b=bank, tb=tb, oh=oh: e.activation(
                out=sb["vtm"][:, tb * HID + oh * 512: tb * HID + (oh + 1) * 512],
                in_=b[:, 0:TILE], func=AF.Copy), "ss")
            return ("ss", self.tok("ss"))
        self.mm_proj("Wv", ev_vtm, tokmajor=True)
        self.qk_tok = self.tok("ss")

        if self.stop_after == "proj":
            return
        self.ph_scales(it)

        if self.stop_after == "scales":
            return

        self.ph_ffn_mix()

        self.projq = []
        if self.stop_after is None:
            self.build_projq(it)

        for a in range(NPAIR):
            self.ph_pair(it, a)
        while self.projq:
            self.projq.pop(0)()
        if self.stop_after in ("pskv", "pskv1", "pskvnoev", "pskv1noev",
                               "builds", "solve", "post", "pairs", "ffn"):
            return

        self.ph_gate_wo(it)

        self.ph_epilogue(it)

        if self.debug and it == self.ntile - 1:
            self.ph_debug_dump()

    def ph_scales(self, it):
        sb, ps = self.sb, self.ps
        self.w("tensor", "sv")
        for k in range(8):
            self.o("tensor", lambda e, k=k: e.matmul(
                ps["pscl"][0:16, 0:TILE], sb["wabt"][:, k * 32:k * 32 + 16],
                sb["xnf"][:, k * TILE:(k + 1) * TILE],
                start=(k == 0), stop=(k == 7)))
            self.o("tensor", lambda e, k=k: e.matmul(
                ps["psq"][0:16, 0:TILE], sb["wabt"][:, k * 32 + 16:k * 32 + 32],
                sb["xnf"][:, k * TILE:(k + 1) * TILE],
                start=(k == 0), stop=(k == 7)), "stt" if k == 7 else None)
        self.w("scalar", "stt")
        self.o("scalar", lambda e: e.activation(
            out=sb["zsig"][:, 0:TILE], in_=ps["pscl"][0:16, 0:TILE], func=AF.Sigmoid,
            bias=sb["babs"][:, 0:1]), "ss")
        self.o("scalar", lambda e: e.activation(
            out=sb["zsig"][:, TILE:2 * TILE], in_=ps["psq"][0:16, 0:TILE], func=AF.Sigmoid,
            bias=sb["babs"][:, 1:2]), "ss")
        self.w("vector", "ss")
        self.o("vector", lambda e: e.tensor_scalar(
            out=sb["t16a"][:, :], in0=sb["zsig"][0:16, 0:TILE], scalar1=1e-35,
            scalar2=None, op0=ALU.max), "sv")
        self.w("scalar", "sv")
        self.o("scalar", lambda e: e.activation(
            out=sb["la"][:, :], in_=sb["t16a"][:, :], func=AF.Ln), "ss")
        self.w("vector", "ss")
        for c in range(NCH):
            self.o("vector", lambda e, c=c: e.tensor_tensor_scan(
                out=sb["lc"][:, c * C:(c + 1) * C],
                data0=sb["la"][:, c * C:(c + 1) * C],
                data1=sb["la"][:, c * C:(c + 1) * C], initial=0.0,
                op0=ALU.add, op1=ALU.bypass), "sv")
        self.w("scalar", "sv")
        self.o("scalar", lambda e: e.activation(
            out=sb["gamv"][:, :], in_=sb["lc"][:, :], func=AF.Exp), "ss")
        self.w("vector", "ss")
        self.o("vector", lambda e: e.tensor_scalar(
            out=sb["t16a"][:, :], in0=sb["lc"][:, :], scalar1=-1.0, scalar2=CLAMP,
            op0=ALU.mult, op1=ALU.min), "sv")
        self.w("scalar", "sv")
        self.o("scalar", lambda e: e.activation(
            out=sb["uv"][:, :], in_=sb["t16a"][:, :], func=AF.Exp), "ss")

        self.w("vector", "ss", self.qk_tok)
        for (nm, soff) in [("nqv", 0), ("nkv", 8 * TILE)]:
            for k in range(8):
                self.w("vector", "stt")
                self.o("vector", lambda e, k=k, soff=soff: e.tensor_tensor(
                    out=sb["sq"][:, :],
                    in0=sb["qk"][:, soff + k * TILE:soff + (k + 1) * TILE],
                    in1=sb["qk"][:, soff + k * TILE:soff + (k + 1) * TILE],
                    op=ALU.mult), "sv")
                self.w("tensor", "sv")
                self.o("tensor", lambda e, k=k: e.matmul(
                    ps["psq"][0:16, 0:TILE],
                    sb["cstb"][:, CI_IND + 16 * k: CI_IND + 16 * (k + 1)],
                    sb["sq"][:, :], start=(k == 0), stop=(k == 7)), "stt")
            self.w("vector", "stt")
            self.w("vector", "ss")
            self.o("vector", lambda e: e.tensor_scalar(
                out=sb["t16a"][:, :], in0=ps["psq"][0:16, 0:TILE],
                scalar1=1e-12, scalar2=None, op0=ALU.max), "sv")
            self.w("scalar", "sv")
            self.o("scalar", lambda e: e.activation(
                out=sb["t16b"][:, :], in_=sb["t16a"][:, :], func=AF.Ln), "ss")
            self.o("scalar", lambda e, nm=nm: e.activation(
                out=sb[nm][:, :], in_=sb["t16b"][:, :], func=AF.Exp, scale=-0.5), "ss")

        self.w("vector", "ss")
        self.o("vector", lambda e: e.tensor_tensor(
            out=sb["t16a"][:, :], in0=sb["nkv"][:, :], in1=sb["nkv"][:, :],
            op=ALU.mult), "sv")
        self.o("vector", lambda e: e.tensor_tensor(
            out=sb["t16a"][:, :], in0=sb["t16a"][:, :], in1=sb["uv"][:, :],
            op=ALU.mult), "sv")
        self.o("vector", lambda e: e.tensor_tensor(
            out=sb["t16a"][:, :], in0=sb["t16a"][:, :], in1=sb["gamv"][:, :],
            op=ALU.mult), "sv")
        self.o("vector", lambda e: e.tensor_tensor(
            out=sb["chiv"][:, :], in0=sb["t16a"][:, :], in1=sb["zsig"][0:16, TILE:2 * TILE],
            op=ALU.mult), "sv")
        self.o("vector", lambda e: e.tensor_tensor(
            out=sb["t16b"][:, :], in0=sb["nkv"][:, :], in1=sb["gamv"][:, :],
            op=ALU.mult), "sv")
        self.o("vector", lambda e: e.reciprocal(
            out=sb["zetv"][:, :], in_=sb["t16b"][:, :]), "sv")
        self.o("vector", lambda e: e.tensor_tensor(
            out=sb["omv"][:, :], in0=sb["nqv"][:, :], in1=sb["gamv"][:, :],
            op=ALU.mult), "sv")

        self.w("tensor", "sv")
        for (vn, tn) in [("chiv", "chiT"), ("zetv", "zetT"), ("omv", "omT")]:
            for blk in range(4):
                self.o("tensor", lambda e, vn=vn, blk=blk: e.transpose(
                    ps["pscl"][:, 256 + blk * 16: 256 + (blk + 1) * 16],
                    sb[vn][:, blk * 128:(blk + 1) * 128],
                    sb["cstf"][0:16, CI_I128:CI_I128 + 16]),
                    "stt" if blk == 3 else None)
            self.w("vector", "stt")
            self.o("vector", lambda e, tn=tn: e.tensor_copy(
                out=sb[tn][:, :], in_=ps["pscl"][:, 256:256 + 64]), "sv")
            self.w("tensor", "sv")

        self.w("gpsimd", "sv")
        self.o("gpsimd", lambda e: e.dma_start(
            out=sb["gamrow"][0:1, 0:128],
            in_=ap(sb["gamv"], 0, 16, C - 1, [[C, NCH]]),
            allow_slow_non_contiguous=True), "sgr", 16)
        self.w("tensor", "sgr")
        self.o("tensor", lambda e: e.matmul(
            ps["pscl"][:, 384:384 + 128], sb["cstf"][0:1, CI_ONES:CI_ONES + 128],
            sb["gamrow"][0:1, 0:128], start=True, stop=True), "stt")
        self.w("vector", "stt")
        self.o("vector", lambda e: e.tensor_copy(
            out=sb["gamrep"][:, :], in_=ps["pscl"][:, 384:384 + 128]), "sv")

    def pq_run(self, n):
        for _ in range(n):
            if self.projq:
                self.projq.pop(0)()

    def ph_pair(self, it, a):
        sb, ps = self.sb, self.ps
        if not hasattr(self, "projq"):
            self.projq = []

        def oin_gather(par, pc):
            return ap(self.sb["oin"], pc, 64, par * 64, [[512, 2], [128, 4], [1, 64]])

        def um_gather(par, phh, cc):
            return ap(self.sb["um"], phh, 64, cc * 512, [[64, 8], [1, 64]])
        cstb = self.sb["cstb"]
        c0 = 2 * a
        if not hasattr(self, "btok"):
            self.btok = {}
        btok = self.btok

        def bank_wait(bn):
            t = btok.get(bn)
            if t is not None:
                self.w("tensor", t[0], t[1])

        def chT(h):
            return sb["chiT"][:, a * 16 + h: a * 16 + h + 1]

        def zeT(h):
            return sb["zetT"][:, a * 16 + h: a * 16 + h + 1]

        POOL = [("ppsk", "psl1"), ("psq", "psl0")]

        nrounds = 1 if self.stop_after in ("pskv1", "pskv1noev") else 8
        noev = self.stop_after in ("pskvnoev", "pskv1noev")
        for rnd in range(nrounds):
            be, bo = POOL[rnd % 2]
            bank_wait(be)
            bank_wait(bo)
            for cc in (0, 1):
                pc = 64 * cc
                for si in range(2):
                    h = 2 * rnd + si
                    ph = 64 * (h % 2)
                    bank = ps[be] if h % 2 == 0 else ps[bo]
                    cl = (h // 2) * TILE + (c0 + cc) * C
                    self.o("tensor", lambda e, ph=ph, cl=cl, pc=pc, b=bank: e.matmul(
                        b[pc:pc + 64, 0:128],
                        sb["qk"][ph:ph + 64, 8 * TILE + cl:8 * TILE + cl + 64],
                        ap(sb["qk"], ph, 64, cl, [[8 * TILE, 2], [1, 64]]),
                        start=True, stop=True, tile_position=(ph, pc)))
                    self.o("tensor", lambda e, ph=ph, cl=cl, pc=pc, b=bank: e.matmul(
                        b[pc:pc + 64, 128:192],
                        sb["qk"][ph:ph + 64, 8 * TILE + cl:8 * TILE + cl + 64],
                        cstb[ph:ph + 64, CI_ISTK:CI_ISTK + 64],
                        start=True, stop=True, tile_position=(ph, pc)),
                        "stt" if (si == 1 and cc == 1) else None)
            if noev:
                continue
            mmtok = self.tok("stt")

            def dve_evacs(h, bn):
                self.o("vector", lambda e, h=h, b=ps[bn]: e.scalar_tensor_tensor(
                    out=ap(sb["csk"], 0, 128, h * 192, [[1, 192]]),
                    in0=ap(b, 0, 128, 0, [[1, 192]]),
                    scalar=chT(h),
                    in1=ap(sb["cstf"], 0, 128, CI_M3B, [[1, 192]]),
                    op0=ALU.mult, op1=ALU.mult), "sv")
                self.o("vector", lambda e, h=h, b=ps[bn]: e.tensor_tensor(
                    out=ap(sb["susl"], 0, 128, h * 64, [[1, 64]]),
                    in0=ap(b, 0, 128, 64, [[1, 64]]),
                    in1=ap(sb["cstf"], 0, 128, CI_MSL, [[1, 64]]),
                    op=ALU.mult), "sv")

            def sc_evacs(h, bn):
                self.o("scalar", lambda e, h=h, a=a: e.activation(
                    out=ap(sb["yA"], 0, 128, h * 128, [[1, 64]]),
                    in_=ap(sb["vtm"], 0, 128, a * HID + h * 64, [[1, 64]]),
                    func=AF.Copy, scale=zeT(h)), "ss")
                self.o("scalar", lambda e, h=h, b=ps[bn]: e.activation(
                    out=ap(sb["yA"], 0, 128, h * 128 + 64, [[1, 64]]),
                    in_=ap(b, 0, 128, 128, [[1, 64]]),
                    func=AF.Copy, scale=-1.0), "ss")
                self.o("scalar", lambda e, h=h, b=ps[bn]: e.activation(
                    out=ap(sb["ktm"], 0, 128, h * 64, [[1, 64]]),
                    in_=ap(b, 0, 128, 128, [[1, 64]]),
                    func=AF.Copy), "ss")

            he, ho = 2 * rnd, 2 * rnd + 1
            self.w("vector", "stt", mmtok)
            self.w("scalar", "stt", mmtok)
            dve_evacs(he, be)
            sc_evacs(ho, bo)
            self.w("vector", "ss")
            self.w("scalar", "sv")
            dve_evacs(ho, bo)
            sc_evacs(he, be)
            btok[be] = ("ss", self.tok("ss"))
            btok[bo] = ("sv", self.tok("sv"))
        if self.stop_after in ("pskv", "pskv1", "pskvnoev", "pskv1noev"):
            return
        self.pq_run(2)

        def sl_slice(h, pc):
            return ap(sb["susl"], pc, 64, h * 64, [[1, 64]])

        def sux_slice(h, pc):
            return ap(sb["csk"], pc, 64, h * 192 + 64, [[1, 64]])

        def pwx_slice(l, h, pc):
            return ap(sb["pwx"], pc, 64, l * H * 64 + h * 64, [[1, 64]])

        def pwt_slice(l, h, pc):
            return ap(sb["pwt"], pc, 64, (l % 2) * H * 64 + h * 64, [[1, 64]])

        self.w("tensor", "sv")
        self.w("tensor", "ss")
        wv = 0
        for L in range(5):
            for part in range(2):
                if L == 4 and part == 1:
                    break
                be, bo = POOL[wv % 2]
                wv += 1
                bank_wait(be)
                bank_wait(bo)
                if L > 0 and part == 0:
                    self.lvl_tok = (self.tok("sv"), self.tok("ss"))
                if L > 0:
                    self.w("tensor", "sv", self.lvl_tok[0])
                    self.w("tensor", "ss", self.lvl_tok[1])
                for cc in (0, 1):
                    pc = 64 * cc
                    for h in range(H):
                        bank = ps[be] if h < 8 else ps[bo]
                        colb = (h % 8) * 64
                        if L == 0:
                            stat = sl_slice(h, pc) if part == 0 else sux_slice(h, pc)
                            mv = sux_slice(h, pc) if part == 0 else sl_slice(h, pc)
                        else:
                            stat = pwt_slice(L - 1, h, pc) if part == 0 else pwx_slice(L - 1, h, pc)
                            mv = pwx_slice(L - 1, h, pc) if part == 0 else pwt_slice(L - 1, h, pc)
                        self.o("tensor", lambda e, st=stat, mv=mv, pc=pc, colb=colb,
                               bank=bank: e.matmul(
                            bank[pc:pc + 64, colb:colb + 64], st, mv,
                            start=True, stop=True, tile_position=(pc, pc)),
                            "stt" if (h == H - 1 and cc == 1) else None)
                self.w("vector", "stt")
                self.w("scalar", "stt")
                for bi, bn in enumerate((be, bo)):
                    if part == 0:
                        self.o("vector", lambda e, L=L, bi=bi, bn=bn: e.tensor_tensor(
                            out=ap(sb["pwx"], 0, 128, L * H * 64 + bi * 512, [[1, 512]]),
                            in0=ap(ps[bn], 0, 128, 0, [[1, 512]]),
                            in1=ap(sb["chiT"], 0, 128, a * 16 + bi * 8, [[1, 8], [0, 64]]),
                            op=ALU.mult), "sv")
                    else:
                        self.o("scalar", lambda e, L=L, bi=bi, bn=bn: e.activation(
                            out=ap(sb["pwt"], 0, 128, (L % 2) * H * 64 + bi * 512,
                                   [[1, 512]]),
                            in_=ap(ps[bn], 0, 128, 0, [[1, 512]]),
                            func=AF.Copy), "ss")
                    btok[bn] = ("sv", self.tok("sv")) if part == 0 else \
                        ("ss", self.tok("ss"))
        if self.stop_after == "builds":
            return
        self.pq_run(2)

        stages = [(4, ALU.add), (3, ALU.add), (2, ALU.add), (1, ALU.add),
                  (0, ALU.add), (-1, ALU.subtract)]
        cur, nxt = "yA", "yB"
        self.w("tensor", "ss")
        bi = 0
        qtok = {q: self.tok("sv") for q in range(4)}
        for sidx, (L, op) in enumerate(stages):
            for quad in range(4):
                bn = [b for pr in POOL for b in pr][bi % 4]
                bi += 1
                bank_wait(bn)
                self.w("tensor", "sv", qtok[quad])
                bank = ps[bn]
                for hh in range(4):
                    h = quad * 4 + hh
                    for cc in (0, 1):
                        pc = 64 * cc
                        stat = sux_slice(h, pc) if L < 0 else pwx_slice(L, h, pc)
                        self.o("tensor", lambda e, st=stat, h=h, hh=hh, pc=pc,
                               bank=bank, cur=cur: e.matmul(
                            bank[pc:pc + 64, hh * 128:(hh + 1) * 128], st,
                            ap(sb[cur], pc, 64, h * 128, [[1, 128]]),
                            start=True, stop=True, tile_position=(pc, pc)),
                            "stt" if (hh == 3 and cc == 1) else None)
                self.w("vector", "stt")
                if op == ALU.add:
                    self.o("vector", lambda e, quad=quad, bank=bank, cur=cur,
                           nxt=nxt: e.tensor_tensor(
                        out=ap(sb[nxt], 0, 128, quad * 512, [[1, 512]]),
                        in0=ap(bank, 0, 128, 0, [[1, 512]]),
                        in1=ap(sb[cur], 0, 128, quad * 512, [[1, 512]]),
                        op=ALU.add), "sv")
                else:
                    self.o("vector", lambda e, quad=quad, bank=bank, cur=cur,
                           nxt=nxt: e.tensor_tensor(
                        out=ap(sb[nxt], 0, 128, quad * 512, [[1, 512]]),
                        in0=ap(sb[cur], 0, 128, quad * 512, [[1, 512]]),
                        in1=ap(bank, 0, 128, 0, [[1, 512]]),
                        op=ALU.subtract), "sv")
                btok[bn] = ("sv", self.tok("sv"))
                qtok[quad] = self.tok("sv")
            cur, nxt = nxt, cur
        zf = cur

        self.o("vector", lambda e, zf=zf: e.tensor_tensor(
            out=ap(sb["zknx"], 0, 128, 0, [[64, 16], [1, 64]]),
            in0=ap(sb[zf], 0, 128, 64, [[128, 16], [1, 64]]),
            in1=ap(sb["chiT"], 0, 128, a * 16, [[1, 16], [0, 64]]),
            op=ALU.mult), "sv")
        zknx_tok = self.tok("sv")
        if self.stop_after == "solve":
            return
        self.pq_run(2)

        def bankcol(h):
            return (h // 2) * 64

        post_in_tok = self.tok("sv")

        def post_wave(mm_fn, evac_fn, need_tok):
            be, bo = POOL[self.pwave % 2]
            self.pwave += 1
            bank_wait(be)
            bank_wait(bo)
            self.w("tensor", "sv", need_tok)
            mm_fn(ps[be], ps[bo])
            self.w("vector", "stt")
            evac_fn(be, bo)
            btok[be] = ("sv", self.tok("sv"))
            btok[bo] = ("sv", self.tok("sv"))

        self.pwave = getattr(self, "pwave", 0)

        def mm_oin(bA, bB):
            for h in range(H):
                bk = bA if h < 8 else bB
                cb = (h % 8) * 64
                for cc in (0, 1):
                    pc = 64 * cc
                    self.o("tensor", lambda e, h=h, pc=pc, b=bk, cb=cb, zf=zf: e.matmul(
                        b[pc:pc + 64, cb:cb + 64],
                        ap(sb["csk"], pc, 64, h * 192, [[1, 64]]),
                        ap(sb[zf], pc, 64, h * 128, [[1, 64]]),
                        start=True, stop=True, tile_position=(pc, pc)),
                        "stt" if (h == H - 1 and cc == 1) else None)

        def ev_oin(be, bo):
            for g, bn in enumerate((be, bo)):
                self.o("vector", lambda e, g=g, bn=bn, a=a: e.tensor_tensor(
                    out=ap(sb["oin"], 0, 128, g * 512, [[1, 512]]),
                    in0=ap(ps[bn], 0, 128, 0, [[1, 512]]),
                    in1=ap(sb["omT"], 0, 128, a * 16 + g * 8, [[1, 8], [0, 64]]),
                    op=ALU.mult), "sv")
        post_wave(mm_oin, ev_oin, post_in_tok)

        def mm_qef(bA, bB):
            for h in range(H):
                ph = 64 * (h % 2)
                for cc in (0, 1):
                    pc = 64 * cc
                    b = bA if cc == 0 else bB
                    self.o("tensor", lambda e, h=h, ph=ph, pc=pc, b=b, zf=zf: e.matmul(
                        b[ph:ph + 64, bankcol(h):bankcol(h) + 64],
                        ap(sb[zf], pc, 64, h * 128 + 64, [[1, 64]]),
                        ap(sb["csk"], pc, 64, h * 192, [[1, 64]]),
                        start=True, stop=True, tile_position=(pc, ph)),
                        "stt" if (h == H - 1 and cc == 1) else None)

        def ev_qef(be, bo):
            for cc, bn in enumerate((be, bo)):
                self.o("vector", lambda e, cc=cc, bn=bn, c0=c0: e.tensor_tensor(
                    out=ap(sb["qef"], 0, 128, cc * 512, [[1, 512]]),
                    in0=ap(ps[bn], 0, 128, 0, [[1, 512]]),
                    in1=ap(sb["qk"], 0, 128, (c0 + cc) * C, [[TILE, 8], [1, 64]]),
                    op=ALU.add), "sv")
        post_wave(mm_qef, ev_qef, post_in_tok)

        def mm_gtm(bA, bB):
            for h in range(H):
                ph = 64 * (h % 2)
                for cc in (0, 1):
                    pc = 64 * cc
                    b = bA if cc == 0 else bB
                    self.o("tensor", lambda e, h=h, ph=ph, pc=pc, b=b: e.matmul(
                        b[ph:ph + 64, bankcol(h):bankcol(h) + 64],
                        ap(sb["zknx"], pc, 64, h * 64, [[1, 64]]),
                        ap(sb["ktm"], pc, 64, h * 64, [[1, 64]]),
                        start=True, stop=True, tile_position=(pc, ph)),
                        "stt" if (h == H - 1 and cc == 1) else None)

        def ev_gtm(be, bo):
            for cc, bn in enumerate((be, bo)):
                self.o("vector", lambda e, cc=cc, bn=bn: e.tensor_tensor(
                    out=ap(sb["gtm"], 0, 128, cc * 512, [[1, 512]]),
                    in0=ap(ps[bn], 0, 128, 0, [[1, 512]]),
                    in1=ap(sb["cstb"], 0, 128, CI_IREP8, [[1, 512]]),
                    op=ALU.add), "sv")
        post_wave(mm_gtm, ev_gtm, zknx_tok)

        def mm_um(bA, bB):
            for h in range(H):
                ph = 64 * (h % 2)
                for cc in (0, 1):
                    pc = 64 * cc
                    b = bA if cc == 0 else bB
                    self.o("tensor", lambda e, h=h, ph=ph, pc=pc, b=b, zf=zf: e.matmul(
                        b[ph:ph + 64, bankcol(h):bankcol(h) + 64],
                        ap(sb["csk"], pc, 64, h * 192 + 128, [[1, 64]]),
                        ap(sb[zf], pc, 64, h * 128, [[1, 64]]),
                        start=True, stop=True, tile_position=(pc, ph)),
                        "stt" if (h == H - 1 and cc == 1) else None)

        def ev_um(be, bo):
            for cc, bn in enumerate((be, bo)):
                for half in (0, 1):
                    self.o("vector", lambda e, cc=cc, half=half, bn=bn, c0=c0: e.tensor_tensor(
                        out=ap(sb["um"], 64 * half, 64, cc * 512, [[1, 512]]),
                        in0=ap(ps[bn], 64 * half, 64, 0, [[1, 512]]),
                        in1=ap(sb["gamrep"], 64 * half, 64, half * 8 + c0 + cc,
                               [[16, 8], [0, 64]]),
                        op=ALU.mult), "sv")
        post_wave(mm_um, ev_um, post_in_tok)
        if self.stop_after == "post":
            return
        self.pq_run(2)

        for cc in (0, 1):
            pc = 64 * cc
            self.w("tensor", "sv")
            for bn in ("psq", "ppsk", "psl0", "psl1"):
                bank_wait(bn)
            for h in range(H):
                ph = 64 * (h % 2)
                bank = ps["psq"] if h % 2 == 0 else ps["ppsk"]
                self.o("tensor", lambda e, h=h, ph=ph, pc=pc, cc=cc, bank=bank: e.matmul(
                    bank[pc:pc + 64, bankcol(h):bankcol(h) + 64],
                    ap(sb["qef"], ph, 64, cc * 512 + bankcol(h), [[1, 64]]),
                    ap(sb["mst"], ph, 64, h * 64, [[1, 64]]),
                    start=True, stop=True, tile_position=(ph, pc)),
                    "stt" if h == H - 1 else None)
            owave_tok = self.tok("stt")
            for h in range(H):
                ph = 64 * (h % 2)
                bank = ps["psl0"] if h % 2 == 0 else ps["psl1"]
                self.o("tensor", lambda e, h=h, ph=ph, cc=cc, bank=bank: e.matmul(
                    bank[ph:ph + 64, bankcol(h):bankcol(h) + 64],
                    ap(sb["gtm"], ph, 64, cc * 512 + bankcol(h), [[1, 64]]),
                    ap(sb["mst"], ph, 64, h * 64, [[1, 64]]),
                    start=True, stop=True, tile_position=(ph, ph)),
                    "stt" if h == H - 1 else None)
            self.w("vector", "stt", owave_tok)
            for par, bn in ((0, "psq"), (1, "ppsk")):
                self.o("vector", lambda e, par=par, bn=bn, pc=pc, a=a: e.tensor_tensor(
                    out=ap(sb["xsq"], pc, 64, 0, [[1, 512]]),
                    in0=ap(ps[bn], pc, 64, 0, [[1, 512]]),
                    in1=ap(sb["omT"], pc, 64, a * 16 + par, [[2, 8], [0, 64]]),
                    op=ALU.mult), "sv")
                self.o("vector", lambda e, par=par, pc=pc, a=a: e.tensor_tensor(
                    out=ap(sb["otm"], pc, 64, a * HID + par * 64, [[128, 8], [1, 64]]),
                    in0=ap(sb["xsq"], pc, 64, 0, [[1, 512]]),
                    in1=oin_gather(par, pc),
                    op=ALU.add), "sv")
            self.w("vector", "stt")
            for par, bn in ((0, "psl0"), (1, "psl1")):
                phh = 64 * par
                self.o("vector", lambda e, par=par, bn=bn, phh=phh, cc=cc, c0=c0:
                       e.tensor_tensor(
                    out=ap(sb["xsq"], phh, 64, 512, [[1, 512]]),
                    in0=ap(ps[bn], phh, 64, 0, [[1, 512]]),
                    in1=ap(sb["gamrep"], phh, 64, par * 8 + c0 + cc, [[16, 8], [0, 64]]),
                    op=ALU.mult), "sv")
                self.o("vector", lambda e, par=par, phh=phh, cc=cc: e.tensor_tensor(
                    out=ap(sb["mst"], phh, 64, par * 64, [[128, 8], [1, 64]]),
                    in0=ap(sb["xsq"], phh, 64, 512, [[1, 512]]),
                    in1=um_gather(par, phh, cc),
                    op=ALU.add), "sv")
            for bn in ("psq", "ppsk", "psl0", "psl1"):
                btok[bn] = ("sv", self.tok("sv"))

    def ph_ffn_mix(self):
        sb = self.sb
        self.w("gpsimd", "swa", 48)
        self.w("gpsimd", "stt")
        self.w("gpsimd", "sv")
        for m in range(8):
            self.o("gpsimd", lambda e, m=m: e.tensor_tensor(
                out=sb["sq"][:, 1:TILE], in0=sb["xnf"][:, m * TILE + 1:(m + 1) * TILE],
                in1=sb["xnf"][:, m * TILE:(m + 1) * TILE - 1], op=ALU.subtract), "sp")
            self.o("gpsimd", lambda e, m=m: e.tensor_tensor(
                out=sb["sq"][:, 0:1], in0=sb["xnf"][:, m * TILE:m * TILE + 1],
                in1=sb["xbnd"][:, m:m + 1], op=ALU.subtract), "sp")
            self.o("gpsimd", lambda e, m=m: e.tensor_copy(
                out=sb["ffb"][:, m * TILE + 1:(m + 1) * TILE],
                in_=sb["xnf"][:, m * TILE:(m + 1) * TILE - 1]), "sp")
            self.o("gpsimd", lambda e, m=m: e.tensor_copy(
                out=sb["ffb"][:, m * TILE:m * TILE + 1], in_=sb["xbnd"][:, m:m + 1]), "sp")
            for dst, col in (("ffa", 2 * m), ("gf", 2 * m + 1)):
                self.o("gpsimd", lambda e, m=m, dst=dst, col=col: e.tensor_tensor(
                    out=sb[dst][:, m * TILE:(m + 1) * TILE], in0=sb["sq"][:, :],
                    in1=ap(sb["mus"], 0, 128, col, [[0, TILE]]),
                    op=ALU.mult), "sp")
                self.o("gpsimd", lambda e, m=m, dst=dst: e.tensor_tensor(
                    out=sb[dst][:, m * TILE:(m + 1) * TILE],
                    in0=sb[dst][:, m * TILE:(m + 1) * TILE],
                    in1=sb["ffb"][:, m * TILE:(m + 1) * TILE],
                    op=ALU.add), "sp")
        self.o("gpsimd", lambda e: e.tensor_copy(
            out=sb["xbnd"][:, 0:8],
            in_=ap(sb["xnf"], 0, 128, TILE - 1, [[TILE, 8], [1, 1]])), "sp")
        self.mix_tok = self.tok("sp")

    def build_projq(self, it):
        sb = self.sb
        q = self.projq

        def ev_r(m, bank):
            self.w("scalar", "stt")
            self.o("scalar", lambda e, m=m, b=bank: e.activation(
                out=sb["ffb"][:, m * TILE:(m + 1) * TILE], in_=b[:, 0:TILE],
                func=AF.Sigmoid), "ss")
            return ("ss", self.tok("ss"))

        def ev_k(m, bank):
            self.w("scalar", "stt")
            self.o("scalar", lambda e, m=m, b=bank: e.activation(
                out=sb["ffa"][:, m * TILE:(m + 1) * TILE], in_=b[:, 0:TILE],
                func=AF.Relu), "ss")
            t = ("ss", self.tok("ss"))
            self.w("gpsimd", "ss")
            self.o("gpsimd", lambda e, m=m: e.tensor_tensor(
                out=sb["ffa"][:, m * TILE:(m + 1) * TILE],
                in0=sb["ffa"][:, m * TILE:(m + 1) * TILE],
                in1=sb["ffa"][:, m * TILE:(m + 1) * TILE], op=ALU.mult), "sp")
            return t

        def ev_v(m, bank):
            self.w("vector", "stt")
            self.w("vector", "ss")
            self.o("vector", lambda e, m=m, b=bank: e.tensor_tensor(
                out=sb["branch"][:, m * TILE:(m + 1) * TILE], in0=b[:, 0:TILE],
                in1=sb["ffb"][:, m * TILE:(m + 1) * TILE], op=ALU.mult), "sv")
            return ("sv", self.tok("sv"))

        def ev_g(m, bank):
            self.w("scalar", "stt")
            self.o("scalar", lambda e, m=m, b=bank: e.activation(
                out=sb["gf"][:, m * TILE:(m + 1) * TILE], in_=b[:, 0:TILE],
                func=AF.Silu), "ss")
            return ("ss", self.tok("ss"))

        mixt = [("sp", self.mix_tok), ("ss", self.tok("ss")), ("sv", self.tok("sv"))]
        plan = [("load", "Wkf"), ("proj", "Wr", ev_r, "ffa", mixt),
                ("load", "Wvf"), ("proj", "Wkf", ev_k, "gf", mixt),
                ("load", "Wg"), ("proj", "Wvf", ev_v, "ffa", None),
                ("load", "Wo"), ("proj", "Wg", ev_g, None, None)]
        if it < self.ntile - 1:
            plan.append(("load", "Wq"))
        for item in plan:
            if item[0] == "load":
                q.append(lambda wn=item[1]: self.load_w(wn))
            else:
                _, wn, ev, src, stok = item
                holder = {}

                def start(wn=wn, ev=ev, src=src, stok=stok, holder=holder):
                    holder["st"] = self.mm_proj_start(wn, ev, src=src, src_tok=stok)

                q.append(start)
                for m in range(8):
                    q.append(lambda m=m, holder=holder: self.mm_proj_step(holder["st"], m))

    def ph_gate_wo(self, it):
        sb, ps = self.sb, self.ps
        self.w("tensor", "sv")
        self.w("vector", "ss")
        for k in range(8):
            for blk in range(4):
                self.o("tensor", lambda e, k=k, blk=blk: e.transpose(
                    ps["ptr"][:, blk * 128:(blk + 1) * 128],
                    sb["otm"][:, blk * HID + k * 128: blk * HID + (k + 1) * 128],
                    sb["cstb"][:, CI_I128:CI_I128 + 128]),
                    "stt" if blk == 3 else None)
            self.w("vector", "stt")
            self.o("vector", lambda e, k=k: e.tensor_tensor(
                out=sb["ffb"][:, k * TILE:(k + 1) * TILE], in0=ps["ptr"][:, 0:512],
                in1=sb["gf"][:, k * TILE:(k + 1) * TILE], op=ALU.mult), "sv")
            self.w("tensor", "sv")

        def ev_o(m, bank):
            self.w("vector", "stt")
            self.o("vector", lambda e, m=m, b=bank: e.tensor_tensor(
                out=sb["branch"][:, m * TILE:(m + 1) * TILE], in0=b[:, 0:TILE],
                in1=sb["branch"][:, m * TILE:(m + 1) * TILE], op=ALU.add), "sv")
            return ("sv", self.tok("sv"))
        self.mm_proj("Wo", ev_o, src="ffb")

    def ph_epilogue(self, it):
        sb, ps = self.sb, self.ps
        t0 = it * TILE
        self.w("tensor", "sp")
        for st in range(4):
            buf = sb["xld0"] if st % 2 == 0 else sb["xld1"]
            xsem = "sxa" if st % 2 == 0 else "sxb"
            self.w("gpsimd", "sv", self.xld_free.get(st % 2, 0))
            self.o("gpsimd", lambda e, b=buf, st=st, t0=t0: e.dma_start(
                out=b[:, :], in_=self.x_hbm[t0 + st * 128: t0 + (st + 1) * 128, :]),
                xsem, 16)
            self.w("tensor", "sv")
            for r in range(2):
                if r == 1:
                    self.w("tensor", "sv")
                for i in range(4):
                    k = r * 4 + i
                    self.o("tensor", lambda e, k=k, i=i, st=st: e.transpose(
                        ps["ptr"][:, i * 128:(i + 1) * 128],
                        ap(sb["branch"], 0, 128, k * TILE + st * 128, [[1, 128]]),
                        sb["cstb"][:, CI_I128:CI_I128 + 128]),
                        "stt" if i == 3 else None)
                self.w("vector", "stt")
                self.w("vector", xsem)
                if r == 0:
                    self.w("vector", "sdo", self.tok("sdo"))
                self.o("vector", lambda e, r=r, b=buf: e.tensor_tensor(
                    out=sb["outs"][:, r * 512:(r + 1) * 512], in0=ps["ptr"][:, 0:512],
                    in1=b[:, r * 512:(r + 1) * 512], op=ALU.add), "sv")
                self.w("tensor", "sv")
            self.w("gpsimd", "sv")
            self.o("gpsimd", lambda e, st=st, t0=t0: e.dma_start(
                out=self.out_hbm[t0 + st * 128: t0 + (st + 1) * 128, :],
                in_=sb["outs"][:, :]), "sdo", 16)
            self.xld_free[st % 2] = self.tok("sv")

    def ph_debug_dump(self):
        sb = self.sb
        self.w("gpsimd", "sv")
        self.w("gpsimd", "ss")
        for n, (src, dst) in self.dbg.items():
            self.o("gpsimd", lambda e, s=src, d=dst: e.dma_start(
                out=d, in_=sb[s][:, :]), "sdo", 16)

    def ph_finish(self):
        self.w("gpsimd", "sdo")


def _prep_inputs(core, x, ins):
    xs = np.ascontiguousarray(x[4 * core: 4 * core + 4].reshape(-1, HID))
    d = {"x": xs}
    d["W_Wq"] = ins["L0_attn_Wq"]
    d["W_Wk"] = ins["L0_attn_Wk"]
    d["W_Wv"] = ins["L0_attn_Wv"]
    d["W_Wg"] = ins["L0_attn_Wg"]
    d["W_Wo"] = ins["L0_attn_Wo"]
    d["W_Wr"] = ins["L0_ffn_W_r"]
    d["W_Wkf"] = ins["L0_ffn_W_k"]
    d["W_Wvf"] = ins["L0_ffn_W_v"]
    d["Wab"] = np.concatenate([ins["L0_attn_Wa"], ins["L0_attn_Wb"]], axis=0)
    bab = np.zeros((16, 2), dtype=np.float32)
    bab[:, 0] = ins["L0_attn_ba"]
    bab[:, 1] = ins["L0_attn_bb"]
    d["bab"] = bab
    mu = np.zeros((128, 16), dtype=np.float32)
    for k in range(8):
        mu[:, 2 * k] = ins["L0_ffn_mu_r"][k * 128:(k + 1) * 128]
        mu[:, 2 * k + 1] = ins["L0_ffn_mu_k"][k * 128:(k + 1) * 128]
    d["mu"] = mu
    d["consts"] = _host_consts()
    return {k: np.ascontiguousarray(v) for k, v in d.items()}


def kernel(**inputs):
    x = inputs["L0_attn_x"]
    B = x.shape[0]
    p = Prog(nb=B // NCORES, t_len=x.shape[1])
    nc = p.build()
    in_maps = [_prep_inputs(c, x, inputs) for c in range(NCORES)]
    res = run_bass_kernel_spmd(nc, in_maps, core_ids=list(range(NCORES)))
    out = np.concatenate(
        [r["out"].reshape(B // NCORES, x.shape[1], HID) for r in res.results], axis=0)
    return out.astype(np.float32)


# version 14
# speedup vs baseline: 1.4995x; 1.1518x over previous
import numpy as np
import ml_dtypes
import concourse.bass as bass
import concourse.mybir as mybir
from concourse.bass_utils import run_bass_kernel_spmd

F32 = mybir.dt.float32
BF16 = mybir.dt.bfloat16
AF = mybir.ActivationFunctionType
ALU = mybir.AluOpType
AX = mybir.AxisListType

NCORES = 8
HID = 1024
H = 16
DH = 64
C = 64
TILE = 512
NCH = TILE // C
NPAIR = NCH // 2
EPS_LN = 1e-5
CLAMP = 60.0
BIG_W = ["Wq", "Wk", "Wv", "Wg", "Wo", "Wr", "Wkf", "Wvf"]

CI_I128 = 0
CI_ISTK = 128
CI_MPM = 192
CI_MSU = 256
CI_MSL = 320
CI_IND = 384
CI_ONES = 512
CI_M3B = 640
CI_SCAL = 832
CI_IREP8 = 848
NCONST = 1360


def _host_consts():
    c = np.zeros((128, NCONST), dtype=np.float32)
    c[np.arange(128), CI_I128 + np.arange(128)] = 1.0
    c[np.arange(128), CI_ISTK + np.tile(np.arange(64), 2)] = 1.0
    s = np.tile(np.arange(64), 2)[:, None]
    t = np.arange(64)[None, :]
    c[:, CI_MPM:CI_MPM + 64] = (s <= t)
    c[:, CI_MSU:CI_MSU + 64] = (s < t)
    c[:, CI_MSL:CI_MSL + 64] = (t < s)
    for k in range(8):
        c[0:64, CI_IND + 16 * k + 2 * k] = 1.0
        c[64:128, CI_IND + 16 * k + 2 * k + 1] = 1.0
    c[:, CI_ONES:CI_ONES + 128] = 1.0
    c[:, CI_M3B:CI_M3B + 64] = (s <= t)
    c[:, CI_M3B + 64:CI_M3B + 128] = (s < t)
    c[:, CI_M3B + 128:CI_M3B + 192] = 1.0
    c[:, CI_SCAL] = EPS_LN
    for a in range(8):
        c[np.arange(128), CI_IREP8 + 64 * a + (np.arange(128) % 64)] = 1.0
    return c


def ap(t, pbase, pcount, off, dims):
    rowlen = t.shape[1]
    return bass.AP(t, pbase * rowlen + off, [[rowlen, pcount]] + list(dims))


class Prog:
    def __init__(self, nb, t_len, debug=False, stop_after=None):
        self.stop_after = stop_after
        self.nb, self.T, self.debug = nb, t_len, debug
        self.ntb = t_len // TILE
        self.ntile = nb * self.ntb
        self.nc = bass.Bass("TRN2", target_bir_lowering=False)
        self.ctx, self.cnt, self.sems, self.sb, self.ps = [], {}, {}, {}, {}
        self.last_wait = {}
        self.prog = {"sync": [], "vector": [], "scalar": [], "tensor": [], "gpsimd": []}

    def sem(self, n):
        s = self.nc.semaphore(n)
        self.sems[n] = s.__enter__()
        self.ctx.append(s)
        self.cnt[n] = 0

    def sbuf(self, n, shape, dt):
        t = self.nc.sbuf_tensor(n, shape, dt)
        self.sb[n] = t.__enter__()
        self.ctx.append(t)

    def psum(self, n, shape, dt=F32):
        t = self.nc.psum_tensor(n, shape, dt)
        self.ps[n] = t.__enter__()
        self.ctx.append(t)

    def w(self, eng, sem, val=None):
        v = self.cnt[sem] if val is None else val
        if v <= 0:
            return
        lw = self.last_wait
        if lw.get((eng, sem), 0) >= v:
            return
        lw[(eng, sem)] = v
        s = self.sems[sem]
        self.prog[eng].append(lambda e, s=s, v=v: e.wait_ge(s, v))

    def o(self, eng, build, sem=None, n=1):
        if self.debug and (eng in ("vector", "scalar")
                           or (eng == "gpsimd" and sem == "sp")):
            ssem = {"vector": "sv", "scalar": "ss", "gpsimd": "sp"}[eng]
            v = self.cnt[ssem]
            if v > 0:
                sh = self.sems[ssem]
                self.prog[eng].append(lambda e, sh=sh, v=v: e.wait_ge(sh, v))
        if sem is None:
            self.prog[eng].append(build)
        else:
            s = self.sems[sem]
            self.prog[eng].append(lambda e, s=s, b=build, n=n: b(e).then_inc(s, n))
            self.cnt[sem] += n

    def tok(self, sem):
        return self.cnt[sem]

    def build(self):
        nc = self.nc
        NTOK = self.nb * self.T
        self.x_hbm = nc.dram_tensor("x", [NTOK, HID], F32, kind="ExternalInput").ap()
        self.out_hbm = nc.dram_tensor("out", [NTOK, HID], F32, kind="ExternalOutput").ap()
        self.w_hbm = {n: nc.dram_tensor("W_" + n, [HID, HID], F32, kind="ExternalInput").ap()
                      for n in BIG_W}
        self.wab_hbm = nc.dram_tensor("Wab", [32, HID], F32, kind="ExternalInput").ap()
        self.bab_hbm = nc.dram_tensor("bab", [16, 2], F32, kind="ExternalInput").ap()
        self.mu_hbm = nc.dram_tensor("mu", [128, 16], F32, kind="ExternalInput").ap()
        self.cst_hbm = nc.dram_tensor("consts", [128, NCONST], F32, kind="ExternalInput").ap()
        self.wt_dram = {n: nc.dram_tensor("WT_" + n, [HID, HID], BF16).ap() for n in BIG_W}

        for s in ["swa", "swb", "sxa", "sxb", "sgr", "sdo", "sv", "ss", "stt", "sg",
                  "sp"]:
            self.sem(s)

        S = self.sbuf
        S("cstf", [128, NCONST], F32)
        self.nc.const_aps.register(F32, EPS_LN,
                                   self.sb["cstf"][:, CI_SCAL:CI_SCAL + 1])
        S("cstb", [128, NCONST], BF16)
        S("wtb0", [128, 8 * HID], BF16)
        S("wtb1", [128, 8 * HID], BF16)
        S("wabt", [128, 8 * 32], BF16)
        S("babs", [16, 2], F32)
        S("babn", [16, 2], F32)
        S("mus", [128, 16], F32)
        S("xld0", [128, HID], F32)
        S("xld1", [128, HID], F32)
        S("xsq", [128, HID], F32)
        S("xntm", [128, HID], BF16)
        S("xnf", [128, 8 * TILE], BF16)
        S("xbnd", [128, 8], BF16)
        S("qk", [128, 16 * TILE], BF16)
        S("vtm", [128, 4 * HID], BF16)
        S("gf", [128, 8 * TILE], BF16)
        S("sq", [128, TILE], BF16)
        S("stat", [128, 8], F32)
        S("zsig", [16, 2 * TILE], F32)
        for n in ["la", "lc", "gamv", "uv", "nqv", "nkv", "chiv", "zetv", "omv",
                  "t16a", "t16b"]:
            S(n, [16, TILE], F32)
        S("chiT", [128, 4 * 16], F32)
        S("zetT", [128, 4 * 16], F32)
        S("omT", [128, 4 * 16], F32)
        S("gamrow", [1, 128], F32)
        S("gamrep", [128, 128], F32)
        S("csk", [128, H * 192], BF16)
        S("susl", [128, H * 64], BF16)
        S("ktm", [128, H * 64], BF16)
        S("zknx", [128, H * 64], BF16)
        S("yA", [128, H * 128], BF16)
        S("yB", [128, H * 128], BF16)
        S("pwx", [128, 5 * H * 64], BF16)
        S("pwt", [128, 2 * H * 64], BF16)
        S("oin", [128, 2 * 512], BF16)
        S("qef", [128, 2 * 512], BF16)
        S("gtm", [128, 2 * 512], BF16)
        S("um", [128, 2 * 512], BF16)
        S("mst", [128, H * 64], BF16)
        S("otm", [128, 4 * HID], BF16)
        S("branch", [128, 8 * TILE], BF16)
        S("ffa", [128, 8 * TILE], BF16)
        S("ffb", [128, 8 * TILE], BF16)
        S("outs", [128, HID], F32)

        Pm = self.psum
        Pm("pp0", [128, 512])
        Pm("pp1", [128, 512])
        Pm("ppsk", [128, 512])
        Pm("psl0", [128, 512])
        Pm("psl1", [128, 512])
        Pm("psq", [128, 512])
        Pm("ptr", [128, 512], BF16)
        Pm("pscl", [128, 512])

        if self.debug:
            self.dbg = {}
            for n, src in [("d_xnf", "xnf"), ("d_qk", "qk"),
                           ("d_vtm", "vtm"), ("d_csk", "csk"), ("d_susl", "susl"),
                           ("d_y", "yA"), ("d_oin", "oin"), ("d_qef", "qef"),
                           ("d_gtm", "gtm"), ("d_um", "um"), ("d_mst", "mst"),
                           ("d_otm", "otm"), ("d_branch", "branch"),
                           ("d_gf", "gf"), ("d_ffa", "ffa")]:
                t = self.sb[src]
                self.dbg[n] = (src, self.nc.dram_tensor(
                    n, [t.shape[0], t.shape[1]], BF16, kind="ExternalOutput").ap())
            for n, src in [("d_chi", "chiv"), ("d_gam", "gamv"), ("d_zet", "zetv"),
                           ("d_om", "omv"), ("d_nq", "nqv"), ("d_nk", "nkv"),
                           ("d_la", "la"), ("d_gamrep", "gamrep")]:
                t = self.sb[src]
                self.dbg[n] = (src, self.nc.dram_tensor(
                    n, [t.shape[0], t.shape[1]], F32, kind="ExternalOutput").ap())

        self.emit_all()

        with nc.Block() as blk:
            pr = self.prog

            @blk.sync
            def _(e):
                for f in pr["sync"]:
                    f(e)

            @blk.vector
            def _(e):
                for f in pr["vector"]:
                    f(e)

            @blk.scalar
            def _(e):
                for f in pr["scalar"]:
                    f(e)

            @blk.tensor
            def _(e):
                for f in pr["tensor"]:
                    f(e)

            @blk.gpsimd
            def _(e):
                for f in pr["gpsimd"]:
                    f(e)
        return nc

    def emit_all(self):
        self.ph_setup()
        self.ph_weight_prep()
        self.slot_free = {0: self.tok("stt"), 1: self.tok("stt")}
        self.xld_free = {0: self.tok("sv"), 1: self.tok("sv")}
        self.wtok = {}
        self.wslot = {}
        self.wload_next = 0
        self.load_w("Wq")
        for it in range(self.ntile):
            self.ph_tile(it)
        self.ph_finish()

    def load_w(self, wn):
        sb = self.sb
        slot = self.wload_next
        self.wload_next ^= 1
        self.wslot[wn] = slot
        sem = "swa" if slot == 0 else "swb"
        dstw = sb["wtb0"] if slot == 0 else sb["wtb1"]
        self.w("sync", "stt", self.slot_free.get(slot, 0))
        self.w("sync", "sv", self.prep_guard.get(sem, 0))
        self.prep_guard[sem] = 0
        self.w("sync", "sg", self.tok("sg"))
        for k in range(8):
            self.o("sync", lambda e, wn=wn, k=k, d=dstw: e.dma_start(
                out=d[:, k * HID:(k + 1) * HID],
                in_=self.wt_dram[wn][k * 128:(k + 1) * 128, :]), sem, 16)
        self.wtok[wn] = (sem, self.tok(sem))

    def ph_setup(self):
        sb = self.sb
        self.o("sync", lambda e: e.dma_start(out=sb["cstf"][:, :], in_=self.cst_hbm), "swa", 16)
        self.o("sync", lambda e: e.dma_start(out=sb["babs"][:, :], in_=self.bab_hbm), "swa", 16)
        self.o("sync", lambda e: e.dma_start(out=sb["mus"][:, :], in_=self.mu_hbm), "swa", 16)
        self.w("vector", "swa")
        self.o("vector", lambda e: e.tensor_copy(out=sb["cstb"][:, :], in_=sb["cstf"][:, :]), "sv")
        self.o("vector", lambda e: e.tensor_scalar(
            out=sb["babn"][:, :], in0=sb["babs"][:, :], scalar1=-1.0, scalar2=None,
            op0=ALU.mult), "sv")
        self.o("vector", lambda e: e.memset(sb["mst"][:, :], 0.0), "sv")
        self.o("vector", lambda e: e.memset(sb["xbnd"][:, :], 0.0), "sv")

    def ph_weight_prep(self):
        sb, ps = self.sb, self.ps
        self.prep_guard = {"swa": self.tok("sv"), "swb": 0}
        I128b = sb["cstb"][:, CI_I128:CI_I128 + 128]
        for wn in BIG_W:
            for j in range(8):
                buf = sb["xld0"] if j % 2 == 0 else sb["xld1"]
                psem = "swa" if j % 2 == 0 else "swb"
                self.w("sync", "sv", self.prep_guard.get(psem, 0))
                self.o("sync", lambda e, s=self.w_hbm[wn], j=j, b=buf: e.dma_start(
                    out=b[:, :], in_=s[j * 128:(j + 1) * 128, :]), psem, 16)
                self.w("vector", psem)
                cols = (j % 2) * HID
                self.o("vector", lambda e, b=buf, c=cols: e.tensor_copy(
                    out=sb["wtb1"][:, c:c + HID], in_=b[:, :]), "sv")
                self.prep_guard[psem] = self.tok("sv")
                for r in range(2):
                    self.w("tensor", "sv")
                    self.w("tensor", "sg")
                    for i in range(4):
                        ii = r * 4 + i
                        self.o("tensor", lambda e, ii=ii, c=cols, i=i: e.transpose(
                            ps["ptr"][:, i * 128:(i + 1) * 128],
                            sb["wtb1"][:, c + ii * 128:c + ii * 128 + 128], I128b),
                            "stt" if i == 3 else None)
                    self.w("vector", "stt")
                    self.o("vector", lambda e, r=r: e.tensor_copy(
                        out=sb["wtb0"][:, r * 512:(r + 1) * 512],
                        in_=ps["ptr"][:, 0:512]), "sv")
                    self.w("gpsimd", "sv")
                    for i in range(4):
                        ii = r * 4 + i
                        self.o("gpsimd", lambda e, ii=ii, i=i, r=r, wn=wn, j=j: e.dma_start(
                            out=self.wt_dram[wn][ii * 128:(ii + 1) * 128,
                                                 j * 128:(j + 1) * 128],
                            in_=sb["wtb0"][:, r * 512 + i * 128: r * 512 + (i + 1) * 128]),
                            "sg", 16)
        self.w("sync", "sv", self.prep_guard.get("swa", 0))
        self.o("sync", lambda e: e.dma_start(out=sb["xld0"][0:32, :], in_=self.wab_hbm),
               "swa", 16)
        self.w("vector", "swa")
        self.w("vector", "sg")
        self.o("vector", lambda e: e.tensor_copy(
            out=sb["wtb1"][0:32, 0:HID], in_=sb["xld0"][0:32, :]), "sv")
        self.w("tensor", "sv")
        self.w("tensor", "sg")
        for i in range(8):
            self.o("tensor", lambda e, i=i: e.transpose(
                ps["ptr"][:, (i % 4) * 32:(i % 4) * 32 + 32],
                sb["wtb1"][0:32, i * 128:i * 128 + 128],
                sb["cstb"][0:32, CI_I128:CI_I128 + 32]),
                "stt" if i % 4 == 3 else None)
            if i % 4 == 3:
                self.w("vector", "stt")
                r = i // 4
                self.o("vector", lambda e, r=r: e.tensor_copy(
                    out=sb["wabt"][:, r * 128:(r + 1) * 128], in_=ps["ptr"][:, 0:128]),
                    "sv")
                if r == 0:
                    self.w("tensor", "sv")

    def mm_proj_start(self, wn, evac, src=None, tokmajor=False, src_tok=None):
        st = {"wn": wn, "evac": evac, "tokmajor": tokmajor,
              "srct": self.sb["xnf"] if src is None else self.sb[src],
              "wt": self.sb["wtb0"] if self.wslot[wn] == 0 else self.sb["wtb1"],
              "ev_tok": {0: None, 1: None}, "first": True,
              "src_tok": src_tok}
        return st

    def mm_proj_step(self, st, m):
        sb, ps = self.sb, self.ps
        wn, evac, srct, wt = st["wn"], st["evac"], st["srct"], st["wt"]
        tokmajor, ev_tok = st["tokmajor"], st["ev_tok"]
        if st["first"]:
            st["first"] = False
            self.w("tensor", self.wtok[wn][0], self.wtok[wn][1])
            if st["src_tok"] is None:
                self.w("tensor", "sv")
                self.w("tensor", "ss")
                self.w("tensor", "sp")
            else:
                for sem, tok in st["src_tok"]:
                    self.w("tensor", sem, tok)
        bank = ps["pp0"] if m % 2 == 0 else ps["pp1"]
        if ev_tok[m % 2] is not None:
            self.w("tensor", ev_tok[m % 2][0], ev_tok[m % 2][1])
        for k in range(8):
            if tokmajor:
                tb, oh = m // 2, m % 2
                self.o("tensor", lambda e, k=k, b=bank, wt=wt, s=srct, tb=tb,
                       oh=oh: e.matmul(
                    b[:, 0:TILE],
                    s[:, k * TILE + tb * 128: k * TILE + (tb + 1) * 128],
                    wt[:, k * HID + oh * 512: k * HID + (oh + 1) * 512],
                    start=(k == 0), stop=(k == 7)), "stt" if k == 7 else None)
            else:
                self.o("tensor", lambda e, m=m, k=k, b=bank, wt=wt, s=srct: e.matmul(
                    b[:, 0:TILE], wt[:, k * HID + m * 128: k * HID + m * 128 + 128],
                    s[:, k * TILE:(k + 1) * TILE],
                    start=(k == 0), stop=(k == 7)), "stt" if k == 7 else None)
        ev_tok[m % 2] = evac(m, bank)
        if m == 7:
            self.slot_free[self.wslot[wn]] = self.tok("stt")

    def mm_proj(self, wn, evac, src=None, tokmajor=False):
        st = self.mm_proj_start(wn, evac, src, tokmajor)
        for m in range(8):
            self.mm_proj_step(st, m)
        return


    def ev_copy(self, dst, off=0):
        sb = self.sb
        def f(m, bank):
            self.w("scalar", "stt")
            self.o("scalar", lambda e, m=m, b=bank: e.activation(
                out=sb[dst][:, off + m * TILE:off + (m + 1) * TILE],
                in_=b[:, 0:TILE], func=AF.Copy), "ss")
            return ("ss", self.tok("ss"))
        return f

    def ph_tile(self, it):
        sb, ps = self.sb, self.ps
        ntb = self.ntb
        tt = it % ntb
        t0 = it * TILE
        first_in_batch = (tt == 0)

        if first_in_batch and it > 0:
            self.w("vector", "stt")
            self.o("vector", lambda e: e.memset(sb["mst"][:, :], 0.0), "sv")
            self.o("gpsimd", lambda e: e.memset(sb["xbnd"][:, :], 0.0), "sp")

        if self.stop_after == "prep":
            return
        for st in range(4):
            buf = sb["xld0"] if st % 2 == 0 else sb["xld1"]
            self.w("gpsimd", "sv", getattr(self, "xld_free", {}).get(st % 2, 0))
            xsem = "sxa" if st % 2 == 0 else "sxb"
            self.o("gpsimd", lambda e, b=buf, st=st, t0=t0: e.dma_start(
                out=b[:, :], in_=self.x_hbm[t0 + st * 128: t0 + (st + 1) * 128, :]),
                xsem, 16)
            self.w("vector", xsem)
            self.w("scalar", xsem)
            self.o("vector", lambda e, b=buf: e.reduce_sum(
                out=sb["stat"][:, 0:1], in_=b[:, :], axis=AX.X), "sv")
            self.w("scalar", "sv")
            self.o("scalar", lambda e, b=buf: e.activation(
                out=sb["xsq"][:, :], in_=b[:, :], func=AF.Square,
                accum_out=sb["stat"][:, 1:2]), "ss")
            self.w("vector", "ss")
            self.o("vector", lambda e: e.tensor_scalar(
                out=sb["stat"][:, 2:3], in0=sb["stat"][:, 0:1],
                scalar1=1.0 / HID, scalar2=None, op0=ALU.mult), "sv")
            self.o("vector", lambda e: e.tensor_scalar(
                out=sb["stat"][:, 4:5], in0=sb["stat"][:, 1:2],
                scalar1=1.0 / HID, scalar2=None, op0=ALU.mult), "sv")
            self.o("vector", lambda e: e.scalar_tensor_tensor(
                out=sb["stat"][:, 5:6], in0=sb["stat"][:, 2:3],
                scalar=sb["stat"][:, 2:3], in1=sb["stat"][:, 4:5],
                op0=ALU.mult, op1=ALU.subtract), "sv")
            self.w("scalar", "sv")
            self.o("scalar", lambda e: e.activation(
                out=sb["stat"][:, 6:7], in_=sb["stat"][:, 5:6], func=AF.Ln,
                scale=-1.0, bias=EPS_LN), "ss")
            self.o("scalar", lambda e: e.activation(
                out=sb["stat"][:, 7:8], in_=sb["stat"][:, 6:7], func=AF.Exp,
                scale=-0.5), "ss")
            self.w("vector", "ss")
            self.o("vector", lambda e, b=buf: e.tensor_scalar(
                out=sb["xntm"][:, :], in0=b[:, :], scalar1=sb["stat"][:, 2:3],
                scalar2=sb["stat"][:, 7:8], op0=ALU.subtract, op1=ALU.mult), "sv")
            if not hasattr(self, "xld_free"):
                self.xld_free = {}
            self.xld_free[st % 2] = self.tok("sv")
            self.w("tensor", "sv")
            for r in range(2):
                if r == 1:
                    self.w("tensor", "sv")
                for i in range(4):
                    k = r * 4 + i
                    self.o("tensor", lambda e, k=k, i=i: e.transpose(
                        ps["ptr"][:, i * 128:(i + 1) * 128],
                        sb["xntm"][:, k * 128:(k + 1) * 128],
                        sb["cstb"][:, CI_I128:CI_I128 + 128]),
                        "stt" if i == 3 else None)
                self.w("vector", "stt")
                self.o("vector", lambda e, r=r, st=st: e.tensor_copy(
                    out=ap(sb["xnf"], 0, 128, r * 4 * TILE + st * 128,
                           [[TILE, 4], [1, 128]]),
                    in_=ps["ptr"][:, 0:512]), "sv")

        if self.stop_after == "ln":
            return
        self.load_w("Wk")
        self.mm_proj("Wq", self.ev_copy("qk", 0))
        self.load_w("Wv")
        self.mm_proj("Wk", self.ev_copy("qk", 8 * TILE))
        self.load_w("Wr")

        def ev_vtm(m, bank):
            tb, oh = m // 2, m % 2
            self.w("scalar", "stt")
            self.o("scalar", lambda e, b=bank, tb=tb, oh=oh: e.activation(
                out=sb["vtm"][:, tb * HID + oh * 512: tb * HID + (oh + 1) * 512],
                in_=b[:, 0:TILE], func=AF.Copy), "ss")
            return ("ss", self.tok("ss"))
        self.mm_proj("Wv", ev_vtm, tokmajor=True)
        self.qk_tok = self.tok("ss")

        if self.stop_after == "proj":
            return
        self.ph_scales(it)

        if self.stop_after == "scales":
            return

        self.ph_ffn_mix()

        self.projq = []
        if self.stop_after is None:
            self.build_projq(it)

        for a in range(NPAIR):
            self.ph_pair(it, a)
        while self.projq:
            self.projq.pop(0)()
        if self.stop_after in ("pskv", "pskv1", "pskvnoev", "pskv1noev",
                               "builds", "solve", "post", "pairs", "ffn"):
            return

        self.ph_gate_wo(it)

        self.ph_epilogue(it)

        if self.debug and it == self.ntile - 1:
            self.ph_debug_dump()

    def ph_scales(self, it):
        sb, ps = self.sb, self.ps
        self.w("tensor", "sv")
        for k in range(8):
            self.o("tensor", lambda e, k=k: e.matmul(
                ps["pscl"][0:16, 0:TILE], sb["wabt"][:, k * 32:k * 32 + 16],
                sb["xnf"][:, k * TILE:(k + 1) * TILE],
                start=(k == 0), stop=(k == 7)))
            self.o("tensor", lambda e, k=k: e.matmul(
                ps["psq"][0:16, 0:TILE], sb["wabt"][:, k * 32 + 16:k * 32 + 32],
                sb["xnf"][:, k * TILE:(k + 1) * TILE],
                start=(k == 0), stop=(k == 7)), "stt" if k == 7 else None)
        self.w("scalar", "stt")
        self.o("scalar", lambda e: e.activation(
            out=sb["zsig"][:, 0:TILE], in_=ps["pscl"][0:16, 0:TILE], func=AF.Sigmoid,
            bias=sb["babs"][:, 0:1]), "ss")
        self.o("scalar", lambda e: e.activation(
            out=sb["zsig"][:, TILE:2 * TILE], in_=ps["psq"][0:16, 0:TILE], func=AF.Sigmoid,
            bias=sb["babs"][:, 1:2]), "ss")
        self.w("vector", "ss")
        self.o("vector", lambda e: e.tensor_scalar(
            out=sb["t16a"][:, :], in0=sb["zsig"][0:16, 0:TILE], scalar1=1e-35,
            scalar2=None, op0=ALU.max), "sv")
        self.w("scalar", "sv")
        self.o("scalar", lambda e: e.activation(
            out=sb["la"][:, :], in_=sb["t16a"][:, :], func=AF.Ln), "ss")
        self.w("vector", "ss")
        for c in range(NCH):
            self.o("vector", lambda e, c=c: e.tensor_tensor_scan(
                out=sb["lc"][:, c * C:(c + 1) * C],
                data0=sb["la"][:, c * C:(c + 1) * C],
                data1=sb["la"][:, c * C:(c + 1) * C], initial=0.0,
                op0=ALU.add, op1=ALU.bypass), "sv")
        self.w("scalar", "sv")
        self.o("scalar", lambda e: e.activation(
            out=sb["gamv"][:, :], in_=sb["lc"][:, :], func=AF.Exp), "ss")
        self.w("vector", "ss")
        self.o("vector", lambda e: e.tensor_scalar(
            out=sb["t16a"][:, :], in0=sb["lc"][:, :], scalar1=-1.0, scalar2=CLAMP,
            op0=ALU.mult, op1=ALU.min), "sv")
        self.w("scalar", "sv")
        self.o("scalar", lambda e: e.activation(
            out=sb["uv"][:, :], in_=sb["t16a"][:, :], func=AF.Exp), "ss")

        self.w("vector", "ss", self.qk_tok)
        for (nm, soff) in [("nqv", 0), ("nkv", 8 * TILE)]:
            for k in range(8):
                self.w("vector", "stt")
                self.o("vector", lambda e, k=k, soff=soff: e.tensor_tensor(
                    out=sb["sq"][:, :],
                    in0=sb["qk"][:, soff + k * TILE:soff + (k + 1) * TILE],
                    in1=sb["qk"][:, soff + k * TILE:soff + (k + 1) * TILE],
                    op=ALU.mult), "sv")
                self.w("tensor", "sv")
                self.o("tensor", lambda e, k=k: e.matmul(
                    ps["psq"][0:16, 0:TILE],
                    sb["cstb"][:, CI_IND + 16 * k: CI_IND + 16 * (k + 1)],
                    sb["sq"][:, :], start=(k == 0), stop=(k == 7)), "stt")
            self.w("vector", "stt")
            self.w("vector", "ss")
            self.o("vector", lambda e: e.tensor_scalar(
                out=sb["t16a"][:, :], in0=ps["psq"][0:16, 0:TILE],
                scalar1=1e-12, scalar2=None, op0=ALU.max), "sv")
            self.w("scalar", "sv")
            self.o("scalar", lambda e: e.activation(
                out=sb["t16b"][:, :], in_=sb["t16a"][:, :], func=AF.Ln), "ss")
            self.o("scalar", lambda e, nm=nm: e.activation(
                out=sb[nm][:, :], in_=sb["t16b"][:, :], func=AF.Exp, scale=-0.5), "ss")

        self.w("vector", "ss")
        self.o("vector", lambda e: e.tensor_tensor(
            out=sb["t16a"][:, :], in0=sb["nkv"][:, :], in1=sb["nkv"][:, :],
            op=ALU.mult), "sv")
        self.o("vector", lambda e: e.tensor_tensor(
            out=sb["t16a"][:, :], in0=sb["t16a"][:, :], in1=sb["uv"][:, :],
            op=ALU.mult), "sv")
        self.o("vector", lambda e: e.tensor_tensor(
            out=sb["t16a"][:, :], in0=sb["t16a"][:, :], in1=sb["gamv"][:, :],
            op=ALU.mult), "sv")
        self.o("vector", lambda e: e.tensor_tensor(
            out=sb["chiv"][:, :], in0=sb["t16a"][:, :], in1=sb["zsig"][0:16, TILE:2 * TILE],
            op=ALU.mult), "sv")
        self.o("vector", lambda e: e.tensor_tensor(
            out=sb["t16b"][:, :], in0=sb["nkv"][:, :], in1=sb["gamv"][:, :],
            op=ALU.mult), "sv")
        self.o("vector", lambda e: e.reciprocal(
            out=sb["zetv"][:, :], in_=sb["t16b"][:, :]), "sv")
        self.o("vector", lambda e: e.tensor_tensor(
            out=sb["omv"][:, :], in0=sb["nqv"][:, :], in1=sb["gamv"][:, :],
            op=ALU.mult), "sv")

        self.w("tensor", "sv")
        for (vn, tn) in [("chiv", "chiT"), ("zetv", "zetT"), ("omv", "omT")]:
            for blk in range(4):
                self.o("tensor", lambda e, vn=vn, blk=blk: e.transpose(
                    ps["pscl"][:, 256 + blk * 16: 256 + (blk + 1) * 16],
                    sb[vn][:, blk * 128:(blk + 1) * 128],
                    sb["cstf"][0:16, CI_I128:CI_I128 + 16]),
                    "stt" if blk == 3 else None)
            self.w("vector", "stt")
            self.o("vector", lambda e, tn=tn: e.tensor_copy(
                out=sb[tn][:, :], in_=ps["pscl"][:, 256:256 + 64]), "sv")
            self.w("tensor", "sv")

        self.w("gpsimd", "sv")
        self.o("gpsimd", lambda e: e.dma_start(
            out=sb["gamrow"][0:1, 0:128],
            in_=ap(sb["gamv"], 0, 16, C - 1, [[C, NCH]]),
            allow_slow_non_contiguous=True), "sgr", 16)
        self.w("tensor", "sgr")
        self.o("tensor", lambda e: e.matmul(
            ps["pscl"][:, 384:384 + 128], sb["cstf"][0:1, CI_ONES:CI_ONES + 128],
            sb["gamrow"][0:1, 0:128], start=True, stop=True), "stt")
        self.w("vector", "stt")
        self.o("vector", lambda e: e.tensor_copy(
            out=sb["gamrep"][:, :], in_=ps["pscl"][:, 384:384 + 128]), "sv")

    def pq_run(self, n):
        for _ in range(n):
            if self.projq:
                self.projq.pop(0)()

    def ph_pair(self, it, a):
        sb, ps = self.sb, self.ps
        if not hasattr(self, "projq"):
            self.projq = []

        def oin_gather(par, pc):
            return ap(self.sb["oin"], pc, 64, par * 64, [[512, 2], [128, 4], [1, 64]])

        def um_gather(par, phh, cc):
            return ap(self.sb["um"], phh, 64, cc * 512, [[64, 8], [1, 64]])
        cstb = self.sb["cstb"]
        c0 = 2 * a
        if not hasattr(self, "btok"):
            self.btok = {}
        btok = self.btok

        def bank_wait(bn):
            t = btok.get(bn)
            if t is not None:
                self.w("tensor", t[0], t[1])

        def chT(h):
            return sb["chiT"][:, a * 16 + h: a * 16 + h + 1]

        def zeT(h):
            return sb["zetT"][:, a * 16 + h: a * 16 + h + 1]

        POOL = [("ppsk", "psl1"), ("psq", "psl0")]

        nrounds = 1 if self.stop_after in ("pskv1", "pskv1noev") else 8
        noev = self.stop_after in ("pskvnoev", "pskv1noev")
        for rnd in range(nrounds):
            be, bo = POOL[rnd % 2]
            bank_wait(be)
            bank_wait(bo)
            for cc in (0, 1):
                pc = 64 * cc
                for si in range(2):
                    h = 2 * rnd + si
                    ph = 64 * (h % 2)
                    bank = ps[be] if h % 2 == 0 else ps[bo]
                    cl = (h // 2) * TILE + (c0 + cc) * C
                    self.o("tensor", lambda e, ph=ph, cl=cl, pc=pc, b=bank: e.matmul(
                        b[pc:pc + 64, 0:128],
                        sb["qk"][ph:ph + 64, 8 * TILE + cl:8 * TILE + cl + 64],
                        ap(sb["qk"], ph, 64, cl, [[8 * TILE, 2], [1, 64]]),
                        start=True, stop=True, tile_position=(ph, pc)))
                    self.o("tensor", lambda e, ph=ph, cl=cl, pc=pc, b=bank: e.matmul(
                        b[pc:pc + 64, 128:192],
                        sb["qk"][ph:ph + 64, 8 * TILE + cl:8 * TILE + cl + 64],
                        cstb[ph:ph + 64, CI_ISTK:CI_ISTK + 64],
                        start=True, stop=True, tile_position=(ph, pc)),
                        "stt" if (si == 1 and cc == 1) else None)
            if noev:
                continue
            mmtok = self.tok("stt")

            def dve_evacs(h, bn):
                self.o("vector", lambda e, h=h, b=ps[bn]: e.scalar_tensor_tensor(
                    out=ap(sb["csk"], 0, 128, h * 192, [[1, 192]]),
                    in0=ap(b, 0, 128, 0, [[1, 192]]),
                    scalar=chT(h),
                    in1=ap(sb["cstf"], 0, 128, CI_M3B, [[1, 192]]),
                    op0=ALU.mult, op1=ALU.mult), "sv")
                self.o("vector", lambda e, h=h, b=ps[bn]: e.tensor_tensor(
                    out=ap(sb["susl"], 0, 128, h * 64, [[1, 64]]),
                    in0=ap(b, 0, 128, 64, [[1, 64]]),
                    in1=ap(sb["cstf"], 0, 128, CI_MSL, [[1, 64]]),
                    op=ALU.mult), "sv")

            def sc_evacs(h, bn):
                self.o("scalar", lambda e, h=h, a=a: e.activation(
                    out=ap(sb["yA"], 0, 128, h * 128, [[1, 64]]),
                    in_=ap(sb["vtm"], 0, 128, a * HID + h * 64, [[1, 64]]),
                    func=AF.Copy, scale=zeT(h)), "ss")
                self.o("scalar", lambda e, h=h, b=ps[bn]: e.activation(
                    out=ap(sb["yA"], 0, 128, h * 128 + 64, [[1, 64]]),
                    in_=ap(b, 0, 128, 128, [[1, 64]]),
                    func=AF.Copy, scale=-1.0), "ss")
                self.o("scalar", lambda e, h=h, b=ps[bn]: e.activation(
                    out=ap(sb["ktm"], 0, 128, h * 64, [[1, 64]]),
                    in_=ap(b, 0, 128, 128, [[1, 64]]),
                    func=AF.Copy), "ss")

            he, ho = 2 * rnd, 2 * rnd + 1
            self.w("vector", "stt", mmtok)
            self.w("scalar", "stt", mmtok)
            dve_evacs(he, be)
            sc_evacs(ho, bo)
            self.w("vector", "ss")
            self.w("scalar", "sv")
            dve_evacs(ho, bo)
            sc_evacs(he, be)
            btok[be] = ("ss", self.tok("ss"))
            btok[bo] = ("sv", self.tok("sv"))
        if self.stop_after in ("pskv", "pskv1", "pskvnoev", "pskv1noev"):
            return
        self.pq_run(2)

        def sl_slice(h, pc):
            return ap(sb["susl"], pc, 64, h * 64, [[1, 64]])

        def sux_slice(h, pc):
            return ap(sb["csk"], pc, 64, h * 192 + 64, [[1, 64]])

        def pwx_slice(l, h, pc):
            return ap(sb["pwx"], pc, 64, l * H * 64 + h * 64, [[1, 64]])

        def pwt_slice(l, h, pc):
            return ap(sb["pwt"], pc, 64, (l % 2) * H * 64 + h * 64, [[1, 64]])

        self.w("tensor", "sv")
        self.w("tensor", "ss")
        wv = 0
        for L in range(5):
            for part in range(2):
                if L == 4 and part == 1:
                    break
                be, bo = POOL[wv % 2]
                wv += 1
                bank_wait(be)
                bank_wait(bo)
                if L > 0 and part == 0:
                    self.lvl_tok = (self.tok("sv"), self.tok("ss"))
                if L > 0:
                    self.w("tensor", "sv", self.lvl_tok[0])
                    self.w("tensor", "ss", self.lvl_tok[1])
                for cc in (0, 1):
                    pc = 64 * cc
                    for h in range(H):
                        bank = ps[be] if h < 8 else ps[bo]
                        colb = (h % 8) * 64
                        if L == 0:
                            stat = sl_slice(h, pc) if part == 0 else sux_slice(h, pc)
                            mv = sux_slice(h, pc) if part == 0 else sl_slice(h, pc)
                        else:
                            stat = pwt_slice(L - 1, h, pc) if part == 0 else pwx_slice(L - 1, h, pc)
                            mv = pwx_slice(L - 1, h, pc) if part == 0 else pwt_slice(L - 1, h, pc)
                        self.o("tensor", lambda e, st=stat, mv=mv, pc=pc, colb=colb,
                               bank=bank: e.matmul(
                            bank[pc:pc + 64, colb:colb + 64], st, mv,
                            start=True, stop=True, tile_position=(pc, pc)),
                            "stt" if (h == H - 1 and cc == 1) else None)
                self.w("vector", "stt")
                self.w("scalar", "stt")
                for bi, bn in enumerate((be, bo)):
                    if part == 0:
                        self.o("vector", lambda e, L=L, bi=bi, bn=bn: e.tensor_tensor(
                            out=ap(sb["pwx"], 0, 128, L * H * 64 + bi * 512, [[1, 512]]),
                            in0=ap(ps[bn], 0, 128, 0, [[1, 512]]),
                            in1=ap(sb["chiT"], 0, 128, a * 16 + bi * 8, [[1, 8], [0, 64]]),
                            op=ALU.mult), "sv")
                    else:
                        self.o("scalar", lambda e, L=L, bi=bi, bn=bn: e.activation(
                            out=ap(sb["pwt"], 0, 128, (L % 2) * H * 64 + bi * 512,
                                   [[1, 512]]),
                            in_=ap(ps[bn], 0, 128, 0, [[1, 512]]),
                            func=AF.Copy), "ss")
                    btok[bn] = ("sv", self.tok("sv")) if part == 0 else \
                        ("ss", self.tok("ss"))
        if self.stop_after == "builds":
            return
        self.pq_run(2)

        stages = [(4, ALU.add), (3, ALU.add), (2, ALU.add), (1, ALU.add),
                  (0, ALU.add), (-1, ALU.subtract)]
        cur, nxt = "yA", "yB"
        self.w("tensor", "ss")
        bi = 0
        qtok = {q: self.tok("sv") for q in range(4)}
        for sidx, (L, op) in enumerate(stages):
            for quad in range(4):
                bn = [b for pr in POOL for b in pr][bi % 4]
                bi += 1
                bank_wait(bn)
                self.w("tensor", "sv", qtok[quad])
                bank = ps[bn]
                for hh in range(4):
                    h = quad * 4 + hh
                    for cc in (0, 1):
                        pc = 64 * cc
                        stat = sux_slice(h, pc) if L < 0 else pwx_slice(L, h, pc)
                        self.o("tensor", lambda e, st=stat, h=h, hh=hh, pc=pc,
                               bank=bank, cur=cur: e.matmul(
                            bank[pc:pc + 64, hh * 128:(hh + 1) * 128], st,
                            ap(sb[cur], pc, 64, h * 128, [[1, 128]]),
                            start=True, stop=True, tile_position=(pc, pc)),
                            "stt" if (hh == 3 and cc == 1) else None)
                self.w("vector", "stt")
                if op == ALU.add:
                    self.o("vector", lambda e, quad=quad, bank=bank, cur=cur,
                           nxt=nxt: e.tensor_tensor(
                        out=ap(sb[nxt], 0, 128, quad * 512, [[1, 512]]),
                        in0=ap(bank, 0, 128, 0, [[1, 512]]),
                        in1=ap(sb[cur], 0, 128, quad * 512, [[1, 512]]),
                        op=ALU.add), "sv")
                else:
                    self.o("vector", lambda e, quad=quad, bank=bank, cur=cur,
                           nxt=nxt: e.tensor_tensor(
                        out=ap(sb[nxt], 0, 128, quad * 512, [[1, 512]]),
                        in0=ap(sb[cur], 0, 128, quad * 512, [[1, 512]]),
                        in1=ap(bank, 0, 128, 0, [[1, 512]]),
                        op=ALU.subtract), "sv")
                btok[bn] = ("sv", self.tok("sv"))
                qtok[quad] = self.tok("sv")
            cur, nxt = nxt, cur
        zf = cur

        self.o("vector", lambda e, zf=zf: e.tensor_tensor(
            out=ap(sb["zknx"], 0, 128, 0, [[64, 16], [1, 64]]),
            in0=ap(sb[zf], 0, 128, 64, [[128, 16], [1, 64]]),
            in1=ap(sb["chiT"], 0, 128, a * 16, [[1, 16], [0, 64]]),
            op=ALU.mult), "sv")
        zknx_tok = self.tok("sv")
        if self.stop_after == "solve":
            return
        self.pq_run(2)

        def bankcol(h):
            return (h // 2) * 64

        post_in_tok = self.tok("sv")

        def post_wave(mm_fn, evac_fn, need_tok):
            be, bo = POOL[self.pwave % 2]
            self.pwave += 1
            bank_wait(be)
            bank_wait(bo)
            self.w("tensor", "sv", need_tok)
            mm_fn(ps[be], ps[bo])
            self.w("vector", "stt")
            evac_fn(be, bo)
            btok[be] = ("sv", self.tok("sv"))
            btok[bo] = ("sv", self.tok("sv"))

        self.pwave = getattr(self, "pwave", 0)

        def mm_oin(bA, bB):
            for h in range(H):
                bk = bA if h < 8 else bB
                cb = (h % 8) * 64
                for cc in (0, 1):
                    pc = 64 * cc
                    self.o("tensor", lambda e, h=h, pc=pc, b=bk, cb=cb, zf=zf: e.matmul(
                        b[pc:pc + 64, cb:cb + 64],
                        ap(sb["csk"], pc, 64, h * 192, [[1, 64]]),
                        ap(sb[zf], pc, 64, h * 128, [[1, 64]]),
                        start=True, stop=True, tile_position=(pc, pc)),
                        "stt" if (h == H - 1 and cc == 1) else None)

        def ev_oin(be, bo):
            for g, bn in enumerate((be, bo)):
                self.o("vector", lambda e, g=g, bn=bn, a=a: e.tensor_tensor(
                    out=ap(sb["oin"], 0, 128, g * 512, [[1, 512]]),
                    in0=ap(ps[bn], 0, 128, 0, [[1, 512]]),
                    in1=ap(sb["omT"], 0, 128, a * 16 + g * 8, [[1, 8], [0, 64]]),
                    op=ALU.mult), "sv")
        post_wave(mm_oin, ev_oin, post_in_tok)

        def mm_qef(bA, bB):
            for h in range(H):
                ph = 64 * (h % 2)
                for cc in (0, 1):
                    pc = 64 * cc
                    b = bA if cc == 0 else bB
                    self.o("tensor", lambda e, h=h, ph=ph, pc=pc, b=b, zf=zf: e.matmul(
                        b[ph:ph + 64, bankcol(h):bankcol(h) + 64],
                        ap(sb[zf], pc, 64, h * 128 + 64, [[1, 64]]),
                        ap(sb["csk"], pc, 64, h * 192, [[1, 64]]),
                        start=True, stop=True, tile_position=(pc, ph)),
                        "stt" if (h == H - 1 and cc == 1) else None)

        def ev_qef(be, bo):
            for cc, bn in enumerate((be, bo)):
                self.o("vector", lambda e, cc=cc, bn=bn, c0=c0: e.tensor_tensor(
                    out=ap(sb["qef"], 0, 128, cc * 512, [[1, 512]]),
                    in0=ap(ps[bn], 0, 128, 0, [[1, 512]]),
                    in1=ap(sb["qk"], 0, 128, (c0 + cc) * C, [[TILE, 8], [1, 64]]),
                    op=ALU.add), "sv")
        post_wave(mm_qef, ev_qef, post_in_tok)

        def mm_gtm(bA, bB):
            for h in range(H):
                ph = 64 * (h % 2)
                for cc in (0, 1):
                    pc = 64 * cc
                    b = bA if cc == 0 else bB
                    self.o("tensor", lambda e, h=h, ph=ph, pc=pc, b=b: e.matmul(
                        b[ph:ph + 64, bankcol(h):bankcol(h) + 64],
                        ap(sb["zknx"], pc, 64, h * 64, [[1, 64]]),
                        ap(sb["ktm"], pc, 64, h * 64, [[1, 64]]),
                        start=True, stop=True, tile_position=(pc, ph)),
                        "stt" if (h == H - 1 and cc == 1) else None)

        def ev_gtm(be, bo):
            for cc, bn in enumerate((be, bo)):
                self.o("vector", lambda e, cc=cc, bn=bn: e.tensor_tensor(
                    out=ap(sb["gtm"], 0, 128, cc * 512, [[1, 512]]),
                    in0=ap(ps[bn], 0, 128, 0, [[1, 512]]),
                    in1=ap(sb["cstb"], 0, 128, CI_IREP8, [[1, 512]]),
                    op=ALU.add), "sv")
        post_wave(mm_gtm, ev_gtm, zknx_tok)

        def mm_um(bA, bB):
            for h in range(H):
                ph = 64 * (h % 2)
                for cc in (0, 1):
                    pc = 64 * cc
                    b = bA if cc == 0 else bB
                    self.o("tensor", lambda e, h=h, ph=ph, pc=pc, b=b, zf=zf: e.matmul(
                        b[ph:ph + 64, bankcol(h):bankcol(h) + 64],
                        ap(sb["csk"], pc, 64, h * 192 + 128, [[1, 64]]),
                        ap(sb[zf], pc, 64, h * 128, [[1, 64]]),
                        start=True, stop=True, tile_position=(pc, ph)),
                        "stt" if (h == H - 1 and cc == 1) else None)

        def ev_um(be, bo):
            for cc, bn in enumerate((be, bo)):
                for half in (0, 1):
                    self.o("vector", lambda e, cc=cc, half=half, bn=bn, c0=c0: e.tensor_tensor(
                        out=ap(sb["um"], 64 * half, 64, cc * 512, [[1, 512]]),
                        in0=ap(ps[bn], 64 * half, 64, 0, [[1, 512]]),
                        in1=ap(sb["gamrep"], 64 * half, 64, half * 8 + c0 + cc,
                               [[16, 8], [0, 64]]),
                        op=ALU.mult), "sv")
        post_wave(mm_um, ev_um, post_in_tok)
        if self.stop_after == "post":
            return
        self.pq_run(2)

        for cc in (0, 1):
            pc = 64 * cc
            self.w("tensor", "sv")
            for bn in ("psq", "ppsk", "psl0", "psl1"):
                bank_wait(bn)
            for h in range(H):
                ph = 64 * (h % 2)
                bank = ps["psq"] if h % 2 == 0 else ps["ppsk"]
                self.o("tensor", lambda e, h=h, ph=ph, pc=pc, cc=cc, bank=bank: e.matmul(
                    bank[pc:pc + 64, bankcol(h):bankcol(h) + 64],
                    ap(sb["qef"], ph, 64, cc * 512 + bankcol(h), [[1, 64]]),
                    ap(sb["mst"], ph, 64, h * 64, [[1, 64]]),
                    start=True, stop=True, tile_position=(ph, pc)),
                    "stt" if h == H - 1 else None)
            owave_tok = self.tok("stt")
            for h in range(H):
                ph = 64 * (h % 2)
                bank = ps["psl0"] if h % 2 == 0 else ps["psl1"]
                self.o("tensor", lambda e, h=h, ph=ph, cc=cc, bank=bank: e.matmul(
                    bank[ph:ph + 64, bankcol(h):bankcol(h) + 64],
                    ap(sb["gtm"], ph, 64, cc * 512 + bankcol(h), [[1, 64]]),
                    ap(sb["mst"], ph, 64, h * 64, [[1, 64]]),
                    start=True, stop=True, tile_position=(ph, ph)),
                    "stt" if h == H - 1 else None)
            self.w("vector", "stt", owave_tok)
            for par, bn in ((0, "psq"), (1, "ppsk")):
                self.o("vector", lambda e, par=par, bn=bn, pc=pc, a=a: e.tensor_tensor(
                    out=ap(sb["xsq"], pc, 64, 0, [[1, 512]]),
                    in0=ap(ps[bn], pc, 64, 0, [[1, 512]]),
                    in1=ap(sb["omT"], pc, 64, a * 16 + par, [[2, 8], [0, 64]]),
                    op=ALU.mult), "sv")
                self.o("vector", lambda e, par=par, pc=pc, a=a: e.tensor_tensor(
                    out=ap(sb["otm"], pc, 64, a * HID + par * 64, [[128, 8], [1, 64]]),
                    in0=ap(sb["xsq"], pc, 64, 0, [[1, 512]]),
                    in1=oin_gather(par, pc),
                    op=ALU.add), "sv")
            self.w("vector", "stt")
            for par, bn in ((0, "psl0"), (1, "psl1")):
                phh = 64 * par
                self.o("vector", lambda e, par=par, bn=bn, phh=phh, cc=cc, c0=c0:
                       e.tensor_tensor(
                    out=ap(sb["xsq"], phh, 64, 512, [[1, 512]]),
                    in0=ap(ps[bn], phh, 64, 0, [[1, 512]]),
                    in1=ap(sb["gamrep"], phh, 64, par * 8 + c0 + cc, [[16, 8], [0, 64]]),
                    op=ALU.mult), "sv")
                self.o("vector", lambda e, par=par, phh=phh, cc=cc: e.tensor_tensor(
                    out=ap(sb["mst"], phh, 64, par * 64, [[128, 8], [1, 64]]),
                    in0=ap(sb["xsq"], phh, 64, 512, [[1, 512]]),
                    in1=um_gather(par, phh, cc),
                    op=ALU.add), "sv")
            for bn in ("psq", "ppsk", "psl0", "psl1"):
                btok[bn] = ("sv", self.tok("sv"))

    def ph_ffn_mix(self):
        sb = self.sb
        self.w("gpsimd", "swa", 48)
        self.w("gpsimd", "stt")
        self.w("gpsimd", "sv")
        for m in range(8):
            self.o("gpsimd", lambda e, m=m: e.tensor_tensor(
                out=sb["sq"][:, 1:TILE], in0=sb["xnf"][:, m * TILE + 1:(m + 1) * TILE],
                in1=sb["xnf"][:, m * TILE:(m + 1) * TILE - 1], op=ALU.subtract), "sp")
            self.o("gpsimd", lambda e, m=m: e.tensor_tensor(
                out=sb["sq"][:, 0:1], in0=sb["xnf"][:, m * TILE:m * TILE + 1],
                in1=sb["xbnd"][:, m:m + 1], op=ALU.subtract), "sp")
            self.o("gpsimd", lambda e, m=m: e.tensor_copy(
                out=sb["ffb"][:, m * TILE + 1:(m + 1) * TILE],
                in_=sb["xnf"][:, m * TILE:(m + 1) * TILE - 1]), "sp")
            self.o("gpsimd", lambda e, m=m: e.tensor_copy(
                out=sb["ffb"][:, m * TILE:m * TILE + 1], in_=sb["xbnd"][:, m:m + 1]), "sp")
            for dst, col in (("ffa", 2 * m), ("gf", 2 * m + 1)):
                self.o("gpsimd", lambda e, m=m, dst=dst, col=col: e.tensor_tensor(
                    out=sb[dst][:, m * TILE:(m + 1) * TILE], in0=sb["sq"][:, :],
                    in1=ap(sb["mus"], 0, 128, col, [[0, TILE]]),
                    op=ALU.mult), "sp")
                self.o("gpsimd", lambda e, m=m, dst=dst: e.tensor_tensor(
                    out=sb[dst][:, m * TILE:(m + 1) * TILE],
                    in0=sb[dst][:, m * TILE:(m + 1) * TILE],
                    in1=sb["ffb"][:, m * TILE:(m + 1) * TILE],
                    op=ALU.add), "sp")
        self.o("gpsimd", lambda e: e.tensor_copy(
            out=sb["xbnd"][:, 0:8],
            in_=ap(sb["xnf"], 0, 128, TILE - 1, [[TILE, 8], [1, 1]])), "sp")
        self.mix_tok = self.tok("sp")

    def build_projq(self, it):
        sb = self.sb
        q = self.projq

        def ev_r(m, bank):
            self.w("scalar", "stt")
            self.o("scalar", lambda e, m=m, b=bank: e.activation(
                out=sb["ffb"][:, m * TILE:(m + 1) * TILE], in_=b[:, 0:TILE],
                func=AF.Sigmoid), "ss")
            return ("ss", self.tok("ss"))

        def ev_k(m, bank):
            self.w("scalar", "stt")
            self.o("scalar", lambda e, m=m, b=bank: e.activation(
                out=sb["ffa"][:, m * TILE:(m + 1) * TILE], in_=b[:, 0:TILE],
                func=AF.Relu), "ss")
            t = ("ss", self.tok("ss"))
            self.w("gpsimd", "ss")
            self.o("gpsimd", lambda e, m=m: e.tensor_tensor(
                out=sb["ffa"][:, m * TILE:(m + 1) * TILE],
                in0=sb["ffa"][:, m * TILE:(m + 1) * TILE],
                in1=sb["ffa"][:, m * TILE:(m + 1) * TILE], op=ALU.mult), "sp")
            return t

        def ev_v(m, bank):
            self.w("vector", "stt")
            self.w("vector", "ss")
            self.o("vector", lambda e, m=m, b=bank: e.tensor_tensor(
                out=sb["branch"][:, m * TILE:(m + 1) * TILE], in0=b[:, 0:TILE],
                in1=sb["ffb"][:, m * TILE:(m + 1) * TILE], op=ALU.mult), "sv")
            return ("sv", self.tok("sv"))

        def ev_g(m, bank):
            self.w("scalar", "stt")
            self.o("scalar", lambda e, m=m, b=bank: e.activation(
                out=sb["gf"][:, m * TILE:(m + 1) * TILE], in_=b[:, 0:TILE],
                func=AF.Silu), "ss")
            return ("ss", self.tok("ss"))

        mixt = [("sp", self.mix_tok), ("ss", self.tok("ss")), ("sv", self.tok("sv"))]
        plan = [("load", "Wkf"), ("proj", "Wr", ev_r, "ffa", mixt),
                ("load", "Wvf"), ("proj", "Wkf", ev_k, "gf", mixt),
                ("load", "Wg"), ("proj", "Wvf", ev_v, "ffa", None),
                ("load", "Wo"), ("proj", "Wg", ev_g, None, None)]
        if it < self.ntile - 1:
            plan.append(("load", "Wq"))
        for item in plan:
            if item[0] == "load":
                q.append(lambda wn=item[1]: self.load_w(wn))
            else:
                _, wn, ev, src, stok = item
                holder = {}

                def start(wn=wn, ev=ev, src=src, stok=stok, holder=holder):
                    holder["st"] = self.mm_proj_start(wn, ev, src=src, src_tok=stok)

                q.append(start)
                for m in range(8):
                    q.append(lambda m=m, holder=holder: self.mm_proj_step(holder["st"], m))

    def ph_gate_wo(self, it):
        sb, ps = self.sb, self.ps
        self.w("tensor", "sv")
        self.w("vector", "ss")
        for k in range(8):
            for blk in range(4):
                self.o("tensor", lambda e, k=k, blk=blk: e.transpose(
                    ps["ptr"][:, blk * 128:(blk + 1) * 128],
                    sb["otm"][:, blk * HID + k * 128: blk * HID + (k + 1) * 128],
                    sb["cstb"][:, CI_I128:CI_I128 + 128]),
                    "stt" if blk == 3 else None)
            self.w("vector", "stt")
            self.o("vector", lambda e, k=k: e.tensor_tensor(
                out=sb["ffb"][:, k * TILE:(k + 1) * TILE], in0=ps["ptr"][:, 0:512],
                in1=sb["gf"][:, k * TILE:(k + 1) * TILE], op=ALU.mult), "sv")
            self.w("tensor", "sv")

        def ev_o(m, bank):
            self.w("vector", "stt")
            self.o("vector", lambda e, m=m, b=bank: e.tensor_tensor(
                out=sb["branch"][:, m * TILE:(m + 1) * TILE], in0=b[:, 0:TILE],
                in1=sb["branch"][:, m * TILE:(m + 1) * TILE], op=ALU.add), "sv")
            return ("sv", self.tok("sv"))
        self.mm_proj("Wo", ev_o, src="ffb")

    def ph_epilogue(self, it):
        sb, ps = self.sb, self.ps
        t0 = it * TILE
        self.w("tensor", "sp")
        for st in range(4):
            buf = sb["xld0"] if st % 2 == 0 else sb["xld1"]
            xsem = "sxa" if st % 2 == 0 else "sxb"
            self.w("gpsimd", "sv", self.xld_free.get(st % 2, 0))
            self.o("gpsimd", lambda e, b=buf, st=st, t0=t0: e.dma_start(
                out=b[:, :], in_=self.x_hbm[t0 + st * 128: t0 + (st + 1) * 128, :]),
                xsem, 16)
            self.w("tensor", "sv")
            for r in range(2):
                if r == 1:
                    self.w("tensor", "sv")
                for i in range(4):
                    k = r * 4 + i
                    self.o("tensor", lambda e, k=k, i=i, st=st: e.transpose(
                        ps["ptr"][:, i * 128:(i + 1) * 128],
                        ap(sb["branch"], 0, 128, k * TILE + st * 128, [[1, 128]]),
                        sb["cstb"][:, CI_I128:CI_I128 + 128]),
                        "stt" if i == 3 else None)
                self.w("vector", "stt")
                self.w("vector", xsem)
                if r == 0:
                    self.w("vector", "sdo", self.tok("sdo"))
                self.o("vector", lambda e, r=r, b=buf: e.tensor_tensor(
                    out=sb["outs"][:, r * 512:(r + 1) * 512], in0=ps["ptr"][:, 0:512],
                    in1=b[:, r * 512:(r + 1) * 512], op=ALU.add), "sv")
                self.w("tensor", "sv")
            self.w("gpsimd", "sv")
            self.o("gpsimd", lambda e, st=st, t0=t0: e.dma_start(
                out=self.out_hbm[t0 + st * 128: t0 + (st + 1) * 128, :],
                in_=sb["outs"][:, :]), "sdo", 16)
            self.xld_free[st % 2] = self.tok("sv")

    def ph_debug_dump(self):
        sb = self.sb
        self.w("gpsimd", "sv")
        self.w("gpsimd", "ss")
        for n, (src, dst) in self.dbg.items():
            self.o("gpsimd", lambda e, s=src, d=dst: e.dma_start(
                out=d, in_=sb[s][:, :]), "sdo", 16)

    def ph_finish(self):
        self.w("gpsimd", "sdo")


def _prep_inputs(core, x, ins):
    xs = np.ascontiguousarray(x[4 * core: 4 * core + 4].reshape(-1, HID))
    d = {"x": xs}
    d["W_Wq"] = ins["L0_attn_Wq"]
    d["W_Wk"] = ins["L0_attn_Wk"]
    d["W_Wv"] = ins["L0_attn_Wv"]
    d["W_Wg"] = ins["L0_attn_Wg"]
    d["W_Wo"] = ins["L0_attn_Wo"]
    d["W_Wr"] = ins["L0_ffn_W_r"]
    d["W_Wkf"] = ins["L0_ffn_W_k"]
    d["W_Wvf"] = ins["L0_ffn_W_v"]
    d["Wab"] = np.concatenate([ins["L0_attn_Wa"], ins["L0_attn_Wb"]], axis=0)
    bab = np.zeros((16, 2), dtype=np.float32)
    bab[:, 0] = ins["L0_attn_ba"]
    bab[:, 1] = ins["L0_attn_bb"]
    d["bab"] = bab
    mu = np.zeros((128, 16), dtype=np.float32)
    for k in range(8):
        mu[:, 2 * k] = ins["L0_ffn_mu_r"][k * 128:(k + 1) * 128]
        mu[:, 2 * k + 1] = ins["L0_ffn_mu_k"][k * 128:(k + 1) * 128]
    d["mu"] = mu
    d["consts"] = _host_consts()
    return {k: np.ascontiguousarray(v) for k, v in d.items()}


def kernel(**inputs):
    x = inputs["L0_attn_x"]
    B = x.shape[0]
    p = Prog(nb=B // NCORES, t_len=x.shape[1])
    nc = p.build()
    in_maps = [_prep_inputs(c, x, inputs) for c in range(NCORES)]
    res = run_bass_kernel_spmd(nc, in_maps, core_ids=list(range(NCORES)))
    out = np.concatenate(
        [r["out"].reshape(B // NCORES, x.shape[1], HID) for r in res.results], axis=0)
    return out.astype(np.float32)


# version 16
# speedup vs baseline: 1.5721x; 1.0484x over previous
import numpy as np
import ml_dtypes
import concourse.bass as bass
import concourse.mybir as mybir
from concourse.bass_utils import run_bass_kernel_spmd

F32 = mybir.dt.float32
BF16 = mybir.dt.bfloat16
AF = mybir.ActivationFunctionType
ALU = mybir.AluOpType
AX = mybir.AxisListType

NCORES = 8
HID = 1024
H = 16
DH = 64
C = 64
TILE = 512
NCH = TILE // C
NPAIR = NCH // 2
EPS_LN = 1e-5
CLAMP = 60.0
BIG_W = ["Wq", "Wk", "Wv", "Wg", "Wo", "Wr", "Wkf", "Wvf"]

CI_I128 = 0
CI_ISTK = 128
CI_MPM = 192
CI_MSU = 256
CI_MSL = 320
CI_IND = 384
CI_ONES = 512
CI_M3B = 640
CI_SCAL = 832
CI_IREP8 = 848
NCONST = 1360


def _host_consts():
    c = np.zeros((128, NCONST), dtype=np.float32)
    c[np.arange(128), CI_I128 + np.arange(128)] = 1.0
    c[np.arange(128), CI_ISTK + np.tile(np.arange(64), 2)] = 1.0
    s = np.tile(np.arange(64), 2)[:, None]
    t = np.arange(64)[None, :]
    c[:, CI_MPM:CI_MPM + 64] = (s <= t)
    c[:, CI_MSU:CI_MSU + 64] = (s < t)
    c[:, CI_MSL:CI_MSL + 64] = (t < s)
    for k in range(8):
        c[0:64, CI_IND + 16 * k + 2 * k] = 1.0
        c[64:128, CI_IND + 16 * k + 2 * k + 1] = 1.0
    c[:, CI_ONES:CI_ONES + 128] = 1.0
    c[:, CI_M3B:CI_M3B + 64] = (s <= t)
    c[:, CI_M3B + 64:CI_M3B + 128] = (s < t)
    c[:, CI_M3B + 128:CI_M3B + 192] = 1.0
    c[:, CI_SCAL] = EPS_LN
    for a in range(8):
        c[np.arange(128), CI_IREP8 + 64 * a + (np.arange(128) % 64)] = 1.0
    return c


def ap(t, pbase, pcount, off, dims):
    rowlen = t.shape[1]
    return bass.AP(t, pbase * rowlen + off, [[rowlen, pcount]] + list(dims))


class Prog:
    def __init__(self, nb, t_len, debug=False, stop_after=None):
        self.stop_after = stop_after
        self.nb, self.T, self.debug = nb, t_len, debug
        self.ntb = t_len // TILE
        self.ntile = nb * self.ntb
        self.nc = bass.Bass("TRN2", target_bir_lowering=False)
        self.ctx, self.cnt, self.sems, self.sb, self.ps = [], {}, {}, {}, {}
        self.last_wait = {}
        self.prog = {"sync": [], "vector": [], "scalar": [], "tensor": [], "gpsimd": []}

    def sem(self, n):
        s = self.nc.semaphore(n)
        self.sems[n] = s.__enter__()
        self.ctx.append(s)
        self.cnt[n] = 0

    def sbuf(self, n, shape, dt):
        t = self.nc.sbuf_tensor(n, shape, dt)
        self.sb[n] = t.__enter__()
        self.ctx.append(t)

    def psum(self, n, shape, dt=F32):
        t = self.nc.psum_tensor(n, shape, dt)
        self.ps[n] = t.__enter__()
        self.ctx.append(t)

    def w(self, eng, sem, val=None):
        v = self.cnt[sem] if val is None else val
        if v <= 0:
            return
        lw = self.last_wait
        if lw.get((eng, sem), 0) >= v:
            return
        lw[(eng, sem)] = v
        s = self.sems[sem]
        self.prog[eng].append(lambda e, s=s, v=v: e.wait_ge(s, v))

    def o(self, eng, build, sem=None, n=1):
        if self.debug and (eng in ("vector", "scalar")
                           or (eng == "gpsimd" and sem == "sp")):
            ssem = {"vector": "sv", "scalar": "ss", "gpsimd": "sp"}[eng]
            v = self.cnt[ssem]
            if v > 0:
                sh = self.sems[ssem]
                self.prog[eng].append(lambda e, sh=sh, v=v: e.wait_ge(sh, v))
        if sem is None:
            self.prog[eng].append(build)
        else:
            s = self.sems[sem]
            self.prog[eng].append(lambda e, s=s, b=build, n=n: b(e).then_inc(s, n))
            self.cnt[sem] += n

    def tok(self, sem):
        return self.cnt[sem]

    def build(self):
        nc = self.nc
        NTOK = self.nb * self.T
        self.x_hbm = nc.dram_tensor("x", [NTOK, HID], F32, kind="ExternalInput").ap()
        self.out_hbm = nc.dram_tensor("out", [NTOK, HID], F32, kind="ExternalOutput").ap()
        self.w_hbm = {n: nc.dram_tensor("W_" + n, [HID, HID], F32, kind="ExternalInput").ap()
                      for n in BIG_W}
        self.wab_hbm = nc.dram_tensor("Wab", [32, HID], F32, kind="ExternalInput").ap()
        self.bab_hbm = nc.dram_tensor("bab", [16, 2], F32, kind="ExternalInput").ap()
        self.mu_hbm = nc.dram_tensor("mu", [128, 16], F32, kind="ExternalInput").ap()
        self.cst_hbm = nc.dram_tensor("consts", [128, NCONST], F32, kind="ExternalInput").ap()
        self.wt_dram = {n: nc.dram_tensor("WT_" + n, [HID, HID], BF16).ap() for n in BIG_W}

        for s in ["swa", "swb", "sxa", "sxb", "sgr", "sdo", "sv", "ss", "stt", "sg",
                  "sp"]:
            self.sem(s)

        S = self.sbuf
        S("cstf", [128, NCONST], F32)
        self.nc.const_aps.register(F32, EPS_LN,
                                   self.sb["cstf"][:, CI_SCAL:CI_SCAL + 1])
        S("cstb", [128, NCONST], BF16)
        S("wtb0", [128, 8 * HID], BF16)
        S("wtb1", [128, 8 * HID], BF16)
        S("wabt", [128, 8 * 32], BF16)
        S("babs", [16, 2], F32)
        S("babn", [16, 2], F32)
        S("mus", [128, 16], F32)
        S("xld0", [128, HID], F32)
        S("xld1", [128, HID], F32)
        S("xsq", [128, HID], F32)
        S("xntm", [128, HID], BF16)
        S("xnf", [128, 8 * TILE], BF16)
        S("xbnd", [128, 8], BF16)
        S("qk", [128, 16 * TILE], BF16)
        S("vtm", [128, 4 * HID], BF16)
        S("gf", [128, 8 * TILE], BF16)
        S("sq", [128, TILE], BF16)
        S("stat", [128, 8], F32)
        S("zsig", [16, 2 * TILE], F32)
        for n in ["la", "lc", "gamv", "uv", "nqv", "nkv", "chiv", "zetv", "omv",
                  "t16a", "t16b"]:
            S(n, [16, TILE], F32)
        S("chiT", [128, 4 * 16], F32)
        S("zetT", [128, 4 * 16], F32)
        S("omT", [128, 4 * 16], F32)
        S("gamrow", [1, 128], F32)
        S("gamrep", [128, 128], F32)
        S("csk", [128, H * 192], BF16)
        S("susl", [128, H * 64], BF16)
        S("ktm", [128, H * 64], BF16)
        S("zknx", [128, H * 64], BF16)
        S("yA", [128, H * 128], BF16)
        S("yB", [128, H * 128], BF16)
        S("pwx", [128, 5 * H * 64], BF16)
        S("pwt", [128, 2 * H * 64], BF16)
        S("oin", [128, 2 * 512], BF16)
        S("qef", [128, 2 * 512], BF16)
        S("gtm", [128, 2 * 512], BF16)
        S("um", [128, 2 * 512], BF16)
        S("mst", [128, H * 64], BF16)
        S("otm", [128, 4 * HID], BF16)
        S("branch", [128, 8 * TILE], BF16)
        S("ffa", [128, 8 * TILE], BF16)
        S("ffb", [128, 8 * TILE], BF16)
        S("outs", [128, HID], F32)

        Pm = self.psum
        Pm("pp0", [128, 512])
        Pm("pp1", [128, 512])
        Pm("ppsk", [128, 512])
        Pm("psl0", [128, 512])
        Pm("psl1", [128, 512])
        Pm("psq", [128, 512])
        Pm("ptr", [128, 512], BF16)
        Pm("pscl", [128, 512])

        if self.debug:
            self.dbg = {}
            for n, src in [("d_xnf", "xnf"), ("d_qk", "qk"),
                           ("d_vtm", "vtm"), ("d_csk", "csk"), ("d_susl", "susl"),
                           ("d_y", "yA"), ("d_oin", "oin"), ("d_qef", "qef"),
                           ("d_gtm", "gtm"), ("d_um", "um"), ("d_mst", "mst"),
                           ("d_otm", "otm"), ("d_branch", "branch"),
                           ("d_gf", "gf"), ("d_ffa", "ffa")]:
                t = self.sb[src]
                self.dbg[n] = (src, self.nc.dram_tensor(
                    n, [t.shape[0], t.shape[1]], BF16, kind="ExternalOutput").ap())
            for n, src in [("d_chi", "chiv"), ("d_gam", "gamv"), ("d_zet", "zetv"),
                           ("d_om", "omv"), ("d_nq", "nqv"), ("d_nk", "nkv"),
                           ("d_la", "la"), ("d_gamrep", "gamrep")]:
                t = self.sb[src]
                self.dbg[n] = (src, self.nc.dram_tensor(
                    n, [t.shape[0], t.shape[1]], F32, kind="ExternalOutput").ap())

        self.emit_all()

        with nc.Block() as blk:
            pr = self.prog

            @blk.sync
            def _(e):
                for f in pr["sync"]:
                    f(e)

            @blk.vector
            def _(e):
                for f in pr["vector"]:
                    f(e)

            @blk.scalar
            def _(e):
                for f in pr["scalar"]:
                    f(e)

            @blk.tensor
            def _(e):
                for f in pr["tensor"]:
                    f(e)

            @blk.gpsimd
            def _(e):
                for f in pr["gpsimd"]:
                    f(e)
        return nc

    def emit_all(self):
        self.ph_setup()
        self.ph_weight_prep()
        self.slot_free = {0: self.tok("stt"), 1: self.tok("stt")}
        self.xld_free = {0: self.tok("sv"), 1: self.tok("sv")}
        self.wtok = {}
        self.wslot = {}
        self.wload_next = 0
        self.load_w("Wq")
        for it in range(self.ntile):
            self.ph_tile(it)
        self.ph_finish()

    def load_w(self, wn):
        sb = self.sb
        slot = self.wload_next
        self.wload_next ^= 1
        self.wslot[wn] = slot
        sem = "swa" if slot == 0 else "swb"
        dstw = sb["wtb0"] if slot == 0 else sb["wtb1"]
        self.w("sync", "stt", self.slot_free.get(slot, 0))
        self.w("sync", "sv", self.prep_guard.get(sem, 0))
        self.prep_guard[sem] = 0
        self.w("sync", "sg", self.tok("sg"))
        for k in range(8):
            self.o("sync", lambda e, wn=wn, k=k, d=dstw: e.dma_start(
                out=d[:, k * HID:(k + 1) * HID],
                in_=self.wt_dram[wn][k * 128:(k + 1) * 128, :]), sem, 16)
        self.wtok[wn] = (sem, self.tok(sem))

    def ph_setup(self):
        sb = self.sb
        self.o("sync", lambda e: e.dma_start(out=sb["cstf"][:, :], in_=self.cst_hbm), "swa", 16)
        self.o("sync", lambda e: e.dma_start(out=sb["babs"][:, :], in_=self.bab_hbm), "swa", 16)
        self.o("sync", lambda e: e.dma_start(out=sb["mus"][:, :], in_=self.mu_hbm), "swa", 16)
        self.w("vector", "swa")
        self.o("vector", lambda e: e.tensor_copy(out=sb["cstb"][:, :], in_=sb["cstf"][:, :]), "sv")
        self.o("vector", lambda e: e.tensor_scalar(
            out=sb["babn"][:, :], in0=sb["babs"][:, :], scalar1=-1.0, scalar2=None,
            op0=ALU.mult), "sv")
        self.o("vector", lambda e: e.memset(sb["mst"][:, :], 0.0), "sv")
        self.o("vector", lambda e: e.memset(sb["xbnd"][:, :], 0.0), "sv")

    def ph_weight_prep(self):
        sb, ps = self.sb, self.ps
        self.prep_guard = {"swa": self.tok("sv"), "swb": 0}
        I128b = sb["cstb"][:, CI_I128:CI_I128 + 128]
        for wn in BIG_W:
            for j in range(8):
                buf = sb["xld0"] if j % 2 == 0 else sb["xld1"]
                psem = "swa" if j % 2 == 0 else "swb"
                self.w("sync", "sv", self.prep_guard.get(psem, 0))
                self.o("sync", lambda e, s=self.w_hbm[wn], j=j, b=buf: e.dma_start(
                    out=b[:, :], in_=s[j * 128:(j + 1) * 128, :]), psem, 16)
                self.w("vector", psem)
                cols = (j % 2) * HID
                self.o("vector", lambda e, b=buf, c=cols: e.tensor_copy(
                    out=sb["wtb1"][:, c:c + HID], in_=b[:, :]), "sv")
                self.prep_guard[psem] = self.tok("sv")
                for r in range(2):
                    self.w("tensor", "sv")
                    self.w("tensor", "sg")
                    for i in range(4):
                        ii = r * 4 + i
                        self.o("tensor", lambda e, ii=ii, c=cols, i=i: e.transpose(
                            ps["ptr"][:, i * 128:(i + 1) * 128],
                            sb["wtb1"][:, c + ii * 128:c + ii * 128 + 128], I128b),
                            "stt" if i == 3 else None)
                    self.w("vector", "stt")
                    self.o("vector", lambda e, r=r: e.tensor_copy(
                        out=sb["wtb0"][:, r * 512:(r + 1) * 512],
                        in_=ps["ptr"][:, 0:512]), "sv")
                    self.w("gpsimd", "sv")
                    for i in range(4):
                        ii = r * 4 + i
                        self.o("gpsimd", lambda e, ii=ii, i=i, r=r, wn=wn, j=j: e.dma_start(
                            out=self.wt_dram[wn][ii * 128:(ii + 1) * 128,
                                                 j * 128:(j + 1) * 128],
                            in_=sb["wtb0"][:, r * 512 + i * 128: r * 512 + (i + 1) * 128]),
                            "sg", 16)
        self.w("sync", "sv", self.prep_guard.get("swa", 0))
        self.o("sync", lambda e: e.dma_start(out=sb["xld0"][0:32, :], in_=self.wab_hbm),
               "swa", 16)
        self.w("vector", "swa")
        self.w("vector", "sg")
        self.o("vector", lambda e: e.tensor_copy(
            out=sb["wtb1"][0:32, 0:HID], in_=sb["xld0"][0:32, :]), "sv")
        self.w("tensor", "sv")
        self.w("tensor", "sg")
        for i in range(8):
            self.o("tensor", lambda e, i=i: e.transpose(
                ps["ptr"][:, (i % 4) * 32:(i % 4) * 32 + 32],
                sb["wtb1"][0:32, i * 128:i * 128 + 128],
                sb["cstb"][0:32, CI_I128:CI_I128 + 32]),
                "stt" if i % 4 == 3 else None)
            if i % 4 == 3:
                self.w("vector", "stt")
                r = i // 4
                self.o("vector", lambda e, r=r: e.tensor_copy(
                    out=sb["wabt"][:, r * 128:(r + 1) * 128], in_=ps["ptr"][:, 0:128]),
                    "sv")
                if r == 0:
                    self.w("tensor", "sv")

    def mm_proj_start(self, wn, evac, src=None, tokmajor=False, src_tok=None):
        if callable(src_tok):
            src_tok = src_tok()
        st = {"wn": wn, "evac": evac, "tokmajor": tokmajor,
              "srct": self.sb["xnf"] if src is None else self.sb[src],
              "wt": self.sb["wtb0"] if self.wslot[wn] == 0 else self.sb["wtb1"],
              "ev_tok": {0: None, 1: None}, "first": True,
              "src_tok": src_tok}
        return st

    def mm_proj_step(self, st, m):
        sb, ps = self.sb, self.ps
        wn, evac, srct, wt = st["wn"], st["evac"], st["srct"], st["wt"]
        tokmajor, ev_tok = st["tokmajor"], st["ev_tok"]
        if st["first"]:
            st["first"] = False
            self.w("tensor", self.wtok[wn][0], self.wtok[wn][1])
            if st["src_tok"] is None:
                self.w("tensor", "sv")
                self.w("tensor", "ss")
                self.w("tensor", "sp")
            else:
                for sem, tok in st["src_tok"]:
                    self.w("tensor", sem, tok)
        bank = ps["pp0"] if m % 2 == 0 else ps["pp1"]
        if ev_tok[m % 2] is not None:
            self.w("tensor", ev_tok[m % 2][0], ev_tok[m % 2][1])
        for k in range(8):
            if tokmajor:
                tb, oh = m // 2, m % 2
                self.o("tensor", lambda e, k=k, b=bank, wt=wt, s=srct, tb=tb,
                       oh=oh: e.matmul(
                    b[:, 0:TILE],
                    s[:, k * TILE + tb * 128: k * TILE + (tb + 1) * 128],
                    wt[:, k * HID + oh * 512: k * HID + (oh + 1) * 512],
                    start=(k == 0), stop=(k == 7)), "stt" if k == 7 else None)
            else:
                self.o("tensor", lambda e, m=m, k=k, b=bank, wt=wt, s=srct: e.matmul(
                    b[:, 0:TILE], wt[:, k * HID + m * 128: k * HID + m * 128 + 128],
                    s[:, k * TILE:(k + 1) * TILE],
                    start=(k == 0), stop=(k == 7)), "stt" if k == 7 else None)
        ev_tok[m % 2] = evac(m, bank)
        if m == 7:
            self.slot_free[self.wslot[wn]] = self.tok("stt")

    def mm_proj(self, wn, evac, src=None, tokmajor=False):
        st = self.mm_proj_start(wn, evac, src, tokmajor)
        for m in range(8):
            self.mm_proj_step(st, m)
        return


    def ev_copy(self, dst, off=0):
        sb = self.sb
        def f(m, bank):
            self.w("scalar", "stt")
            self.o("scalar", lambda e, m=m, b=bank: e.activation(
                out=sb[dst][:, off + m * TILE:off + (m + 1) * TILE],
                in_=b[:, 0:TILE], func=AF.Copy), "ss")
            return ("ss", self.tok("ss"))
        return f

    def ph_tile(self, it):
        sb, ps = self.sb, self.ps
        ntb = self.ntb
        tt = it % ntb
        t0 = it * TILE
        first_in_batch = (tt == 0)

        if first_in_batch and it > 0:
            self.w("vector", "stt")
            self.o("vector", lambda e: e.memset(sb["mst"][:, :], 0.0), "sv")
            self.o("gpsimd", lambda e: e.memset(sb["xbnd"][:, :], 0.0), "sp")

        if self.stop_after == "prep":
            return
        for st in range(4):
            buf = sb["xld0"] if st % 2 == 0 else sb["xld1"]
            self.w("gpsimd", "sv", getattr(self, "xld_free", {}).get(st % 2, 0))
            xsem = "sxa" if st % 2 == 0 else "sxb"
            self.o("gpsimd", lambda e, b=buf, st=st, t0=t0: e.dma_start(
                out=b[:, :], in_=self.x_hbm[t0 + st * 128: t0 + (st + 1) * 128, :]),
                xsem, 16)
            self.w("vector", xsem)
            self.w("scalar", xsem)
            self.o("vector", lambda e, b=buf: e.reduce_sum(
                out=sb["stat"][:, 0:1], in_=b[:, :], axis=AX.X), "sv")
            self.w("scalar", "sv")
            self.o("scalar", lambda e, b=buf: e.activation(
                out=sb["xsq"][:, :], in_=b[:, :], func=AF.Square,
                accum_out=sb["stat"][:, 1:2]), "ss")
            self.w("vector", "ss")
            self.o("vector", lambda e: e.tensor_scalar(
                out=sb["stat"][:, 2:3], in0=sb["stat"][:, 0:1],
                scalar1=1.0 / HID, scalar2=None, op0=ALU.mult), "sv")
            self.o("vector", lambda e: e.tensor_scalar(
                out=sb["stat"][:, 4:5], in0=sb["stat"][:, 1:2],
                scalar1=1.0 / HID, scalar2=None, op0=ALU.mult), "sv")
            self.o("vector", lambda e: e.scalar_tensor_tensor(
                out=sb["stat"][:, 5:6], in0=sb["stat"][:, 2:3],
                scalar=sb["stat"][:, 2:3], in1=sb["stat"][:, 4:5],
                op0=ALU.mult, op1=ALU.subtract), "sv")
            self.w("scalar", "sv")
            self.o("scalar", lambda e: e.activation(
                out=sb["stat"][:, 6:7], in_=sb["stat"][:, 5:6], func=AF.Ln,
                scale=-1.0, bias=EPS_LN), "ss")
            self.o("scalar", lambda e: e.activation(
                out=sb["stat"][:, 7:8], in_=sb["stat"][:, 6:7], func=AF.Exp,
                scale=-0.5), "ss")
            self.w("vector", "ss")
            self.o("vector", lambda e, b=buf: e.tensor_scalar(
                out=sb["xntm"][:, :], in0=b[:, :], scalar1=sb["stat"][:, 2:3],
                scalar2=sb["stat"][:, 7:8], op0=ALU.subtract, op1=ALU.mult), "sv")
            if not hasattr(self, "xld_free"):
                self.xld_free = {}
            self.xld_free[st % 2] = self.tok("sv")
            self.w("tensor", "sv")
            for r in range(2):
                if r == 1:
                    self.w("tensor", "sv")
                for i in range(4):
                    k = r * 4 + i
                    self.o("tensor", lambda e, k=k, i=i: e.transpose(
                        ps["ptr"][:, i * 128:(i + 1) * 128],
                        sb["xntm"][:, k * 128:(k + 1) * 128],
                        sb["cstb"][:, CI_I128:CI_I128 + 128]),
                        "stt" if i == 3 else None)
                self.w("vector", "stt")
                self.o("vector", lambda e, r=r, st=st: e.tensor_copy(
                    out=ap(sb["xnf"], 0, 128, r * 4 * TILE + st * 128,
                           [[TILE, 4], [1, 128]]),
                    in_=ps["ptr"][:, 0:512]), "sv")

        self.xnf_tok = self.tok("sv")
        if self.stop_after == "ln":
            return
        self.load_w("Wk")
        self.mm_proj("Wq", self.ev_copy("qk", 0))
        self.load_w("Wv")
        self.mm_proj("Wk", self.ev_copy("qk", 8 * TILE))
        self.load_w("Wr")

        def ev_vtm(m, bank):
            tb, oh = m // 2, m % 2
            self.w("scalar", "stt")
            self.o("scalar", lambda e, b=bank, tb=tb, oh=oh: e.activation(
                out=sb["vtm"][:, tb * HID + oh * 512: tb * HID + (oh + 1) * 512],
                in_=b[:, 0:TILE], func=AF.Copy), "ss")
            return ("ss", self.tok("ss"))
        self.mm_proj("Wv", ev_vtm, tokmajor=True)
        self.qk_tok = self.tok("ss")

        if self.stop_after == "proj":
            return
        self.ph_scales(it)

        if self.stop_after == "scales":
            return

        self.ph_ffn_mix()

        self.projq = []
        if self.stop_after is None:
            self.build_projq(it)

        for a in range(NPAIR):
            self.ph_pair(it, a)
        while self.projq:
            self.projq.pop(0)()
        if self.stop_after in ("pskv", "pskv1", "pskvnoev", "pskv1noev",
                               "builds", "solve", "post", "pairs", "ffn"):
            return

        self.ph_gate_wo(it)

        self.ph_epilogue(it)

        if self.debug and it == self.ntile - 1:
            self.ph_debug_dump()

    def ph_scales(self, it):
        sb, ps = self.sb, self.ps
        self.w("tensor", "sv")
        for k in range(8):
            self.o("tensor", lambda e, k=k: e.matmul(
                ps["pscl"][0:16, 0:TILE], sb["wabt"][:, k * 32:k * 32 + 16],
                sb["xnf"][:, k * TILE:(k + 1) * TILE],
                start=(k == 0), stop=(k == 7)))
            self.o("tensor", lambda e, k=k: e.matmul(
                ps["psq"][0:16, 0:TILE], sb["wabt"][:, k * 32 + 16:k * 32 + 32],
                sb["xnf"][:, k * TILE:(k + 1) * TILE],
                start=(k == 0), stop=(k == 7)), "stt" if k == 7 else None)
        self.w("scalar", "stt")
        self.o("scalar", lambda e: e.activation(
            out=sb["zsig"][:, 0:TILE], in_=ps["pscl"][0:16, 0:TILE], func=AF.Sigmoid,
            bias=sb["babs"][:, 0:1]), "ss")
        self.o("scalar", lambda e: e.activation(
            out=sb["zsig"][:, TILE:2 * TILE], in_=ps["psq"][0:16, 0:TILE], func=AF.Sigmoid,
            bias=sb["babs"][:, 1:2]), "ss")
        self.w("vector", "ss")
        self.o("vector", lambda e: e.tensor_scalar(
            out=sb["t16a"][:, :], in0=sb["zsig"][0:16, 0:TILE], scalar1=1e-35,
            scalar2=None, op0=ALU.max), "sv")
        self.w("scalar", "sv")
        self.o("scalar", lambda e: e.activation(
            out=sb["la"][:, :], in_=sb["t16a"][:, :], func=AF.Ln), "ss")
        self.w("vector", "ss")
        for c in range(NCH):
            self.o("vector", lambda e, c=c: e.tensor_tensor_scan(
                out=sb["lc"][:, c * C:(c + 1) * C],
                data0=sb["la"][:, c * C:(c + 1) * C],
                data1=sb["la"][:, c * C:(c + 1) * C], initial=0.0,
                op0=ALU.add, op1=ALU.bypass), "sv")
        self.w("scalar", "sv")
        self.o("scalar", lambda e: e.activation(
            out=sb["gamv"][:, :], in_=sb["lc"][:, :], func=AF.Exp), "ss")
        self.w("vector", "ss")
        self.o("vector", lambda e: e.tensor_scalar(
            out=sb["t16a"][:, :], in0=sb["lc"][:, :], scalar1=-1.0, scalar2=CLAMP,
            op0=ALU.mult, op1=ALU.min), "sv")
        self.w("scalar", "sv")
        self.o("scalar", lambda e: e.activation(
            out=sb["uv"][:, :], in_=sb["t16a"][:, :], func=AF.Exp), "ss")

        self.w("vector", "ss", self.qk_tok)
        for (nm, soff) in [("nqv", 0), ("nkv", 8 * TILE)]:
            for k in range(8):
                self.w("vector", "stt")
                self.o("vector", lambda e, k=k, soff=soff: e.tensor_tensor(
                    out=sb["sq"][:, :],
                    in0=sb["qk"][:, soff + k * TILE:soff + (k + 1) * TILE],
                    in1=sb["qk"][:, soff + k * TILE:soff + (k + 1) * TILE],
                    op=ALU.mult), "sv")
                self.w("tensor", "sv")
                self.o("tensor", lambda e, k=k: e.matmul(
                    ps["psq"][0:16, 0:TILE],
                    sb["cstb"][:, CI_IND + 16 * k: CI_IND + 16 * (k + 1)],
                    sb["sq"][:, :], start=(k == 0), stop=(k == 7)), "stt")
            self.w("vector", "stt")
            self.w("vector", "ss")
            self.o("vector", lambda e: e.tensor_scalar(
                out=sb["t16a"][:, :], in0=ps["psq"][0:16, 0:TILE],
                scalar1=1e-12, scalar2=None, op0=ALU.max), "sv")
            self.w("scalar", "sv")
            self.o("scalar", lambda e: e.activation(
                out=sb["t16b"][:, :], in_=sb["t16a"][:, :], func=AF.Ln), "ss")
            self.o("scalar", lambda e, nm=nm: e.activation(
                out=sb[nm][:, :], in_=sb["t16b"][:, :], func=AF.Exp, scale=-0.5), "ss")

        self.w("vector", "ss")
        self.o("vector", lambda e: e.tensor_tensor(
            out=sb["t16a"][:, :], in0=sb["nkv"][:, :], in1=sb["nkv"][:, :],
            op=ALU.mult), "sv")
        self.o("vector", lambda e: e.tensor_tensor(
            out=sb["t16a"][:, :], in0=sb["t16a"][:, :], in1=sb["uv"][:, :],
            op=ALU.mult), "sv")
        self.o("vector", lambda e: e.tensor_tensor(
            out=sb["t16a"][:, :], in0=sb["t16a"][:, :], in1=sb["gamv"][:, :],
            op=ALU.mult), "sv")
        self.o("vector", lambda e: e.tensor_tensor(
            out=sb["chiv"][:, :], in0=sb["t16a"][:, :], in1=sb["zsig"][0:16, TILE:2 * TILE],
            op=ALU.mult), "sv")
        self.o("vector", lambda e: e.tensor_tensor(
            out=sb["t16b"][:, :], in0=sb["nkv"][:, :], in1=sb["gamv"][:, :],
            op=ALU.mult), "sv")
        self.o("vector", lambda e: e.reciprocal(
            out=sb["zetv"][:, :], in_=sb["t16b"][:, :]), "sv")
        self.o("vector", lambda e: e.tensor_tensor(
            out=sb["omv"][:, :], in0=sb["nqv"][:, :], in1=sb["gamv"][:, :],
            op=ALU.mult), "sv")

        self.w("tensor", "sv")
        for (vn, tn) in [("chiv", "chiT"), ("zetv", "zetT"), ("omv", "omT")]:
            for blk in range(4):
                self.o("tensor", lambda e, vn=vn, blk=blk: e.transpose(
                    ps["pscl"][:, 256 + blk * 16: 256 + (blk + 1) * 16],
                    sb[vn][:, blk * 128:(blk + 1) * 128],
                    sb["cstf"][0:16, CI_I128:CI_I128 + 16]),
                    "stt" if blk == 3 else None)
            self.w("vector", "stt")
            self.o("vector", lambda e, tn=tn: e.tensor_copy(
                out=sb[tn][:, :], in_=ps["pscl"][:, 256:256 + 64]), "sv")
            self.w("tensor", "sv")

        self.w("gpsimd", "sv")
        self.o("gpsimd", lambda e: e.dma_start(
            out=sb["gamrow"][0:1, 0:128],
            in_=ap(sb["gamv"], 0, 16, C - 1, [[C, NCH]]),
            allow_slow_non_contiguous=True), "sgr", 16)
        self.w("tensor", "sgr")
        self.o("tensor", lambda e: e.matmul(
            ps["pscl"][:, 384:384 + 128], sb["cstf"][0:1, CI_ONES:CI_ONES + 128],
            sb["gamrow"][0:1, 0:128], start=True, stop=True), "stt")
        self.w("vector", "stt")
        self.o("vector", lambda e: e.tensor_copy(
            out=sb["gamrep"][:, :], in_=ps["pscl"][:, 384:384 + 128]), "sv")

    def pq_run(self, n):
        for _ in range(n):
            if self.projq:
                self.projq.pop(0)()

    def ph_pair(self, it, a):
        sb, ps = self.sb, self.ps
        if not hasattr(self, "projq"):
            self.projq = []

        def oin_gather(par, pc):
            return ap(self.sb["oin"], pc, 64, par * 64, [[512, 2], [128, 4], [1, 64]])

        def um_gather(par, phh, cc):
            return ap(self.sb["um"], phh, 64, cc * 512, [[64, 8], [1, 64]])
        cstb = self.sb["cstb"]
        c0 = 2 * a
        if not hasattr(self, "btok"):
            self.btok = {}
        btok = self.btok

        def bank_wait(bn):
            t = btok.get(bn)
            if t is not None:
                self.w("tensor", t[0], t[1])

        def chT(h):
            return sb["chiT"][:, a * 16 + h: a * 16 + h + 1]

        def zeT(h):
            return sb["zetT"][:, a * 16 + h: a * 16 + h + 1]

        POOL = [("ppsk", "psl1"), ("psq", "psl0")]

        nrounds = 1 if self.stop_after in ("pskv1", "pskv1noev") else 8
        noev = self.stop_after in ("pskvnoev", "pskv1noev")
        for rnd in range(nrounds):
            be, bo = POOL[rnd % 2]
            bank_wait(be)
            bank_wait(bo)
            for cc in (0, 1):
                pc = 64 * cc
                for si in range(2):
                    h = 2 * rnd + si
                    ph = 64 * (h % 2)
                    bank = ps[be] if h % 2 == 0 else ps[bo]
                    cl = (h // 2) * TILE + (c0 + cc) * C
                    self.o("tensor", lambda e, ph=ph, cl=cl, pc=pc, b=bank: e.matmul(
                        b[pc:pc + 64, 0:128],
                        sb["qk"][ph:ph + 64, 8 * TILE + cl:8 * TILE + cl + 64],
                        ap(sb["qk"], ph, 64, cl, [[8 * TILE, 2], [1, 64]]),
                        start=True, stop=True, tile_position=(ph, pc)))
                    self.o("tensor", lambda e, ph=ph, cl=cl, pc=pc, b=bank: e.matmul(
                        b[pc:pc + 64, 128:192],
                        sb["qk"][ph:ph + 64, 8 * TILE + cl:8 * TILE + cl + 64],
                        cstb[ph:ph + 64, CI_ISTK:CI_ISTK + 64],
                        start=True, stop=True, tile_position=(ph, pc)),
                        "stt" if (si == 1 and cc == 1) else None)
            if noev:
                continue
            mmtok = self.tok("stt")

            def dve_evacs(h, bn):
                self.o("vector", lambda e, h=h, b=ps[bn]: e.scalar_tensor_tensor(
                    out=ap(sb["csk"], 0, 128, h * 192, [[1, 192]]),
                    in0=ap(b, 0, 128, 0, [[1, 192]]),
                    scalar=chT(h),
                    in1=ap(sb["cstf"], 0, 128, CI_M3B, [[1, 192]]),
                    op0=ALU.mult, op1=ALU.mult), "sv")
                self.o("vector", lambda e, h=h, b=ps[bn]: e.tensor_tensor(
                    out=ap(sb["susl"], 0, 128, h * 64, [[1, 64]]),
                    in0=ap(b, 0, 128, 64, [[1, 64]]),
                    in1=ap(sb["cstf"], 0, 128, CI_MSL, [[1, 64]]),
                    op=ALU.mult), "sv")

            def sc_evacs(h, bn):
                self.o("scalar", lambda e, h=h, a=a: e.activation(
                    out=ap(sb["yA"], 0, 128, h * 128, [[1, 64]]),
                    in_=ap(sb["vtm"], 0, 128, a * HID + h * 64, [[1, 64]]),
                    func=AF.Copy, scale=zeT(h)), "ss")
                self.o("scalar", lambda e, h=h, b=ps[bn]: e.activation(
                    out=ap(sb["yA"], 0, 128, h * 128 + 64, [[1, 64]]),
                    in_=ap(b, 0, 128, 128, [[1, 64]]),
                    func=AF.Copy, scale=-1.0), "ss")
                self.o("scalar", lambda e, h=h, b=ps[bn]: e.activation(
                    out=ap(sb["ktm"], 0, 128, h * 64, [[1, 64]]),
                    in_=ap(b, 0, 128, 128, [[1, 64]]),
                    func=AF.Copy), "ss")

            he, ho = 2 * rnd, 2 * rnd + 1
            self.w("vector", "stt", mmtok)
            self.w("scalar", "stt", mmtok)
            dve_evacs(he, be)
            sc_evacs(ho, bo)
            self.w("vector", "ss")
            self.w("scalar", "sv")
            dve_evacs(ho, bo)
            sc_evacs(he, be)
            btok[be] = ("ss", self.tok("ss"))
            btok[bo] = ("sv", self.tok("sv"))
        if self.stop_after in ("pskv", "pskv1", "pskvnoev", "pskv1noev"):
            return
        self.pq_run(2)

        def sl_slice(h, pc):
            return ap(sb["susl"], pc, 64, h * 64, [[1, 64]])

        def sux_slice(h, pc):
            return ap(sb["csk"], pc, 64, h * 192 + 64, [[1, 64]])

        def pwx_slice(l, h, pc):
            return ap(sb["pwx"], pc, 64, l * H * 64 + h * 64, [[1, 64]])

        def pwt_slice(l, h, pc):
            return ap(sb["pwt"], pc, 64, (l % 2) * H * 64 + h * 64, [[1, 64]])

        self.w("tensor", "sv")
        self.w("tensor", "ss")
        wv = 0
        for L in range(5):
            for part in range(2):
                if L == 4 and part == 1:
                    break
                be, bo = POOL[wv % 2]
                wv += 1
                bank_wait(be)
                bank_wait(bo)
                if L > 0 and part == 0:
                    self.lvl_tok = (self.tok("sv"), self.tok("ss"))
                if L > 0:
                    self.w("tensor", "sv", self.lvl_tok[0])
                    self.w("tensor", "ss", self.lvl_tok[1])
                for cc in (0, 1):
                    pc = 64 * cc
                    for h in range(H):
                        bank = ps[be] if h < 8 else ps[bo]
                        colb = (h % 8) * 64
                        if L == 0:
                            stat = sl_slice(h, pc) if part == 0 else sux_slice(h, pc)
                            mv = sux_slice(h, pc) if part == 0 else sl_slice(h, pc)
                        else:
                            stat = pwt_slice(L - 1, h, pc) if part == 0 else pwx_slice(L - 1, h, pc)
                            mv = pwx_slice(L - 1, h, pc) if part == 0 else pwt_slice(L - 1, h, pc)
                        self.o("tensor", lambda e, st=stat, mv=mv, pc=pc, colb=colb,
                               bank=bank: e.matmul(
                            bank[pc:pc + 64, colb:colb + 64], st, mv,
                            start=True, stop=True, tile_position=(pc, pc)),
                            "stt" if (h == H - 1 and cc == 1) else None)
                self.w("vector", "stt")
                self.w("scalar", "stt")
                for bi, bn in enumerate((be, bo)):
                    if part == 0:
                        self.o("vector", lambda e, L=L, bi=bi, bn=bn: e.tensor_tensor(
                            out=ap(sb["pwx"], 0, 128, L * H * 64 + bi * 512, [[1, 512]]),
                            in0=ap(ps[bn], 0, 128, 0, [[1, 512]]),
                            in1=ap(sb["chiT"], 0, 128, a * 16 + bi * 8, [[1, 8], [0, 64]]),
                            op=ALU.mult), "sv")
                    else:
                        self.o("scalar", lambda e, L=L, bi=bi, bn=bn: e.activation(
                            out=ap(sb["pwt"], 0, 128, (L % 2) * H * 64 + bi * 512,
                                   [[1, 512]]),
                            in_=ap(ps[bn], 0, 128, 0, [[1, 512]]),
                            func=AF.Copy), "ss")
                    btok[bn] = ("sv", self.tok("sv")) if part == 0 else \
                        ("ss", self.tok("ss"))
        if self.stop_after == "builds":
            return
        self.pq_run(2)

        stages = [(4, ALU.add), (3, ALU.add), (2, ALU.add), (1, ALU.add),
                  (0, ALU.add), (-1, ALU.subtract)]
        cur, nxt = "yA", "yB"
        self.w("tensor", "ss")
        bi = 0
        qtok = {q: self.tok("sv") for q in range(4)}
        for sidx, (L, op) in enumerate(stages):
            for quad in range(4):
                bn = [b for pr in POOL for b in pr][bi % 4]
                bi += 1
                bank_wait(bn)
                self.w("tensor", "sv", qtok[quad])
                bank = ps[bn]
                for hh in range(4):
                    h = quad * 4 + hh
                    for cc in (0, 1):
                        pc = 64 * cc
                        stat = sux_slice(h, pc) if L < 0 else pwx_slice(L, h, pc)
                        self.o("tensor", lambda e, st=stat, h=h, hh=hh, pc=pc,
                               bank=bank, cur=cur: e.matmul(
                            bank[pc:pc + 64, hh * 128:(hh + 1) * 128], st,
                            ap(sb[cur], pc, 64, h * 128, [[1, 128]]),
                            start=True, stop=True, tile_position=(pc, pc)),
                            "stt" if (hh == 3 and cc == 1) else None)
                self.w("vector", "stt")
                if op == ALU.add:
                    self.o("vector", lambda e, quad=quad, bank=bank, cur=cur,
                           nxt=nxt: e.tensor_tensor(
                        out=ap(sb[nxt], 0, 128, quad * 512, [[1, 512]]),
                        in0=ap(bank, 0, 128, 0, [[1, 512]]),
                        in1=ap(sb[cur], 0, 128, quad * 512, [[1, 512]]),
                        op=ALU.add), "sv")
                else:
                    self.o("vector", lambda e, quad=quad, bank=bank, cur=cur,
                           nxt=nxt: e.tensor_tensor(
                        out=ap(sb[nxt], 0, 128, quad * 512, [[1, 512]]),
                        in0=ap(sb[cur], 0, 128, quad * 512, [[1, 512]]),
                        in1=ap(bank, 0, 128, 0, [[1, 512]]),
                        op=ALU.subtract), "sv")
                btok[bn] = ("sv", self.tok("sv"))
                qtok[quad] = self.tok("sv")
            cur, nxt = nxt, cur
        zf = cur

        self.o("vector", lambda e, zf=zf: e.tensor_tensor(
            out=ap(sb["zknx"], 0, 128, 0, [[64, 16], [1, 64]]),
            in0=ap(sb[zf], 0, 128, 64, [[128, 16], [1, 64]]),
            in1=ap(sb["chiT"], 0, 128, a * 16, [[1, 16], [0, 64]]),
            op=ALU.mult), "sv")
        zknx_tok = self.tok("sv")
        if self.stop_after == "solve":
            return
        self.pq_run(2)

        def bankcol(h):
            return (h // 2) * 64

        post_in_tok = self.tok("sv")

        def post_wave(mm_fn, evac_fn, need_tok):
            be, bo = POOL[self.pwave % 2]
            self.pwave += 1
            bank_wait(be)
            bank_wait(bo)
            self.w("tensor", "sv", need_tok)
            mm_fn(ps[be], ps[bo])
            self.w("vector", "stt")
            evac_fn(be, bo)
            btok[be] = ("sv", self.tok("sv"))
            btok[bo] = ("sv", self.tok("sv"))

        self.pwave = getattr(self, "pwave", 0)

        def mm_oin(bA, bB):
            for h in range(H):
                bk = bA if h < 8 else bB
                cb = (h % 8) * 64
                for cc in (0, 1):
                    pc = 64 * cc
                    self.o("tensor", lambda e, h=h, pc=pc, b=bk, cb=cb, zf=zf: e.matmul(
                        b[pc:pc + 64, cb:cb + 64],
                        ap(sb["csk"], pc, 64, h * 192, [[1, 64]]),
                        ap(sb[zf], pc, 64, h * 128, [[1, 64]]),
                        start=True, stop=True, tile_position=(pc, pc)),
                        "stt" if (h == H - 1 and cc == 1) else None)

        def ev_oin(be, bo):
            for g, bn in enumerate((be, bo)):
                self.o("vector", lambda e, g=g, bn=bn, a=a: e.tensor_tensor(
                    out=ap(sb["oin"], 0, 128, g * 512, [[1, 512]]),
                    in0=ap(ps[bn], 0, 128, 0, [[1, 512]]),
                    in1=ap(sb["omT"], 0, 128, a * 16 + g * 8, [[1, 8], [0, 64]]),
                    op=ALU.mult), "sv")
        post_wave(mm_oin, ev_oin, post_in_tok)

        def mm_qef(bA, bB):
            for h in range(H):
                ph = 64 * (h % 2)
                for cc in (0, 1):
                    pc = 64 * cc
                    b = bA if cc == 0 else bB
                    self.o("tensor", lambda e, h=h, ph=ph, pc=pc, b=b, zf=zf: e.matmul(
                        b[ph:ph + 64, bankcol(h):bankcol(h) + 64],
                        ap(sb[zf], pc, 64, h * 128 + 64, [[1, 64]]),
                        ap(sb["csk"], pc, 64, h * 192, [[1, 64]]),
                        start=True, stop=True, tile_position=(pc, ph)),
                        "stt" if (h == H - 1 and cc == 1) else None)

        def ev_qef(be, bo):
            for cc, bn in enumerate((be, bo)):
                self.o("vector", lambda e, cc=cc, bn=bn, c0=c0: e.tensor_tensor(
                    out=ap(sb["qef"], 0, 128, cc * 512, [[1, 512]]),
                    in0=ap(ps[bn], 0, 128, 0, [[1, 512]]),
                    in1=ap(sb["qk"], 0, 128, (c0 + cc) * C, [[TILE, 8], [1, 64]]),
                    op=ALU.add), "sv")
        post_wave(mm_qef, ev_qef, post_in_tok)

        def mm_gtm(bA, bB):
            for h in range(H):
                ph = 64 * (h % 2)
                for cc in (0, 1):
                    pc = 64 * cc
                    b = bA if cc == 0 else bB
                    self.o("tensor", lambda e, h=h, ph=ph, pc=pc, b=b: e.matmul(
                        b[ph:ph + 64, bankcol(h):bankcol(h) + 64],
                        ap(sb["zknx"], pc, 64, h * 64, [[1, 64]]),
                        ap(sb["ktm"], pc, 64, h * 64, [[1, 64]]),
                        start=True, stop=True, tile_position=(pc, ph)),
                        "stt" if (h == H - 1 and cc == 1) else None)

        def ev_gtm(be, bo):
            for cc, bn in enumerate((be, bo)):
                self.o("vector", lambda e, cc=cc, bn=bn: e.tensor_tensor(
                    out=ap(sb["gtm"], 0, 128, cc * 512, [[1, 512]]),
                    in0=ap(ps[bn], 0, 128, 0, [[1, 512]]),
                    in1=ap(sb["cstb"], 0, 128, CI_IREP8, [[1, 512]]),
                    op=ALU.add), "sv")
        post_wave(mm_gtm, ev_gtm, zknx_tok)

        def mm_um(bA, bB):
            for h in range(H):
                ph = 64 * (h % 2)
                for cc in (0, 1):
                    pc = 64 * cc
                    b = bA if cc == 0 else bB
                    self.o("tensor", lambda e, h=h, ph=ph, pc=pc, b=b, zf=zf: e.matmul(
                        b[ph:ph + 64, bankcol(h):bankcol(h) + 64],
                        ap(sb["csk"], pc, 64, h * 192 + 128, [[1, 64]]),
                        ap(sb[zf], pc, 64, h * 128, [[1, 64]]),
                        start=True, stop=True, tile_position=(pc, ph)),
                        "stt" if (h == H - 1 and cc == 1) else None)

        def ev_um(be, bo):
            for cc, bn in enumerate((be, bo)):
                for half in (0, 1):
                    self.o("vector", lambda e, cc=cc, half=half, bn=bn, c0=c0: e.tensor_tensor(
                        out=ap(sb["um"], 64 * half, 64, cc * 512, [[1, 512]]),
                        in0=ap(ps[bn], 64 * half, 64, 0, [[1, 512]]),
                        in1=ap(sb["gamrep"], 64 * half, 64, half * 8 + c0 + cc,
                               [[16, 8], [0, 64]]),
                        op=ALU.mult), "sv")
        post_wave(mm_um, ev_um, post_in_tok)
        if self.stop_after == "post":
            return
        self.pq_run(2)

        for cc in (0, 1):
            pc = 64 * cc
            for bn in ("psq", "ppsk", "psl0", "psl1"):
                bank_wait(bn)
            for h in range(H):
                ph = 64 * (h % 2)
                bank = ps["psq"] if h % 2 == 0 else ps["ppsk"]
                self.o("tensor", lambda e, h=h, ph=ph, pc=pc, cc=cc, bank=bank: e.matmul(
                    bank[pc:pc + 64, bankcol(h):bankcol(h) + 64],
                    ap(sb["qef"], ph, 64, cc * 512 + bankcol(h), [[1, 64]]),
                    ap(sb["mst"], ph, 64, h * 64, [[1, 64]]),
                    start=True, stop=True, tile_position=(ph, pc)),
                    "stt" if h == H - 1 else None)
            owave_tok = self.tok("stt")
            for h in range(H):
                ph = 64 * (h % 2)
                bank = ps["psl0"] if h % 2 == 0 else ps["psl1"]
                self.o("tensor", lambda e, h=h, ph=ph, cc=cc, bank=bank: e.matmul(
                    bank[ph:ph + 64, bankcol(h):bankcol(h) + 64],
                    ap(sb["gtm"], ph, 64, cc * 512 + bankcol(h), [[1, 64]]),
                    ap(sb["mst"], ph, 64, h * 64, [[1, 64]]),
                    start=True, stop=True, tile_position=(ph, ph)),
                    "stt" if h == H - 1 else None)
            self.w("vector", "stt", owave_tok)
            for par, bn in ((0, "psq"), (1, "ppsk")):
                self.o("vector", lambda e, par=par, bn=bn, pc=pc, a=a: e.tensor_tensor(
                    out=ap(sb["xsq"], pc, 64, 0, [[1, 512]]),
                    in0=ap(ps[bn], pc, 64, 0, [[1, 512]]),
                    in1=ap(sb["omT"], pc, 64, a * 16 + par, [[2, 8], [0, 64]]),
                    op=ALU.mult), "sv")
                self.o("vector", lambda e, par=par, pc=pc, a=a: e.tensor_tensor(
                    out=ap(sb["otm"], pc, 64, a * HID + par * 64, [[128, 8], [1, 64]]),
                    in0=ap(sb["xsq"], pc, 64, 0, [[1, 512]]),
                    in1=oin_gather(par, pc),
                    op=ALU.add), "sv")
            self.w("vector", "stt")
            for par, bn in ((0, "psl0"), (1, "psl1")):
                phh = 64 * par
                self.o("vector", lambda e, par=par, bn=bn, phh=phh, cc=cc, c0=c0:
                       e.tensor_tensor(
                    out=ap(sb["xsq"], phh, 64, 512, [[1, 512]]),
                    in0=ap(ps[bn], phh, 64, 0, [[1, 512]]),
                    in1=ap(sb["gamrep"], phh, 64, par * 8 + c0 + cc, [[16, 8], [0, 64]]),
                    op=ALU.mult), "sv")
                self.o("vector", lambda e, par=par, phh=phh, cc=cc: e.tensor_tensor(
                    out=ap(sb["mst"], phh, 64, par * 64, [[128, 8], [1, 64]]),
                    in0=ap(sb["xsq"], phh, 64, 512, [[1, 512]]),
                    in1=um_gather(par, phh, cc),
                    op=ALU.add), "sv")
            for bn in ("psq", "ppsk", "psl0", "psl1"):
                btok[bn] = ("sv", self.tok("sv"))
            self.pq_run(1)

    def ph_ffn_mix(self):
        sb = self.sb
        self.w("gpsimd", "swa", 48)
        self.w("gpsimd", "stt")
        self.w("gpsimd", "sv")
        for m in range(8):
            self.o("gpsimd", lambda e, m=m: e.tensor_tensor(
                out=sb["sq"][:, 1:TILE], in0=sb["xnf"][:, m * TILE + 1:(m + 1) * TILE],
                in1=sb["xnf"][:, m * TILE:(m + 1) * TILE - 1], op=ALU.subtract), "sp")
            self.o("gpsimd", lambda e, m=m: e.tensor_tensor(
                out=sb["sq"][:, 0:1], in0=sb["xnf"][:, m * TILE:m * TILE + 1],
                in1=sb["xbnd"][:, m:m + 1], op=ALU.subtract), "sp")
            self.o("gpsimd", lambda e, m=m: e.tensor_copy(
                out=sb["ffb"][:, m * TILE + 1:(m + 1) * TILE],
                in_=sb["xnf"][:, m * TILE:(m + 1) * TILE - 1]), "sp")
            self.o("gpsimd", lambda e, m=m: e.tensor_copy(
                out=sb["ffb"][:, m * TILE:m * TILE + 1], in_=sb["xbnd"][:, m:m + 1]), "sp")
            for dst, col in (("ffa", 2 * m), ("gf", 2 * m + 1)):
                self.o("gpsimd", lambda e, m=m, dst=dst, col=col: e.tensor_tensor(
                    out=sb[dst][:, m * TILE:(m + 1) * TILE], in0=sb["sq"][:, :],
                    in1=ap(sb["mus"], 0, 128, col, [[0, TILE]]),
                    op=ALU.mult), "sp")
                self.o("gpsimd", lambda e, m=m, dst=dst: e.tensor_tensor(
                    out=sb[dst][:, m * TILE:(m + 1) * TILE],
                    in0=sb[dst][:, m * TILE:(m + 1) * TILE],
                    in1=sb["ffb"][:, m * TILE:(m + 1) * TILE],
                    op=ALU.add), "sp")
        self.o("gpsimd", lambda e: e.tensor_copy(
            out=sb["xbnd"][:, 0:8],
            in_=ap(sb["xnf"], 0, 128, TILE - 1, [[TILE, 8], [1, 1]])), "sp")
        self.mix_tok = self.tok("sp")

    def build_projq(self, it):
        sb = self.sb
        q = self.projq

        def ev_r(m, bank):
            self.w("scalar", "stt")
            self.o("scalar", lambda e, m=m, b=bank: e.activation(
                out=sb["ffb"][:, m * TILE:(m + 1) * TILE], in_=b[:, 0:TILE],
                func=AF.Sigmoid), "ss")
            return ("ss", self.tok("ss"))

        def ev_k(m, bank):
            self.w("scalar", "stt")
            self.o("scalar", lambda e, m=m, b=bank: e.activation(
                out=sb["ffa"][:, m * TILE:(m + 1) * TILE], in_=b[:, 0:TILE],
                func=AF.Relu), "ss")
            t = ("ss", self.tok("ss"))
            self.w("gpsimd", "ss")
            self.o("gpsimd", lambda e, m=m: e.tensor_tensor(
                out=sb["ffa"][:, m * TILE:(m + 1) * TILE],
                in0=sb["ffa"][:, m * TILE:(m + 1) * TILE],
                in1=sb["ffa"][:, m * TILE:(m + 1) * TILE], op=ALU.mult), "sp")
            if m == 7:
                self.kact_tok = [t, ("sp", self.tok("sp"))]
            return t

        def ev_v(m, bank):
            self.w("vector", "stt")
            self.w("vector", "ss")
            self.o("vector", lambda e, m=m, b=bank: e.tensor_tensor(
                out=sb["branch"][:, m * TILE:(m + 1) * TILE], in0=b[:, 0:TILE],
                in1=sb["ffb"][:, m * TILE:(m + 1) * TILE], op=ALU.mult), "sv")
            if m == 7:
                self.vout_tok = self.tok("sv")
            return ("sv", self.tok("sv"))

        def ev_g(m, bank):
            self.w("scalar", "stt")
            self.o("scalar", lambda e, m=m, b=bank: e.activation(
                out=sb["gf"][:, m * TILE:(m + 1) * TILE], in_=b[:, 0:TILE],
                func=AF.Silu), "ss")
            return ("ss", self.tok("ss"))

        mixt = [("sp", self.mix_tok), ("ss", self.tok("ss")), ("sv", self.tok("sv"))]
        plan = [("load", "Wkf"), ("proj", "Wr", ev_r, "ffa", mixt),
                ("load", "Wvf"), ("proj", "Wkf", ev_k, "gf", mixt),
                ("load", "Wg"),
                ("proj", "Wvf", ev_v, "ffa", lambda: self.kact_tok),
                ("load", "Wo"),
                ("proj", "Wg", ev_g, None,
                 lambda: [("sv", max(self.xnf_tok, self.vout_tok))])]
        if it < self.ntile - 1:
            plan.append(("load", "Wq"))
        for item in plan:
            if item[0] == "load":
                q.append(lambda wn=item[1]: self.load_w(wn))
            else:
                _, wn, ev, src, stok = item
                holder = {}

                def start(wn=wn, ev=ev, src=src, stok=stok, holder=holder):
                    holder["st"] = self.mm_proj_start(wn, ev, src=src, src_tok=stok)

                q.append(start)
                for m in range(8):
                    q.append(lambda m=m, holder=holder: self.mm_proj_step(holder["st"], m))

    def ph_gate_wo(self, it):
        sb, ps = self.sb, self.ps
        self.w("tensor", "sv")
        self.w("vector", "ss")
        for k in range(8):
            for blk in range(4):
                self.o("tensor", lambda e, k=k, blk=blk: e.transpose(
                    ps["ptr"][:, blk * 128:(blk + 1) * 128],
                    sb["otm"][:, blk * HID + k * 128: blk * HID + (k + 1) * 128],
                    sb["cstb"][:, CI_I128:CI_I128 + 128]),
                    "stt" if blk == 3 else None)
            self.w("vector", "stt")
            self.o("vector", lambda e, k=k: e.tensor_tensor(
                out=sb["ffb"][:, k * TILE:(k + 1) * TILE], in0=ps["ptr"][:, 0:512],
                in1=sb["gf"][:, k * TILE:(k + 1) * TILE], op=ALU.mult), "sv")
            self.w("tensor", "sv")

        def ev_o(m, bank):
            self.w("vector", "stt")
            self.o("vector", lambda e, m=m, b=bank: e.tensor_tensor(
                out=sb["branch"][:, m * TILE:(m + 1) * TILE], in0=b[:, 0:TILE],
                in1=sb["branch"][:, m * TILE:(m + 1) * TILE], op=ALU.add), "sv")
            return ("sv", self.tok("sv"))
        self.mm_proj("Wo", ev_o, src="ffb")

    def ph_epilogue(self, it):
        sb, ps = self.sb, self.ps
        t0 = it * TILE
        self.w("tensor", "sp")
        for st in range(4):
            buf = sb["xld0"] if st % 2 == 0 else sb["xld1"]
            xsem = "sxa" if st % 2 == 0 else "sxb"
            self.w("gpsimd", "sv", self.xld_free.get(st % 2, 0))
            self.o("gpsimd", lambda e, b=buf, st=st, t0=t0: e.dma_start(
                out=b[:, :], in_=self.x_hbm[t0 + st * 128: t0 + (st + 1) * 128, :]),
                xsem, 16)
            self.w("tensor", "sv")
            for r in range(2):
                if r == 1:
                    self.w("tensor", "sv")
                for i in range(4):
                    k = r * 4 + i
                    self.o("tensor", lambda e, k=k, i=i, st=st: e.transpose(
                        ps["ptr"][:, i * 128:(i + 1) * 128],
                        ap(sb["branch"], 0, 128, k * TILE + st * 128, [[1, 128]]),
                        sb["cstb"][:, CI_I128:CI_I128 + 128]),
                        "stt" if i == 3 else None)
                self.w("vector", "stt")
                self.w("vector", xsem)
                if r == 0:
                    self.w("vector", "sdo", self.tok("sdo"))
                self.o("vector", lambda e, r=r, b=buf: e.tensor_tensor(
                    out=sb["outs"][:, r * 512:(r + 1) * 512], in0=ps["ptr"][:, 0:512],
                    in1=b[:, r * 512:(r + 1) * 512], op=ALU.add), "sv")
                self.w("tensor", "sv")
            self.w("gpsimd", "sv")
            self.o("gpsimd", lambda e, st=st, t0=t0: e.dma_start(
                out=self.out_hbm[t0 + st * 128: t0 + (st + 1) * 128, :],
                in_=sb["outs"][:, :]), "sdo", 16)
            self.xld_free[st % 2] = self.tok("sv")

    def ph_debug_dump(self):
        sb = self.sb
        self.w("gpsimd", "sv")
        self.w("gpsimd", "ss")
        for n, (src, dst) in self.dbg.items():
            self.o("gpsimd", lambda e, s=src, d=dst: e.dma_start(
                out=d, in_=sb[s][:, :]), "sdo", 16)

    def ph_finish(self):
        self.w("gpsimd", "sdo")


def _prep_inputs(core, x, ins):
    xs = np.ascontiguousarray(x[4 * core: 4 * core + 4].reshape(-1, HID))
    d = {"x": xs}
    d["W_Wq"] = ins["L0_attn_Wq"]
    d["W_Wk"] = ins["L0_attn_Wk"]
    d["W_Wv"] = ins["L0_attn_Wv"]
    d["W_Wg"] = ins["L0_attn_Wg"]
    d["W_Wo"] = ins["L0_attn_Wo"]
    d["W_Wr"] = ins["L0_ffn_W_r"]
    d["W_Wkf"] = ins["L0_ffn_W_k"]
    d["W_Wvf"] = ins["L0_ffn_W_v"]
    d["Wab"] = np.concatenate([ins["L0_attn_Wa"], ins["L0_attn_Wb"]], axis=0)
    bab = np.zeros((16, 2), dtype=np.float32)
    bab[:, 0] = ins["L0_attn_ba"]
    bab[:, 1] = ins["L0_attn_bb"]
    d["bab"] = bab
    mu = np.zeros((128, 16), dtype=np.float32)
    for k in range(8):
        mu[:, 2 * k] = ins["L0_ffn_mu_r"][k * 128:(k + 1) * 128]
        mu[:, 2 * k + 1] = ins["L0_ffn_mu_k"][k * 128:(k + 1) * 128]
    d["mu"] = mu
    d["consts"] = _host_consts()
    return {k: np.ascontiguousarray(v) for k, v in d.items()}


def kernel(**inputs):
    x = inputs["L0_attn_x"]
    B = x.shape[0]
    p = Prog(nb=B // NCORES, t_len=x.shape[1])
    nc = p.build()
    in_maps = [_prep_inputs(c, x, inputs) for c in range(NCORES)]
    res = run_bass_kernel_spmd(nc, in_maps, core_ids=list(range(NCORES)))
    out = np.concatenate(
        [r["out"].reshape(B // NCORES, x.shape[1], HID) for r in res.results], axis=0)
    return out.astype(np.float32)
